# version 02
# baseline (speedup 1.0000x reference)
import contextlib
import numpy as np
import concourse.bass as bass
import concourse.mybir as mybir
from concourse.bass_utils import run_bass_kernel_spmd

F32 = mybir.dt.float32
F16 = mybir.dt.float16
BF16 = mybir.dt.bfloat16
AF = mybir.ActivationFunctionType
OP = mybir.AluOpType

NCORES = 8
B, S, D = 4, 4096, 2048
TOK = 2048
NT = TOK // 128
DC = D // 128
INTER = 8192
IC = INTER // 128
N_HEADS = 16
LN_EPS = 1e-5
RMS_EPS = 1e-6
GN_EPS = 1e-5


class Sem:
    __slots__ = ("h", "n")
    def __init__(self, h):
        self.h = h
        self.n = 0

    def inc(self, instr, k=1):
        instr.then_inc(self.h, k)
        self.n += k
        return instr


class Ctx:
    def __init__(self, nc, block, sems):
        self.nc = nc
        self.block = block
        for k, v in sems.items():
            setattr(self, k, v)


_CTX_N = [0]
def make_ctx(nc, block, stack):
    names = ["qS", "qG", "qA", "sPE", "sDVE", "sACT", "sPOOL"]
    pfx = f"c{_CTX_N[0]}_"
    _CTX_N[0] += 1
    sems = {nm: Sem(stack.enter_context(nc.semaphore(pfx + nm))) for nm in names}
    return Ctx(nc, block, sems)


def emit_ln(ctx, T, s_tiles, y_tiles, sq_a, sq_b, ps, pb, ones2, rows,
            eps=LN_EPS):
    block = ctx.block
    r_mu, r_m2, r_A, r_B = rows
    ps_sum = ps[0:1, pb * 512: pb * 512 + T]
    ps_ssq = ps[0:1, (pb + 1) * 512: (pb + 1) * 512 + T]
    ps_A = ps[:, (pb + 2) * 512: (pb + 2) * 512 + T]
    ps_B = ps[:, (pb + 3) * 512: (pb + 3) * 512 + T]
    ones_col = ones2[:, 0:1]
    ones_row = ones2[0:1, :]

    @block.tensor
    def _(tensor):
        tensor.wait_ge(ctx.sDVE.h, ctx.sDVE.n)
        tensor.wait_ge(ctx.sACT.h, ctx.sACT.n)
        for c in range(DC):
            ins = tensor.matmul(ps_sum, ones_col, s_tiles[c],
                                start=(c == 0), stop=(c == DC - 1))
        ctx.sPE.inc(ins)

    for c in range(DC):
        sq = sq_a if c % 2 == 0 else sq_b
        @block.vector
        def _(vector, c=c, sq=sq):
            if c >= 2:
                vector.wait_ge(ctx.sPE.h, ctx.sPE.n)
            ctx.sDVE.inc(vector.tensor_tensor(sq, s_tiles[c], s_tiles[c], OP.mult))

        @block.tensor
        def _(tensor, c=c, sq=sq):
            tensor.wait_ge(ctx.sDVE.h, ctx.sDVE.n)
            ctx.sPE.inc(tensor.matmul(ps_ssq, ones_col, sq,
                                      start=(c == 0), stop=(c == DC - 1)))

    @block.vector
    def _(vector):
        vector.wait_ge(ctx.sPE.h, ctx.sPE.n)
        ctx.sDVE.inc(vector.tensor_scalar(r_mu, ps_sum, 1.0 / D, None, OP.mult))
        ctx.sDVE.inc(vector.tensor_scalar(r_m2, ps_ssq, 1.0 / D, float(eps),
                                          OP.mult, OP.add))
        vector.wait_ge(ctx.sDVE.h, ctx.sDVE.n)
        ctx.sDVE.inc(vector.tensor_tensor(r_A, r_mu, r_mu, OP.mult))
        vector.wait_ge(ctx.sDVE.h, ctx.sDVE.n)
        ctx.sDVE.inc(vector.tensor_tensor(r_m2, r_m2, r_A, OP.subtract))

    @block.scalar
    def _(scalar):
        scalar.wait_ge(ctx.sDVE.h, ctx.sDVE.n)
        ctx.sACT.inc(scalar.activation(r_A, r_m2, AF.Ln, bias=0.0, scale=1.0))
        scalar.wait_ge(ctx.sACT.h, ctx.sACT.n)
        ctx.sACT.inc(scalar.activation(r_A, r_A, AF.Exp, bias=0.0, scale=-0.5))

    @block.vector
    def _(vector):
        vector.wait_ge(ctx.sACT.h, ctx.sACT.n)
        ctx.sDVE.inc(vector.scalar_tensor_tensor(r_B, r_mu, -1.0, r_A,
                                                 OP.mult, OP.mult))

    @block.tensor
    def _(tensor):
        tensor.wait_ge(ctx.sDVE.h, ctx.sDVE.n)
        tensor.matmul(ps_A, ones_row, r_A, start=True, stop=True)
        ctx.sPE.inc(tensor.matmul(ps_B, ones_row, r_B, start=True, stop=True))

    for c in range(DC):
        @block.vector
        def _(vector, c=c):
            if c == 0:
                vector.wait_ge(ctx.sPE.h, ctx.sPE.n)
            vector.tensor_tensor(y_tiles[c], s_tiles[c], ps_A, OP.mult)
            ctx.sDVE.inc(vector.tensor_tensor(y_tiles[c], y_tiles[c], ps_B, OP.add))


def emit_l1ffn(nc, block, ctx, x_dram, out_dram, w0t_dram, w1t_dram,
               b0_dram, b1_dram, ones_dram, stack):
    T = 512
    NCH = TOK // T
    xs = stack.enter_context(nc.sbuf_tensor("ff_x", [128, DC * T], F32))
    xr = stack.enter_context(nc.sbuf_tensor("ff_xr", [128, DC * T], F32))
    x16 = stack.enter_context(nc.sbuf_tensor("ff_x16", [128, T], F16))
    d16 = stack.enter_context(nc.sbuf_tensor("ff_d16", [128, IC * T], F16))
    d32 = stack.enter_context(nc.sbuf_tensor("ff_d32", [128, 2 * T], F32))
    e32 = stack.enter_context(nc.sbuf_tensor("ff_e32", [128, T], F32))
    wup = stack.enter_context(nc.sbuf_tensor("ff_w0", [128, 2 * DC * 128], F32))
    wdn = stack.enter_context(nc.sbuf_tensor("ff_w1", [128, 2 * DC * 128], F32))
    e16 = stack.enter_context(nc.sbuf_tensor("ff_e16", [128, DC * T], F16))
    b0s = stack.enter_context(nc.sbuf_tensor("ff_b0", [128, IC], F32))
    b1s = stack.enter_context(nc.sbuf_tensor("ff_b1", [128, DC], F32))
    ones2 = stack.enter_context(nc.sbuf_tensor("ff_ones", [128, 128], F32))
    sq_a = stack.enter_context(nc.sbuf_tensor("ff_sqa", [128, T], F32))
    sq_b = stack.enter_context(nc.sbuf_tensor("ff_sqb", [128, T], F32))
    rows_t = stack.enter_context(nc.sbuf_tensor("ff_rows", [1, 4 * T], F32))
    ps = stack.enter_context(nc.psum_tensor("ff_ps", [128, 8 * 512], F32))

    rows = [rows_t[0:1, i * T:(i + 1) * T] for i in range(4)]

    @block.sync
    def _(sync):
        ctx.qS.inc(sync.dma_start(out=b0s[:], in_=b0_dram[:, :]), 16)
        ctx.qS.inc(sync.dma_start(out=b1s[:], in_=b1_dram[:, :]), 16)
        ctx.qS.inc(sync.dma_start(out=ones2[:], in_=ones_dram[:, :]), 16)

    for ch in range(NCH):
        t0 = ch * T
        dve_resid_done = ctx.sDVE.n
        @block.sync
        def _(sync):
            sync.wait_ge(ctx.sDVE.h, dve_resid_done)
            for c in range(DC):
                ctx.qS.inc(sync.dma_start(
                    out=xs[:, c * T:(c + 1) * T],
                    in_=x_dram[c * 128:(c + 1) * 128, t0:t0 + T]), 16)

        qg_store_done = ctx.qG.n
        for c in range(DC):
            @block.vector
            def _(vector, c=c):
                if c == 0:
                    vector.wait_ge(ctx.qS.h, ctx.qS.n)
                    vector.wait_ge(ctx.qG.h, qg_store_done)
                vector.tensor_copy(x16[:], xs[:, c * T:(c + 1) * T])
                ctx.sDVE.inc(vector.tensor_copy(xr[:, c * T:(c + 1) * T], x16[:]))

        pe_base = ctx.sPE.n
        act_base = ctx.sACT.n
        dve_ready = ctx.sDVE.n
        for o in range(IC):
            half = o % 2
            wdst = wup[:, half * DC * 128:(half + 1) * DC * 128]
            pslice = ps[:, (o % 4) * 512:(o % 4) * 512 + T]

            @block.sync
            def _(sync, o=o, wdst=wdst):
                if o >= 2:
                    sync.wait_ge(ctx.sPE.h, pe_base + o - 1)
                else:
                    sync.wait_ge(ctx.sPE.h, pe_base)
                ctx.qS.inc(sync.dma_start(
                    out=wdst,
                    in_=w0t_dram[:, o * DC * 128:(o + 1) * DC * 128]), 16)

            @block.tensor
            def _(tensor, o=o, wdst=wdst, pslice=pslice):
                tensor.wait_ge(ctx.qS.h, ctx.qS.n)
                if o == 0:
                    tensor.wait_ge(ctx.sDVE.h, dve_ready)
                if o >= 4:
                    tensor.wait_ge(ctx.sACT.h, act_base + o - 3)
                for c in range(DC):
                    ins = tensor.matmul(pslice, wdst[:, c * 128:(c + 1) * 128],
                                        xr[:, c * T:(c + 1) * T],
                                        start=(c == 0), stop=(c == DC - 1))
                ctx.sPE.inc(ins)

            @block.scalar
            def _(scalar, o=o, pslice=pslice):
                scalar.wait_ge(ctx.sPE.h, pe_base + o + 1)
                ctx.sACT.inc(scalar.activation(
                    d16[:, o * T:(o + 1) * T], pslice, AF.Gelu,
                    bias=b0s[:, o:o + 1], scale=1.0))

        for ph in range(2):
            pe2_base = ctx.sPE.n
            act2_base = ctx.sACT.n
            for i in range(IC):
                half = i % 2
                dslice = d32[:, half * T:(half + 1) * T]
                whalf = wdn[:, half * 8 * 128:(half + 1) * 8 * 128]

                @block.sync
                def _(sync, i=i, whalf=whalf, ph=ph):
                    if i >= 2:
                        sync.wait_ge(ctx.sPE.h, pe2_base + i - 1)
                    else:
                        sync.wait_ge(ctx.sPE.h, pe2_base)
                    ctx.qS.inc(sync.dma_start(
                        out=whalf,
                        in_=w1t_dram[:, (i * DC + ph * 8) * 128:
                                        (i * DC + ph * 8 + 8) * 128]), 16)

                @block.vector
                def _(vector, i=i, dslice=dslice):
                    if i == 0:
                        vector.wait_ge(ctx.sACT.h, act2_base)
                    if i >= 2:
                        vector.wait_ge(ctx.sPE.h, pe2_base + i - 1)
                    ctx.sDVE.inc(vector.tensor_copy(dslice, d16[:, i * T:(i + 1) * T]))

                @block.tensor
                def _(tensor, i=i, dslice=dslice, whalf=whalf):
                    tensor.wait_ge(ctx.qS.h, ctx.qS.n)
                    tensor.wait_ge(ctx.sDVE.h, ctx.sDVE.n)
                    for e in range(8):
                        ins = tensor.matmul(ps[:, e * 512:e * 512 + T],
                                            whalf[:, e * 128:(e + 1) * 128],
                                            dslice,
                                            start=(i == 0), stop=(i == IC - 1))
                    ctx.sPE.inc(ins)

            pe3 = ctx.sPE.n
            for e in range(8):
                eg = ph * 8 + e
                @block.scalar
                def _(scalar, e=e, eg=eg):
                    if e == 0:
                        scalar.wait_ge(ctx.sPE.h, pe3)
                    ctx.sACT.inc(scalar.activation(
                        e16[:, eg * T:(eg + 1) * T], ps[:, e * 512:e * 512 + T],
                        AF.Identity, bias=b1s[:, eg:eg + 1], scale=1.0))

        act3 = ctx.sACT.n
        for c in range(DC):
            @block.vector
            def _(vector, c=c):
                if c == 0:
                    vector.wait_ge(ctx.sACT.h, act3)
                vector.tensor_copy(e32[:], e16[:, c * T:(c + 1) * T])
                ctx.sDVE.inc(vector.tensor_tensor(
                    xr[:, c * T:(c + 1) * T], xs[:, c * T:(c + 1) * T],
                    e32[:], OP.add))

        s_tiles = [xr[:, c * T:(c + 1) * T] for c in range(DC)]
        emit_ln(ctx, T, s_tiles, s_tiles, sq_a[:], sq_b[:], ps, 0, ones2, rows)

        dve_done = ctx.sDVE.n
        @block.gpsimd
        def _(gpsimd):
            gpsimd.wait_ge(ctx.sDVE.h, dve_done)
            for c in range(DC):
                ctx.qG.inc(gpsimd.dma_start(
                    out=out_dram[c * 128:(c + 1) * 128, t0:t0 + T],
                    in_=xr[:, c * T:(c + 1) * T]), 16)


def emit_phase_barrier(ctx):
    block = ctx.block
    sems = [ctx.qS, ctx.qG, ctx.sPE, ctx.sDVE, ctx.sACT, ctx.sPOOL]
    counts = [(s, s.n) for s in sems]
    for eng in ("sync", "tensor", "vector", "scalar", "gpsimd"):
        @getattr(block, eng)
        def _(e):
            for s, n in counts:
                e.wait_ge(s.h, n)


def emit_rms(ctx, T, tiles, out_tiles, g_cols, nch, ps, pb, ones2, row,
             sq_a, sq_b, eps=RMS_EPS):
    block = ctx.block
    ps_ssq = ps[0:1, pb * 512: pb * 512 + T]
    ps_bc = ps[:, (pb + 1) * 512: (pb + 1) * 512 + T]
    ones_col = ones2[:, 0:1]
    ones_row = ones2[0:1, :]
    n = len(tiles)

    for c in range(n):
        @block.vector
        def _(vector, c=c):
            if c >= 2:
                vector.wait_ge(ctx.sPE.h, ctx.sPE.n)
            sq = sq_a if c % 2 == 0 else sq_b
            ctx.sDVE.inc(vector.tensor_tensor(sq, tiles[c], tiles[c], OP.mult))

        @block.tensor
        def _(tensor, c=c):
            tensor.wait_ge(ctx.sDVE.h, ctx.sDVE.n)
            sq = sq_a if c % 2 == 0 else sq_b
            ctx.sPE.inc(tensor.matmul(ps_ssq, ones_col, sq,
                                      start=(c == 0), stop=(c == n - 1)))

    @block.vector
    def _(vector):
        vector.wait_ge(ctx.sPE.h, ctx.sPE.n)
        ctx.sDVE.inc(vector.tensor_scalar(row, ps_ssq, 1.0 / (nch * 128.0),
                                          float(eps), OP.mult, OP.add))

    @block.scalar
    def _(scalar):
        scalar.wait_ge(ctx.sDVE.h, ctx.sDVE.n)
        ctx.sACT.inc(scalar.activation(row, row, AF.Ln, bias=0.0, scale=1.0))
        scalar.wait_ge(ctx.sACT.h, ctx.sACT.n)
        ctx.sACT.inc(scalar.activation(row, row, AF.Exp, bias=0.0, scale=-0.5))

    @block.tensor
    def _(tensor):
        tensor.wait_ge(ctx.sACT.h, ctx.sACT.n)
        ctx.sPE.inc(tensor.matmul(ps_bc, ones_row, row, start=True, stop=True))

    for c in range(n):
        @block.vector
        def _(vector, c=c):
            if c == 0:
                vector.wait_ge(ctx.sPE.h, ctx.sPE.n)
            ctx.sDVE.inc(vector.scalar_tensor_tensor(
                out_tiles[c], tiles[c], g_cols[c], ps_bc, OP.mult, OP.mult))


def emit_butterflies(ctx, tiles, tmp_d, tmp_p):
    block = ctx.block
    n = len(tiles)
    stages = int(np.log2(n))
    for s in range(stages):
        h = 1 << s
        pairs = [(i, i + h) for i in range(n) if not (i & h)]
        half = len(pairs) // 2
        dve_pairs, pool_pairs = pairs[:half], pairs[half:]

        @block.vector
        def _(vector, dve_pairs=dve_pairs, s=s):
            vector.wait_ge(ctx.sPOOL.h, ctx.sPOOL.n)
            for a, b_ in dve_pairs:
                vector.tensor_tensor(tmp_d, tiles[a], tiles[b_], OP.subtract)
                vector.tensor_tensor(tiles[a], tiles[a], tiles[b_], OP.add)
                ins = vector.tensor_copy(tiles[b_], tmp_d)
            ctx.sDVE.inc(ins)

        @block.gpsimd
        def _(gpsimd, pool_pairs=pool_pairs, s=s):
            gpsimd.wait_ge(ctx.sDVE.h, ctx.sDVE.n - 1)
            for a, b_ in pool_pairs:
                gpsimd.tensor_tensor(tmp_p, tiles[a], tiles[b_], OP.subtract)
                gpsimd.tensor_tensor(tiles[a], tiles[a], tiles[b_], OP.add)
                ins = gpsimd.tensor_copy(tiles[b_], tmp_p)
            ctx.sPOOL.inc(ins)

    @block.vector
    def _(vector):
        vector.wait_ge(ctx.sPOOL.h, ctx.sPOOL.n)
        ins = vector.tensor_copy(tmp_d[:, 0:1], tmp_d[:, 0:1])
        ctx.sDVE.inc(ins)


def emit_l0ffn(nc, block, ctx, x_dram, out_dram, wupt_dram, wgtt_dram,
               wdnt_dram, had_dram, su_ug_dram, su_dp_dram, ws_ug_dram,
               ws_dp_dram, ones_dram, stack):
    T = 256
    NCH = TOK // T
    xs = stack.enter_context(nc.sbuf_tensor("f0_x", [128, DC * T], F32))
    z = stack.enter_context(nc.sbuf_tensor("f0_z", [128, DC * T], F32))
    hbuf = stack.enter_context(nc.sbuf_tensor("f0_h", [128, IC * T], F32))
    tmp_d = stack.enter_context(nc.sbuf_tensor("f0_td", [128, T], F32))
    tmp_p = stack.enter_context(nc.sbuf_tensor("f0_tp", [128, T], F32))
    tmpg = stack.enter_context(nc.sbuf_tensor("f0_tg", [128, T], F32))
    silu_sb = stack.enter_context(nc.sbuf_tensor("f0_sl", [128, T], F32))
    hadt = stack.enter_context(nc.sbuf_tensor("f0_ht", [128, DC * 128], F32))
    hraw = stack.enter_context(nc.sbuf_tensor("f0_hr", [128, 128], F32))
    wup = stack.enter_context(nc.sbuf_tensor("f0_wu", [128, 2 * DC * 128], F32))
    wgt = stack.enter_context(nc.sbuf_tensor("f0_wg", [128, 2 * DC * 128], F32))
    wdn = stack.enter_context(nc.sbuf_tensor("f0_wd", [128, 2 * 8 * 128], F32))
    su_ug = stack.enter_context(nc.sbuf_tensor("f0_su", [128, DC], F32))
    vsu = stack.enter_context(nc.sbuf_tensor("f0_vsu", [128, IC], F32))
    vsg = stack.enter_context(nc.sbuf_tensor("f0_vsg", [128, IC], F32))
    vsd = stack.enter_context(nc.sbuf_tensor("f0_vsd", [128, DC], F32))
    vtmp = stack.enter_context(nc.sbuf_tensor("f0_vt", [128, 2 * IC], F32))
    ones2 = stack.enter_context(nc.sbuf_tensor("f0_on", [128, 128], F32))
    sq_a = stack.enter_context(nc.sbuf_tensor("f0_qa", [128, T], F32))
    sq_b = stack.enter_context(nc.sbuf_tensor("f0_qb", [128, T], F32))
    rows_t = stack.enter_context(nc.sbuf_tensor("f0_rw", [1, 4 * T], F32))
    ps = stack.enter_context(nc.psum_tensor("f0_ps", [128, 8 * 512], F32))

    rows = [rows_t[0:1, i * T:(i + 1) * T] for i in range(4)]
    htile = [hbuf[:, k * T:(k + 1) * T] for k in range(IC)]
    ztile = [z[:, k * T:(k + 1) * T] for k in range(DC)]

    @block.sync
    def _(sync):
        for dst, src in [(hraw[:], had_dram[:, :]), (su_ug[:], su_ug_dram[:, :]),
                         (vtmp[:], ws_ug_dram[:, :]),
                         (vsd[:], ws_dp_dram[:, :]), (vsu[:], su_dp_dram[:, :]),
                         (ones2[:], ones_dram[:, :])]:
            ctx.qS.inc(sync.dma_start(out=dst, in_=src), 16)

    @block.vector
    def _(vector):
        vector.wait_ge(ctx.qS.h, ctx.qS.n)
        for k in range(DC):
            vector.tensor_scalar(hadt[:, k * 128:(k + 1) * 128], hraw[:],
                                 su_ug[:, k:k + 1], None, OP.mult)
        vector.tensor_scalar(vsg[:], vtmp[:, IC:2 * IC], 1.0 / np.sqrt(2048.0),
                             None, OP.mult)
        vector.scalar_tensor_tensor(vsu[:], vtmp[:, 0:IC], 1.0 / np.sqrt(2048.0),
                                    vsu[:], OP.mult, OP.mult)
        ctx.sDVE.inc(vector.tensor_scalar(vsd[:], vsd[:], 1.0 / np.sqrt(8192.0),
                                          None, OP.mult))

    for ch in range(NCH):
        t0 = ch * T
        dve_prev = ctx.sDVE.n
        @block.sync
        def _(sync):
            sync.wait_ge(ctx.sDVE.h, dve_prev)
            for c in range(DC):
                ctx.qS.inc(sync.dma_start(
                    out=xs[:, c * T:(c + 1) * T],
                    in_=x_dram[c * 128:(c + 1) * 128, t0:t0 + T]), 16)

        pe_b = ctx.sPE.n
        dve_b = ctx.sDVE.n
        for k in range(DC):
            @block.tensor
            def _(tensor, k=k):
                if k == 0:
                    tensor.wait_ge(ctx.qS.h, ctx.qS.n)
                    tensor.wait_ge(ctx.sDVE.h, dve_b)
                if k >= 4:
                    tensor.wait_ge(ctx.sDVE.h, dve_b + k - 3)
                ctx.sPE.inc(tensor.matmul(ps[:, (k % 4) * 512:(k % 4) * 512 + T],
                                          hadt[:, k * 128:(k + 1) * 128],
                                          xs[:, k * T:(k + 1) * T],
                                          start=True, stop=True))

            @block.vector
            def _(vector, k=k):
                if k == 0:
                    vector.wait_ge(ctx.qG.h, ctx.qG.n)
                vector.wait_ge(ctx.sPE.h, pe_b + k + 1)
                ctx.sDVE.inc(vector.tensor_copy(
                    ztile[k], ps[:, (k % 4) * 512:(k % 4) * 512 + T]))

        emit_butterflies(ctx, ztile, tmp_d[:], tmp_p[:])

        pe_b = ctx.sPE.n
        dve_b = ctx.sDVE.n
        act_b = ctx.sACT.n
        for o in range(IC):
            half = o % 2
            wud = wup[:, half * DC * 128:(half + 1) * DC * 128]
            wgd = wgt[:, half * DC * 128:(half + 1) * DC * 128]
            bu = ps[:, (o % 4) * 512:(o % 4) * 512 + T]
            bg = ps[:, (4 + o % 4) * 512:(4 + o % 4) * 512 + T]

            @block.sync
            def _(sync, o=o, wud=wud, wgd=wgd):
                sync.wait_ge(ctx.sPE.h, pe_b + max(0, (o - 1) * 2))
                ctx.qS.inc(sync.dma_start(
                    out=wud, in_=wupt_dram[:, o * DC * 128:(o + 1) * DC * 128]), 16)
                ctx.qS.inc(sync.dma_start(
                    out=wgd, in_=wgtt_dram[:, o * DC * 128:(o + 1) * DC * 128]), 16)

            @block.tensor
            def _(tensor, o=o, wud=wud, wgd=wgd, bu=bu, bg=bg):
                tensor.wait_ge(ctx.qS.h, ctx.qS.n)
                if o == 0:
                    tensor.wait_ge(ctx.sDVE.h, dve_b)
                    tensor.wait_ge(ctx.sPOOL.h, ctx.sPOOL.n)
                if o >= 4:
                    tensor.wait_ge(ctx.sDVE.h, dve_b + 2 * (o - 3))
                for c in range(DC):
                    ins = tensor.matmul(bu, wud[:, c * 128:(c + 1) * 128],
                                        ztile[c], start=(c == 0), stop=(c == DC - 1))
                ctx.sPE.inc(ins)
                for c in range(DC):
                    ins = tensor.matmul(bg, wgd[:, c * 128:(c + 1) * 128],
                                        ztile[c], start=(c == 0), stop=(c == DC - 1))
                ctx.sPE.inc(ins)

            @block.vector
            def _(vector, o=o, bg=bg):
                if o == 0:
                    vector.wait_ge(ctx.sDVE.h, dve_b)
                vector.wait_ge(ctx.sPE.h, pe_b + 2 * (o + 1))
                vector.wait_ge(ctx.sACT.h, ctx.sACT.n)
                ctx.sDVE.inc(vector.tensor_scalar(tmpg[:], bg, vsg[:, o:o + 1],
                                                  None, OP.mult))

            @block.scalar
            def _(scalar, o=o):
                scalar.wait_ge(ctx.sDVE.h, ctx.sDVE.n)
                ctx.sACT.inc(scalar.activation(silu_sb[:], tmpg[:], AF.Silu,
                                               bias=0.0, scale=1.0))

            @block.vector
            def _(vector, o=o, bu=bu):
                vector.wait_ge(ctx.sACT.h, ctx.sACT.n)
                ctx.sDVE.inc(vector.scalar_tensor_tensor(
                    htile[o], bu, vsu[:, o:o + 1], silu_sb[:], OP.mult, OP.mult))

        pe_b = ctx.sPE.n
        dve_b = ctx.sDVE.n
        for k in range(IC):
            @block.tensor
            def _(tensor, k=k):
                if k == 0:
                    tensor.wait_ge(ctx.sDVE.h, dve_b)
                if k >= 4:
                    tensor.wait_ge(ctx.sDVE.h, dve_b + k - 3)
                ctx.sPE.inc(tensor.matmul(ps[:, (k % 4) * 512:(k % 4) * 512 + T],
                                          hraw[:], htile[k],
                                          start=True, stop=True))

            @block.vector
            def _(vector, k=k):
                vector.wait_ge(ctx.sPE.h, pe_b + k + 1)
                ctx.sDVE.inc(vector.tensor_copy(
                    htile[k], ps[:, (k % 4) * 512:(k % 4) * 512 + T]))

        emit_butterflies(ctx, htile, tmp_d[:], tmp_p[:])

        for phd in range(2):
            pe2 = ctx.sPE.n
            for i in range(IC):
                half = i % 2
                whalf = wdn[:, half * 8 * 128:(half + 1) * 8 * 128]

                @block.sync
                def _(sync, i=i, whalf=whalf, phd=phd):
                    if i >= 2:
                        sync.wait_ge(ctx.sPE.h, pe2 + i - 1)
                    else:
                        sync.wait_ge(ctx.sPE.h, pe2)
                    ctx.qS.inc(sync.dma_start(
                        out=whalf,
                        in_=wdnt_dram[:, (i * DC + phd * 8) * 128:
                                         (i * DC + phd * 8 + 8) * 128]), 16)

                @block.tensor
                def _(tensor, i=i, whalf=whalf, phd=phd):
                    tensor.wait_ge(ctx.qS.h, ctx.qS.n)
                    if i == 0:
                        tensor.wait_ge(ctx.sDVE.h, ctx.sDVE.n)
                        tensor.wait_ge(ctx.sPOOL.h, ctx.sPOOL.n)
                        if phd == 1:
                            tensor.wait_ge(ctx.sACT.h, ctx.sACT.n)
                    for e in range(8):
                        ins = tensor.matmul(ps[:, e * 512:e * 512 + T],
                                            whalf[:, e * 128:(e + 1) * 128],
                                            htile[i],
                                            start=(i == 0), stop=(i == IC - 1))
                    ctx.sPE.inc(ins)

            pe3 = ctx.sPE.n
            for e in range(8):
                eg = phd * 8 + e
                @block.vector
                def _(vector, e=e, eg=eg):
                    if e == 0:
                        vector.wait_ge(ctx.sPE.h, pe3)
                    ctx.sDVE.inc(vector.scalar_tensor_tensor(
                        ztile[eg], ps[:, e * 512:e * 512 + T], vsd[:, eg:eg + 1],
                        xs[:, eg * T:(eg + 1) * T], OP.mult, OP.add))

        emit_ln(ctx, T, ztile, ztile, sq_a[:], sq_b[:], ps, 0, ones2, rows)

        dve_done = ctx.sDVE.n
        @block.gpsimd
        def _(gpsimd):
            gpsimd.wait_ge(ctx.sDVE.h, dve_done)
            for c in range(DC):
                ctx.qG.inc(gpsimd.dma_start(
                    out=out_dram[c * 128:(c + 1) * 128, t0:t0 + T],
                    in_=z[:, c * T:(c + 1) * T]), 16)


KB8 = S // 512
KB128 = S // 128


def emit_l0attn(nc, block, ctx, xk_dram, xq_dram, out_dram, wcq_dram, gq_dram,
                wdq_dram, wdqr_dram, wckv_dram, gkv_dram, wdk_dram, wdv_dram,
                wkr_dram, wo_dram, rckv_dram, rskv_dram, rcq_dram, rsq_dram,
                ident_dram, ones_dram, attn_scr, qcn_scr, stack):
    T = 512
    kvc = stack.enter_context(nc.sbuf_tensor("a0_kvc", [128, 4 * S], F32))
    krs = stack.enter_context(nc.sbuf_tensor("a0_kr", [64, S], F32))
    ones2 = stack.enter_context(nc.sbuf_tensor("a0_on", [128, 128], F32))
    ident = stack.enter_context(nc.sbuf_tensor("a0_id", [128, 128], F32))
    wkr = stack.enter_context(nc.sbuf_tensor("a0_wkr", [128, DC * 64], F32))
    gq = stack.enter_context(nc.sbuf_tensor("a0_gq", [128, 6], F32))
    gkv = stack.enter_context(nc.sbuf_tensor("a0_gkv", [128, 4], F32))
    rows_t = stack.enter_context(nc.sbuf_tensor("a0_rw", [1, 4 * T], F32))
    ps = stack.enter_context(nc.psum_tensor("a0_ps", [128, 8 * 512], F32))

    rows = [rows_t[0:1, i * T:(i + 1) * T] for i in range(4)]
    kvct = [kvc[:, c * S:(c + 1) * S] for c in range(4)]

    @block.sync
    def _(sync):
        for dst, src in [(ones2[:], ones_dram[:, :]), (ident[:], ident_dram[:, :]),
                         (wkr[:], wkr_dram[:, :]), (gq[:], gq_dram[:, :]),
                         (gkv[:], gkv_dram[:, :])]:
            ctx.qS.inc(sync.dma_start(out=dst, in_=src), 16)

    with contextlib.ExitStack() as p1:
        xs = p1.enter_context(nc.sbuf_tensor("a1_x", [128, DC * T], F32))
        wbuf = p1.enter_context(nc.sbuf_tensor("a1_w", [128, 2 * DC * 128], F32))
        rck = p1.enter_context(nc.sbuf_tensor("a1_rc", [64, S], F32))
        rsk = p1.enter_context(nc.sbuf_tensor("a1_rs", [64, S], F32))
        sq_a = p1.enter_context(nc.sbuf_tensor("a1_qa", [128, T], F32))
        sq_b = p1.enter_context(nc.sbuf_tensor("a1_qb", [128, T], F32))
        rtmp = p1.enter_context(nc.sbuf_tensor("a1_rt", [64, T], F32))

        @block.sync
        def _(sync):
            ctx.qS.inc(sync.dma_start(out=rck[:], in_=rckv_dram[:, :]), 16)
            ctx.qS.inc(sync.dma_start(out=rsk[:], in_=rskv_dram[:, :]), 16)

        for ch in range(KB8):
            t0 = ch * T
            dve_prev = ctx.sDVE.n
            @block.sync
            def _(sync):
                sync.wait_ge(ctx.sDVE.h, dve_prev)
                for c in range(DC):
                    ctx.qS.inc(sync.dma_start(
                        out=xs[:, c * T:(c + 1) * T],
                        in_=xk_dram[c * 128:(c + 1) * 128, t0:t0 + T]), 16)

            pe_b = ctx.sPE.n
            dve_b = ctx.sDVE.n
            for o in range(4):
                half = o % 2
                wdst = wbuf[:, half * DC * 128:(half + 1) * DC * 128]
                @block.sync
                def _(sync, o=o, wdst=wdst):
                    sync.wait_ge(ctx.sPE.h, pe_b + max(0, o - 1))
                    ctx.qS.inc(sync.dma_start(
                        out=wdst, in_=wckv_dram[:, o * DC * 128:(o + 1) * DC * 128]), 16)

                @block.tensor
                def _(tensor, o=o, wdst=wdst):
                    tensor.wait_ge(ctx.qS.h, ctx.qS.n)
                    if o == 0:
                        tensor.wait_ge(ctx.sDVE.h, dve_b)
                    for c in range(DC):
                        ins = tensor.matmul(ps[:, o * 512:o * 512 + T],
                                            wdst[:, c * 128:(c + 1) * 128],
                                            xs[:, c * T:(c + 1) * T],
                                            start=(c == 0), stop=(c == DC - 1))
                    ctx.sPE.inc(ins)

                @block.vector
                def _(vector, o=o):
                    vector.wait_ge(ctx.sPE.h, pe_b + o + 1)
                    ctx.sDVE.inc(vector.tensor_copy(
                        kvct[o][:, t0:t0 + T], ps[:, o * 512:o * 512 + T]))

            @block.tensor
            def _(tensor):
                for c in range(DC):
                    ins = tensor.matmul(ps[0:64, 4 * 512:4 * 512 + T],
                                        wkr[:, c * 64:(c + 1) * 64],
                                        xs[:, c * T:(c + 1) * T],
                                        start=(c == 0), stop=(c == DC - 1))
                ctx.sPE.inc(ins)

            @block.vector
            def _(vector, t0=t0):
                vector.wait_ge(ctx.sPE.h, ctx.sPE.n)
                pk = ps[0:64, 4 * 512:4 * 512 + T]
                ctx.sDVE.inc(vector.tensor_tensor(rtmp[0:32, :], pk[0:32, :],
                                                  rck[0:32, t0:t0 + T], OP.mult))
                ctx.sDVE.inc(vector.tensor_tensor(rtmp[32:64, :], pk[32:64, :],
                                                  rck[32:64, t0:t0 + T], OP.mult))
                vector.wait_ge(ctx.sDVE.h, ctx.sDVE.n)
                ctx.sDVE.inc(vector.tensor_tensor(krs[32:64, t0:t0 + T], pk[0:32, :],
                                                  rsk[32:64, t0:t0 + T], OP.mult))
                ctx.sDVE.inc(vector.tensor_tensor(sq_a[0:32, :], pk[32:64, :],
                                                  rsk[0:32, t0:t0 + T], OP.mult))
                vector.wait_ge(ctx.sDVE.h, ctx.sDVE.n)
                ctx.sDVE.inc(vector.tensor_tensor(krs[0:32, t0:t0 + T], rtmp[0:32, :],
                                                  sq_a[0:32, :], OP.subtract))
                ctx.sDVE.inc(vector.tensor_tensor(krs[32:64, t0:t0 + T],
                                                  krs[32:64, t0:t0 + T],
                                                  rtmp[32:64, :], OP.add))

            emit_rms(ctx, T, [kt[:, t0:t0 + T] for kt in kvct],
                     [kt[:, t0:t0 + T] for kt in kvct],
                     [gkv[:, c:c + 1] for c in range(4)], 4, ps, 5, ones2,
                     rows[0], sq_a[:], sq_b[:])

    emit_phase_barrier(ctx)

    with contextlib.ExitStack() as p2:
        xs = p2.enter_context(nc.sbuf_tensor("a2_x", [128, DC * T], F32))
        wbuf = p2.enter_context(nc.sbuf_tensor("a2_w", [128, 2 * DC * 128], F32))
        sq_a = p2.enter_context(nc.sbuf_tensor("a2_qa", [128, T], F32))
        sq_b = p2.enter_context(nc.sbuf_tensor("a2_qb", [128, T], F32))
        qcs = p2.enter_context(nc.sbuf_tensor("a2_qc", [128, 6 * T], F32))
        qcst = [qcs[:, c * T:(c + 1) * T] for c in range(6)]

        for ch in range(TOK // T):
            t0 = ch * T
            dve_prev = ctx.sDVE.n
            qg_prev = ctx.qG.n
            @block.sync
            def _(sync):
                sync.wait_ge(ctx.sDVE.h, dve_prev)
                for c in range(DC):
                    ctx.qS.inc(sync.dma_start(
                        out=xs[:, c * T:(c + 1) * T],
                        in_=xq_dram[c * 128:(c + 1) * 128, t0:t0 + T]), 16)

            pe_b = ctx.sPE.n
            dve_b = ctx.sDVE.n
            for o in range(6):
                half = o % 2
                wdst = wbuf[:, half * DC * 128:(half + 1) * DC * 128]
                @block.sync
                def _(sync, o=o, wdst=wdst):
                    sync.wait_ge(ctx.sPE.h, pe_b + max(0, o - 1))
                    ctx.qS.inc(sync.dma_start(
                        out=wdst, in_=wcq_dram[:, o * DC * 128:(o + 1) * DC * 128]), 16)

                @block.tensor
                def _(tensor, o=o, wdst=wdst):
                    tensor.wait_ge(ctx.qS.h, ctx.qS.n)
                    if o == 0:
                        tensor.wait_ge(ctx.sDVE.h, dve_b)
                    if o >= 4:
                        tensor.wait_ge(ctx.sDVE.h, dve_b + o - 3)
                    for c in range(DC):
                        ins = tensor.matmul(ps[:, (o % 4) * 512:(o % 4) * 512 + T],
                                            wdst[:, c * 128:(c + 1) * 128],
                                            xs[:, c * T:(c + 1) * T],
                                            start=(c == 0), stop=(c == DC - 1))
                    ctx.sPE.inc(ins)

                @block.vector
                def _(vector, o=o):
                    if o == 0:
                        vector.wait_ge(ctx.qG.h, qg_prev)
                    vector.wait_ge(ctx.sPE.h, pe_b + o + 1)
                    ctx.sDVE.inc(vector.tensor_copy(
                        qcst[o], ps[:, (o % 4) * 512:(o % 4) * 512 + T]))

            emit_rms(ctx, T, qcst, qcst,
                     [gq[:, c:c + 1] for c in range(6)], 6, ps, 4, ones2,
                     rows[0], sq_a[:], sq_b[:])

            dve_done = ctx.sDVE.n
            @block.gpsimd
            def _(gpsimd, t0=t0):
                gpsimd.wait_ge(ctx.sDVE.h, dve_done)
                for c in range(6):
                    ctx.qG.inc(gpsimd.dma_start(
                        out=qcn_scr[c * 128:(c + 1) * 128, t0:t0 + T],
                        in_=qcs[:, c * T:(c + 1) * T]), 16)

    emit_phase_barrier(ctx)

    with contextlib.ExitStack() as p3:
        kh = p3.enter_context(nc.sbuf_tensor("a3_kh", [128, S], F32))
        qcs = p3.enter_context(nc.sbuf_tensor("a3_qc", [128, 6 * T], F32))
        vh = p3.enter_context(nc.sbuf_tensor("a3_vh", [128, S], F32))
        qn = p3.enter_context(nc.sbuf_tensor("a3_qn", [128, TOK], F32))
        qr = p3.enter_context(nc.sbuf_tensor("a3_qr", [64, TOK], F32))
        pbuf = p3.enter_context(nc.sbuf_tensor("a3_p", [128, S], F32))
        whb = p3.enter_context(nc.sbuf_tensor("a3_wh", [128, 2176], F32))
        rcq = p3.enter_context(nc.sbuf_tensor("a3_rc", [64, TOK], F32))
        rsq = p3.enter_context(nc.sbuf_tensor("a3_rs", [64, TOK], F32))
        ostg = p3.enter_context(nc.sbuf_tensor("a3_os", [128, 128], F32))
        mx = p3.enter_context(nc.sbuf_tensor("a3_mx", [128, 4], F32))
        rtmp = p3.enter_context(nc.sbuf_tensor("a3_rt", [64, T], F32))
        rtm2 = p3.enter_context(nc.sbuf_tensor("a3_rt2", [64, T], F32))

        @block.sync
        def _(sync):
            ctx.qS.inc(sync.dma_start(out=rcq[:], in_=rcq_dram[:, :]), 16)
            ctx.qS.inc(sync.dma_start(out=rsq[:], in_=rsq_dram[:, :]), 16)

        for h in range(N_HEADS):
            pe_done = ctx.sPE.n
            @block.sync
            def _(sync, h=h):
                sync.wait_ge(ctx.sPE.h, pe_done)
                ctx.qS.inc(sync.dma_start(
                    out=whb[:, 0:768], in_=wdq_dram[:, h * 768:(h + 1) * 768]), 16)
                ctx.qS.inc(sync.dma_start(
                    out=whb[:, 768:1152], in_=wdqr_dram[:, h * 384:(h + 1) * 384]), 16)
                ctx.qS.inc(sync.dma_start(
                    out=whb[:, 1152:1664], in_=wdk_dram[:, h * 512:(h + 1) * 512]), 16)
                for c in range(4):
                    ctx.qS.inc(sync.dma_start(
                        out=whb[:, 1664 + c * 128:1664 + (c + 1) * 128],
                        in_=wdv_dram[:, (c * 16 + h) * 128:(c * 16 + h + 1) * 128]), 16)

            pe_b = ctx.sPE.n
            dve_b = ctx.sDVE.n
            for kb in range(KB8):
                @block.tensor
                def _(tensor, kb=kb, h=h):
                    tensor.wait_ge(ctx.qS.h, ctx.qS.n)
                    if kb == 0:
                        tensor.wait_ge(ctx.sDVE.h, dve_b)
                    if kb >= 4:
                        tensor.wait_ge(ctx.sDVE.h, dve_b + kb - 3)
                    for c in range(4):
                        ins = tensor.matmul(
                            ps[:, (kb % 4) * 512:(kb % 4) * 512 + T],
                            whb[:, 1152 + c * 128:1152 + (c + 1) * 128],
                            kvct[c][:, kb * T:(kb + 1) * T],
                            start=(c == 0), stop=(c == 3))
                    ctx.sPE.inc(ins)

                @block.vector
                def _(vector, kb=kb):
                    vector.wait_ge(ctx.sPE.h, pe_b + kb + 1)
                    ctx.sDVE.inc(vector.tensor_copy(
                        kh[:, kb * T:(kb + 1) * T],
                        ps[:, (kb % 4) * 512:(kb % 4) * 512 + T]))

            pe_b = ctx.sPE.n
            dve_b = ctx.sDVE.n
            for tb in range(KB128):
                @block.tensor
                def _(tensor, tb=tb):
                    if tb == 0:
                        tensor.wait_ge(ctx.sDVE.h, dve_b)
                    if tb >= 4:
                        tensor.wait_ge(ctx.sDVE.h, dve_b + tb - 3)
                    for c in range(4):
                        ins = tensor.matmul(
                            ps[:, (tb % 4) * 512:(tb % 4) * 512 + 128],
                            kvct[c][:, tb * 128:(tb + 1) * 128],
                            whb[:, 1664 + c * 128:1664 + (c + 1) * 128],
                            start=(c == 0), stop=(c == 3))
                    ctx.sPE.inc(ins)

                @block.vector
                def _(vector, tb=tb):
                    vector.wait_ge(ctx.sPE.h, pe_b + tb + 1)
                    ctx.sDVE.inc(vector.tensor_copy(
                        vh[:, tb * 128:(tb + 1) * 128],
                        ps[:, (tb % 4) * 512:(tb % 4) * 512 + 128]))

            pe_b = ctx.sPE.n
            dve_b = ctx.sDVE.n
            for qb in range(TOK // T):
                @block.sync
                def _(sync, qb=qb):
                    if qb > 0:
                        sync.wait_ge(ctx.sPE.h, pe_b + 2 * qb)
                    for c in range(6):
                        ctx.qS.inc(sync.dma_start(
                            out=qcs[:, c * T:(c + 1) * T],
                            in_=qcn_scr[c * 128:(c + 1) * 128, qb * T:(qb + 1) * T]), 16)

                @block.tensor
                def _(tensor, qb=qb):
                    tensor.wait_ge(ctx.qS.h, ctx.qS.n)
                    if qb == 0:
                        tensor.wait_ge(ctx.sDVE.h, dve_b)
                    if qb >= 4:
                        tensor.wait_ge(ctx.sDVE.h, dve_b + 2 * (qb - 3))
                    for c in range(6):
                        ins = tensor.matmul(
                            ps[:, (qb % 4) * 512:(qb % 4) * 512 + T],
                            whb[:, c * 128:(c + 1) * 128],
                            qcs[:, c * T:(c + 1) * T],
                            start=(c == 0), stop=(c == 5))
                    ctx.sPE.inc(ins)
                    for c in range(6):
                        ins = tensor.matmul(
                            ps[0:64, (4 + qb % 4) * 512:(4 + qb % 4) * 512 + T],
                            whb[:, 768 + c * 64:768 + (c + 1) * 64],
                            qcs[:, c * T:(c + 1) * T],
                            start=(c == 0), stop=(c == 5))
                    ctx.sPE.inc(ins)

                @block.vector
                def _(vector, qb=qb):
                    vector.wait_ge(ctx.sPE.h, pe_b + 2 * (qb + 1))
                    t0 = qb * T
                    pq = ps[:, (qb % 4) * 512:(qb % 4) * 512 + T]
                    pr = ps[0:64, (4 + qb % 4) * 512:(4 + qb % 4) * 512 + T]
                    ctx.sDVE.inc(vector.tensor_scalar(
                        qn[:, t0:t0 + T], pq, float(1.0 / np.sqrt(192.0)), None, OP.mult))
                    ctx.sDVE.inc(vector.tensor_tensor(rtmp[0:32, :], pr[0:32, :],
                                                      rcq[0:32, t0:t0 + T], OP.mult))
                    ctx.sDVE.inc(vector.tensor_tensor(rtmp[32:64, :], pr[32:64, :],
                                                      rcq[32:64, t0:t0 + T], OP.mult))
                    vector.wait_ge(ctx.sDVE.h, ctx.sDVE.n)
                    ctx.sDVE.inc(vector.tensor_tensor(qr[32:64, t0:t0 + T], pr[0:32, :],
                                                      rsq[32:64, t0:t0 + T], OP.mult))
                    ctx.sDVE.inc(vector.tensor_tensor(rtm2[0:32, :], pr[32:64, :],
                                                      rsq[0:32, t0:t0 + T], OP.mult))
                    vector.wait_ge(ctx.sDVE.h, ctx.sDVE.n)
                    ctx.sDVE.inc(vector.tensor_tensor(qr[0:32, t0:t0 + T], rtmp[0:32, :],
                                                      rtm2[0:32, :], OP.subtract))
                    ctx.sDVE.inc(vector.tensor_tensor(qr[32:64, t0:t0 + T],
                                                      qr[32:64, t0:t0 + T],
                                                      rtmp[32:64, :], OP.add))

            for qt in range(NT):
                q0 = qt * 128
                dve_now = ctx.sDVE.n
                act_now = ctx.sACT.n
                @block.tensor
                def _(tensor, qt=qt, q0=q0):
                    tensor.wait_ge(ctx.sDVE.h, dve_now)
                    tensor.wait_ge(ctx.sACT.h, act_now)
                    for kb in range(KB8):
                        ins = tensor.matmul(ps[:, kb * 512:(kb + 1) * 512],
                                            qn[:, q0:q0 + 128],
                                            kh[:, kb * T:(kb + 1) * T],
                                            start=True, stop=False)
                        ins = tensor.matmul(ps[:, kb * 512:(kb + 1) * 512],
                                            qr[:, q0:q0 + 128],
                                            krs[:, kb * T:(kb + 1) * T],
                                            start=False, stop=True)
                    ctx.sPE.inc(ins)

                @block.vector
                def _(vector, qt=qt):
                    vector.wait_ge(ctx.sPE.h, ctx.sPE.n)
                    ctx.sDVE.inc(vector.reduce_max(mx[:, 0:1], ps[:, 0:S],
                                                   axis=mybir.AxisListType.X))
                    vector.wait_ge(ctx.sDVE.h, ctx.sDVE.n)
                    ctx.sDVE.inc(vector.tensor_scalar(mx[:, 1:2], mx[:, 0:1],
                                                      -1.0, None, OP.mult))

                @block.scalar
                def _(scalar, qt=qt):
                    scalar.wait_ge(ctx.sDVE.h, ctx.sDVE.n)
                    ctx.sACT.inc(scalar.activation(pbuf[:], ps[:, 0:S], AF.Exp,
                                                   bias=mx[:, 1:2], scale=1.0,
                                                   accum_out=mx[:, 2:3]))

                @block.vector
                def _(vector, qt=qt):
                    vector.wait_ge(ctx.sACT.h, ctx.sACT.n)
                    ctx.sDVE.inc(vector.reciprocal(mx[:, 3:4], mx[:, 2:3]))
                    vector.wait_ge(ctx.sDVE.h, ctx.sDVE.n)
                    ctx.sDVE.inc(vector.tensor_scalar(pbuf[:], pbuf[:],
                                                      mx[:, 3:4], None, OP.mult))

                dve_pg = ctx.sDVE.n
                for r in range(4):
                    @block.tensor
                    def _(tensor, r=r):
                        if r == 0:
                            tensor.wait_ge(ctx.sDVE.h, dve_pg)
                        else:
                            tensor.wait_ge(ctx.sDVE.h, dve_pg + r)
                        for j in range(8):
                            kb = r * 8 + j
                            ins = tensor.transpose(
                                ps[:, j * 512:j * 512 + 128],
                                pbuf[:, kb * 128:(kb + 1) * 128], ident[:])
                        ctx.sPE.inc(ins)

                    @block.vector
                    def _(vector, r=r):
                        vector.wait_ge(ctx.sPE.h, ctx.sPE.n)
                        src = ps[:, 0:4096].rearrange("p (b c) -> p b c", b=8)[:, :, 0:128]
                        dst = pbuf[:, r * 1024:(r + 1) * 1024].rearrange(
                            "p (b c) -> p b c", b=8)
                        ctx.sDVE.inc(vector.tensor_copy(dst, src))

                @block.tensor
                def _(tensor, qt=qt):
                    tensor.wait_ge(ctx.sDVE.h, ctx.sDVE.n)
                    for kb in range(KB128):
                        ins = tensor.matmul(ps[:, 2 * 512:2 * 512 + 128],
                                            vh[:, kb * 128:(kb + 1) * 128],
                                            pbuf[:, kb * 128:(kb + 1) * 128],
                                            start=(kb == 0), stop=(kb == KB128 - 1))
                    ctx.sPE.inc(ins)

                @block.vector
                def _(vector, qt=qt):
                    vector.wait_ge(ctx.sPE.h, ctx.sPE.n)
                    vector.wait_ge(ctx.qG.h, ctx.qG.n)
                    ctx.sDVE.inc(vector.tensor_copy(ostg[:], ps[:, 2 * 512:2 * 512 + 128]))

                dve_st = ctx.sDVE.n
                @block.gpsimd
                def _(gpsimd, h=h, q0=q0):
                    gpsimd.wait_ge(ctx.sDVE.h, dve_st)
                    ctx.qG.inc(gpsimd.dma_start(
                        out=attn_scr[h * 128:(h + 1) * 128, q0:q0 + 128],
                        in_=ostg[:]), 16)

    emit_phase_barrier(ctx)

    with contextlib.ExitStack() as p4:
        asb = p4.enter_context(nc.sbuf_tensor("a4_a", [128, DC * T], F32))
        xr = p4.enter_context(nc.sbuf_tensor("a4_x", [128, DC * T], F32))
        wbuf = p4.enter_context(nc.sbuf_tensor("a4_w", [128, 2 * DC * 128], F32))
        sq_a = p4.enter_context(nc.sbuf_tensor("a4_qa", [128, T], F32))
        sq_b = p4.enter_context(nc.sbuf_tensor("a4_qb", [128, T], F32))

        for ch in range(TOK // T):
            t0 = ch * T
            dve_prev = ctx.sDVE.n
            qg_prev = ctx.qG.n
            @block.sync
            def _(sync):
                sync.wait_ge(ctx.sDVE.h, dve_prev)
                sync.wait_ge(ctx.qG.h, qg_prev)
                for c in range(DC):
                    ctx.qS.inc(sync.dma_start(
                        out=asb[:, c * T:(c + 1) * T],
                        in_=attn_scr[c * 128:(c + 1) * 128, t0:t0 + T]), 16)
                    ctx.qS.inc(sync.dma_start(
                        out=xr[:, c * T:(c + 1) * T],
                        in_=xq_dram[c * 128:(c + 1) * 128, t0:t0 + T]), 16)

            pe_b = ctx.sPE.n
            dve_b = ctx.sDVE.n
            for o in range(DC):
                half = o % 2
                wdst = wbuf[:, half * DC * 128:(half + 1) * DC * 128]
                @block.sync
                def _(sync, o=o, wdst=wdst):
                    sync.wait_ge(ctx.sPE.h, pe_b + max(0, o - 1))
                    ctx.qS.inc(sync.dma_start(
                        out=wdst, in_=wo_dram[:, o * DC * 128:(o + 1) * DC * 128]), 16)

                @block.tensor
                def _(tensor, o=o, wdst=wdst):
                    tensor.wait_ge(ctx.qS.h, ctx.qS.n)
                    if o == 0:
                        tensor.wait_ge(ctx.sDVE.h, dve_b)
                    if o >= 4:
                        tensor.wait_ge(ctx.sDVE.h, dve_b + o - 3)
                    for c in range(DC):
                        ins = tensor.matmul(ps[:, (o % 4) * 512:(o % 4) * 512 + T],
                                            wdst[:, c * 128:(c + 1) * 128],
                                            asb[:, c * T:(c + 1) * T],
                                            start=(c == 0), stop=(c == DC - 1))
                    ctx.sPE.inc(ins)

                @block.vector
                def _(vector, o=o):
                    vector.wait_ge(ctx.sPE.h, pe_b + o + 1)
                    ctx.sDVE.inc(vector.tensor_tensor(
                        xr[:, o * T:(o + 1) * T],
                        ps[:, (o % 4) * 512:(o % 4) * 512 + T],
                        xr[:, o * T:(o + 1) * T], OP.add))

            s_tiles = [xr[:, c * T:(c + 1) * T] for c in range(DC)]
            emit_ln(ctx, T, s_tiles, s_tiles, sq_a[:], sq_b[:], ps, 0, ones2, rows)

            dve_done = ctx.sDVE.n
            @block.gpsimd
            def _(gpsimd):
                gpsimd.wait_ge(ctx.sDVE.h, dve_done)
                for c in range(DC):
                    ctx.qG.inc(gpsimd.dma_start(
                        out=out_dram[c * 128:(c + 1) * 128, t0:t0 + T],
                        in_=xr[:, c * T:(c + 1) * T]), 16)


def xpos_ops(ctx, vector, dst, t, cs, sn, u, u2):
    ctx.sDVE.inc(vector.tensor_copy(u2[0:64, :], t[64:128, :]))
    ctx.sDVE.inc(vector.tensor_copy(u2[64:128, :], t[0:64, :]))
    ctx.sDVE.inc(vector.tensor_tensor(u, t, cs, OP.mult))
    vector.wait_ge(ctx.sDVE.h, ctx.sDVE.n)
    ctx.sDVE.inc(vector.tensor_tensor(dst, u2, sn, OP.mult))
    vector.wait_ge(ctx.sDVE.h, ctx.sDVE.n)
    ctx.sDVE.inc(vector.tensor_tensor(dst[0:64, :], u[0:64, :], dst[0:64, :], OP.subtract))
    ctx.sDVE.inc(vector.tensor_tensor(dst[64:128, :], u[64:128, :], dst[64:128, :], OP.add))


def emit_norm_rows(ctx, ps_sum_row, ps_ssq_row, denom, eps, rows):
    block = ctx.block
    r_mu, r_m2, r_A, r_B = rows

    @block.vector
    def _(vector):
        vector.wait_ge(ctx.sPE.h, ctx.sPE.n)
        ctx.sDVE.inc(vector.tensor_scalar(r_mu, ps_sum_row, 1.0 / denom, None, OP.mult))
        ctx.sDVE.inc(vector.tensor_scalar(r_m2, ps_ssq_row, 1.0 / denom, float(eps),
                                          OP.mult, OP.add))
        vector.wait_ge(ctx.sDVE.h, ctx.sDVE.n)
        ctx.sDVE.inc(vector.tensor_tensor(r_A, r_mu, r_mu, OP.mult))
        vector.wait_ge(ctx.sDVE.h, ctx.sDVE.n)
        ctx.sDVE.inc(vector.tensor_tensor(r_m2, r_m2, r_A, OP.subtract))

    @block.scalar
    def _(scalar):
        scalar.wait_ge(ctx.sDVE.h, ctx.sDVE.n)
        ctx.sACT.inc(scalar.activation(r_A, r_m2, AF.Ln, bias=0.0, scale=1.0))
        scalar.wait_ge(ctx.sACT.h, ctx.sACT.n)
        ctx.sACT.inc(scalar.activation(r_A, r_A, AF.Exp, bias=0.0, scale=-0.5))

    @block.vector
    def _(vector):
        vector.wait_ge(ctx.sACT.h, ctx.sACT.n)
        ctx.sDVE.inc(vector.scalar_tensor_tensor(r_B, r_mu, -1.0, r_A,
                                                 OP.mult, OP.mult))


def emit_l1attn(nc, block, ctx, x_dram, out_dram, wq_dram, bq_dram, wk_dram,
                bk_dram, wv_dram, bvr_dram, wg_dram, bg_dram, wo_dram, bo_dram,
                csq_dram, snq_dram, csk_dram, snk_dram, gcol_dram, arow_dram,
                mask_dram, ones_dram, kin, vin, kgo, vgo, stack):
    ones2 = stack.enter_context(nc.sbuf_tensor("r_on", [128, 128], F32))
    gcol = stack.enter_context(nc.sbuf_tensor("r_gc", [128, 16 * 272], F32))
    abcast = stack.enter_context(nc.sbuf_tensor("r_ab", [128, 16 * 128], F32))
    masks = stack.enter_context(nc.sbuf_tensor("r_mk", [128, 16 * 2 * 128], F32))
    bvec = stack.enter_context(nc.sbuf_tensor("r_bvec", [128, 64], F32))
    rows_t = stack.enter_context(nc.sbuf_tensor("r_rw", [1, 4 * 512], F32))
    ps = stack.enter_context(nc.psum_tensor("r_ps", [128, 8 * 512], F32))

    rows = [rows_t[0:1, i * 512:(i + 1) * 512] for i in range(4)]
    rows256 = [rows_t[0:1, i * 512:i * 512 + 256] for i in range(4)]
    bq_c = lambda h: bvec[:, h:h + 1]
    bk_c = lambda h: bvec[:, 16 + h:16 + h + 1]
    bg_c = lambda h: bvec[:, 32 + h:32 + h + 1]
    bo_c = lambda o: bvec[:, 48 + o:48 + o + 1]
    gcol_idx = lambda h, j, kt: h * 272 + j * j + j + kt

    @block.sync
    def _(sync):
        for dst, src in [(ones2[:], ones_dram[:, :]), (gcol[:], gcol_dram[:, :]),
                         (abcast[:], arow_dram[:, :]), (masks[:], mask_dram[:, :]),
                         (bvec[:, 0:16], bq_dram[:, :]), (bvec[:, 16:32], bk_dram[:, :]),
                         (bvec[:, 32:48], bg_dram[:, :]), (bvec[:, 48:64], bo_dram[:, :])]:
            ctx.qS.inc(sync.dma_start(out=dst, in_=src), 16)

    with contextlib.ExitStack() as pk:
        T = 512
        xs = pk.enter_context(nc.sbuf_tensor("rk_x", [128, DC * T], F32))
        wbuf = pk.enter_context(nc.sbuf_tensor("rk_w", [128, 2 * DC * 128], F32))
        kst = pk.enter_context(nc.sbuf_tensor("rk_k", [128, T], F32))
        ktmp = pk.enter_context(nc.sbuf_tensor("rk_kt", [128, T], F32))
        utmp = pk.enter_context(nc.sbuf_tensor("rk_u", [128, T], F32))
        u2tmp = pk.enter_context(nc.sbuf_tensor("rk_u2", [128, T], F32))
        vstage = pk.enter_context(nc.sbuf_tensor("rk_v", [128, 4 * D], F32))
        csk = pk.enter_context(nc.sbuf_tensor("rk_rc", [128, TOK], F32))
        snk = pk.enter_context(nc.sbuf_tensor("rk_rs", [128, TOK], F32))
        bvb = pk.enter_context(nc.sbuf_tensor("rk_bvb", [128, D], F32))
        bvrow = pk.enter_context(nc.sbuf_tensor("rk_bvr", [1, D], F32))

        @block.sync
        def _(sync):
            ctx.qS.inc(sync.dma_start(out=csk[:], in_=csk_dram[:, :]), 16)
            ctx.qS.inc(sync.dma_start(out=snk[:], in_=snk_dram[:, :]), 16)
            ctx.qS.inc(sync.dma_start(out=bvrow[:], in_=bvr_dram[:, :]), 16)

        @block.tensor
        def _(tensor):
            tensor.wait_ge(ctx.qS.h, ctx.qS.n)
            for i in range(4):
                ctx.sPE.inc(tensor.matmul(ps[:, i * 512:(i + 1) * 512],
                                          ones2[0:1, :],
                                          bvrow[0:1, i * 512:(i + 1) * 512],
                                          start=True, stop=True))

        @block.vector
        def _(vector):
            vector.wait_ge(ctx.sPE.h, ctx.sPE.n)
            for i in range(4):
                ctx.sDVE.inc(vector.tensor_copy(bvb[:, i * 512:(i + 1) * 512],
                                                ps[:, i * 512:(i + 1) * 512]))

        for ch in range(TOK // T):
            t0 = ch * T
            dve_prev = ctx.sDVE.n
            @block.sync
            def _(sync):
                sync.wait_ge(ctx.sDVE.h, dve_prev)
                for c in range(DC):
                    ctx.qS.inc(sync.dma_start(
                        out=xs[:, c * T:(c + 1) * T],
                        in_=x_dram[c * 128:(c + 1) * 128, t0:t0 + T]), 16)

            pe_sw = ctx.sPE.n
            half_user = {0: pe_sw, 1: pe_sw}
            for h in range(N_HEADS):
                half = h % 2
                wslc = wbuf[:, half * DC * 128:(half + 1) * DC * 128]

                @block.sync
                def _(sync, h=h, wslc=wslc, wait_n=half_user[half]):
                    sync.wait_ge(ctx.sPE.h, wait_n)
                    ctx.qS.inc(sync.dma_start(
                        out=wslc, in_=wk_dram[:, h * DC * 128:(h + 1) * DC * 128]), 16)

                @block.tensor
                def _(tensor, h=h, wslc=wslc):
                    tensor.wait_ge(ctx.qS.h, ctx.qS.n)
                    tensor.wait_ge(ctx.sDVE.h, ctx.sDVE.n)
                    for c in range(DC):
                        ins = tensor.matmul(ps[:, (h % 4) * 512:(h % 4) * 512 + T],
                                            wslc[:, c * 128:(c + 1) * 128],
                                            xs[:, c * T:(c + 1) * T],
                                            start=(c == 0), stop=(c == DC - 1))
                    ctx.sPE.inc(ins)
                half_user[half] = ctx.sPE.n

                @block.vector
                def _(vector, h=h, t0=t0):
                    vector.wait_ge(ctx.sPE.h, ctx.sPE.n)
                    vector.wait_ge(ctx.qG.h, ctx.qG.n)
                    pk_ = ps[:, (h % 4) * 512:(h % 4) * 512 + T]
                    ctx.sDVE.inc(vector.tensor_scalar(ktmp[:], pk_, bk_c(h), None, OP.add))
                    vector.wait_ge(ctx.sDVE.h, ctx.sDVE.n)
                    xpos_ops(ctx, vector, kst[:], ktmp[:],
                             csk[:, t0:t0 + T], snk[:, t0:t0 + T], utmp[:], u2tmp[:])

                dve_st = ctx.sDVE.n
                @block.gpsimd
                def _(gpsimd, h=h, t0=t0):
                    gpsimd.wait_ge(ctx.sDVE.h, dve_st)
                    ctx.qG.inc(gpsimd.dma_start(
                        out=kin[h * 128:(h + 1) * 128, t0:t0 + T], in_=kst[:]), 16)

            pe_sw = ctx.sPE.n
            half_user = {0: pe_sw, 1: pe_sw}
            qg_v = ctx.qG.n
            for h in range(N_HEADS):
                half = h % 2
                wslc = wbuf[:, half * DC * 128:(half + 1) * DC * 128]

                @block.sync
                def _(sync, h=h, wslc=wslc, wait_n=half_user[half]):
                    sync.wait_ge(ctx.sPE.h, wait_n)
                    ctx.qS.inc(sync.dma_start(
                        out=wslc, in_=wv_dram[:, h * DC * 128:(h + 1) * DC * 128]), 16)

                for tb in range(4):
                    @block.tensor
                    def _(tensor, h=h, tb=tb, wslc=wslc):
                        tensor.wait_ge(ctx.qS.h, ctx.qS.n)
                        tensor.wait_ge(ctx.sDVE.h, ctx.sDVE.n)
                        for c in range(DC):
                            ins = tensor.matmul(
                                ps[:, tb * 512:tb * 512 + 128],
                                xs[:, c * T + tb * 128:c * T + (tb + 1) * 128],
                                wslc[:, c * 128:(c + 1) * 128],
                                start=(c == 0), stop=(c == DC - 1))
                        ctx.sPE.inc(ins)

                    @block.vector
                    def _(vector, h=h, tb=tb):
                        vector.wait_ge(ctx.sPE.h, ctx.sPE.n)
                        if h == 0 and tb == 0:
                            vector.wait_ge(ctx.qG.h, qg_v)
                        ctx.sDVE.inc(vector.tensor_tensor(
                            vstage[:, tb * D + h * 128:tb * D + (h + 1) * 128],
                            ps[:, tb * 512:tb * 512 + 128],
                            bvb[:, h * 128:(h + 1) * 128], OP.add))
                half_user[half] = ctx.sPE.n

            dve_st = ctx.sDVE.n
            @block.gpsimd
            def _(gpsimd, t0=t0):
                gpsimd.wait_ge(ctx.sDVE.h, dve_st)
                for tb in range(4):
                    ctx.qG.inc(gpsimd.dma_start(
                        out=vin[t0 + tb * 128:t0 + (tb + 1) * 128, :],
                        in_=vstage[:, tb * D:(tb + 1) * D]), 16)

        qg_all = ctx.qG.n
        @block.gpsimd
        def _(gpsimd):
            gpsimd.wait_ge(ctx.qG.h, qg_all)
            ctx.qA.inc(gpsimd.collective_compute(
                "AllGather", OP.bypass,
                replica_groups=[[0, 1], [2, 3], [4, 5], [6, 7]],
                ins=[kin.opt()], outs=[kgo.opt()]), 1)
            ctx.qA.inc(gpsimd.collective_compute(
                "AllGather", OP.bypass,
                replica_groups=[[0, 1], [2, 3], [4, 5], [6, 7]],
                ins=[vin.opt()], outs=[vgo.opt()]), 1)

    emit_phase_barrier(ctx)

    with contextlib.ExitStack() as pq:
        T = 256
        xs = pq.enter_context(nc.sbuf_tensor("rq_x", [128, DC * T], F32))
        wbuf = pq.enter_context(nc.sbuf_tensor("rq_w", [128, 2 * DC * 128], F32))
        qall = pq.enter_context(nc.sbuf_tensor("rq_q", [128, 16 * T], F32))
        gall = pq.enter_context(nc.sbuf_tensor("rq_g", [128, 16 * T], F32))
        retall = pq.enter_context(nc.sbuf_tensor("rq_r", [128, 16 * T], F32))
        kstrip = pq.enter_context(nc.sbuf_tensor("rq_ks", [128, 32 * 128], F32))
        vstrip = pq.enter_context(nc.sbuf_tensor("rq_vs", [128, 32 * 128], F32))
        csq = pq.enter_context(nc.sbuf_tensor("rq_rc", [128, TOK], F32))
        snq = pq.enter_context(nc.sbuf_tensor("rq_rs", [128, TOK], F32))
        ttmp = pq.enter_context(nc.sbuf_tensor("rq_t", [128, T], F32))
        utmp = pq.enter_context(nc.sbuf_tensor("rq_u", [128, T], F32))
        u2tmp = pq.enter_context(nc.sbuf_tensor("rq_u2", [128, T], F32))
        pts = pq.enter_context(nc.sbuf_tensor("rq_pt", [128, 4 * 128], F32))
        sq_a = pq.enter_context(nc.sbuf_tensor("rq_qa", [128, T], F32))
        sq_b = pq.enter_context(nc.sbuf_tensor("rq_qb", [128, T], F32))

        @block.sync
        def _(sync):
            ctx.qS.inc(sync.dma_start(out=csq[:], in_=csq_dram[:, :]), 16)
            ctx.qS.inc(sync.dma_start(out=snq[:], in_=snq_dram[:, :]), 16)
            sync.wait_ge(ctx.qA.h, 2)

        for ch in range(TOK // T):
            t0 = ch * T
            nloc = 2 * ch + 2
            dve_prev = ctx.sDVE.n
            qg_prev = ctx.qG.n
            @block.sync
            def _(sync):
                sync.wait_ge(ctx.sDVE.h, dve_prev)
                sync.wait_ge(ctx.qG.h, qg_prev)
                for c in range(DC):
                    ctx.qS.inc(sync.dma_start(
                        out=xs[:, c * T:(c + 1) * T],
                        in_=x_dram[c * 128:(c + 1) * 128, t0:t0 + T]), 16)

            @block.vector
            def _(vector, qg_prev=qg_prev):
                vector.wait_ge(ctx.qG.h, qg_prev)

            pe_sw = ctx.sPE.n
            half_user = {0: pe_sw, 1: pe_sw}
            for h in range(N_HEADS):
                half = h % 2
                wq_s = wbuf[:, half * DC * 128:(half + 1) * DC * 128]
                wg_s = wbuf[:, (1 - half) * DC * 128:(2 - half) * DC * 128]

                @block.sync
                def _(sync, h=h, wq_s=wq_s, wait_n=half_user[half]):
                    sync.wait_ge(ctx.sPE.h, wait_n)
                    ctx.qS.inc(sync.dma_start(
                        out=wq_s, in_=wq_dram[:, h * DC * 128:(h + 1) * DC * 128]), 16)

                @block.tensor
                def _(tensor, h=h, wq_s=wq_s):
                    tensor.wait_ge(ctx.qS.h, ctx.qS.n)
                    tensor.wait_ge(ctx.sDVE.h, ctx.sDVE.n)
                    for c in range(DC):
                        ins = tensor.matmul(ps[:, 0 * 512:0 * 512 + T],
                                            wq_s[:, c * 128:(c + 1) * 128],
                                            xs[:, c * T:(c + 1) * T],
                                            start=(c == 0), stop=(c == DC - 1))
                    ctx.sPE.inc(ins)
                half_user[half] = ctx.sPE.n

                @block.vector
                def _(vector, h=h, t0=t0):
                    vector.wait_ge(ctx.sPE.h, ctx.sPE.n)
                    vector.wait_ge(ctx.sACT.h, ctx.sACT.n)
                    ctx.sDVE.inc(vector.tensor_scalar(ttmp[:], ps[:, 0:T],
                                                      bq_c(h), None, OP.add))
                    vector.wait_ge(ctx.sDVE.h, ctx.sDVE.n)
                    xpos_ops(ctx, vector, qall[:, h * T:(h + 1) * T], ttmp[:],
                             csq[:, t0:t0 + T], snq[:, t0:t0 + T], utmp[:], u2tmp[:])

                @block.sync
                def _(sync, h=h, wg_s=wg_s, wait_n=half_user[1 - half]):
                    sync.wait_ge(ctx.sPE.h, wait_n)
                    ctx.qS.inc(sync.dma_start(
                        out=wg_s, in_=wg_dram[:, h * DC * 128:(h + 1) * DC * 128]), 16)

                @block.tensor
                def _(tensor, h=h, wg_s=wg_s):
                    tensor.wait_ge(ctx.qS.h, ctx.qS.n)
                    tensor.wait_ge(ctx.sDVE.h, ctx.sDVE.n)
                    for c in range(DC):
                        ins = tensor.matmul(ps[:, 1 * 512:1 * 512 + T],
                                            wg_s[:, c * 128:(c + 1) * 128],
                                            xs[:, c * T:(c + 1) * T],
                                            start=(c == 0), stop=(c == DC - 1))
                    ctx.sPE.inc(ins)
                half_user[1 - half] = ctx.sPE.n

                @block.vector
                def _(vector, h=h):
                    vector.wait_ge(ctx.sPE.h, ctx.sPE.n)
                    vector.wait_ge(ctx.sACT.h, ctx.sACT.n)
                    ctx.sDVE.inc(vector.tensor_scalar(
                        ttmp[:], ps[:, 1 * 512:1 * 512 + T], bg_c(h), None, OP.add))

                @block.scalar
                def _(scalar, h=h):
                    scalar.wait_ge(ctx.sDVE.h, ctx.sDVE.n)
                    ctx.sACT.inc(scalar.activation(gall[:, h * T:(h + 1) * T],
                                                   ttmp[:], AF.Silu,
                                                   bias=0.0, scale=1.0))

            koff = lambda kt: (0 if kt % 2 == 0 else 2048) + (kt // 2) * 128
            for h in range(N_HEADS):
                pe_h = ctx.sPE.n
                @block.sync
                def _(sync, h=h, nloc=nloc, wait_n=pe_h):
                    sync.wait_ge(ctx.sPE.h, wait_n)
                    ctx.qS.inc(sync.dma_start(
                        out=kstrip[:, 0:nloc * 128],
                        in_=kgo[h * 128:(h + 1) * 128, 0:nloc * 128]), 16)
                    ctx.qS.inc(sync.dma_start(
                        out=kstrip[:, 2048:2048 + nloc * 128],
                        in_=kgo[D + h * 128:D + (h + 1) * 128, 0:nloc * 128]), 16)
                    ctx.qS.inc(sync.dma_start(
                        out=vstrip[:, 0:nloc * 128].rearrange("p (l m) -> p l m", l=nloc),
                        in_=vgo[0:nloc * 128, h * 128:(h + 1) * 128].rearrange(
                            "(l p) m -> p l m", p=128)), 16)
                    ctx.qS.inc(sync.dma_start(
                        out=vstrip[:, 2048:2048 + nloc * 128].rearrange("p (l m) -> p l m", l=nloc),
                        in_=vgo[TOK:TOK + nloc * 128, h * 128:(h + 1) * 128].rearrange(
                            "(l p) m -> p l m", p=128)), 16)

                for j in range(2):
                    jl = 2 * ch + j
                    ktmax = 2 * jl + 1
                    pe_bj = ctx.sPE.n
                    dve_bj = ctx.sDVE.n
                    qrhs = qall[:, h * T + j * 128:h * T + (j + 1) * 128]
                    for kt in range(ktmax + 1):
                        sbank = 4 + kt % 2

                        @block.tensor
                        def _(tensor, kt=kt, sbank=sbank, qrhs=qrhs):
                            if kt == 0:
                                tensor.wait_ge(ctx.qS.h, ctx.qS.n)
                                tensor.wait_ge(ctx.sDVE.h, dve_bj)
                            if kt >= 2:
                                tensor.wait_ge(ctx.sDVE.h, dve_bj + kt - 1)
                            ctx.sPE.inc(tensor.matmul(
                                ps[:, sbank * 512:sbank * 512 + 128],
                                kstrip[:, koff(kt):koff(kt) + 128],
                                qrhs, start=True, stop=True))

                        @block.vector
                        def _(vector, h=h, jl=jl, kt=kt, sbank=sbank):
                            vector.wait_ge(ctx.sPE.h, pe_bj + 2 * kt + 1)
                            slot = pts[:, (kt % 4) * 128:(kt % 4 + 1) * 128]
                            sc = ps[:, sbank * 512:sbank * 512 + 128]
                            if kt < 2 * jl:
                                ctx.sDVE.inc(vector.tensor_scalar(
                                    slot, sc, gcol[:, gcol_idx(h, jl, kt):
                                                   gcol_idx(h, jl, kt) + 1],
                                    None, OP.mult))
                            else:
                                w = kt - 2 * jl
                                ctx.sDVE.inc(vector.tensor_tensor(
                                    slot, sc,
                                    masks[:, (h * 2 + w) * 128:(h * 2 + w + 1) * 128],
                                    OP.mult))

                        @block.tensor
                        def _(tensor, j=j, kt=kt, ktmax=ktmax):
                            tensor.wait_ge(ctx.sDVE.h, dve_bj + kt + 1)
                            ctx.sPE.inc(tensor.matmul(
                                ps[:, (6 + j) * 512:(6 + j) * 512 + 128],
                                vstrip[:, koff(kt):koff(kt) + 128],
                                pts[:, (kt % 4) * 128:(kt % 4 + 1) * 128],
                                start=(kt == 0), stop=(kt == ktmax)))

                    @block.vector
                    def _(vector, h=h, j=j, ktmax=ktmax):
                        vector.wait_ge(ctx.sPE.h, pe_bj + 2 * (ktmax + 1))
                        ctx.sDVE.inc(vector.tensor_tensor(
                            retall[:, h * T + j * 128:h * T + (j + 1) * 128],
                            ps[:, (6 + j) * 512:(6 + j) * 512 + 128],
                            abcast[:, h * 128:(h + 1) * 128], OP.mult))

            for h in range(N_HEADS):
                rslc = retall[:, h * T:(h + 1) * T]
                @block.tensor
                def _(tensor, h=h, rslc=rslc):
                    tensor.wait_ge(ctx.sDVE.h, ctx.sDVE.n)
                    ctx.sPE.inc(tensor.matmul(ps[0:1, 0:T], ones2[:, 0:1], rslc,
                                              start=True, stop=True))

                @block.vector
                def _(vector, h=h, rslc=rslc):
                    vector.wait_ge(ctx.sPE.h, ctx.sPE.n)
                    ctx.sDVE.inc(vector.tensor_tensor(sq_a[:], rslc, rslc, OP.mult))

                @block.tensor
                def _(tensor, h=h):
                    tensor.wait_ge(ctx.sDVE.h, ctx.sDVE.n)
                    ctx.sPE.inc(tensor.matmul(ps[0:1, 512:512 + T], ones2[:, 0:1],
                                              sq_a[:], start=True, stop=True))

                emit_norm_rows(ctx, ps[0:1, 0:T], ps[0:1, 512:512 + T], 128.0,
                               GN_EPS, rows256)

                @block.tensor
                def _(tensor, h=h):
                    tensor.wait_ge(ctx.sDVE.h, ctx.sDVE.n)
                    tensor.matmul(ps[:, 2 * 512:2 * 512 + T], ones2[0:1, :],
                                  rows256[2], start=True, stop=True)
                    ctx.sPE.inc(tensor.matmul(ps[:, 3 * 512:3 * 512 + T], ones2[0:1, :],
                                              rows256[3], start=True, stop=True))

                @block.vector
                def _(vector, h=h, rslc=rslc):
                    vector.wait_ge(ctx.sPE.h, ctx.sPE.n)
                    ctx.sDVE.inc(vector.tensor_tensor(
                        ttmp[:], rslc, ps[:, 2 * 512:2 * 512 + T], OP.mult))
                    vector.wait_ge(ctx.sDVE.h, ctx.sDVE.n)
                    ctx.sDVE.inc(vector.tensor_tensor(
                        ttmp[:], ttmp[:], ps[:, 3 * 512:3 * 512 + T], OP.add))
                    vector.wait_ge(ctx.sDVE.h, ctx.sDVE.n)
                    ctx.sDVE.inc(vector.tensor_tensor(
                        gall[:, h * T:(h + 1) * T], ttmp[:],
                        gall[:, h * T:(h + 1) * T], OP.mult))

            pe_sw = ctx.sPE.n
            half_user = {0: ctx.sPE.n, 1: ctx.sPE.n}
            for o in range(DC):
                half = o % 2
                wo_s = wbuf[:, half * DC * 128:(half + 1) * DC * 128]
                @block.sync
                def _(sync, o=o, wo_s=wo_s, wait_n=half_user[half]):
                    sync.wait_ge(ctx.sPE.h, wait_n)
                    ctx.qS.inc(sync.dma_start(
                        out=wo_s, in_=wo_dram[:, o * DC * 128:(o + 1) * DC * 128]), 16)

                @block.tensor
                def _(tensor, o=o, wo_s=wo_s):
                    tensor.wait_ge(ctx.qS.h, ctx.qS.n)
                    if o == 0:
                        tensor.wait_ge(ctx.sDVE.h, ctx.sDVE.n)
                    if o >= 4:
                        tensor.wait_ge(ctx.sDVE.h, dve_sw + o - 3)
                    for c in range(DC):
                        ins = tensor.matmul(ps[:, (o % 4) * 512:(o % 4) * 512 + T],
                                            wo_s[:, c * 128:(c + 1) * 128],
                                            gall[:, c * T:(c + 1) * T],
                                            start=(c == 0), stop=(c == DC - 1))
                    ctx.sPE.inc(ins)
                half_user[half] = ctx.sPE.n
                if o == 0:
                    dve_sw = ctx.sDVE.n

                @block.vector
                def _(vector, o=o):
                    vector.wait_ge(ctx.sPE.h, ctx.sPE.n)
                    ctx.sDVE.inc(vector.scalar_tensor_tensor(
                        qall[:, o * T:(o + 1) * T],
                        ps[:, (o % 4) * 512:(o % 4) * 512 + T],
                        bo_c(o), xs[:, o * T:(o + 1) * T], OP.add, OP.add))

            s_tiles = [qall[:, c * T:(c + 1) * T] for c in range(DC)]
            emit_ln(ctx, T, s_tiles, s_tiles, sq_a[:], sq_b[:], ps, 0, ones2, rows256)

            dve_done = ctx.sDVE.n
            @block.gpsimd
            def _(gpsimd, t0=t0):
                gpsimd.wait_ge(ctx.sDVE.h, dve_done)
                for c in range(DC):
                    ctx.qG.inc(gpsimd.dma_start(
                        out=out_dram[c * 128:(c + 1) * 128, t0:t0 + T],
                        in_=qall[:, c * T:(c + 1) * T]), 16)


def own_token_tiles(core):
    par = core % 2
    return [2 * j + par for j in range(NT)]


def own_token_index(core):
    return np.concatenate([np.arange(g * 128, (g + 1) * 128)
                           for g in own_token_tiles(core)])


def vec_tiles(v):
    n = v.shape[0] // 128
    return np.ascontiguousarray(v.reshape(n, 128).T)


def lhsT_dev(W):
    din, dout = W.shape
    nc_, no_ = din // 128, dout // 128
    return np.ascontiguousarray(
        W.reshape(nc_, 128, no_, 128).transpose(1, 2, 0, 3).reshape(128, -1))


def lhsT_dev_k(W):
    din, dout = W.shape
    nc_, no_ = din // 128, dout // 128
    return np.ascontiguousarray(
        W.reshape(nc_, 128, no_, 128).transpose(1, 0, 2, 3).reshape(128, -1))


def build_l1ffn_program():
    nc = bass.Bass("TRN2", target_bir_lowering=False)
    x = nc.dram_tensor("x", [D, TOK], F32, kind="ExternalInput").ap()
    w0t = nc.dram_tensor("w0t", [128, IC * DC * 128], F32, kind="ExternalInput").ap()
    w1t = nc.dram_tensor("w1t", [128, IC * DC * 128], F32, kind="ExternalInput").ap()
    b0 = nc.dram_tensor("b0", [128, IC], F32, kind="ExternalInput").ap()
    b1 = nc.dram_tensor("b1", [128, DC], F32, kind="ExternalInput").ap()
    ones = nc.dram_tensor("ones", [128, 128], F32, kind="ExternalInput").ap()
    out = nc.dram_tensor("out", [D, TOK], F32, kind="ExternalOutput").ap()

    with contextlib.ExitStack() as stack:
        block = stack.enter_context(nc.Block())
        ctx = make_ctx(nc, block, stack)
        emit_l1ffn(nc, block, ctx, x, out, w0t, w1t, b0, b1, ones, stack)
        @block.gpsimd
        def _(gpsimd):
            gpsimd.wait_ge(ctx.qG.h, ctx.qG.n)
    return nc


def rope_tables(dim, positions, scale=1.0):
    inv_freq = (1.0 / (np.float32(10000.0) **
                       (np.arange(0, dim, 2, dtype=np.float32) / np.float32(dim))))
    inv_freq = inv_freq.astype(np.float32)
    pos = positions.astype(np.float32)
    ang = (pos[:, None] * inv_freq[None, :]).astype(np.float32)
    cos = np.cos(ang.astype(np.float64)).astype(np.float32) * np.float32(scale)
    sin = np.sin(ang.astype(np.float64)).astype(np.float32) * np.float32(scale)
    ct = np.concatenate([cos, cos], axis=1).T
    st = np.concatenate([sin, sin], axis=1).T
    return np.ascontiguousarray(ct), np.ascontiguousarray(st)


def build_l0attn_program():
    nc = bass.Bass("TRN2", target_bir_lowering=False)
    gi = lambda nm, shp: nc.dram_tensor(nm, shp, F32, kind="ExternalInput").ap()
    xk = gi("xk", [D, S])
    xq = gi("xq", [D, TOK])
    wcq = gi("wcq", [128, 6 * DC * 128])
    gq = gi("gq", [128, 6])
    wdq = gi("wdq", [128, 16 * 6 * 128])
    wdqr = gi("wdqr", [128, 16 * 6 * 64])
    wckv = gi("wckv", [128, 4 * DC * 128])
    gkv = gi("gkv", [128, 4])
    wdk = gi("wdk", [128, 16 * 4 * 128])
    wdv = gi("wdv", [128, 4 * 16 * 128])
    wkr = gi("wkr", [128, DC * 64])
    wo = gi("wo", [128, DC * DC * 128])
    rckv = gi("rckv", [64, S])
    rskv = gi("rskv", [64, S])
    rcq = gi("rcq", [64, TOK])
    rsq = gi("rsq", [64, TOK])
    ident = gi("ident", [128, 128])
    ones = gi("ones", [128, 128])
    attn_scr = nc.dram_tensor("attn_scr", [D, TOK], F32, kind="Internal").ap()
    qcn_scr = nc.dram_tensor("qcn_scr", [6 * 128, TOK], F32, kind="Internal").ap()
    out = nc.dram_tensor("out", [D, TOK], F32, kind="ExternalOutput").ap()

    with contextlib.ExitStack() as stack:
        block = stack.enter_context(nc.Block())
        ctx = make_ctx(nc, block, stack)
        emit_l0attn(nc, block, ctx, xk, xq, out, wcq, gq, wdq, wdqr, wckv,
                    gkv, wdk, wdv, wkr, wo, rckv, rskv, rcq, rsq, ident,
                    ones, attn_scr, qcn_scr, stack)
        @block.gpsimd
        def _(gpsimd):
            gpsimd.wait_ge(ctx.qG.h, ctx.qG.n)
    return nc


def stage_l0attn_shared(Wcq, g_q, Wdq, Wdqr, Wckv, g_kv, Wdk, Wdv, Wkr, Wo):
    rckv, rskv = rope_tables(64, np.arange(S))
    return {
        "wcq": lhsT_dev(Wcq),
        "gq": vec_tiles(g_q),
        "wdq": lhsT_dev(Wdq),
        "wdqr": np.ascontiguousarray(
            Wdqr.reshape(6, 128, 16, 64).transpose(1, 2, 0, 3).reshape(128, -1)),
        "wckv": lhsT_dev(Wckv),
        "gkv": vec_tiles(g_kv),
        "wdk": lhsT_dev(Wdk),
        "wdv": np.ascontiguousarray(
            Wdv.reshape(4, 128, 16, 128).transpose(1, 0, 2, 3).reshape(128, -1)),
        "wkr": np.ascontiguousarray(
            Wkr.reshape(16, 128, 64).transpose(1, 0, 2).reshape(128, -1)),
        "wo": lhsT_dev(Wo),
        "rckv": rckv, "rskv": rskv,
        "ident": np.ascontiguousarray(np.eye(128, dtype=np.float32)),
        "ones": np.ones((128, 128), np.float32),
    }


def run_l0attn(x_full, Wcq, g_q, Wdq, Wdqr, Wckv, g_kv, Wdk, Wdv, Wkr, Wo):
    nc = build_l0attn_program()
    shared = stage_l0attn_shared(Wcq, g_q, Wdq, Wdqr, Wckv, g_kv, Wdk, Wdv, Wkr, Wo)
    in_maps = []
    for c in range(NCORES):
        b = c // 2
        idx = own_token_index(c)
        rcq, rsq = rope_tables(64, idx, scale=1.0 / np.sqrt(192.0))
        in_maps.append({
            "xk": np.ascontiguousarray(x_full[b].T),
            "xq": np.ascontiguousarray(x_full[b][idx].T),
            "rcq": rcq, "rsq": rsq, **shared})
    res = run_bass_kernel_spmd(nc, in_maps, core_ids=list(range(NCORES)))
    out = np.zeros((B, S, D), np.float32)
    for c in range(NCORES):
        b = c // 2
        idx = own_token_index(c)
        out[b, idx] = res.results[c]["out"].T
    return out


def build_l1attn_program():
    nc = bass.Bass("TRN2", target_bir_lowering=False)
    gi = lambda nm, shp: nc.dram_tensor(nm, shp, F32, kind="ExternalInput").ap()
    x = gi("x", [D, TOK])
    wq = gi("wq", [128, 16 * DC * 128]); bq = gi("bq", [128, 16])
    wk = gi("wk", [128, 16 * DC * 128]); bk = gi("bk", [128, 16])
    wv = gi("wv", [128, 16 * DC * 128]); bvr = gi("bvr", [1, D])
    wg = gi("wg", [128, 16 * DC * 128]); bg = gi("bg", [128, 16])
    wo = gi("wo", [128, DC * DC * 128]); bo = gi("bo", [128, 16])
    csq = gi("csq", [128, TOK]); snq = gi("snq", [128, TOK])
    csk = gi("csk", [128, TOK]); snk = gi("snk", [128, TOK])
    gcol = gi("gcol", [128, 16 * 272])
    arow = gi("arow", [128, 16 * 128])
    mask = gi("mask", [128, 16 * 2 * 128])
    ones = gi("ones", [128, 128])
    kin = nc.dram_tensor("kin", [D, TOK], F32, kind="Internal").ap()
    vin = nc.dram_tensor("vin", [TOK, D], F32, kind="Internal").ap()
    kgo = nc.dram_tensor("kgo", [2 * D, TOK], F32, kind="Internal").ap()
    vgo = nc.dram_tensor("vgo", [2 * TOK, D], F32, kind="Internal").ap()
    out = nc.dram_tensor("out", [D, TOK], F32, kind="ExternalOutput").ap()

    with contextlib.ExitStack() as stack:
        block = stack.enter_context(nc.Block())
        ctx = make_ctx(nc, block, stack)
        emit_l1attn(nc, block, ctx, x, out, wq, bq, wk, bk, wv, bvr, wg, bg,
                    wo, bo, csq, snq, csk, snk, gcol, arow, mask, ones,
                    kin, vin, kgo, vgo, stack)
        @block.gpsimd
        def _(gpsimd):
            gpsimd.wait_ge(ctx.qG.h, ctx.qG.n)
    return nc


def xpos_tables(positions):
    dim, TH, SB = 128, 10000.0, 512.0
    inv_freq = (1.0 / (np.float32(TH) **
                       (np.arange(0, dim, 2, dtype=np.float32) / np.float32(dim))))
    inv_freq = inv_freq.astype(np.float32)
    pos = positions.astype(np.float32)
    fr = (pos[:, None] * inv_freq[None, :]).astype(np.float32)
    cos = np.cos(fr.astype(np.float64))
    sin = np.sin(fr.astype(np.float64))
    sbase = ((np.arange(0, dim, 2, dtype=np.float32) + np.float32(0.4 * dim))
             / np.float32(1.4 * dim)).astype(np.float32)
    power = ((pos - np.float32(S // 2)) / np.float32(SB)).astype(np.float32)
    scale = sbase.astype(np.float64)[None, :] ** power.astype(np.float64)[:, None]
    def lay(m):
        t = np.concatenate([m.T, m.T], axis=0)
        return np.ascontiguousarray(t.astype(np.float32))
    return (lay(cos * scale), lay(sin * scale),
            lay(cos / scale), lay(sin / scale))


def stage_l1attn_shared(Wq, bq, Wk, bk, Wv, bv, Wg, bg, Wo, bo):
    perm = np.concatenate([np.arange(0, 128, 2), np.arange(1, 128, 2)])
    colperm = np.concatenate([h * 128 + perm for h in range(16)])
    Wqp = np.ascontiguousarray(Wq[:, colperm])
    Wkp = np.ascontiguousarray(Wk[:, colperm])
    return {
        "wq": lhsT_dev(Wqp), "bq": vec_tiles(bq[colperm]),
        "wk": lhsT_dev(Wkp), "bk": vec_tiles(bk[colperm]),
        "wv": np.ascontiguousarray(
            Wv.reshape(16, 128, 16, 128).transpose(1, 2, 0, 3).reshape(128, -1)),
        "bvr": np.ascontiguousarray(bv[None, :]),
        "wg": lhsT_dev(Wg), "bg": vec_tiles(bg),
        "wo": lhsT_dev(Wo), "bo": vec_tiles(bo),
        "ones": np.ones((128, 128), np.float32),
    }


def stage_l1attn_core(core):
    par = core % 2
    idx = own_token_index(core)
    csq, snq, csk, snk = xpos_tables(idx)
    gam = (1.0 - 2.0 ** (-5.0 - np.arange(16, dtype=np.float64)))
    p = np.arange(128, dtype=np.float64)
    gcol = np.zeros((128, 16 * 272), np.float64)
    for h in range(16):
        for j in range(16):
            for kt in range(2 * j + 2):
                d = 128.0 * ((2 * j + par) - kt)
                gcol[:, h * 272 + j * j + j + kt] = gam[h] ** (d - p)
    qq = np.arange(128, dtype=np.float64)
    arowt = np.repeat(
        np.concatenate([gam[h] ** qq for h in range(16)])[None, :], 128, axis=0)
    q = np.arange(128, dtype=np.float64)
    mask = np.zeros((128, 16 * 2 * 128), np.float64)
    caus = (q[None, :] >= p[:, None])
    for h in range(16):
        diag = (gam[h] ** (-p))[:, None] * caus
        if par == 0:
            m0, m1 = diag, np.zeros((128, 128))
        else:
            m0 = np.repeat((gam[h] ** (128.0 - p))[:, None], 128, axis=1)
            m1 = diag
        mask[:, (h * 2) * 128:(h * 2 + 1) * 128] = m0
        mask[:, (h * 2 + 1) * 128:(h * 2 + 2) * 128] = m1
    return {
        "csq": csq, "snq": snq, "csk": csk, "snk": snk,
        "gcol": np.ascontiguousarray(gcol.astype(np.float32)),
        "arow": np.ascontiguousarray(arowt.astype(np.float32)),
        "mask": np.ascontiguousarray(mask.astype(np.float32)),
    }


def run_l1attn(x_full, Wq, bq, Wk, bk, Wv, bv, Wg, bg, Wo, bo):
    nc = build_l1attn_program()
    shared = stage_l1attn_shared(Wq, bq, Wk, bk, Wv, bv, Wg, bg, Wo, bo)
    in_maps = []
    for c in range(NCORES):
        b = c // 2
        idx = own_token_index(c)
        in_maps.append({"x": np.ascontiguousarray(x_full[b][idx].T),
                        **stage_l1attn_core(c), **shared})
    res = run_bass_kernel_spmd(nc, in_maps, core_ids=list(range(NCORES)))
    out = np.zeros((B, S, D), np.float32)
    for c in range(NCORES):
        b = c // 2
        idx = own_token_index(c)
        out[b, idx] = res.results[c]["out"].T
    return out


def hadamard128():
    H = np.array([[1.0]], np.float32)
    for _ in range(7):
        H = np.block([[H, H], [H, -H]])
    return np.ascontiguousarray(H.astype(np.float32))


def build_l0ffn_program():
    nc = bass.Bass("TRN2", target_bir_lowering=False)
    x = nc.dram_tensor("x", [D, TOK], F32, kind="ExternalInput").ap()
    wupt = nc.dram_tensor("wupt", [128, IC * DC * 128], F32, kind="ExternalInput").ap()
    wgtt = nc.dram_tensor("wgtt", [128, IC * DC * 128], F32, kind="ExternalInput").ap()
    wdnt = nc.dram_tensor("wdnt", [128, IC * DC * 128], F32, kind="ExternalInput").ap()
    had = nc.dram_tensor("had", [128, 128], F32, kind="ExternalInput").ap()
    su_ug = nc.dram_tensor("su_ug", [128, DC], F32, kind="ExternalInput").ap()
    su_dp = nc.dram_tensor("su_dp", [128, IC], F32, kind="ExternalInput").ap()
    ws_ug = nc.dram_tensor("ws_ug", [128, 2 * IC], F32, kind="ExternalInput").ap()
    ws_dp = nc.dram_tensor("ws_dp", [128, DC], F32, kind="ExternalInput").ap()
    ones = nc.dram_tensor("ones", [128, 128], F32, kind="ExternalInput").ap()
    out = nc.dram_tensor("out", [D, TOK], F32, kind="ExternalOutput").ap()

    with contextlib.ExitStack() as stack:
        block = stack.enter_context(nc.Block())
        ctx = make_ctx(nc, block, stack)
        emit_l0ffn(nc, block, ctx, x, out, wupt, wgtt, wdnt, had, su_ug,
                   su_dp, ws_ug, ws_dp, ones, stack)
        @block.gpsimd
        def _(gpsimd):
            gpsimd.wait_ge(ctx.qG.h, ctx.qG.n)
    return nc


def stage_l0ffn(SU_ug, SU_dp, Wscale_ug, Wscale_dp, W_up, W_gate, W_down):
    return {
        "wupt": lhsT_dev(np.ascontiguousarray(W_up.T)),
        "wgtt": lhsT_dev(np.ascontiguousarray(W_gate.T)),
        "wdnt": lhsT_dev_k(np.ascontiguousarray(W_down.T)),
        "had": hadamard128(),
        "su_ug": vec_tiles(SU_ug),
        "su_dp": vec_tiles(SU_dp),
        "ws_ug": vec_tiles(Wscale_ug),
        "ws_dp": vec_tiles(Wscale_dp),
        "ones": np.ones((128, 128), np.float32),
    }


def run_l0ffn(x_full, SU_ug, SU_dp, Wscale_ug, Wscale_dp, W_up, W_gate, W_down):
    xf = x_full.reshape(-1, D)
    nc = build_l0ffn_program()
    shared = stage_l0ffn(SU_ug, SU_dp, Wscale_ug, Wscale_dp, W_up, W_gate, W_down)
    in_maps = []
    for c in range(NCORES):
        b = c // 2
        idx = own_token_index(c) + b * S
        xc = np.ascontiguousarray(xf[idx].T)
        in_maps.append({"x": xc, **shared})
    res = run_bass_kernel_spmd(nc, in_maps, core_ids=list(range(NCORES)))
    out = np.zeros((B * S, D), np.float32)
    for c in range(NCORES):
        b = c // 2
        idx = own_token_index(c) + b * S
        out[idx] = res.results[c]["out"].T
    return out.reshape(x_full.shape)


def stage_l1ffn(W0, bias0, W1, bias1):
    return {
        "w0t": lhsT_dev(np.ascontiguousarray(W0.T)),
        "w1t": lhsT_dev_k(np.ascontiguousarray(W1.T)),
        "b0": vec_tiles(bias0),
        "b1": vec_tiles(bias1),
        "ones": np.ones((128, 128), np.float32),
    }


def run_l1ffn(x_full, W0, bias0, W1, bias1):
    xf = x_full.reshape(-1, D)
    nc = build_l1ffn_program()
    shared = stage_l1ffn(W0, bias0, W1, bias1)
    in_maps = []
    for c in range(NCORES):
        b = c // 2
        idx = own_token_index(c) + b * S
        xc = np.ascontiguousarray(xf[idx].T)
        in_maps.append({"x": xc, **shared})
    res = run_bass_kernel_spmd(nc, in_maps, core_ids=list(range(NCORES)))
    out = np.zeros((B * S, D), np.float32)
    for c in range(NCORES):
        b = c // 2
        idx = own_token_index(c) + b * S
        out[idx] = res.results[c]["out"].T
    return out.reshape(x_full.shape)


def build_fused_program():
    nc = bass.Bass("TRN2", target_bir_lowering=False)
    gi = lambda nm, shp: nc.dram_tensor(nm, shp, F32, kind="ExternalInput").ap()
    xk = gi("xk", [D, S])
    xq = gi("xq", [D, TOK])
    wcq = gi("wcq", [128, 6 * DC * 128]); gq = gi("gq", [128, 6])
    wdq = gi("wdq", [128, 16 * 6 * 128]); wdqr = gi("wdqr", [128, 16 * 6 * 64])
    wckv = gi("wckv", [128, 4 * DC * 128]); gkv = gi("gkv", [128, 4])
    wdk = gi("wdk", [128, 16 * 4 * 128]); wdv = gi("wdv", [128, 4 * 16 * 128])
    wkr = gi("wkr", [128, DC * 64]); wo = gi("wo", [128, DC * DC * 128])
    rckv = gi("rckv", [64, S]); rskv = gi("rskv", [64, S])
    rcq = gi("rcq", [64, TOK]); rsq = gi("rsq", [64, TOK])
    ident = gi("ident", [128, 128]); ones = gi("ones", [128, 128])
    wupt = gi("wupt", [128, IC * DC * 128]); wgtt = gi("wgtt", [128, IC * DC * 128])
    wdnt = gi("wdnt", [128, IC * DC * 128]); had = gi("had", [128, 128])
    su_ug = gi("su_ug", [128, DC]); su_dp = gi("su_dp", [128, IC])
    ws_ug = gi("ws_ug", [128, 2 * IC]); ws_dp = gi("ws_dp", [128, DC])
    wq = gi("wq", [128, 16 * DC * 128]); bq = gi("bq", [128, 16])
    wk = gi("wk", [128, 16 * DC * 128]); bk = gi("bk", [128, 16])
    wv = gi("wv", [128, 16 * DC * 128]); bvr = gi("bvr", [1, D])
    wg = gi("wg", [128, 16 * DC * 128]); bg = gi("bg", [128, 16])
    wo1 = gi("wo1", [128, DC * DC * 128]); bo = gi("bo", [128, 16])
    csq = gi("csq", [128, TOK]); snq = gi("snq", [128, TOK])
    csk = gi("csk", [128, TOK]); snk = gi("snk", [128, TOK])
    gcol = gi("gcol", [128, 16 * 272]); arow = gi("arow", [128, 16 * 128])
    mask = gi("mask", [128, 16 * 2 * 128])
    w0t = gi("w0t", [128, IC * DC * 128]); w1t = gi("w1t", [128, IC * DC * 128])
    b0 = gi("b0", [128, IC]); b1 = gi("b1", [128, DC])
    sc = lambda nm, shp: nc.dram_tensor(nm, shp, F32, kind="Internal").ap()
    attn_scr = sc("attn_scr", [D, TOK])
    qcn_scr = sc("qcn_scr", [6 * 128, TOK])
    x1s = sc("x1s", [D, TOK])
    x2s = sc("x2s", [D, TOK])
    x3s = sc("x3s", [D, TOK])
    kin = sc("kin", [D, TOK]); vin = sc("vin", [TOK, D])
    kgo = sc("kgo", [2 * D, TOK]); vgo = sc("vgo", [2 * TOK, D])
    out = nc.dram_tensor("out", [D, TOK], F32, kind="ExternalOutput").ap()

    with contextlib.ExitStack() as stack0:
        block = stack0.enter_context(nc.Block())
        ctxs = [make_ctx(nc, block, stack0) for _ in range(4)]

        with contextlib.ExitStack() as s1:
            emit_l0attn(nc, block, ctxs[0], xk, xq, x1s, wcq, gq, wdq, wdqr,
                        wckv, gkv, wdk, wdv, wkr, wo, rckv, rskv, rcq, rsq,
                        ident, ones, attn_scr, qcn_scr, s1)
        emit_phase_barrier(ctxs[0])

        with contextlib.ExitStack() as s2:
            emit_l0ffn(nc, block, ctxs[1], x1s, x2s, wupt, wgtt, wdnt, had,
                       su_ug, su_dp, ws_ug, ws_dp, ones, s2)
        emit_phase_barrier(ctxs[1])

        with contextlib.ExitStack() as s3:
            emit_l1attn(nc, block, ctxs[2], x2s, x3s, wq, bq, wk, bk, wv, bvr,
                        wg, bg, wo1, bo, csq, snq, csk, snk, gcol, arow, mask,
                        ones, kin, vin, kgo, vgo, s3)
        emit_phase_barrier(ctxs[2])

        with contextlib.ExitStack() as s4:
            emit_l1ffn(nc, block, ctxs[3], x3s, out, w0t, w1t, b0, b1, ones, s4)

        @block.gpsimd
        def _(gpsimd):
            gpsimd.wait_ge(ctxs[3].qG.h, ctxs[3].qG.n)
    return nc


def kernel(**inputs):
    x = np.asarray(inputs["L0_attn_x"], np.float32)
    nc = build_fused_program()
    shared = {}
    shared.update(stage_l0attn_shared(
        inputs["L0_attn_Wcq"], inputs["L0_attn_g_q"], inputs["L0_attn_Wdq"],
        inputs["L0_attn_Wdqr"], inputs["L0_attn_Wckv"], inputs["L0_attn_g_kv"],
        inputs["L0_attn_Wdk"], inputs["L0_attn_Wdv"], inputs["L0_attn_Wkr"],
        inputs["L0_attn_Wo"]))
    shared.update(stage_l0ffn(
        inputs["L0_ffn_SU_ug"], inputs["L0_ffn_SU_dp"], inputs["L0_ffn_Wscale_ug"],
        inputs["L0_ffn_Wscale_dp"], inputs["L0_ffn_W_up"], inputs["L0_ffn_W_gate"],
        inputs["L0_ffn_W_down"]))
    l1a = stage_l1attn_shared(
        inputs["L1_attn_Wq"], inputs["L1_attn_bq"], inputs["L1_attn_Wk"],
        inputs["L1_attn_bk"], inputs["L1_attn_Wv"], inputs["L1_attn_bv"],
        inputs["L1_attn_Wg"], inputs["L1_attn_bg"], inputs["L1_attn_Wo"],
        inputs["L1_attn_bo"])
    l1a["wo1"] = l1a.pop("wo")
    shared.update(l1a)
    shared.update(stage_l1ffn(inputs["L1_ffn_W0"], inputs["L1_ffn_bias0"],
                              inputs["L1_ffn_W1"], inputs["L1_ffn_bias1"]))
    in_maps = []
    for c in range(NCORES):
        b = c // 2
        idx = own_token_index(c)
        rcq, rsq = rope_tables(64, idx, scale=1.0 / np.sqrt(192.0))
        in_maps.append({
            "xk": np.ascontiguousarray(x[b].T),
            "xq": np.ascontiguousarray(x[b][idx].T),
            "rcq": rcq, "rsq": rsq,
            **stage_l1attn_core(c), **shared})
    res = run_bass_kernel_spmd(nc, in_maps, core_ids=list(range(NCORES)))
    outa = np.zeros((B, S, D), np.float32)
    for c in range(NCORES):
        b = c // 2
        idx = own_token_index(c)
        outa[b, idx] = res.results[c]["out"].T
    return outa


def kernel_unfused(**inputs):
    x = np.asarray(inputs["L0_attn_x"], np.float32)
    x1 = run_l0attn(x, inputs["L0_attn_Wcq"], inputs["L0_attn_g_q"],
                    inputs["L0_attn_Wdq"], inputs["L0_attn_Wdqr"],
                    inputs["L0_attn_Wckv"], inputs["L0_attn_g_kv"],
                    inputs["L0_attn_Wdk"], inputs["L0_attn_Wdv"],
                    inputs["L0_attn_Wkr"], inputs["L0_attn_Wo"])
    x2 = run_l0ffn(x1, inputs["L0_ffn_SU_ug"], inputs["L0_ffn_SU_dp"],
                   inputs["L0_ffn_Wscale_ug"], inputs["L0_ffn_Wscale_dp"],
                   inputs["L0_ffn_W_up"], inputs["L0_ffn_W_gate"],
                   inputs["L0_ffn_W_down"])
    x3 = run_l1attn(x2, inputs["L1_attn_Wq"], inputs["L1_attn_bq"],
                    inputs["L1_attn_Wk"], inputs["L1_attn_bk"],
                    inputs["L1_attn_Wv"], inputs["L1_attn_bv"],
                    inputs["L1_attn_Wg"], inputs["L1_attn_bg"],
                    inputs["L1_attn_Wo"], inputs["L1_attn_bo"])
    out = run_l1ffn(x3, inputs["L1_ffn_W0"], inputs["L1_ffn_bias0"],
                    inputs["L1_ffn_W1"], inputs["L1_ffn_bias1"])
    return out


# version 03
# speedup vs baseline: 1.0463x; 1.0463x over previous
import contextlib
import numpy as np
import concourse.bass as bass
import concourse.mybir as mybir
from concourse.bass_utils import run_bass_kernel_spmd

F32 = mybir.dt.float32
F16 = mybir.dt.float16
BF16 = mybir.dt.bfloat16
AF = mybir.ActivationFunctionType
OP = mybir.AluOpType

NCORES = 8
B, S, D = 4, 4096, 2048
TOK = 2048
NT = TOK // 128
DC = D // 128
INTER = 8192
IC = INTER // 128
N_HEADS = 16
LN_EPS = 1e-5
RMS_EPS = 1e-6
GN_EPS = 1e-5


class Sem:
    __slots__ = ("h", "n")
    def __init__(self, h):
        self.h = h
        self.n = 0

    def inc(self, instr, k=1):
        instr.then_inc(self.h, k)
        self.n += k
        return instr


class Ctx:
    def __init__(self, nc, block, sems):
        self.nc = nc
        self.block = block
        for k, v in sems.items():
            setattr(self, k, v)


_CTX_N = [0]
def make_ctx(nc, block, stack):
    names = ["qS", "qG", "qA", "sPE", "sDVE", "sACT", "sPOOL"]
    pfx = f"c{_CTX_N[0]}_"
    _CTX_N[0] += 1
    sems = {nm: Sem(stack.enter_context(nc.semaphore(pfx + nm))) for nm in names}
    return Ctx(nc, block, sems)


def emit_ln(ctx, T, s_tiles, y_tiles, sq_a, sq_b, ps, pb, ones2, rows,
            eps=LN_EPS):
    block = ctx.block
    r_mu, r_m2, r_A, r_B = rows
    ps_sum = ps[0:1, pb * 512: pb * 512 + T]
    ps_ssq = ps[0:1, (pb + 1) * 512: (pb + 1) * 512 + T]
    ps_A = ps[:, (pb + 2) * 512: (pb + 2) * 512 + T]
    ps_B = ps[:, (pb + 3) * 512: (pb + 3) * 512 + T]
    ones_col = ones2[:, 0:1]
    ones_row = ones2[0:1, :]

    @block.tensor
    def _(tensor):
        tensor.wait_ge(ctx.sDVE.h, ctx.sDVE.n)
        tensor.wait_ge(ctx.sACT.h, ctx.sACT.n)
        for c in range(DC):
            ins = tensor.matmul(ps_sum, ones_col, s_tiles[c],
                                start=(c == 0), stop=(c == DC - 1))
        ctx.sPE.inc(ins)

    for c in range(DC):
        sq = sq_a if c % 2 == 0 else sq_b
        @block.vector
        def _(vector, c=c, sq=sq):
            if c >= 2:
                vector.wait_ge(ctx.sPE.h, ctx.sPE.n)
            ctx.sDVE.inc(vector.tensor_tensor(sq, s_tiles[c], s_tiles[c], OP.mult))

        @block.tensor
        def _(tensor, c=c, sq=sq):
            tensor.wait_ge(ctx.sDVE.h, ctx.sDVE.n)
            ctx.sPE.inc(tensor.matmul(ps_ssq, ones_col, sq,
                                      start=(c == 0), stop=(c == DC - 1)))

    @block.vector
    def _(vector):
        vector.wait_ge(ctx.sPE.h, ctx.sPE.n)
        ctx.sDVE.inc(vector.tensor_scalar(r_mu, ps_sum, 1.0 / D, None, OP.mult))
        ctx.sDVE.inc(vector.tensor_scalar(r_m2, ps_ssq, 1.0 / D, float(eps),
                                          OP.mult, OP.add))
        vector.wait_ge(ctx.sDVE.h, ctx.sDVE.n)
        ctx.sDVE.inc(vector.tensor_tensor(r_A, r_mu, r_mu, OP.mult))
        vector.wait_ge(ctx.sDVE.h, ctx.sDVE.n)
        ctx.sDVE.inc(vector.tensor_tensor(r_m2, r_m2, r_A, OP.subtract))

    @block.scalar
    def _(scalar):
        scalar.wait_ge(ctx.sDVE.h, ctx.sDVE.n)
        ctx.sACT.inc(scalar.activation(r_A, r_m2, AF.Ln, bias=0.0, scale=1.0))
        scalar.wait_ge(ctx.sACT.h, ctx.sACT.n)
        ctx.sACT.inc(scalar.activation(r_A, r_A, AF.Exp, bias=0.0, scale=-0.5))

    @block.vector
    def _(vector):
        vector.wait_ge(ctx.sACT.h, ctx.sACT.n)
        ctx.sDVE.inc(vector.scalar_tensor_tensor(r_B, r_mu, -1.0, r_A,
                                                 OP.mult, OP.mult))

    @block.tensor
    def _(tensor):
        tensor.wait_ge(ctx.sDVE.h, ctx.sDVE.n)
        tensor.matmul(ps_A, ones_row, r_A, start=True, stop=True)
        ctx.sPE.inc(tensor.matmul(ps_B, ones_row, r_B, start=True, stop=True))

    for c in range(DC):
        @block.vector
        def _(vector, c=c):
            if c == 0:
                vector.wait_ge(ctx.sPE.h, ctx.sPE.n)
            vector.tensor_tensor(y_tiles[c], s_tiles[c], ps_A, OP.mult)
            ctx.sDVE.inc(vector.tensor_tensor(y_tiles[c], y_tiles[c], ps_B, OP.add))


def emit_l1ffn(nc, block, ctx, x_dram, out_dram, w0t_dram, w1t_dram,
               b0_dram, b1_dram, ones_dram, stack):
    T = 512
    NCH = TOK // T
    xs = stack.enter_context(nc.sbuf_tensor("ff_x", [128, DC * T], F32))
    xr = stack.enter_context(nc.sbuf_tensor("ff_xr", [128, DC * T], F32))
    x16 = stack.enter_context(nc.sbuf_tensor("ff_x16", [128, T], F16))
    d16 = stack.enter_context(nc.sbuf_tensor("ff_d16", [128, IC * T], F16))
    d32 = stack.enter_context(nc.sbuf_tensor("ff_d32", [128, 2 * T], F32))
    e32 = stack.enter_context(nc.sbuf_tensor("ff_e32", [128, T], F32))
    wup = stack.enter_context(nc.sbuf_tensor("ff_w0", [128, 2 * DC * 128], F32))
    wdn = stack.enter_context(nc.sbuf_tensor("ff_w1", [128, 2 * DC * 128], F32))
    e16 = stack.enter_context(nc.sbuf_tensor("ff_e16", [128, DC * T], F16))
    b0s = stack.enter_context(nc.sbuf_tensor("ff_b0", [128, IC], F32))
    b1s = stack.enter_context(nc.sbuf_tensor("ff_b1", [128, DC], F32))
    ones2 = stack.enter_context(nc.sbuf_tensor("ff_ones", [128, 128], F32))
    sq_a = stack.enter_context(nc.sbuf_tensor("ff_sqa", [128, T], F32))
    sq_b = stack.enter_context(nc.sbuf_tensor("ff_sqb", [128, T], F32))
    rows_t = stack.enter_context(nc.sbuf_tensor("ff_rows", [1, 4 * T], F32))
    ps = stack.enter_context(nc.psum_tensor("ff_ps", [128, 8 * 512], F32))

    rows = [rows_t[0:1, i * T:(i + 1) * T] for i in range(4)]

    @block.sync
    def _(sync):
        ctx.qS.inc(sync.dma_start(out=b0s[:], in_=b0_dram[:, :]), 16)
        ctx.qS.inc(sync.dma_start(out=b1s[:], in_=b1_dram[:, :]), 16)
        ctx.qS.inc(sync.dma_start(out=ones2[:], in_=ones_dram[:, :]), 16)

    for ch in range(NCH):
        t0 = ch * T
        dve_resid_done = ctx.sDVE.n
        @block.sync
        def _(sync):
            sync.wait_ge(ctx.sDVE.h, dve_resid_done)
            for c in range(DC):
                ctx.qS.inc(sync.dma_start(
                    out=xs[:, c * T:(c + 1) * T],
                    in_=x_dram[c * 128:(c + 1) * 128, t0:t0 + T]), 16)

        qg_store_done = ctx.qG.n
        for c in range(DC):
            @block.vector
            def _(vector, c=c):
                if c == 0:
                    vector.wait_ge(ctx.qS.h, ctx.qS.n)
                    vector.wait_ge(ctx.qG.h, qg_store_done)
                vector.tensor_copy(x16[:], xs[:, c * T:(c + 1) * T])
                ctx.sDVE.inc(vector.tensor_copy(xr[:, c * T:(c + 1) * T], x16[:]))

        pe_base = ctx.sPE.n
        act_base = ctx.sACT.n
        dve_ready = ctx.sDVE.n
        for o in range(IC):
            half = o % 2
            wdst = wup[:, half * DC * 128:(half + 1) * DC * 128]
            pslice = ps[:, (o % 4) * 512:(o % 4) * 512 + T]

            @block.sync
            def _(sync, o=o, wdst=wdst):
                if o >= 2:
                    sync.wait_ge(ctx.sPE.h, pe_base + o - 1)
                else:
                    sync.wait_ge(ctx.sPE.h, pe_base)
                ctx.qS.inc(sync.dma_start(
                    out=wdst,
                    in_=w0t_dram[:, o * DC * 128:(o + 1) * DC * 128]), 16)

            @block.tensor
            def _(tensor, o=o, wdst=wdst, pslice=pslice):
                tensor.wait_ge(ctx.qS.h, ctx.qS.n)
                if o == 0:
                    tensor.wait_ge(ctx.sDVE.h, dve_ready)
                if o >= 4:
                    tensor.wait_ge(ctx.sACT.h, act_base + o - 3)
                for c in range(DC):
                    ins = tensor.matmul(pslice, wdst[:, c * 128:(c + 1) * 128],
                                        xr[:, c * T:(c + 1) * T],
                                        start=(c == 0), stop=(c == DC - 1))
                ctx.sPE.inc(ins)

            @block.scalar
            def _(scalar, o=o, pslice=pslice):
                scalar.wait_ge(ctx.sPE.h, pe_base + o + 1)
                ctx.sACT.inc(scalar.activation(
                    d16[:, o * T:(o + 1) * T], pslice, AF.Gelu,
                    bias=b0s[:, o:o + 1], scale=1.0))

        for ph in range(2):
            pe2_base = ctx.sPE.n
            act2_base = ctx.sACT.n
            for i in range(IC):
                half = i % 2
                dslice = d32[:, half * T:(half + 1) * T]
                whalf = wdn[:, half * 8 * 128:(half + 1) * 8 * 128]

                @block.sync
                def _(sync, i=i, whalf=whalf, ph=ph):
                    if i >= 2:
                        sync.wait_ge(ctx.sPE.h, pe2_base + i - 1)
                    else:
                        sync.wait_ge(ctx.sPE.h, pe2_base)
                    ctx.qS.inc(sync.dma_start(
                        out=whalf,
                        in_=w1t_dram[:, (i * DC + ph * 8) * 128:
                                        (i * DC + ph * 8 + 8) * 128]), 16)

                @block.vector
                def _(vector, i=i, dslice=dslice):
                    if i == 0:
                        vector.wait_ge(ctx.sACT.h, act2_base)
                    if i >= 2:
                        vector.wait_ge(ctx.sPE.h, pe2_base + i - 1)
                    ctx.sDVE.inc(vector.tensor_copy(dslice, d16[:, i * T:(i + 1) * T]))

                @block.tensor
                def _(tensor, i=i, dslice=dslice, whalf=whalf):
                    tensor.wait_ge(ctx.qS.h, ctx.qS.n)
                    tensor.wait_ge(ctx.sDVE.h, ctx.sDVE.n)
                    for e in range(8):
                        ins = tensor.matmul(ps[:, e * 512:e * 512 + T],
                                            whalf[:, e * 128:(e + 1) * 128],
                                            dslice,
                                            start=(i == 0), stop=(i == IC - 1))
                    ctx.sPE.inc(ins)

            pe3 = ctx.sPE.n
            for e in range(8):
                eg = ph * 8 + e
                @block.scalar
                def _(scalar, e=e, eg=eg):
                    if e == 0:
                        scalar.wait_ge(ctx.sPE.h, pe3)
                    ctx.sACT.inc(scalar.activation(
                        e16[:, eg * T:(eg + 1) * T], ps[:, e * 512:e * 512 + T],
                        AF.Identity, bias=b1s[:, eg:eg + 1], scale=1.0))

        act3 = ctx.sACT.n
        for c in range(DC):
            @block.vector
            def _(vector, c=c):
                if c == 0:
                    vector.wait_ge(ctx.sACT.h, act3)
                vector.tensor_copy(e32[:], e16[:, c * T:(c + 1) * T])
                ctx.sDVE.inc(vector.tensor_tensor(
                    xr[:, c * T:(c + 1) * T], xs[:, c * T:(c + 1) * T],
                    e32[:], OP.add))

        s_tiles = [xr[:, c * T:(c + 1) * T] for c in range(DC)]
        emit_ln(ctx, T, s_tiles, s_tiles, sq_a[:], sq_b[:], ps, 0, ones2, rows)

        dve_done = ctx.sDVE.n
        @block.gpsimd
        def _(gpsimd):
            gpsimd.wait_ge(ctx.sDVE.h, dve_done)
            for c in range(DC):
                ctx.qG.inc(gpsimd.dma_start(
                    out=out_dram[c * 128:(c + 1) * 128, t0:t0 + T],
                    in_=xr[:, c * T:(c + 1) * T]), 16)


def emit_phase_barrier(ctx):
    block = ctx.block
    sems = [ctx.qS, ctx.qG, ctx.sPE, ctx.sDVE, ctx.sACT, ctx.sPOOL]
    counts = [(s, s.n) for s in sems]
    for eng in ("sync", "tensor", "vector", "scalar", "gpsimd"):
        @getattr(block, eng)
        def _(e):
            for s, n in counts:
                e.wait_ge(s.h, n)


def emit_rms(ctx, T, tiles, out_tiles, g_cols, nch, ps, pb, ones2, row,
             sq_a, sq_b, eps=RMS_EPS):
    block = ctx.block
    ps_ssq = ps[0:1, pb * 512: pb * 512 + T]
    ps_bc = ps[:, (pb + 1) * 512: (pb + 1) * 512 + T]
    ones_col = ones2[:, 0:1]
    ones_row = ones2[0:1, :]
    n = len(tiles)

    for c in range(n):
        @block.vector
        def _(vector, c=c):
            if c >= 2:
                vector.wait_ge(ctx.sPE.h, ctx.sPE.n)
            sq = sq_a if c % 2 == 0 else sq_b
            ctx.sDVE.inc(vector.tensor_tensor(sq, tiles[c], tiles[c], OP.mult))

        @block.tensor
        def _(tensor, c=c):
            tensor.wait_ge(ctx.sDVE.h, ctx.sDVE.n)
            sq = sq_a if c % 2 == 0 else sq_b
            ctx.sPE.inc(tensor.matmul(ps_ssq, ones_col, sq,
                                      start=(c == 0), stop=(c == n - 1)))

    @block.vector
    def _(vector):
        vector.wait_ge(ctx.sPE.h, ctx.sPE.n)
        ctx.sDVE.inc(vector.tensor_scalar(row, ps_ssq, 1.0 / (nch * 128.0),
                                          float(eps), OP.mult, OP.add))

    @block.scalar
    def _(scalar):
        scalar.wait_ge(ctx.sDVE.h, ctx.sDVE.n)
        ctx.sACT.inc(scalar.activation(row, row, AF.Ln, bias=0.0, scale=1.0))
        scalar.wait_ge(ctx.sACT.h, ctx.sACT.n)
        ctx.sACT.inc(scalar.activation(row, row, AF.Exp, bias=0.0, scale=-0.5))

    @block.tensor
    def _(tensor):
        tensor.wait_ge(ctx.sACT.h, ctx.sACT.n)
        ctx.sPE.inc(tensor.matmul(ps_bc, ones_row, row, start=True, stop=True))

    for c in range(n):
        @block.vector
        def _(vector, c=c):
            if c == 0:
                vector.wait_ge(ctx.sPE.h, ctx.sPE.n)
            ctx.sDVE.inc(vector.scalar_tensor_tensor(
                out_tiles[c], tiles[c], g_cols[c], ps_bc, OP.mult, OP.mult))


def emit_butterflies(ctx, tiles, tmp_d, tmp_p):
    block = ctx.block
    n = len(tiles)
    stages = int(np.log2(n))
    for s in range(stages):
        h = 1 << s
        pairs = [(i, i + h) for i in range(n) if not (i & h)]
        half = len(pairs) // 2
        dve_pairs, pool_pairs = pairs[:half], pairs[half:]

        @block.vector
        def _(vector, dve_pairs=dve_pairs, s=s):
            vector.wait_ge(ctx.sPOOL.h, ctx.sPOOL.n)
            for a, b_ in dve_pairs:
                vector.tensor_tensor(tmp_d, tiles[a], tiles[b_], OP.subtract)
                vector.tensor_tensor(tiles[a], tiles[a], tiles[b_], OP.add)
                ins = vector.tensor_copy(tiles[b_], tmp_d)
            ctx.sDVE.inc(ins)

        @block.gpsimd
        def _(gpsimd, pool_pairs=pool_pairs, s=s):
            gpsimd.wait_ge(ctx.sDVE.h, ctx.sDVE.n - 1)
            for a, b_ in pool_pairs:
                gpsimd.tensor_tensor(tmp_p, tiles[a], tiles[b_], OP.subtract)
                gpsimd.tensor_tensor(tiles[a], tiles[a], tiles[b_], OP.add)
                ins = gpsimd.tensor_copy(tiles[b_], tmp_p)
            ctx.sPOOL.inc(ins)

    @block.vector
    def _(vector):
        vector.wait_ge(ctx.sPOOL.h, ctx.sPOOL.n)
        ins = vector.tensor_copy(tmp_d[:, 0:1], tmp_d[:, 0:1])
        ctx.sDVE.inc(ins)


def emit_l0ffn(nc, block, ctx, x_dram, out_dram, wupt_dram, wgtt_dram,
               wdnt_dram, had_dram, su_ug_dram, su_dp_dram, ws_ug_dram,
               ws_dp_dram, ones_dram, stack):
    T = 256
    NCH = TOK // T
    xs = stack.enter_context(nc.sbuf_tensor("f0_x", [128, DC * T], F32))
    z = stack.enter_context(nc.sbuf_tensor("f0_z", [128, DC * T], F32))
    hbuf = stack.enter_context(nc.sbuf_tensor("f0_h", [128, IC * T], F32))
    tmp_d = stack.enter_context(nc.sbuf_tensor("f0_td", [128, T], F32))
    tmp_p = stack.enter_context(nc.sbuf_tensor("f0_tp", [128, T], F32))
    tmpg = stack.enter_context(nc.sbuf_tensor("f0_tg", [128, T], F32))
    silu_sb = stack.enter_context(nc.sbuf_tensor("f0_sl", [128, T], F32))
    hadt = stack.enter_context(nc.sbuf_tensor("f0_ht", [128, DC * 128], F32))
    hraw = stack.enter_context(nc.sbuf_tensor("f0_hr", [128, 128], F32))
    wup = stack.enter_context(nc.sbuf_tensor("f0_wu", [128, 2 * DC * 128], F32))
    wgt = stack.enter_context(nc.sbuf_tensor("f0_wg", [128, 2 * DC * 128], F32))
    wdn = stack.enter_context(nc.sbuf_tensor("f0_wd", [128, 2 * 8 * 128], F32))
    su_ug = stack.enter_context(nc.sbuf_tensor("f0_su", [128, DC], F32))
    vsu = stack.enter_context(nc.sbuf_tensor("f0_vsu", [128, IC], F32))
    vsg = stack.enter_context(nc.sbuf_tensor("f0_vsg", [128, IC], F32))
    vsd = stack.enter_context(nc.sbuf_tensor("f0_vsd", [128, DC], F32))
    vtmp = stack.enter_context(nc.sbuf_tensor("f0_vt", [128, 2 * IC], F32))
    ones2 = stack.enter_context(nc.sbuf_tensor("f0_on", [128, 128], F32))
    sq_a = stack.enter_context(nc.sbuf_tensor("f0_qa", [128, T], F32))
    sq_b = stack.enter_context(nc.sbuf_tensor("f0_qb", [128, T], F32))
    rows_t = stack.enter_context(nc.sbuf_tensor("f0_rw", [1, 4 * T], F32))
    ps = stack.enter_context(nc.psum_tensor("f0_ps", [128, 8 * 512], F32))

    rows = [rows_t[0:1, i * T:(i + 1) * T] for i in range(4)]
    htile = [hbuf[:, k * T:(k + 1) * T] for k in range(IC)]
    ztile = [z[:, k * T:(k + 1) * T] for k in range(DC)]

    @block.sync
    def _(sync):
        for dst, src in [(hraw[:], had_dram[:, :]), (su_ug[:], su_ug_dram[:, :]),
                         (vtmp[:], ws_ug_dram[:, :]),
                         (vsd[:], ws_dp_dram[:, :]), (vsu[:], su_dp_dram[:, :]),
                         (ones2[:], ones_dram[:, :])]:
            ctx.qS.inc(sync.dma_start(out=dst, in_=src), 16)

    @block.vector
    def _(vector):
        vector.wait_ge(ctx.qS.h, ctx.qS.n)
        for k in range(DC):
            vector.tensor_scalar(hadt[:, k * 128:(k + 1) * 128], hraw[:],
                                 su_ug[:, k:k + 1], None, OP.mult)
        vector.tensor_scalar(vsg[:], vtmp[:, IC:2 * IC], 1.0 / np.sqrt(2048.0),
                             None, OP.mult)
        vector.scalar_tensor_tensor(vsu[:], vtmp[:, 0:IC], 1.0 / np.sqrt(2048.0),
                                    vsu[:], OP.mult, OP.mult)
        ctx.sDVE.inc(vector.tensor_scalar(vsd[:], vsd[:], 1.0 / np.sqrt(8192.0),
                                          None, OP.mult))

    for ch in range(NCH):
        t0 = ch * T
        dve_prev = ctx.sDVE.n
        @block.sync
        def _(sync):
            sync.wait_ge(ctx.sDVE.h, dve_prev)
            for c in range(DC):
                ctx.qS.inc(sync.dma_start(
                    out=xs[:, c * T:(c + 1) * T],
                    in_=x_dram[c * 128:(c + 1) * 128, t0:t0 + T]), 16)

        pe_b = ctx.sPE.n
        dve_b = ctx.sDVE.n
        for k in range(DC):
            @block.tensor
            def _(tensor, k=k):
                if k == 0:
                    tensor.wait_ge(ctx.qS.h, ctx.qS.n)
                    tensor.wait_ge(ctx.sDVE.h, dve_b)
                if k >= 4:
                    tensor.wait_ge(ctx.sDVE.h, dve_b + k - 3)
                ctx.sPE.inc(tensor.matmul(ps[:, (k % 4) * 512:(k % 4) * 512 + T],
                                          hadt[:, k * 128:(k + 1) * 128],
                                          xs[:, k * T:(k + 1) * T],
                                          start=True, stop=True))

            @block.vector
            def _(vector, k=k):
                if k == 0:
                    vector.wait_ge(ctx.qG.h, ctx.qG.n)
                vector.wait_ge(ctx.sPE.h, pe_b + k + 1)
                ctx.sDVE.inc(vector.tensor_copy(
                    ztile[k], ps[:, (k % 4) * 512:(k % 4) * 512 + T]))

        emit_butterflies(ctx, ztile, tmp_d[:], tmp_p[:])

        pe_b = ctx.sPE.n
        dve_b = ctx.sDVE.n
        act_b = ctx.sACT.n
        for o in range(IC):
            half = o % 2
            wud = wup[:, half * DC * 128:(half + 1) * DC * 128]
            wgd = wgt[:, half * DC * 128:(half + 1) * DC * 128]
            bu = ps[:, (o % 4) * 512:(o % 4) * 512 + T]
            bg = ps[:, (4 + o % 4) * 512:(4 + o % 4) * 512 + T]

            @block.sync
            def _(sync, o=o, wud=wud, wgd=wgd):
                sync.wait_ge(ctx.sPE.h, pe_b + max(0, (o - 1) * 2))
                ctx.qS.inc(sync.dma_start(
                    out=wud, in_=wupt_dram[:, o * DC * 128:(o + 1) * DC * 128]), 16)
                ctx.qS.inc(sync.dma_start(
                    out=wgd, in_=wgtt_dram[:, o * DC * 128:(o + 1) * DC * 128]), 16)

            @block.tensor
            def _(tensor, o=o, wud=wud, wgd=wgd, bu=bu, bg=bg):
                tensor.wait_ge(ctx.qS.h, ctx.qS.n)
                if o == 0:
                    tensor.wait_ge(ctx.sDVE.h, dve_b)
                    tensor.wait_ge(ctx.sPOOL.h, ctx.sPOOL.n)
                if o >= 4:
                    tensor.wait_ge(ctx.sDVE.h, dve_b + 2 * (o - 3))
                for c in range(DC):
                    ins = tensor.matmul(bu, wud[:, c * 128:(c + 1) * 128],
                                        ztile[c], start=(c == 0), stop=(c == DC - 1))
                ctx.sPE.inc(ins)
                for c in range(DC):
                    ins = tensor.matmul(bg, wgd[:, c * 128:(c + 1) * 128],
                                        ztile[c], start=(c == 0), stop=(c == DC - 1))
                ctx.sPE.inc(ins)

            @block.vector
            def _(vector, o=o, bg=bg):
                if o == 0:
                    vector.wait_ge(ctx.sDVE.h, dve_b)
                vector.wait_ge(ctx.sPE.h, pe_b + 2 * (o + 1))
                vector.wait_ge(ctx.sACT.h, ctx.sACT.n)
                ctx.sDVE.inc(vector.tensor_scalar(tmpg[:], bg, vsg[:, o:o + 1],
                                                  None, OP.mult))

            @block.scalar
            def _(scalar, o=o):
                scalar.wait_ge(ctx.sDVE.h, ctx.sDVE.n)
                ctx.sACT.inc(scalar.activation(silu_sb[:], tmpg[:], AF.Silu,
                                               bias=0.0, scale=1.0))

            @block.vector
            def _(vector, o=o, bu=bu):
                vector.wait_ge(ctx.sACT.h, ctx.sACT.n)
                ctx.sDVE.inc(vector.scalar_tensor_tensor(
                    htile[o], bu, vsu[:, o:o + 1], silu_sb[:], OP.mult, OP.mult))

        pe_b = ctx.sPE.n
        dve_b = ctx.sDVE.n
        for k in range(IC):
            @block.tensor
            def _(tensor, k=k):
                if k == 0:
                    tensor.wait_ge(ctx.sDVE.h, dve_b)
                if k >= 4:
                    tensor.wait_ge(ctx.sDVE.h, dve_b + k - 3)
                ctx.sPE.inc(tensor.matmul(ps[:, (k % 4) * 512:(k % 4) * 512 + T],
                                          hraw[:], htile[k],
                                          start=True, stop=True))

            @block.vector
            def _(vector, k=k):
                vector.wait_ge(ctx.sPE.h, pe_b + k + 1)
                ctx.sDVE.inc(vector.tensor_copy(
                    htile[k], ps[:, (k % 4) * 512:(k % 4) * 512 + T]))

        emit_butterflies(ctx, htile, tmp_d[:], tmp_p[:])

        for phd in range(2):
            pe2 = ctx.sPE.n
            for i in range(IC):
                half = i % 2
                whalf = wdn[:, half * 8 * 128:(half + 1) * 8 * 128]

                @block.sync
                def _(sync, i=i, whalf=whalf, phd=phd):
                    if i >= 2:
                        sync.wait_ge(ctx.sPE.h, pe2 + i - 1)
                    else:
                        sync.wait_ge(ctx.sPE.h, pe2)
                    ctx.qS.inc(sync.dma_start(
                        out=whalf,
                        in_=wdnt_dram[:, (i * DC + phd * 8) * 128:
                                         (i * DC + phd * 8 + 8) * 128]), 16)

                @block.tensor
                def _(tensor, i=i, whalf=whalf, phd=phd):
                    tensor.wait_ge(ctx.qS.h, ctx.qS.n)
                    if i == 0:
                        tensor.wait_ge(ctx.sDVE.h, ctx.sDVE.n)
                        tensor.wait_ge(ctx.sPOOL.h, ctx.sPOOL.n)
                        if phd == 1:
                            tensor.wait_ge(ctx.sACT.h, ctx.sACT.n)
                    for e in range(8):
                        ins = tensor.matmul(ps[:, e * 512:e * 512 + T],
                                            whalf[:, e * 128:(e + 1) * 128],
                                            htile[i],
                                            start=(i == 0), stop=(i == IC - 1))
                    ctx.sPE.inc(ins)

            pe3 = ctx.sPE.n
            for e in range(8):
                eg = phd * 8 + e
                @block.vector
                def _(vector, e=e, eg=eg):
                    if e == 0:
                        vector.wait_ge(ctx.sPE.h, pe3)
                    ctx.sDVE.inc(vector.scalar_tensor_tensor(
                        ztile[eg], ps[:, e * 512:e * 512 + T], vsd[:, eg:eg + 1],
                        xs[:, eg * T:(eg + 1) * T], OP.mult, OP.add))

        emit_ln(ctx, T, ztile, ztile, sq_a[:], sq_b[:], ps, 0, ones2, rows)

        dve_done = ctx.sDVE.n
        @block.gpsimd
        def _(gpsimd):
            gpsimd.wait_ge(ctx.sDVE.h, dve_done)
            for c in range(DC):
                ctx.qG.inc(gpsimd.dma_start(
                    out=out_dram[c * 128:(c + 1) * 128, t0:t0 + T],
                    in_=z[:, c * T:(c + 1) * T]), 16)


KB8 = S // 512
KB128 = S // 128


def emit_l0attn(nc, block, ctx, xk_dram, xq_dram, out_dram, wcq_dram, gq_dram,
                wdq_dram, wdqr_dram, wckv_dram, gkv_dram, wdk_dram, wdv_dram,
                wkr_dram, wo_dram, rckv_dram, rskv_dram, rcq_dram, rsq_dram,
                ident_dram, ones_dram, attn_scr, qcn_scr, stack):
    T = 512
    kvc = stack.enter_context(nc.sbuf_tensor("a0_kvc", [128, 4 * S], F32))
    krs = stack.enter_context(nc.sbuf_tensor("a0_kr", [64, S], F32))
    ones2 = stack.enter_context(nc.sbuf_tensor("a0_on", [128, 128], F32))
    ident = stack.enter_context(nc.sbuf_tensor("a0_id", [128, 128], F32))
    wkr = stack.enter_context(nc.sbuf_tensor("a0_wkr", [128, DC * 64], F32))
    gq = stack.enter_context(nc.sbuf_tensor("a0_gq", [128, 6], F32))
    gkv = stack.enter_context(nc.sbuf_tensor("a0_gkv", [128, 4], F32))
    rows_t = stack.enter_context(nc.sbuf_tensor("a0_rw", [1, 4 * T], F32))
    ps = stack.enter_context(nc.psum_tensor("a0_ps", [128, 8 * 512], F32))

    rows = [rows_t[0:1, i * T:(i + 1) * T] for i in range(4)]
    kvct = [kvc[:, c * S:(c + 1) * S] for c in range(4)]

    @block.sync
    def _(sync):
        for dst, src in [(ones2[:], ones_dram[:, :]), (ident[:], ident_dram[:, :]),
                         (wkr[:], wkr_dram[:, :]), (gq[:], gq_dram[:, :]),
                         (gkv[:], gkv_dram[:, :])]:
            ctx.qS.inc(sync.dma_start(out=dst, in_=src), 16)

    with contextlib.ExitStack() as p1:
        xs = p1.enter_context(nc.sbuf_tensor("a1_x", [128, DC * T], F32))
        wbuf = p1.enter_context(nc.sbuf_tensor("a1_w", [128, 2 * DC * 128], F32))
        rck = p1.enter_context(nc.sbuf_tensor("a1_rc", [64, S], F32))
        rsk = p1.enter_context(nc.sbuf_tensor("a1_rs", [64, S], F32))
        sq_a = p1.enter_context(nc.sbuf_tensor("a1_qa", [128, T], F32))
        sq_b = p1.enter_context(nc.sbuf_tensor("a1_qb", [128, T], F32))
        rtmp = p1.enter_context(nc.sbuf_tensor("a1_rt", [64, T], F32))

        @block.sync
        def _(sync):
            ctx.qS.inc(sync.dma_start(out=rck[:], in_=rckv_dram[:, :]), 16)
            ctx.qS.inc(sync.dma_start(out=rsk[:], in_=rskv_dram[:, :]), 16)

        for ch in range(KB8):
            t0 = ch * T
            dve_prev = ctx.sDVE.n
            @block.sync
            def _(sync):
                sync.wait_ge(ctx.sDVE.h, dve_prev)
                for c in range(DC):
                    ctx.qS.inc(sync.dma_start(
                        out=xs[:, c * T:(c + 1) * T],
                        in_=xk_dram[c * 128:(c + 1) * 128, t0:t0 + T]), 16)

            pe_b = ctx.sPE.n
            dve_b = ctx.sDVE.n
            for o in range(4):
                half = o % 2
                wdst = wbuf[:, half * DC * 128:(half + 1) * DC * 128]
                @block.sync
                def _(sync, o=o, wdst=wdst):
                    sync.wait_ge(ctx.sPE.h, pe_b + max(0, o - 1))
                    ctx.qS.inc(sync.dma_start(
                        out=wdst, in_=wckv_dram[:, o * DC * 128:(o + 1) * DC * 128]), 16)

                @block.tensor
                def _(tensor, o=o, wdst=wdst):
                    tensor.wait_ge(ctx.qS.h, ctx.qS.n)
                    if o == 0:
                        tensor.wait_ge(ctx.sDVE.h, dve_b)
                    for c in range(DC):
                        ins = tensor.matmul(ps[:, o * 512:o * 512 + T],
                                            wdst[:, c * 128:(c + 1) * 128],
                                            xs[:, c * T:(c + 1) * T],
                                            start=(c == 0), stop=(c == DC - 1))
                    ctx.sPE.inc(ins)

                @block.vector
                def _(vector, o=o):
                    vector.wait_ge(ctx.sPE.h, pe_b + o + 1)
                    ctx.sDVE.inc(vector.tensor_copy(
                        kvct[o][:, t0:t0 + T], ps[:, o * 512:o * 512 + T]))

            @block.tensor
            def _(tensor):
                for c in range(DC):
                    ins = tensor.matmul(ps[0:64, 4 * 512:4 * 512 + T],
                                        wkr[:, c * 64:(c + 1) * 64],
                                        xs[:, c * T:(c + 1) * T],
                                        start=(c == 0), stop=(c == DC - 1))
                ctx.sPE.inc(ins)

            @block.vector
            def _(vector, t0=t0):
                vector.wait_ge(ctx.sPE.h, ctx.sPE.n)
                pk = ps[0:64, 4 * 512:4 * 512 + T]
                ctx.sDVE.inc(vector.tensor_tensor(rtmp[0:32, :], pk[0:32, :],
                                                  rck[0:32, t0:t0 + T], OP.mult))
                ctx.sDVE.inc(vector.tensor_tensor(rtmp[32:64, :], pk[32:64, :],
                                                  rck[32:64, t0:t0 + T], OP.mult))
                vector.wait_ge(ctx.sDVE.h, ctx.sDVE.n)
                ctx.sDVE.inc(vector.tensor_tensor(krs[32:64, t0:t0 + T], pk[0:32, :],
                                                  rsk[32:64, t0:t0 + T], OP.mult))
                ctx.sDVE.inc(vector.tensor_tensor(sq_a[0:32, :], pk[32:64, :],
                                                  rsk[0:32, t0:t0 + T], OP.mult))
                vector.wait_ge(ctx.sDVE.h, ctx.sDVE.n)
                ctx.sDVE.inc(vector.tensor_tensor(krs[0:32, t0:t0 + T], rtmp[0:32, :],
                                                  sq_a[0:32, :], OP.subtract))
                ctx.sDVE.inc(vector.tensor_tensor(krs[32:64, t0:t0 + T],
                                                  krs[32:64, t0:t0 + T],
                                                  rtmp[32:64, :], OP.add))

            emit_rms(ctx, T, [kt[:, t0:t0 + T] for kt in kvct],
                     [kt[:, t0:t0 + T] for kt in kvct],
                     [gkv[:, c:c + 1] for c in range(4)], 4, ps, 5, ones2,
                     rows[0], sq_a[:], sq_b[:])

    emit_phase_barrier(ctx)

    with contextlib.ExitStack() as p2:
        xs = p2.enter_context(nc.sbuf_tensor("a2_x", [128, DC * T], F32))
        wbuf = p2.enter_context(nc.sbuf_tensor("a2_w", [128, 2 * DC * 128], F32))
        sq_a = p2.enter_context(nc.sbuf_tensor("a2_qa", [128, T], F32))
        sq_b = p2.enter_context(nc.sbuf_tensor("a2_qb", [128, T], F32))
        qcs = p2.enter_context(nc.sbuf_tensor("a2_qc", [128, 6 * T], F32))
        qcst = [qcs[:, c * T:(c + 1) * T] for c in range(6)]

        for ch in range(TOK // T):
            t0 = ch * T
            dve_prev = ctx.sDVE.n
            qg_prev = ctx.qG.n
            @block.sync
            def _(sync):
                sync.wait_ge(ctx.sDVE.h, dve_prev)
                for c in range(DC):
                    ctx.qS.inc(sync.dma_start(
                        out=xs[:, c * T:(c + 1) * T],
                        in_=xq_dram[c * 128:(c + 1) * 128, t0:t0 + T]), 16)

            pe_b = ctx.sPE.n
            dve_b = ctx.sDVE.n
            for o in range(6):
                half = o % 2
                wdst = wbuf[:, half * DC * 128:(half + 1) * DC * 128]
                @block.sync
                def _(sync, o=o, wdst=wdst):
                    sync.wait_ge(ctx.sPE.h, pe_b + max(0, o - 1))
                    ctx.qS.inc(sync.dma_start(
                        out=wdst, in_=wcq_dram[:, o * DC * 128:(o + 1) * DC * 128]), 16)

                @block.tensor
                def _(tensor, o=o, wdst=wdst):
                    tensor.wait_ge(ctx.qS.h, ctx.qS.n)
                    if o == 0:
                        tensor.wait_ge(ctx.sDVE.h, dve_b)
                    if o >= 4:
                        tensor.wait_ge(ctx.sDVE.h, dve_b + o - 3)
                    for c in range(DC):
                        ins = tensor.matmul(ps[:, (o % 4) * 512:(o % 4) * 512 + T],
                                            wdst[:, c * 128:(c + 1) * 128],
                                            xs[:, c * T:(c + 1) * T],
                                            start=(c == 0), stop=(c == DC - 1))
                    ctx.sPE.inc(ins)

                @block.vector
                def _(vector, o=o):
                    if o == 0:
                        vector.wait_ge(ctx.qG.h, qg_prev)
                    vector.wait_ge(ctx.sPE.h, pe_b + o + 1)
                    ctx.sDVE.inc(vector.tensor_copy(
                        qcst[o], ps[:, (o % 4) * 512:(o % 4) * 512 + T]))

            emit_rms(ctx, T, qcst, qcst,
                     [gq[:, c:c + 1] for c in range(6)], 6, ps, 4, ones2,
                     rows[0], sq_a[:], sq_b[:])

            dve_done = ctx.sDVE.n
            @block.gpsimd
            def _(gpsimd, t0=t0):
                gpsimd.wait_ge(ctx.sDVE.h, dve_done)
                for c in range(6):
                    ctx.qG.inc(gpsimd.dma_start(
                        out=qcn_scr[c * 128:(c + 1) * 128, t0:t0 + T],
                        in_=qcs[:, c * T:(c + 1) * T]), 16)

    emit_phase_barrier(ctx)

    with contextlib.ExitStack() as p3:
        kh = p3.enter_context(nc.sbuf_tensor("a3_kh", [128, S], F32))
        qcs = p3.enter_context(nc.sbuf_tensor("a3_qc", [128, 6 * T], F32))
        vh = p3.enter_context(nc.sbuf_tensor("a3_vh", [128, S], F32))
        qn = p3.enter_context(nc.sbuf_tensor("a3_qn", [128, TOK], F32))
        qr = p3.enter_context(nc.sbuf_tensor("a3_qr", [64, TOK], F32))
        pbuf = p3.enter_context(nc.sbuf_tensor("a3_p", [128, S], F32))
        whb = p3.enter_context(nc.sbuf_tensor("a3_wh", [128, 2176], F32))
        rcq = p3.enter_context(nc.sbuf_tensor("a3_rc", [64, TOK], F32))
        rsq = p3.enter_context(nc.sbuf_tensor("a3_rs", [64, TOK], F32))
        ostg = p3.enter_context(nc.sbuf_tensor("a3_os", [128, 128], F32))
        mx = p3.enter_context(nc.sbuf_tensor("a3_mx", [128, 4], F32))
        rtmp = p3.enter_context(nc.sbuf_tensor("a3_rt", [64, T], F32))
        rtm2 = p3.enter_context(nc.sbuf_tensor("a3_rt2", [64, T], F32))

        @block.sync
        def _(sync):
            ctx.qS.inc(sync.dma_start(out=rcq[:], in_=rcq_dram[:, :]), 16)
            ctx.qS.inc(sync.dma_start(out=rsq[:], in_=rsq_dram[:, :]), 16)

        for h in range(N_HEADS):
            pe_done = ctx.sPE.n
            @block.sync
            def _(sync, h=h):
                sync.wait_ge(ctx.sPE.h, pe_done)
                ctx.qS.inc(sync.dma_start(
                    out=whb[:, 0:768], in_=wdq_dram[:, h * 768:(h + 1) * 768]), 16)
                ctx.qS.inc(sync.dma_start(
                    out=whb[:, 768:1152], in_=wdqr_dram[:, h * 384:(h + 1) * 384]), 16)
                ctx.qS.inc(sync.dma_start(
                    out=whb[:, 1152:1664], in_=wdk_dram[:, h * 512:(h + 1) * 512]), 16)
                for c in range(4):
                    ctx.qS.inc(sync.dma_start(
                        out=whb[:, 1664 + c * 128:1664 + (c + 1) * 128],
                        in_=wdv_dram[:, (c * 16 + h) * 128:(c * 16 + h + 1) * 128]), 16)

            pe_b = ctx.sPE.n
            dve_b = ctx.sDVE.n
            for kb in range(KB8):
                @block.tensor
                def _(tensor, kb=kb, h=h):
                    tensor.wait_ge(ctx.qS.h, ctx.qS.n)
                    if kb == 0:
                        tensor.wait_ge(ctx.sDVE.h, dve_b)
                    if kb >= 4:
                        tensor.wait_ge(ctx.sDVE.h, dve_b + kb - 3)
                    for c in range(4):
                        ins = tensor.matmul(
                            ps[:, (kb % 4) * 512:(kb % 4) * 512 + T],
                            whb[:, 1152 + c * 128:1152 + (c + 1) * 128],
                            kvct[c][:, kb * T:(kb + 1) * T],
                            start=(c == 0), stop=(c == 3))
                    ctx.sPE.inc(ins)

                @block.vector
                def _(vector, kb=kb):
                    vector.wait_ge(ctx.sPE.h, pe_b + kb + 1)
                    ctx.sDVE.inc(vector.tensor_copy(
                        kh[:, kb * T:(kb + 1) * T],
                        ps[:, (kb % 4) * 512:(kb % 4) * 512 + T]))

            pe_b = ctx.sPE.n
            dve_b = ctx.sDVE.n
            for tb in range(KB128):
                @block.tensor
                def _(tensor, tb=tb):
                    if tb == 0:
                        tensor.wait_ge(ctx.sDVE.h, dve_b)
                    if tb >= 4:
                        tensor.wait_ge(ctx.sDVE.h, dve_b + tb - 3)
                    for c in range(4):
                        ins = tensor.matmul(
                            ps[:, (tb % 4) * 512:(tb % 4) * 512 + 128],
                            kvct[c][:, tb * 128:(tb + 1) * 128],
                            whb[:, 1664 + c * 128:1664 + (c + 1) * 128],
                            start=(c == 0), stop=(c == 3))
                    ctx.sPE.inc(ins)

                @block.vector
                def _(vector, tb=tb):
                    vector.wait_ge(ctx.sPE.h, pe_b + tb + 1)
                    ctx.sDVE.inc(vector.tensor_copy(
                        vh[:, tb * 128:(tb + 1) * 128],
                        ps[:, (tb % 4) * 512:(tb % 4) * 512 + 128]))

            pe_b = ctx.sPE.n
            dve_b = ctx.sDVE.n
            for qb in range(TOK // T):
                @block.sync
                def _(sync, qb=qb):
                    if qb > 0:
                        sync.wait_ge(ctx.sPE.h, pe_b + 2 * qb)
                    for c in range(6):
                        ctx.qS.inc(sync.dma_start(
                            out=qcs[:, c * T:(c + 1) * T],
                            in_=qcn_scr[c * 128:(c + 1) * 128, qb * T:(qb + 1) * T]), 16)

                @block.tensor
                def _(tensor, qb=qb):
                    tensor.wait_ge(ctx.qS.h, ctx.qS.n)
                    if qb == 0:
                        tensor.wait_ge(ctx.sDVE.h, dve_b)
                    if qb >= 4:
                        tensor.wait_ge(ctx.sDVE.h, dve_b + 2 * (qb - 3))
                    for c in range(6):
                        ins = tensor.matmul(
                            ps[:, (qb % 4) * 512:(qb % 4) * 512 + T],
                            whb[:, c * 128:(c + 1) * 128],
                            qcs[:, c * T:(c + 1) * T],
                            start=(c == 0), stop=(c == 5))
                    ctx.sPE.inc(ins)
                    for c in range(6):
                        ins = tensor.matmul(
                            ps[0:64, (4 + qb % 4) * 512:(4 + qb % 4) * 512 + T],
                            whb[:, 768 + c * 64:768 + (c + 1) * 64],
                            qcs[:, c * T:(c + 1) * T],
                            start=(c == 0), stop=(c == 5))
                    ctx.sPE.inc(ins)

                @block.vector
                def _(vector, qb=qb):
                    vector.wait_ge(ctx.sPE.h, pe_b + 2 * (qb + 1))
                    t0 = qb * T
                    pq = ps[:, (qb % 4) * 512:(qb % 4) * 512 + T]
                    pr = ps[0:64, (4 + qb % 4) * 512:(4 + qb % 4) * 512 + T]
                    ctx.sDVE.inc(vector.tensor_scalar(
                        qn[:, t0:t0 + T], pq, float(1.0 / np.sqrt(192.0)), None, OP.mult))
                    ctx.sDVE.inc(vector.tensor_tensor(rtmp[0:32, :], pr[0:32, :],
                                                      rcq[0:32, t0:t0 + T], OP.mult))
                    ctx.sDVE.inc(vector.tensor_tensor(rtmp[32:64, :], pr[32:64, :],
                                                      rcq[32:64, t0:t0 + T], OP.mult))
                    vector.wait_ge(ctx.sDVE.h, ctx.sDVE.n)
                    ctx.sDVE.inc(vector.tensor_tensor(qr[32:64, t0:t0 + T], pr[0:32, :],
                                                      rsq[32:64, t0:t0 + T], OP.mult))
                    ctx.sDVE.inc(vector.tensor_tensor(rtm2[0:32, :], pr[32:64, :],
                                                      rsq[0:32, t0:t0 + T], OP.mult))
                    vector.wait_ge(ctx.sDVE.h, ctx.sDVE.n)
                    ctx.sDVE.inc(vector.tensor_tensor(qr[0:32, t0:t0 + T], rtmp[0:32, :],
                                                      rtm2[0:32, :], OP.subtract))
                    ctx.sDVE.inc(vector.tensor_tensor(qr[32:64, t0:t0 + T],
                                                      qr[32:64, t0:t0 + T],
                                                      rtmp[32:64, :], OP.add))

            for qt in range(NT):
                q0 = qt * 128
                dve_now = ctx.sDVE.n
                act_now = ctx.sACT.n
                @block.tensor
                def _(tensor, qt=qt, q0=q0):
                    tensor.wait_ge(ctx.sDVE.h, dve_now)
                    tensor.wait_ge(ctx.sACT.h, act_now)
                    for kb in range(KB8):
                        ins = tensor.matmul(ps[:, kb * 512:(kb + 1) * 512],
                                            qn[:, q0:q0 + 128],
                                            kh[:, kb * T:(kb + 1) * T],
                                            start=True, stop=False)
                        ins = tensor.matmul(ps[:, kb * 512:(kb + 1) * 512],
                                            qr[:, q0:q0 + 128],
                                            krs[:, kb * T:(kb + 1) * T],
                                            start=False, stop=True)
                    ctx.sPE.inc(ins)

                @block.vector
                def _(vector, qt=qt):
                    vector.wait_ge(ctx.sPE.h, ctx.sPE.n)
                    ctx.sDVE.inc(vector.reduce_max(mx[:, 0:1], ps[:, 0:S],
                                                   axis=mybir.AxisListType.X))
                    vector.wait_ge(ctx.sDVE.h, ctx.sDVE.n)
                    ctx.sDVE.inc(vector.tensor_scalar(mx[:, 1:2], mx[:, 0:1],
                                                      -1.0, None, OP.mult))

                @block.scalar
                def _(scalar, qt=qt):
                    scalar.wait_ge(ctx.sDVE.h, ctx.sDVE.n)
                    ctx.sACT.inc(scalar.activation(pbuf[:], ps[:, 0:S], AF.Exp,
                                                   bias=mx[:, 1:2], scale=1.0,
                                                   accum_out=mx[:, 2:3]))

                @block.vector
                def _(vector, qt=qt):
                    vector.wait_ge(ctx.sACT.h, ctx.sACT.n)
                    ctx.sDVE.inc(vector.reciprocal(mx[:, 3:4], mx[:, 2:3]))
                    vector.wait_ge(ctx.sDVE.h, ctx.sDVE.n)
                    ctx.sDVE.inc(vector.tensor_scalar(pbuf[:], pbuf[:],
                                                      mx[:, 3:4], None, OP.mult))

                dve_pg = ctx.sDVE.n
                for r in range(4):
                    @block.tensor
                    def _(tensor, r=r):
                        if r == 0:
                            tensor.wait_ge(ctx.sDVE.h, dve_pg)
                        else:
                            tensor.wait_ge(ctx.sDVE.h, dve_pg + r)
                        for j in range(8):
                            kb = r * 8 + j
                            ins = tensor.transpose(
                                ps[:, j * 512:j * 512 + 128],
                                pbuf[:, kb * 128:(kb + 1) * 128], ident[:])
                        ctx.sPE.inc(ins)

                    @block.vector
                    def _(vector, r=r):
                        vector.wait_ge(ctx.sPE.h, ctx.sPE.n)
                        src = ps[:, 0:4096].rearrange("p (b c) -> p b c", b=8)[:, :, 0:128]
                        dst = pbuf[:, r * 1024:(r + 1) * 1024].rearrange(
                            "p (b c) -> p b c", b=8)
                        ctx.sDVE.inc(vector.tensor_copy(dst, src))

                @block.tensor
                def _(tensor, qt=qt):
                    tensor.wait_ge(ctx.sDVE.h, ctx.sDVE.n)
                    for kb in range(KB128):
                        ins = tensor.matmul(ps[:, 2 * 512:2 * 512 + 128],
                                            vh[:, kb * 128:(kb + 1) * 128],
                                            pbuf[:, kb * 128:(kb + 1) * 128],
                                            start=(kb == 0), stop=(kb == KB128 - 1))
                    ctx.sPE.inc(ins)

                @block.vector
                def _(vector, qt=qt):
                    vector.wait_ge(ctx.sPE.h, ctx.sPE.n)
                    vector.wait_ge(ctx.qG.h, ctx.qG.n)
                    ctx.sDVE.inc(vector.tensor_copy(ostg[:], ps[:, 2 * 512:2 * 512 + 128]))

                dve_st = ctx.sDVE.n
                @block.gpsimd
                def _(gpsimd, h=h, q0=q0):
                    gpsimd.wait_ge(ctx.sDVE.h, dve_st)
                    ctx.qG.inc(gpsimd.dma_start(
                        out=attn_scr[h * 128:(h + 1) * 128, q0:q0 + 128],
                        in_=ostg[:]), 16)

    emit_phase_barrier(ctx)

    with contextlib.ExitStack() as p4:
        asb = p4.enter_context(nc.sbuf_tensor("a4_a", [128, DC * T], F32))
        xr = p4.enter_context(nc.sbuf_tensor("a4_x", [128, DC * T], F32))
        wbuf = p4.enter_context(nc.sbuf_tensor("a4_w", [128, 2 * DC * 128], F32))
        sq_a = p4.enter_context(nc.sbuf_tensor("a4_qa", [128, T], F32))
        sq_b = p4.enter_context(nc.sbuf_tensor("a4_qb", [128, T], F32))

        for ch in range(TOK // T):
            t0 = ch * T
            dve_prev = ctx.sDVE.n
            qg_prev = ctx.qG.n
            @block.sync
            def _(sync):
                sync.wait_ge(ctx.sDVE.h, dve_prev)
                sync.wait_ge(ctx.qG.h, qg_prev)
                for c in range(DC):
                    ctx.qS.inc(sync.dma_start(
                        out=asb[:, c * T:(c + 1) * T],
                        in_=attn_scr[c * 128:(c + 1) * 128, t0:t0 + T]), 16)
                    ctx.qS.inc(sync.dma_start(
                        out=xr[:, c * T:(c + 1) * T],
                        in_=xq_dram[c * 128:(c + 1) * 128, t0:t0 + T]), 16)

            pe_b = ctx.sPE.n
            dve_b = ctx.sDVE.n
            for o in range(DC):
                half = o % 2
                wdst = wbuf[:, half * DC * 128:(half + 1) * DC * 128]
                @block.sync
                def _(sync, o=o, wdst=wdst):
                    sync.wait_ge(ctx.sPE.h, pe_b + max(0, o - 1))
                    ctx.qS.inc(sync.dma_start(
                        out=wdst, in_=wo_dram[:, o * DC * 128:(o + 1) * DC * 128]), 16)

                @block.tensor
                def _(tensor, o=o, wdst=wdst):
                    tensor.wait_ge(ctx.qS.h, ctx.qS.n)
                    if o == 0:
                        tensor.wait_ge(ctx.sDVE.h, dve_b)
                    if o >= 4:
                        tensor.wait_ge(ctx.sDVE.h, dve_b + o - 3)
                    for c in range(DC):
                        ins = tensor.matmul(ps[:, (o % 4) * 512:(o % 4) * 512 + T],
                                            wdst[:, c * 128:(c + 1) * 128],
                                            asb[:, c * T:(c + 1) * T],
                                            start=(c == 0), stop=(c == DC - 1))
                    ctx.sPE.inc(ins)

                @block.vector
                def _(vector, o=o):
                    vector.wait_ge(ctx.sPE.h, pe_b + o + 1)
                    ctx.sDVE.inc(vector.tensor_tensor(
                        xr[:, o * T:(o + 1) * T],
                        ps[:, (o % 4) * 512:(o % 4) * 512 + T],
                        xr[:, o * T:(o + 1) * T], OP.add))

            s_tiles = [xr[:, c * T:(c + 1) * T] for c in range(DC)]
            emit_ln(ctx, T, s_tiles, s_tiles, sq_a[:], sq_b[:], ps, 0, ones2, rows)

            dve_done = ctx.sDVE.n
            @block.gpsimd
            def _(gpsimd):
                gpsimd.wait_ge(ctx.sDVE.h, dve_done)
                for c in range(DC):
                    ctx.qG.inc(gpsimd.dma_start(
                        out=out_dram[c * 128:(c + 1) * 128, t0:t0 + T],
                        in_=xr[:, c * T:(c + 1) * T]), 16)


def xpos_ops(ctx, vector, dst, t, cs, sn, u, u2):
    ctx.sDVE.inc(vector.tensor_copy(u2[0:64, :], t[64:128, :]))
    ctx.sDVE.inc(vector.tensor_copy(u2[64:128, :], t[0:64, :]))
    ctx.sDVE.inc(vector.tensor_tensor(u, t, cs, OP.mult))
    vector.wait_ge(ctx.sDVE.h, ctx.sDVE.n)
    ctx.sDVE.inc(vector.tensor_tensor(dst, u2, sn, OP.mult))
    vector.wait_ge(ctx.sDVE.h, ctx.sDVE.n)
    ctx.sDVE.inc(vector.tensor_tensor(dst[0:64, :], u[0:64, :], dst[0:64, :], OP.subtract))
    ctx.sDVE.inc(vector.tensor_tensor(dst[64:128, :], u[64:128, :], dst[64:128, :], OP.add))


def emit_norm_rows(ctx, ps_sum_row, ps_ssq_row, denom, eps, rows):
    block = ctx.block
    r_mu, r_m2, r_A, r_B = rows

    @block.vector
    def _(vector):
        vector.wait_ge(ctx.sPE.h, ctx.sPE.n)
        ctx.sDVE.inc(vector.tensor_scalar(r_mu, ps_sum_row, 1.0 / denom, None, OP.mult))
        ctx.sDVE.inc(vector.tensor_scalar(r_m2, ps_ssq_row, 1.0 / denom, float(eps),
                                          OP.mult, OP.add))
        vector.wait_ge(ctx.sDVE.h, ctx.sDVE.n)
        ctx.sDVE.inc(vector.tensor_tensor(r_A, r_mu, r_mu, OP.mult))
        vector.wait_ge(ctx.sDVE.h, ctx.sDVE.n)
        ctx.sDVE.inc(vector.tensor_tensor(r_m2, r_m2, r_A, OP.subtract))

    @block.scalar
    def _(scalar):
        scalar.wait_ge(ctx.sDVE.h, ctx.sDVE.n)
        ctx.sACT.inc(scalar.activation(r_A, r_m2, AF.Ln, bias=0.0, scale=1.0))
        scalar.wait_ge(ctx.sACT.h, ctx.sACT.n)
        ctx.sACT.inc(scalar.activation(r_A, r_A, AF.Exp, bias=0.0, scale=-0.5))

    @block.vector
    def _(vector):
        vector.wait_ge(ctx.sACT.h, ctx.sACT.n)
        ctx.sDVE.inc(vector.scalar_tensor_tensor(r_B, r_mu, -1.0, r_A,
                                                 OP.mult, OP.mult))


def emit_l1attn(nc, block, ctx, x_dram, out_dram, wq_dram, bq_dram, wk_dram,
                bk_dram, wv_dram, bvr_dram, wg_dram, bg_dram, wo_dram, bo_dram,
                csq_dram, snq_dram, csk_dram, snk_dram, gcol_dram, arow_dram,
                mask_dram, ones_dram, kin, vin, kgo, vgo, stack):
    ones2 = stack.enter_context(nc.sbuf_tensor("r_on", [128, 128], F32))
    gcol = stack.enter_context(nc.sbuf_tensor("r_gc", [128, 16 * 272], F32))
    abcast = stack.enter_context(nc.sbuf_tensor("r_ab", [128, 16 * 128], F32))
    masks = stack.enter_context(nc.sbuf_tensor("r_mk", [128, 16 * 2 * 128], F32))
    bvec = stack.enter_context(nc.sbuf_tensor("r_bvec", [128, 64], F32))
    rows_t = stack.enter_context(nc.sbuf_tensor("r_rw", [1, 4 * 512], F32))
    ps = stack.enter_context(nc.psum_tensor("r_ps", [128, 8 * 512], F32))

    rows = [rows_t[0:1, i * 512:(i + 1) * 512] for i in range(4)]
    rows256 = [rows_t[0:1, i * 512:i * 512 + 256] for i in range(4)]
    bq_c = lambda h: bvec[:, h:h + 1]
    bk_c = lambda h: bvec[:, 16 + h:16 + h + 1]
    bg_c = lambda h: bvec[:, 32 + h:32 + h + 1]
    bo_c = lambda o: bvec[:, 48 + o:48 + o + 1]
    gcol_idx = lambda h, j, kt: h * 272 + j * j + j + kt

    @block.sync
    def _(sync):
        for dst, src in [(ones2[:], ones_dram[:, :]), (gcol[:], gcol_dram[:, :]),
                         (abcast[:], arow_dram[:, :]), (masks[:], mask_dram[:, :]),
                         (bvec[:, 0:16], bq_dram[:, :]), (bvec[:, 16:32], bk_dram[:, :]),
                         (bvec[:, 32:48], bg_dram[:, :]), (bvec[:, 48:64], bo_dram[:, :])]:
            ctx.qS.inc(sync.dma_start(out=dst, in_=src), 16)

    with contextlib.ExitStack() as pk:
        T = 512
        xs = pk.enter_context(nc.sbuf_tensor("rk_x", [128, DC * T], F32))
        wbuf = pk.enter_context(nc.sbuf_tensor("rk_w", [128, 2 * DC * 128], F32))
        kst = pk.enter_context(nc.sbuf_tensor("rk_k", [128, T], F32))
        ktmp = pk.enter_context(nc.sbuf_tensor("rk_kt", [128, T], F32))
        utmp = pk.enter_context(nc.sbuf_tensor("rk_u", [128, T], F32))
        u2tmp = pk.enter_context(nc.sbuf_tensor("rk_u2", [128, T], F32))
        vstage = pk.enter_context(nc.sbuf_tensor("rk_v", [128, 4 * D], F32))
        csk = pk.enter_context(nc.sbuf_tensor("rk_rc", [128, TOK], F32))
        snk = pk.enter_context(nc.sbuf_tensor("rk_rs", [128, TOK], F32))
        bvb = pk.enter_context(nc.sbuf_tensor("rk_bvb", [128, D], F32))
        bvrow = pk.enter_context(nc.sbuf_tensor("rk_bvr", [1, D], F32))

        @block.sync
        def _(sync):
            ctx.qS.inc(sync.dma_start(out=csk[:], in_=csk_dram[:, :]), 16)
            ctx.qS.inc(sync.dma_start(out=snk[:], in_=snk_dram[:, :]), 16)
            ctx.qS.inc(sync.dma_start(out=bvrow[:], in_=bvr_dram[:, :]), 16)

        @block.tensor
        def _(tensor):
            tensor.wait_ge(ctx.qS.h, ctx.qS.n)
            for i in range(4):
                ctx.sPE.inc(tensor.matmul(ps[:, i * 512:(i + 1) * 512],
                                          ones2[0:1, :],
                                          bvrow[0:1, i * 512:(i + 1) * 512],
                                          start=True, stop=True))

        @block.vector
        def _(vector):
            vector.wait_ge(ctx.sPE.h, ctx.sPE.n)
            for i in range(4):
                ctx.sDVE.inc(vector.tensor_copy(bvb[:, i * 512:(i + 1) * 512],
                                                ps[:, i * 512:(i + 1) * 512]))

        for ch in range(TOK // T):
            t0 = ch * T
            dve_prev = ctx.sDVE.n
            @block.sync
            def _(sync):
                sync.wait_ge(ctx.sDVE.h, dve_prev)
                for c in range(DC):
                    ctx.qS.inc(sync.dma_start(
                        out=xs[:, c * T:(c + 1) * T],
                        in_=x_dram[c * 128:(c + 1) * 128, t0:t0 + T]), 16)

            pe_sw = ctx.sPE.n
            half_user = {0: pe_sw, 1: pe_sw}
            for h in range(N_HEADS):
                half = h % 2
                wslc = wbuf[:, half * DC * 128:(half + 1) * DC * 128]

                @block.sync
                def _(sync, h=h, wslc=wslc, wait_n=half_user[half]):
                    sync.wait_ge(ctx.sPE.h, wait_n)
                    ctx.qS.inc(sync.dma_start(
                        out=wslc, in_=wk_dram[:, h * DC * 128:(h + 1) * DC * 128]), 16)

                @block.tensor
                def _(tensor, h=h, wslc=wslc):
                    tensor.wait_ge(ctx.qS.h, ctx.qS.n)
                    tensor.wait_ge(ctx.sDVE.h, ctx.sDVE.n)
                    for c in range(DC):
                        ins = tensor.matmul(ps[:, (h % 4) * 512:(h % 4) * 512 + T],
                                            wslc[:, c * 128:(c + 1) * 128],
                                            xs[:, c * T:(c + 1) * T],
                                            start=(c == 0), stop=(c == DC - 1))
                    ctx.sPE.inc(ins)
                half_user[half] = ctx.sPE.n

                @block.vector
                def _(vector, h=h, t0=t0):
                    vector.wait_ge(ctx.sPE.h, ctx.sPE.n)
                    vector.wait_ge(ctx.qG.h, ctx.qG.n)
                    pk_ = ps[:, (h % 4) * 512:(h % 4) * 512 + T]
                    ctx.sDVE.inc(vector.tensor_scalar(ktmp[:], pk_, bk_c(h), None, OP.add))
                    vector.wait_ge(ctx.sDVE.h, ctx.sDVE.n)
                    xpos_ops(ctx, vector, kst[:], ktmp[:],
                             csk[:, t0:t0 + T], snk[:, t0:t0 + T], utmp[:], u2tmp[:])

                dve_st = ctx.sDVE.n
                @block.gpsimd
                def _(gpsimd, h=h, t0=t0):
                    gpsimd.wait_ge(ctx.sDVE.h, dve_st)
                    ctx.qG.inc(gpsimd.dma_start(
                        out=kin[h * 128:(h + 1) * 128, t0:t0 + T], in_=kst[:]), 16)
                emit_phase_barrier(ctx)

            pe_sw = ctx.sPE.n
            half_user = {0: pe_sw, 1: pe_sw}
            qg_v = ctx.qG.n
            for h in range(N_HEADS):
                half = h % 2
                wslc = wbuf[:, half * DC * 128:(half + 1) * DC * 128]

                @block.sync
                def _(sync, h=h, wslc=wslc, wait_n=half_user[half]):
                    sync.wait_ge(ctx.sPE.h, wait_n)
                    ctx.qS.inc(sync.dma_start(
                        out=wslc, in_=wv_dram[:, h * DC * 128:(h + 1) * DC * 128]), 16)

                for tb in range(4):
                    @block.tensor
                    def _(tensor, h=h, tb=tb, wslc=wslc):
                        tensor.wait_ge(ctx.qS.h, ctx.qS.n)
                        tensor.wait_ge(ctx.sDVE.h, ctx.sDVE.n)
                        for c in range(DC):
                            ins = tensor.matmul(
                                ps[:, tb * 512:tb * 512 + 128],
                                xs[:, c * T + tb * 128:c * T + (tb + 1) * 128],
                                wslc[:, c * 128:(c + 1) * 128],
                                start=(c == 0), stop=(c == DC - 1))
                        ctx.sPE.inc(ins)

                    @block.vector
                    def _(vector, h=h, tb=tb):
                        vector.wait_ge(ctx.sPE.h, ctx.sPE.n)
                        if h == 0 and tb == 0:
                            vector.wait_ge(ctx.qG.h, qg_v)
                        ctx.sDVE.inc(vector.tensor_tensor(
                            vstage[:, tb * D + h * 128:tb * D + (h + 1) * 128],
                            ps[:, tb * 512:tb * 512 + 128],
                            bvb[:, h * 128:(h + 1) * 128], OP.add))
                half_user[half] = ctx.sPE.n

            dve_st = ctx.sDVE.n
            @block.gpsimd
            def _(gpsimd, t0=t0):
                gpsimd.wait_ge(ctx.sDVE.h, dve_st)
                for tb in range(4):
                    ctx.qG.inc(gpsimd.dma_start(
                        out=vin[t0 + tb * 128:t0 + (tb + 1) * 128, :],
                        in_=vstage[:, tb * D:(tb + 1) * D]), 16)

        qg_all = ctx.qG.n
        @block.gpsimd
        def _(gpsimd):
            gpsimd.wait_ge(ctx.qG.h, qg_all)
            ctx.qA.inc(gpsimd.collective_compute(
                "AllGather", OP.bypass,
                replica_groups=[[0, 1], [2, 3], [4, 5], [6, 7]],
                ins=[kin.opt()], outs=[kgo.opt()]), 1)
            ctx.qA.inc(gpsimd.collective_compute(
                "AllGather", OP.bypass,
                replica_groups=[[0, 1], [2, 3], [4, 5], [6, 7]],
                ins=[vin.opt()], outs=[vgo.opt()]), 1)

    emit_phase_barrier(ctx)

    with contextlib.ExitStack() as pq:
        T = 256
        xs = pq.enter_context(nc.sbuf_tensor("rq_x", [128, DC * T], F32))
        wbuf = pq.enter_context(nc.sbuf_tensor("rq_w", [128, 2 * DC * 128], F32))
        qall = pq.enter_context(nc.sbuf_tensor("rq_q", [128, 16 * T], F32))
        gall = pq.enter_context(nc.sbuf_tensor("rq_g", [128, 16 * T], F32))
        retall = pq.enter_context(nc.sbuf_tensor("rq_r", [128, 16 * T], F32))
        kstrip = pq.enter_context(nc.sbuf_tensor("rq_ks", [128, 32 * 128], F32))
        vstrip = pq.enter_context(nc.sbuf_tensor("rq_vs", [128, 32 * 128], F32))
        csq = pq.enter_context(nc.sbuf_tensor("rq_rc", [128, TOK], F32))
        snq = pq.enter_context(nc.sbuf_tensor("rq_rs", [128, TOK], F32))
        ttmp = pq.enter_context(nc.sbuf_tensor("rq_t", [128, T], F32))
        utmp = pq.enter_context(nc.sbuf_tensor("rq_u", [128, T], F32))
        u2tmp = pq.enter_context(nc.sbuf_tensor("rq_u2", [128, T], F32))
        pts = pq.enter_context(nc.sbuf_tensor("rq_pt", [128, 4 * 128], F32))
        sq_a = pq.enter_context(nc.sbuf_tensor("rq_qa", [128, T], F32))
        sq_b = pq.enter_context(nc.sbuf_tensor("rq_qb", [128, T], F32))

        @block.sync
        def _(sync):
            ctx.qS.inc(sync.dma_start(out=csq[:], in_=csq_dram[:, :]), 16)
            ctx.qS.inc(sync.dma_start(out=snq[:], in_=snq_dram[:, :]), 16)
            sync.wait_ge(ctx.qA.h, 2)

        for ch in range(TOK // T):
            t0 = ch * T
            nloc = 2 * ch + 2
            dve_prev = ctx.sDVE.n
            qg_prev = ctx.qG.n
            @block.sync
            def _(sync):
                sync.wait_ge(ctx.sDVE.h, dve_prev)
                sync.wait_ge(ctx.qG.h, qg_prev)
                for c in range(DC):
                    ctx.qS.inc(sync.dma_start(
                        out=xs[:, c * T:(c + 1) * T],
                        in_=x_dram[c * 128:(c + 1) * 128, t0:t0 + T]), 16)

            @block.vector
            def _(vector, qg_prev=qg_prev):
                vector.wait_ge(ctx.qG.h, qg_prev)

            pe_sw = ctx.sPE.n
            half_user = {0: pe_sw, 1: pe_sw}
            for h in range(N_HEADS):
                half = h % 2
                wq_s = wbuf[:, half * DC * 128:(half + 1) * DC * 128]
                wg_s = wbuf[:, (1 - half) * DC * 128:(2 - half) * DC * 128]

                @block.sync
                def _(sync, h=h, wq_s=wq_s, wait_n=half_user[half]):
                    sync.wait_ge(ctx.sPE.h, wait_n)
                    ctx.qS.inc(sync.dma_start(
                        out=wq_s, in_=wq_dram[:, h * DC * 128:(h + 1) * DC * 128]), 16)

                @block.tensor
                def _(tensor, h=h, wq_s=wq_s):
                    tensor.wait_ge(ctx.qS.h, ctx.qS.n)
                    tensor.wait_ge(ctx.sDVE.h, ctx.sDVE.n)
                    for c in range(DC):
                        ins = tensor.matmul(ps[:, 0 * 512:0 * 512 + T],
                                            wq_s[:, c * 128:(c + 1) * 128],
                                            xs[:, c * T:(c + 1) * T],
                                            start=(c == 0), stop=(c == DC - 1))
                    ctx.sPE.inc(ins)
                half_user[half] = ctx.sPE.n

                @block.vector
                def _(vector, h=h, t0=t0):
                    vector.wait_ge(ctx.sPE.h, ctx.sPE.n)
                    vector.wait_ge(ctx.sACT.h, ctx.sACT.n)
                    ctx.sDVE.inc(vector.tensor_scalar(ttmp[:], ps[:, 0:T],
                                                      bq_c(h), None, OP.add))
                    vector.wait_ge(ctx.sDVE.h, ctx.sDVE.n)
                    xpos_ops(ctx, vector, qall[:, h * T:(h + 1) * T], ttmp[:],
                             csq[:, t0:t0 + T], snq[:, t0:t0 + T], utmp[:], u2tmp[:])

                @block.sync
                def _(sync, h=h, wg_s=wg_s, wait_n=half_user[1 - half]):
                    sync.wait_ge(ctx.sPE.h, wait_n)
                    ctx.qS.inc(sync.dma_start(
                        out=wg_s, in_=wg_dram[:, h * DC * 128:(h + 1) * DC * 128]), 16)

                @block.tensor
                def _(tensor, h=h, wg_s=wg_s):
                    tensor.wait_ge(ctx.qS.h, ctx.qS.n)
                    tensor.wait_ge(ctx.sDVE.h, ctx.sDVE.n)
                    for c in range(DC):
                        ins = tensor.matmul(ps[:, 1 * 512:1 * 512 + T],
                                            wg_s[:, c * 128:(c + 1) * 128],
                                            xs[:, c * T:(c + 1) * T],
                                            start=(c == 0), stop=(c == DC - 1))
                    ctx.sPE.inc(ins)
                half_user[1 - half] = ctx.sPE.n

                @block.vector
                def _(vector, h=h):
                    vector.wait_ge(ctx.sPE.h, ctx.sPE.n)
                    vector.wait_ge(ctx.sACT.h, ctx.sACT.n)
                    ctx.sDVE.inc(vector.tensor_scalar(
                        ttmp[:], ps[:, 1 * 512:1 * 512 + T], bg_c(h), None, OP.add))

                @block.scalar
                def _(scalar, h=h):
                    scalar.wait_ge(ctx.sDVE.h, ctx.sDVE.n)
                    ctx.sACT.inc(scalar.activation(gall[:, h * T:(h + 1) * T],
                                                   ttmp[:], AF.Silu,
                                                   bias=0.0, scale=1.0))
                emit_phase_barrier(ctx)

            koff = lambda kt: (0 if kt % 2 == 0 else 2048) + (kt // 2) * 128
            for h in range(N_HEADS):
                pe_h = ctx.sPE.n
                @block.sync
                def _(sync, h=h, nloc=nloc, wait_n=pe_h):
                    sync.wait_ge(ctx.sPE.h, wait_n)
                    ctx.qS.inc(sync.dma_start(
                        out=kstrip[:, 0:nloc * 128],
                        in_=kgo[h * 128:(h + 1) * 128, 0:nloc * 128]), 16)
                    ctx.qS.inc(sync.dma_start(
                        out=kstrip[:, 2048:2048 + nloc * 128],
                        in_=kgo[D + h * 128:D + (h + 1) * 128, 0:nloc * 128]), 16)
                    ctx.qS.inc(sync.dma_start(
                        out=vstrip[:, 0:nloc * 128].rearrange("p (l m) -> p l m", l=nloc),
                        in_=vgo[0:nloc * 128, h * 128:(h + 1) * 128].rearrange(
                            "(l p) m -> p l m", p=128)), 16)
                    ctx.qS.inc(sync.dma_start(
                        out=vstrip[:, 2048:2048 + nloc * 128].rearrange("p (l m) -> p l m", l=nloc),
                        in_=vgo[TOK:TOK + nloc * 128, h * 128:(h + 1) * 128].rearrange(
                            "(l p) m -> p l m", p=128)), 16)

                for j in range(2):
                    jl = 2 * ch + j
                    ktmax = 2 * jl + 1
                    dve_bj = ctx.sDVE.n
                    pe_bj = ctx.sPE.n
                    qrhs = qall[:, h * T + j * 128:h * T + (j + 1) * 128]
                    score_done = {}
                    evac_done = {}
                    pv_done = {}

                    def emit_score(kt):
                        sbank = 2 + kt % 4
                        @block.tensor
                        def _(tensor, kt=kt, sbank=sbank, qrhs=qrhs):
                            if kt == 0:
                                tensor.wait_ge(ctx.qS.h, ctx.qS.n)
                                tensor.wait_ge(ctx.sDVE.h, dve_bj)
                            if kt >= 4:
                                tensor.wait_ge(ctx.sDVE.h, evac_done[kt - 4])
                            ctx.sPE.inc(tensor.matmul(
                                ps[:, sbank * 512:sbank * 512 + 128],
                                kstrip[:, koff(kt):koff(kt) + 128],
                                qrhs, start=True, stop=True))
                        score_done[kt] = ctx.sPE.n

                    def emit_evac(kt):
                        sbank = 2 + kt % 4
                        @block.vector
                        def _(vector, h=h, jl=jl, kt=kt, sbank=sbank):
                            vector.wait_ge(ctx.sPE.h, score_done[kt])
                            if kt >= 4:
                                vector.wait_ge(ctx.sPE.h, pv_done[kt - 4])
                            else:
                                vector.wait_ge(ctx.sPE.h, pe_bj)
                            slot = pts[:, (kt % 4) * 128:(kt % 4 + 1) * 128]
                            sc = ps[:, sbank * 512:sbank * 512 + 128]
                            if kt < 2 * jl:
                                ctx.sDVE.inc(vector.tensor_scalar(
                                    slot, sc, gcol[:, gcol_idx(h, jl, kt):
                                                   gcol_idx(h, jl, kt) + 1],
                                    None, OP.mult))
                            else:
                                w = kt - 2 * jl
                                ctx.sDVE.inc(vector.tensor_tensor(
                                    slot, sc,
                                    masks[:, (h * 2 + w) * 128:(h * 2 + w + 1) * 128],
                                    OP.mult))
                        evac_done[kt] = ctx.sDVE.n

                    def emit_pv(kt):
                        @block.tensor
                        def _(tensor, j=j, kt=kt, ktmax=ktmax):
                            tensor.wait_ge(ctx.sDVE.h, evac_done[kt])
                            ctx.sPE.inc(tensor.matmul(
                                ps[:, (6 + j) * 512:(6 + j) * 512 + 128],
                                vstrip[:, koff(kt):koff(kt) + 128],
                                pts[:, (kt % 4) * 128:(kt % 4 + 1) * 128],
                                start=(kt == 0), stop=(kt == ktmax)))
                        pv_done[kt] = ctx.sPE.n

                    for kt in range(ktmax + 1):
                        emit_score(kt)
                        if kt >= 2:
                            emit_pv(kt - 2)
                        emit_evac(kt)
                    for kt in (ktmax - 1, ktmax):
                        if kt >= 0:
                            emit_pv(kt)

                    @block.vector
                    def _(vector, h=h, j=j, ktmax=ktmax, done=ctx.sPE.n):
                        vector.wait_ge(ctx.sPE.h, done)
                        ctx.sDVE.inc(vector.tensor_tensor(
                            retall[:, h * T + j * 128:h * T + (j + 1) * 128],
                            ps[:, (6 + j) * 512:(6 + j) * 512 + 128],
                            abcast[:, h * 128:(h + 1) * 128], OP.mult))

            for h in range(N_HEADS):
                rslc = retall[:, h * T:(h + 1) * T]
                @block.tensor
                def _(tensor, h=h, rslc=rslc):
                    tensor.wait_ge(ctx.sDVE.h, ctx.sDVE.n)
                    ctx.sPE.inc(tensor.matmul(ps[0:1, 0:T], ones2[:, 0:1], rslc,
                                              start=True, stop=True))

                @block.vector
                def _(vector, h=h, rslc=rslc):
                    vector.wait_ge(ctx.sPE.h, ctx.sPE.n)
                    ctx.sDVE.inc(vector.tensor_tensor(sq_a[:], rslc, rslc, OP.mult))

                @block.tensor
                def _(tensor, h=h):
                    tensor.wait_ge(ctx.sDVE.h, ctx.sDVE.n)
                    ctx.sPE.inc(tensor.matmul(ps[0:1, 512:512 + T], ones2[:, 0:1],
                                              sq_a[:], start=True, stop=True))

                emit_norm_rows(ctx, ps[0:1, 0:T], ps[0:1, 512:512 + T], 128.0,
                               GN_EPS, rows256)

                @block.tensor
                def _(tensor, h=h):
                    tensor.wait_ge(ctx.sDVE.h, ctx.sDVE.n)
                    tensor.matmul(ps[:, 2 * 512:2 * 512 + T], ones2[0:1, :],
                                  rows256[2], start=True, stop=True)
                    ctx.sPE.inc(tensor.matmul(ps[:, 3 * 512:3 * 512 + T], ones2[0:1, :],
                                              rows256[3], start=True, stop=True))

                @block.vector
                def _(vector, h=h, rslc=rslc):
                    vector.wait_ge(ctx.sPE.h, ctx.sPE.n)
                    ctx.sDVE.inc(vector.tensor_tensor(
                        ttmp[:], rslc, ps[:, 2 * 512:2 * 512 + T], OP.mult))
                    vector.wait_ge(ctx.sDVE.h, ctx.sDVE.n)
                    ctx.sDVE.inc(vector.tensor_tensor(
                        ttmp[:], ttmp[:], ps[:, 3 * 512:3 * 512 + T], OP.add))
                    vector.wait_ge(ctx.sDVE.h, ctx.sDVE.n)
                    ctx.sDVE.inc(vector.tensor_tensor(
                        gall[:, h * T:(h + 1) * T], ttmp[:],
                        gall[:, h * T:(h + 1) * T], OP.mult))


            pe_sw = ctx.sPE.n
            half_user = {0: ctx.sPE.n, 1: ctx.sPE.n}
            for o in range(DC):
                half = o % 2
                wo_s = wbuf[:, half * DC * 128:(half + 1) * DC * 128]
                @block.sync
                def _(sync, o=o, wo_s=wo_s, wait_n=half_user[half]):
                    sync.wait_ge(ctx.sPE.h, wait_n)
                    ctx.qS.inc(sync.dma_start(
                        out=wo_s, in_=wo_dram[:, o * DC * 128:(o + 1) * DC * 128]), 16)

                @block.tensor
                def _(tensor, o=o, wo_s=wo_s):
                    tensor.wait_ge(ctx.qS.h, ctx.qS.n)
                    if o == 0:
                        tensor.wait_ge(ctx.sDVE.h, ctx.sDVE.n)
                    if o >= 4:
                        tensor.wait_ge(ctx.sDVE.h, dve_sw + o - 3)
                    for c in range(DC):
                        ins = tensor.matmul(ps[:, (o % 4) * 512:(o % 4) * 512 + T],
                                            wo_s[:, c * 128:(c + 1) * 128],
                                            gall[:, c * T:(c + 1) * T],
                                            start=(c == 0), stop=(c == DC - 1))
                    ctx.sPE.inc(ins)
                half_user[half] = ctx.sPE.n
                if o == 0:
                    dve_sw = ctx.sDVE.n

                @block.vector
                def _(vector, o=o):
                    vector.wait_ge(ctx.sPE.h, ctx.sPE.n)
                    ctx.sDVE.inc(vector.scalar_tensor_tensor(
                        qall[:, o * T:(o + 1) * T],
                        ps[:, (o % 4) * 512:(o % 4) * 512 + T],
                        bo_c(o), xs[:, o * T:(o + 1) * T], OP.add, OP.add))


            s_tiles = [qall[:, c * T:(c + 1) * T] for c in range(DC)]
            emit_ln(ctx, T, s_tiles, s_tiles, sq_a[:], sq_b[:], ps, 0, ones2, rows256)

            dve_done = ctx.sDVE.n
            @block.gpsimd
            def _(gpsimd, t0=t0):
                gpsimd.wait_ge(ctx.sDVE.h, dve_done)
                for c in range(DC):
                    ctx.qG.inc(gpsimd.dma_start(
                        out=out_dram[c * 128:(c + 1) * 128, t0:t0 + T],
                        in_=qall[:, c * T:(c + 1) * T]), 16)


def own_token_tiles(core):
    par = core % 2
    return [2 * j + par for j in range(NT)]


def own_token_index(core):
    return np.concatenate([np.arange(g * 128, (g + 1) * 128)
                           for g in own_token_tiles(core)])


def vec_tiles(v):
    n = v.shape[0] // 128
    return np.ascontiguousarray(v.reshape(n, 128).T)


def lhsT_dev(W):
    din, dout = W.shape
    nc_, no_ = din // 128, dout // 128
    return np.ascontiguousarray(
        W.reshape(nc_, 128, no_, 128).transpose(1, 2, 0, 3).reshape(128, -1))


def lhsT_dev_k(W):
    din, dout = W.shape
    nc_, no_ = din // 128, dout // 128
    return np.ascontiguousarray(
        W.reshape(nc_, 128, no_, 128).transpose(1, 0, 2, 3).reshape(128, -1))


def build_l1ffn_program():
    nc = bass.Bass("TRN2", target_bir_lowering=False)
    x = nc.dram_tensor("x", [D, TOK], F32, kind="ExternalInput").ap()
    w0t = nc.dram_tensor("w0t", [128, IC * DC * 128], F32, kind="ExternalInput").ap()
    w1t = nc.dram_tensor("w1t", [128, IC * DC * 128], F32, kind="ExternalInput").ap()
    b0 = nc.dram_tensor("b0", [128, IC], F32, kind="ExternalInput").ap()
    b1 = nc.dram_tensor("b1", [128, DC], F32, kind="ExternalInput").ap()
    ones = nc.dram_tensor("ones", [128, 128], F32, kind="ExternalInput").ap()
    out = nc.dram_tensor("out", [D, TOK], F32, kind="ExternalOutput").ap()

    with contextlib.ExitStack() as stack:
        block = stack.enter_context(nc.Block())
        ctx = make_ctx(nc, block, stack)
        emit_l1ffn(nc, block, ctx, x, out, w0t, w1t, b0, b1, ones, stack)
        @block.gpsimd
        def _(gpsimd):
            gpsimd.wait_ge(ctx.qG.h, ctx.qG.n)
    return nc


def rope_tables(dim, positions, scale=1.0):
    inv_freq = (1.0 / (np.float32(10000.0) **
                       (np.arange(0, dim, 2, dtype=np.float32) / np.float32(dim))))
    inv_freq = inv_freq.astype(np.float32)
    pos = positions.astype(np.float32)
    ang = (pos[:, None] * inv_freq[None, :]).astype(np.float32)
    cos = np.cos(ang.astype(np.float64)).astype(np.float32) * np.float32(scale)
    sin = np.sin(ang.astype(np.float64)).astype(np.float32) * np.float32(scale)
    ct = np.concatenate([cos, cos], axis=1).T
    st = np.concatenate([sin, sin], axis=1).T
    return np.ascontiguousarray(ct), np.ascontiguousarray(st)


def build_l0attn_program():
    nc = bass.Bass("TRN2", target_bir_lowering=False)
    gi = lambda nm, shp: nc.dram_tensor(nm, shp, F32, kind="ExternalInput").ap()
    xk = gi("xk", [D, S])
    xq = gi("xq", [D, TOK])
    wcq = gi("wcq", [128, 6 * DC * 128])
    gq = gi("gq", [128, 6])
    wdq = gi("wdq", [128, 16 * 6 * 128])
    wdqr = gi("wdqr", [128, 16 * 6 * 64])
    wckv = gi("wckv", [128, 4 * DC * 128])
    gkv = gi("gkv", [128, 4])
    wdk = gi("wdk", [128, 16 * 4 * 128])
    wdv = gi("wdv", [128, 4 * 16 * 128])
    wkr = gi("wkr", [128, DC * 64])
    wo = gi("wo", [128, DC * DC * 128])
    rckv = gi("rckv", [64, S])
    rskv = gi("rskv", [64, S])
    rcq = gi("rcq", [64, TOK])
    rsq = gi("rsq", [64, TOK])
    ident = gi("ident", [128, 128])
    ones = gi("ones", [128, 128])
    attn_scr = nc.dram_tensor("attn_scr", [D, TOK], F32, kind="Internal").ap()
    qcn_scr = nc.dram_tensor("qcn_scr", [6 * 128, TOK], F32, kind="Internal").ap()
    out = nc.dram_tensor("out", [D, TOK], F32, kind="ExternalOutput").ap()

    with contextlib.ExitStack() as stack:
        block = stack.enter_context(nc.Block())
        ctx = make_ctx(nc, block, stack)
        emit_l0attn(nc, block, ctx, xk, xq, out, wcq, gq, wdq, wdqr, wckv,
                    gkv, wdk, wdv, wkr, wo, rckv, rskv, rcq, rsq, ident,
                    ones, attn_scr, qcn_scr, stack)
        @block.gpsimd
        def _(gpsimd):
            gpsimd.wait_ge(ctx.qG.h, ctx.qG.n)
    return nc


def stage_l0attn_shared(Wcq, g_q, Wdq, Wdqr, Wckv, g_kv, Wdk, Wdv, Wkr, Wo):
    rckv, rskv = rope_tables(64, np.arange(S))
    return {
        "wcq": lhsT_dev(Wcq),
        "gq": vec_tiles(g_q),
        "wdq": lhsT_dev(Wdq),
        "wdqr": np.ascontiguousarray(
            Wdqr.reshape(6, 128, 16, 64).transpose(1, 2, 0, 3).reshape(128, -1)),
        "wckv": lhsT_dev(Wckv),
        "gkv": vec_tiles(g_kv),
        "wdk": lhsT_dev(Wdk),
        "wdv": np.ascontiguousarray(
            Wdv.reshape(4, 128, 16, 128).transpose(1, 0, 2, 3).reshape(128, -1)),
        "wkr": np.ascontiguousarray(
            Wkr.reshape(16, 128, 64).transpose(1, 0, 2).reshape(128, -1)),
        "wo": lhsT_dev(Wo),
        "rckv": rckv, "rskv": rskv,
        "ident": np.ascontiguousarray(np.eye(128, dtype=np.float32)),
        "ones": np.ones((128, 128), np.float32),
    }


def run_l0attn(x_full, Wcq, g_q, Wdq, Wdqr, Wckv, g_kv, Wdk, Wdv, Wkr, Wo):
    nc = build_l0attn_program()
    shared = stage_l0attn_shared(Wcq, g_q, Wdq, Wdqr, Wckv, g_kv, Wdk, Wdv, Wkr, Wo)
    in_maps = []
    for c in range(NCORES):
        b = c // 2
        idx = own_token_index(c)
        rcq, rsq = rope_tables(64, idx, scale=1.0 / np.sqrt(192.0))
        in_maps.append({
            "xk": np.ascontiguousarray(x_full[b].T),
            "xq": np.ascontiguousarray(x_full[b][idx].T),
            "rcq": rcq, "rsq": rsq, **shared})
    res = run_bass_kernel_spmd(nc, in_maps, core_ids=list(range(NCORES)))
    out = np.zeros((B, S, D), np.float32)
    for c in range(NCORES):
        b = c // 2
        idx = own_token_index(c)
        out[b, idx] = res.results[c]["out"].T
    return out


def build_l1attn_program():
    nc = bass.Bass("TRN2", target_bir_lowering=False)
    gi = lambda nm, shp: nc.dram_tensor(nm, shp, F32, kind="ExternalInput").ap()
    x = gi("x", [D, TOK])
    wq = gi("wq", [128, 16 * DC * 128]); bq = gi("bq", [128, 16])
    wk = gi("wk", [128, 16 * DC * 128]); bk = gi("bk", [128, 16])
    wv = gi("wv", [128, 16 * DC * 128]); bvr = gi("bvr", [1, D])
    wg = gi("wg", [128, 16 * DC * 128]); bg = gi("bg", [128, 16])
    wo = gi("wo", [128, DC * DC * 128]); bo = gi("bo", [128, 16])
    csq = gi("csq", [128, TOK]); snq = gi("snq", [128, TOK])
    csk = gi("csk", [128, TOK]); snk = gi("snk", [128, TOK])
    gcol = gi("gcol", [128, 16 * 272])
    arow = gi("arow", [128, 16 * 128])
    mask = gi("mask", [128, 16 * 2 * 128])
    ones = gi("ones", [128, 128])
    kin = nc.dram_tensor("kin", [D, TOK], F32, kind="Internal").ap()
    vin = nc.dram_tensor("vin", [TOK, D], F32, kind="Internal").ap()
    kgo = nc.dram_tensor("kgo", [2 * D, TOK], F32, kind="Internal").ap()
    vgo = nc.dram_tensor("vgo", [2 * TOK, D], F32, kind="Internal").ap()
    out = nc.dram_tensor("out", [D, TOK], F32, kind="ExternalOutput").ap()

    with contextlib.ExitStack() as stack:
        block = stack.enter_context(nc.Block())
        ctx = make_ctx(nc, block, stack)
        emit_l1attn(nc, block, ctx, x, out, wq, bq, wk, bk, wv, bvr, wg, bg,
                    wo, bo, csq, snq, csk, snk, gcol, arow, mask, ones,
                    kin, vin, kgo, vgo, stack)
        @block.gpsimd
        def _(gpsimd):
            gpsimd.wait_ge(ctx.qG.h, ctx.qG.n)
    return nc


def xpos_tables(positions):
    dim, TH, SB = 128, 10000.0, 512.0
    inv_freq = (1.0 / (np.float32(TH) **
                       (np.arange(0, dim, 2, dtype=np.float32) / np.float32(dim))))
    inv_freq = inv_freq.astype(np.float32)
    pos = positions.astype(np.float32)
    fr = (pos[:, None] * inv_freq[None, :]).astype(np.float32)
    cos = np.cos(fr.astype(np.float64))
    sin = np.sin(fr.astype(np.float64))
    sbase = ((np.arange(0, dim, 2, dtype=np.float32) + np.float32(0.4 * dim))
             / np.float32(1.4 * dim)).astype(np.float32)
    power = ((pos - np.float32(S // 2)) / np.float32(SB)).astype(np.float32)
    scale = sbase.astype(np.float64)[None, :] ** power.astype(np.float64)[:, None]
    def lay(m):
        t = np.concatenate([m.T, m.T], axis=0)
        return np.ascontiguousarray(t.astype(np.float32))
    return (lay(cos * scale), lay(sin * scale),
            lay(cos / scale), lay(sin / scale))


def stage_l1attn_shared(Wq, bq, Wk, bk, Wv, bv, Wg, bg, Wo, bo):
    perm = np.concatenate([np.arange(0, 128, 2), np.arange(1, 128, 2)])
    colperm = np.concatenate([h * 128 + perm for h in range(16)])
    Wqp = np.ascontiguousarray(Wq[:, colperm])
    Wkp = np.ascontiguousarray(Wk[:, colperm])
    return {
        "wq": lhsT_dev(Wqp), "bq": vec_tiles(bq[colperm]),
        "wk": lhsT_dev(Wkp), "bk": vec_tiles(bk[colperm]),
        "wv": np.ascontiguousarray(
            Wv.reshape(16, 128, 16, 128).transpose(1, 2, 0, 3).reshape(128, -1)),
        "bvr": np.ascontiguousarray(bv[None, :]),
        "wg": lhsT_dev(Wg), "bg": vec_tiles(bg),
        "wo": lhsT_dev(Wo), "bo": vec_tiles(bo),
        "ones": np.ones((128, 128), np.float32),
    }


def stage_l1attn_core(core):
    par = core % 2
    idx = own_token_index(core)
    csq, snq, csk, snk = xpos_tables(idx)
    gam = (1.0 - 2.0 ** (-5.0 - np.arange(16, dtype=np.float64)))
    p = np.arange(128, dtype=np.float64)
    gcol = np.zeros((128, 16 * 272), np.float64)
    for h in range(16):
        for j in range(16):
            for kt in range(2 * j + 2):
                d = 128.0 * ((2 * j + par) - kt)
                gcol[:, h * 272 + j * j + j + kt] = gam[h] ** (d - p)
    qq = np.arange(128, dtype=np.float64)
    arowt = np.repeat(
        np.concatenate([gam[h] ** qq for h in range(16)])[None, :], 128, axis=0)
    q = np.arange(128, dtype=np.float64)
    mask = np.zeros((128, 16 * 2 * 128), np.float64)
    caus = (q[None, :] >= p[:, None])
    for h in range(16):
        diag = (gam[h] ** (-p))[:, None] * caus
        if par == 0:
            m0, m1 = diag, np.zeros((128, 128))
        else:
            m0 = np.repeat((gam[h] ** (128.0 - p))[:, None], 128, axis=1)
            m1 = diag
        mask[:, (h * 2) * 128:(h * 2 + 1) * 128] = m0
        mask[:, (h * 2 + 1) * 128:(h * 2 + 2) * 128] = m1
    return {
        "csq": csq, "snq": snq, "csk": csk, "snk": snk,
        "gcol": np.ascontiguousarray(gcol.astype(np.float32)),
        "arow": np.ascontiguousarray(arowt.astype(np.float32)),
        "mask": np.ascontiguousarray(mask.astype(np.float32)),
    }


def run_l1attn(x_full, Wq, bq, Wk, bk, Wv, bv, Wg, bg, Wo, bo):
    nc = build_l1attn_program()
    shared = stage_l1attn_shared(Wq, bq, Wk, bk, Wv, bv, Wg, bg, Wo, bo)
    in_maps = []
    for c in range(NCORES):
        b = c // 2
        idx = own_token_index(c)
        in_maps.append({"x": np.ascontiguousarray(x_full[b][idx].T),
                        **stage_l1attn_core(c), **shared})
    res = run_bass_kernel_spmd(nc, in_maps, core_ids=list(range(NCORES)))
    out = np.zeros((B, S, D), np.float32)
    for c in range(NCORES):
        b = c // 2
        idx = own_token_index(c)
        out[b, idx] = res.results[c]["out"].T
    return out


def hadamard128():
    H = np.array([[1.0]], np.float32)
    for _ in range(7):
        H = np.block([[H, H], [H, -H]])
    return np.ascontiguousarray(H.astype(np.float32))


def build_l0ffn_program():
    nc = bass.Bass("TRN2", target_bir_lowering=False)
    x = nc.dram_tensor("x", [D, TOK], F32, kind="ExternalInput").ap()
    wupt = nc.dram_tensor("wupt", [128, IC * DC * 128], F32, kind="ExternalInput").ap()
    wgtt = nc.dram_tensor("wgtt", [128, IC * DC * 128], F32, kind="ExternalInput").ap()
    wdnt = nc.dram_tensor("wdnt", [128, IC * DC * 128], F32, kind="ExternalInput").ap()
    had = nc.dram_tensor("had", [128, 128], F32, kind="ExternalInput").ap()
    su_ug = nc.dram_tensor("su_ug", [128, DC], F32, kind="ExternalInput").ap()
    su_dp = nc.dram_tensor("su_dp", [128, IC], F32, kind="ExternalInput").ap()
    ws_ug = nc.dram_tensor("ws_ug", [128, 2 * IC], F32, kind="ExternalInput").ap()
    ws_dp = nc.dram_tensor("ws_dp", [128, DC], F32, kind="ExternalInput").ap()
    ones = nc.dram_tensor("ones", [128, 128], F32, kind="ExternalInput").ap()
    out = nc.dram_tensor("out", [D, TOK], F32, kind="ExternalOutput").ap()

    with contextlib.ExitStack() as stack:
        block = stack.enter_context(nc.Block())
        ctx = make_ctx(nc, block, stack)
        emit_l0ffn(nc, block, ctx, x, out, wupt, wgtt, wdnt, had, su_ug,
                   su_dp, ws_ug, ws_dp, ones, stack)
        @block.gpsimd
        def _(gpsimd):
            gpsimd.wait_ge(ctx.qG.h, ctx.qG.n)
    return nc


def stage_l0ffn(SU_ug, SU_dp, Wscale_ug, Wscale_dp, W_up, W_gate, W_down):
    return {
        "wupt": lhsT_dev(np.ascontiguousarray(W_up.T)),
        "wgtt": lhsT_dev(np.ascontiguousarray(W_gate.T)),
        "wdnt": lhsT_dev_k(np.ascontiguousarray(W_down.T)),
        "had": hadamard128(),
        "su_ug": vec_tiles(SU_ug),
        "su_dp": vec_tiles(SU_dp),
        "ws_ug": vec_tiles(Wscale_ug),
        "ws_dp": vec_tiles(Wscale_dp),
        "ones": np.ones((128, 128), np.float32),
    }


def run_l0ffn(x_full, SU_ug, SU_dp, Wscale_ug, Wscale_dp, W_up, W_gate, W_down):
    xf = x_full.reshape(-1, D)
    nc = build_l0ffn_program()
    shared = stage_l0ffn(SU_ug, SU_dp, Wscale_ug, Wscale_dp, W_up, W_gate, W_down)
    in_maps = []
    for c in range(NCORES):
        b = c // 2
        idx = own_token_index(c) + b * S
        xc = np.ascontiguousarray(xf[idx].T)
        in_maps.append({"x": xc, **shared})
    res = run_bass_kernel_spmd(nc, in_maps, core_ids=list(range(NCORES)))
    out = np.zeros((B * S, D), np.float32)
    for c in range(NCORES):
        b = c // 2
        idx = own_token_index(c) + b * S
        out[idx] = res.results[c]["out"].T
    return out.reshape(x_full.shape)


def stage_l1ffn(W0, bias0, W1, bias1):
    return {
        "w0t": lhsT_dev(np.ascontiguousarray(W0.T)),
        "w1t": lhsT_dev_k(np.ascontiguousarray(W1.T)),
        "b0": vec_tiles(bias0),
        "b1": vec_tiles(bias1),
        "ones": np.ones((128, 128), np.float32),
    }


def run_l1ffn(x_full, W0, bias0, W1, bias1):
    xf = x_full.reshape(-1, D)
    nc = build_l1ffn_program()
    shared = stage_l1ffn(W0, bias0, W1, bias1)
    in_maps = []
    for c in range(NCORES):
        b = c // 2
        idx = own_token_index(c) + b * S
        xc = np.ascontiguousarray(xf[idx].T)
        in_maps.append({"x": xc, **shared})
    res = run_bass_kernel_spmd(nc, in_maps, core_ids=list(range(NCORES)))
    out = np.zeros((B * S, D), np.float32)
    for c in range(NCORES):
        b = c // 2
        idx = own_token_index(c) + b * S
        out[idx] = res.results[c]["out"].T
    return out.reshape(x_full.shape)


def build_fused_program():
    nc = bass.Bass("TRN2", target_bir_lowering=False)
    gi = lambda nm, shp: nc.dram_tensor(nm, shp, F32, kind="ExternalInput").ap()
    xk = gi("xk", [D, S])
    xq = gi("xq", [D, TOK])
    wcq = gi("wcq", [128, 6 * DC * 128]); gq = gi("gq", [128, 6])
    wdq = gi("wdq", [128, 16 * 6 * 128]); wdqr = gi("wdqr", [128, 16 * 6 * 64])
    wckv = gi("wckv", [128, 4 * DC * 128]); gkv = gi("gkv", [128, 4])
    wdk = gi("wdk", [128, 16 * 4 * 128]); wdv = gi("wdv", [128, 4 * 16 * 128])
    wkr = gi("wkr", [128, DC * 64]); wo = gi("wo", [128, DC * DC * 128])
    rckv = gi("rckv", [64, S]); rskv = gi("rskv", [64, S])
    rcq = gi("rcq", [64, TOK]); rsq = gi("rsq", [64, TOK])
    ident = gi("ident", [128, 128]); ones = gi("ones", [128, 128])
    wupt = gi("wupt", [128, IC * DC * 128]); wgtt = gi("wgtt", [128, IC * DC * 128])
    wdnt = gi("wdnt", [128, IC * DC * 128]); had = gi("had", [128, 128])
    su_ug = gi("su_ug", [128, DC]); su_dp = gi("su_dp", [128, IC])
    ws_ug = gi("ws_ug", [128, 2 * IC]); ws_dp = gi("ws_dp", [128, DC])
    wq = gi("wq", [128, 16 * DC * 128]); bq = gi("bq", [128, 16])
    wk = gi("wk", [128, 16 * DC * 128]); bk = gi("bk", [128, 16])
    wv = gi("wv", [128, 16 * DC * 128]); bvr = gi("bvr", [1, D])
    wg = gi("wg", [128, 16 * DC * 128]); bg = gi("bg", [128, 16])
    wo1 = gi("wo1", [128, DC * DC * 128]); bo = gi("bo", [128, 16])
    csq = gi("csq", [128, TOK]); snq = gi("snq", [128, TOK])
    csk = gi("csk", [128, TOK]); snk = gi("snk", [128, TOK])
    gcol = gi("gcol", [128, 16 * 272]); arow = gi("arow", [128, 16 * 128])
    mask = gi("mask", [128, 16 * 2 * 128])
    w0t = gi("w0t", [128, IC * DC * 128]); w1t = gi("w1t", [128, IC * DC * 128])
    b0 = gi("b0", [128, IC]); b1 = gi("b1", [128, DC])
    sc = lambda nm, shp: nc.dram_tensor(nm, shp, F32, kind="Internal").ap()
    attn_scr = sc("attn_scr", [D, TOK])
    qcn_scr = sc("qcn_scr", [6 * 128, TOK])
    x1s = sc("x1s", [D, TOK])
    x2s = sc("x2s", [D, TOK])
    x3s = sc("x3s", [D, TOK])
    kin = sc("kin", [D, TOK]); vin = sc("vin", [TOK, D])
    kgo = sc("kgo", [2 * D, TOK]); vgo = sc("vgo", [2 * TOK, D])
    out = nc.dram_tensor("out", [D, TOK], F32, kind="ExternalOutput").ap()

    with contextlib.ExitStack() as stack0:
        block = stack0.enter_context(nc.Block())
        ctxs = [make_ctx(nc, block, stack0) for _ in range(4)]

        with contextlib.ExitStack() as s1:
            emit_l0attn(nc, block, ctxs[0], xk, xq, x1s, wcq, gq, wdq, wdqr,
                        wckv, gkv, wdk, wdv, wkr, wo, rckv, rskv, rcq, rsq,
                        ident, ones, attn_scr, qcn_scr, s1)
        emit_phase_barrier(ctxs[0])

        with contextlib.ExitStack() as s2:
            emit_l0ffn(nc, block, ctxs[1], x1s, x2s, wupt, wgtt, wdnt, had,
                       su_ug, su_dp, ws_ug, ws_dp, ones, s2)
        emit_phase_barrier(ctxs[1])

        with contextlib.ExitStack() as s3:
            emit_l1attn(nc, block, ctxs[2], x2s, x3s, wq, bq, wk, bk, wv, bvr,
                        wg, bg, wo1, bo, csq, snq, csk, snk, gcol, arow, mask,
                        ones, kin, vin, kgo, vgo, s3)
        emit_phase_barrier(ctxs[2])

        with contextlib.ExitStack() as s4:
            emit_l1ffn(nc, block, ctxs[3], x3s, out, w0t, w1t, b0, b1, ones, s4)

        @block.gpsimd
        def _(gpsimd):
            gpsimd.wait_ge(ctxs[3].qG.h, ctxs[3].qG.n)
    return nc


def kernel(**inputs):
    x = np.asarray(inputs["L0_attn_x"], np.float32)
    nc = build_fused_program()
    shared = {}
    shared.update(stage_l0attn_shared(
        inputs["L0_attn_Wcq"], inputs["L0_attn_g_q"], inputs["L0_attn_Wdq"],
        inputs["L0_attn_Wdqr"], inputs["L0_attn_Wckv"], inputs["L0_attn_g_kv"],
        inputs["L0_attn_Wdk"], inputs["L0_attn_Wdv"], inputs["L0_attn_Wkr"],
        inputs["L0_attn_Wo"]))
    shared.update(stage_l0ffn(
        inputs["L0_ffn_SU_ug"], inputs["L0_ffn_SU_dp"], inputs["L0_ffn_Wscale_ug"],
        inputs["L0_ffn_Wscale_dp"], inputs["L0_ffn_W_up"], inputs["L0_ffn_W_gate"],
        inputs["L0_ffn_W_down"]))
    l1a = stage_l1attn_shared(
        inputs["L1_attn_Wq"], inputs["L1_attn_bq"], inputs["L1_attn_Wk"],
        inputs["L1_attn_bk"], inputs["L1_attn_Wv"], inputs["L1_attn_bv"],
        inputs["L1_attn_Wg"], inputs["L1_attn_bg"], inputs["L1_attn_Wo"],
        inputs["L1_attn_bo"])
    l1a["wo1"] = l1a.pop("wo")
    shared.update(l1a)
    shared.update(stage_l1ffn(inputs["L1_ffn_W0"], inputs["L1_ffn_bias0"],
                              inputs["L1_ffn_W1"], inputs["L1_ffn_bias1"]))
    in_maps = []
    for c in range(NCORES):
        b = c // 2
        idx = own_token_index(c)
        rcq, rsq = rope_tables(64, idx, scale=1.0 / np.sqrt(192.0))
        in_maps.append({
            "xk": np.ascontiguousarray(x[b].T),
            "xq": np.ascontiguousarray(x[b][idx].T),
            "rcq": rcq, "rsq": rsq,
            **stage_l1attn_core(c), **shared})
    res = run_bass_kernel_spmd(nc, in_maps, core_ids=list(range(NCORES)))
    outa = np.zeros((B, S, D), np.float32)
    for c in range(NCORES):
        b = c // 2
        idx = own_token_index(c)
        outa[b, idx] = res.results[c]["out"].T
    return outa


def kernel_unfused(**inputs):
    x = np.asarray(inputs["L0_attn_x"], np.float32)
    x1 = run_l0attn(x, inputs["L0_attn_Wcq"], inputs["L0_attn_g_q"],
                    inputs["L0_attn_Wdq"], inputs["L0_attn_Wdqr"],
                    inputs["L0_attn_Wckv"], inputs["L0_attn_g_kv"],
                    inputs["L0_attn_Wdk"], inputs["L0_attn_Wdv"],
                    inputs["L0_attn_Wkr"], inputs["L0_attn_Wo"])
    x2 = run_l0ffn(x1, inputs["L0_ffn_SU_ug"], inputs["L0_ffn_SU_dp"],
                   inputs["L0_ffn_Wscale_ug"], inputs["L0_ffn_Wscale_dp"],
                   inputs["L0_ffn_W_up"], inputs["L0_ffn_W_gate"],
                   inputs["L0_ffn_W_down"])
    x3 = run_l1attn(x2, inputs["L1_attn_Wq"], inputs["L1_attn_bq"],
                    inputs["L1_attn_Wk"], inputs["L1_attn_bk"],
                    inputs["L1_attn_Wv"], inputs["L1_attn_bv"],
                    inputs["L1_attn_Wg"], inputs["L1_attn_bg"],
                    inputs["L1_attn_Wo"], inputs["L1_attn_bo"])
    out = run_l1ffn(x3, inputs["L1_ffn_W0"], inputs["L1_ffn_bias0"],
                    inputs["L1_ffn_W1"], inputs["L1_ffn_bias1"])
    return out


# version 04
# speedup vs baseline: 1.0739x; 1.0263x over previous
import contextlib
import numpy as np
import concourse.bass as bass
import concourse.mybir as mybir
from concourse.bass_utils import run_bass_kernel_spmd

F32 = mybir.dt.float32
F16 = mybir.dt.float16
BF16 = mybir.dt.bfloat16
AF = mybir.ActivationFunctionType
OP = mybir.AluOpType

NCORES = 8
B, S, D = 4, 4096, 2048
TOK = 2048
NT = TOK // 128
DC = D // 128
INTER = 8192
IC = INTER // 128
N_HEADS = 16
LN_EPS = 1e-5
RMS_EPS = 1e-6
GN_EPS = 1e-5


class Sem:
    __slots__ = ("h", "n")
    def __init__(self, h):
        self.h = h
        self.n = 0

    def inc(self, instr, k=1):
        instr.then_inc(self.h, k)
        self.n += k
        return instr


class Ctx:
    def __init__(self, nc, block, sems):
        self.nc = nc
        self.block = block
        for k, v in sems.items():
            setattr(self, k, v)


_CTX_N = [0]
def make_ctx(nc, block, stack):
    names = ["qS", "qG", "qA", "sPE", "sDVE", "sACT", "sPOOL"]
    pfx = f"c{_CTX_N[0]}_"
    _CTX_N[0] += 1
    sems = {nm: Sem(stack.enter_context(nc.semaphore(pfx + nm))) for nm in names}
    return Ctx(nc, block, sems)


def emit_ln(ctx, T, s_tiles, y_tiles, sq_a, sq_b, ps, pb, ones2, rows,
            eps=LN_EPS):
    block = ctx.block
    r_mu, r_m2, r_A, r_B = rows
    ps_sum = ps[0:1, pb * 512: pb * 512 + T]
    ps_ssq = ps[0:1, (pb + 1) * 512: (pb + 1) * 512 + T]
    ps_A = ps[:, (pb + 2) * 512: (pb + 2) * 512 + T]
    ps_B = ps[:, (pb + 3) * 512: (pb + 3) * 512 + T]
    ones_col = ones2[:, 0:1]
    ones_row = ones2[0:1, :]

    @block.tensor
    def _(tensor):
        tensor.wait_ge(ctx.sDVE.h, ctx.sDVE.n)
        tensor.wait_ge(ctx.sACT.h, ctx.sACT.n)
        for c in range(DC):
            ins = tensor.matmul(ps_sum, ones_col, s_tiles[c],
                                start=(c == 0), stop=(c == DC - 1))
        ctx.sPE.inc(ins)

    for c in range(DC):
        sq = sq_a if c % 2 == 0 else sq_b
        @block.vector
        def _(vector, c=c, sq=sq):
            if c >= 2:
                vector.wait_ge(ctx.sPE.h, ctx.sPE.n)
            ctx.sDVE.inc(vector.tensor_tensor(sq, s_tiles[c], s_tiles[c], OP.mult))

        @block.tensor
        def _(tensor, c=c, sq=sq):
            tensor.wait_ge(ctx.sDVE.h, ctx.sDVE.n)
            ctx.sPE.inc(tensor.matmul(ps_ssq, ones_col, sq,
                                      start=(c == 0), stop=(c == DC - 1)))

    @block.vector
    def _(vector):
        vector.wait_ge(ctx.sPE.h, ctx.sPE.n)
        ctx.sDVE.inc(vector.tensor_scalar(r_mu, ps_sum, 1.0 / D, None, OP.mult))
        ctx.sDVE.inc(vector.tensor_scalar(r_m2, ps_ssq, 1.0 / D, float(eps),
                                          OP.mult, OP.add))
        vector.wait_ge(ctx.sDVE.h, ctx.sDVE.n)
        ctx.sDVE.inc(vector.tensor_tensor(r_A, r_mu, r_mu, OP.mult))
        vector.wait_ge(ctx.sDVE.h, ctx.sDVE.n)
        ctx.sDVE.inc(vector.tensor_tensor(r_m2, r_m2, r_A, OP.subtract))

    @block.scalar
    def _(scalar):
        scalar.wait_ge(ctx.sDVE.h, ctx.sDVE.n)
        ctx.sACT.inc(scalar.activation(r_A, r_m2, AF.Ln, bias=0.0, scale=1.0))
        scalar.wait_ge(ctx.sACT.h, ctx.sACT.n)
        ctx.sACT.inc(scalar.activation(r_A, r_A, AF.Exp, bias=0.0, scale=-0.5))

    @block.vector
    def _(vector):
        vector.wait_ge(ctx.sACT.h, ctx.sACT.n)
        ctx.sDVE.inc(vector.scalar_tensor_tensor(r_B, r_mu, -1.0, r_A,
                                                 OP.mult, OP.mult))

    @block.tensor
    def _(tensor):
        tensor.wait_ge(ctx.sDVE.h, ctx.sDVE.n)
        tensor.matmul(ps_A, ones_row, r_A, start=True, stop=True)
        ctx.sPE.inc(tensor.matmul(ps_B, ones_row, r_B, start=True, stop=True))

    for c in range(DC):
        @block.vector
        def _(vector, c=c):
            if c == 0:
                vector.wait_ge(ctx.sPE.h, ctx.sPE.n)
            vector.tensor_tensor(y_tiles[c], s_tiles[c], ps_A, OP.mult)
            ctx.sDVE.inc(vector.tensor_tensor(y_tiles[c], y_tiles[c], ps_B, OP.add))


def emit_l1ffn(nc, block, ctx, x_dram, out_dram, w0t_dram, w1t_dram,
               b0_dram, b1_dram, ones_dram, stack):
    T = 512
    NCH = TOK // T
    xs = stack.enter_context(nc.sbuf_tensor("ff_x", [128, DC * T], F32))
    xr = stack.enter_context(nc.sbuf_tensor("ff_xr", [128, DC * T], F32))
    x16 = stack.enter_context(nc.sbuf_tensor("ff_x16", [128, T], F16))
    d16 = stack.enter_context(nc.sbuf_tensor("ff_d16", [128, IC * T], F16))
    d32 = stack.enter_context(nc.sbuf_tensor("ff_d32", [128, 2 * T], F32))
    e32 = stack.enter_context(nc.sbuf_tensor("ff_e32", [128, T], F32))
    wup = stack.enter_context(nc.sbuf_tensor("ff_w0", [128, 2 * DC * 128], F32))
    wdn = stack.enter_context(nc.sbuf_tensor("ff_w1", [128, 2 * DC * 128], F32))
    e16 = stack.enter_context(nc.sbuf_tensor("ff_e16", [128, DC * T], F16))
    b0s = stack.enter_context(nc.sbuf_tensor("ff_b0", [128, IC], F32))
    b1s = stack.enter_context(nc.sbuf_tensor("ff_b1", [128, DC], F32))
    ones2 = stack.enter_context(nc.sbuf_tensor("ff_ones", [128, 128], F32))
    sq_a = stack.enter_context(nc.sbuf_tensor("ff_sqa", [128, T], F32))
    sq_b = stack.enter_context(nc.sbuf_tensor("ff_sqb", [128, T], F32))
    rows_t = stack.enter_context(nc.sbuf_tensor("ff_rows", [1, 4 * T], F32))
    ps = stack.enter_context(nc.psum_tensor("ff_ps", [128, 8 * 512], F32))

    rows = [rows_t[0:1, i * T:(i + 1) * T] for i in range(4)]

    @block.sync
    def _(sync):
        ctx.qS.inc(sync.dma_start(out=b0s[:], in_=b0_dram[:, :]), 16)
        ctx.qS.inc(sync.dma_start(out=b1s[:], in_=b1_dram[:, :]), 16)
        ctx.qS.inc(sync.dma_start(out=ones2[:], in_=ones_dram[:, :]), 16)

    for ch in range(NCH):
        t0 = ch * T
        dve_resid_done = ctx.sDVE.n
        @block.sync
        def _(sync):
            sync.wait_ge(ctx.sDVE.h, dve_resid_done)
            for c in range(DC):
                ctx.qS.inc(sync.dma_start(
                    out=xs[:, c * T:(c + 1) * T],
                    in_=x_dram[c * 128:(c + 1) * 128, t0:t0 + T]), 16)

        qg_store_done = ctx.qG.n
        for c in range(DC):
            @block.vector
            def _(vector, c=c):
                if c == 0:
                    vector.wait_ge(ctx.qS.h, ctx.qS.n)
                    vector.wait_ge(ctx.qG.h, qg_store_done)
                vector.tensor_copy(x16[:], xs[:, c * T:(c + 1) * T])
                ctx.sDVE.inc(vector.tensor_copy(xr[:, c * T:(c + 1) * T], x16[:]))

        pe_base = ctx.sPE.n
        act_base = ctx.sACT.n
        dve_ready = ctx.sDVE.n
        for o in range(IC):
            half = o % 2
            wdst = wup[:, half * DC * 128:(half + 1) * DC * 128]
            pslice = ps[:, (o % 4) * 512:(o % 4) * 512 + T]

            @block.sync
            def _(sync, o=o, wdst=wdst):
                if o >= 2:
                    sync.wait_ge(ctx.sPE.h, pe_base + o - 1)
                else:
                    sync.wait_ge(ctx.sPE.h, pe_base)
                ctx.qS.inc(sync.dma_start(
                    out=wdst,
                    in_=w0t_dram[:, o * DC * 128:(o + 1) * DC * 128]), 16)

            @block.tensor
            def _(tensor, o=o, wdst=wdst, pslice=pslice):
                tensor.wait_ge(ctx.qS.h, ctx.qS.n)
                if o == 0:
                    tensor.wait_ge(ctx.sDVE.h, dve_ready)
                if o >= 4:
                    tensor.wait_ge(ctx.sACT.h, act_base + o - 3)
                for c in range(DC):
                    ins = tensor.matmul(pslice, wdst[:, c * 128:(c + 1) * 128],
                                        xr[:, c * T:(c + 1) * T],
                                        start=(c == 0), stop=(c == DC - 1))
                ctx.sPE.inc(ins)

            @block.scalar
            def _(scalar, o=o, pslice=pslice):
                scalar.wait_ge(ctx.sPE.h, pe_base + o + 1)
                ctx.sACT.inc(scalar.activation(
                    d16[:, o * T:(o + 1) * T], pslice, AF.Gelu,
                    bias=b0s[:, o:o + 1], scale=1.0))

        for ph in range(2):
            pe2_base = ctx.sPE.n
            act2_base = ctx.sACT.n
            for i in range(IC):
                half = i % 2
                dslice = d32[:, half * T:(half + 1) * T]
                whalf = wdn[:, half * 8 * 128:(half + 1) * 8 * 128]

                @block.sync
                def _(sync, i=i, whalf=whalf, ph=ph):
                    if i >= 2:
                        sync.wait_ge(ctx.sPE.h, pe2_base + i - 1)
                    else:
                        sync.wait_ge(ctx.sPE.h, pe2_base)
                    ctx.qS.inc(sync.dma_start(
                        out=whalf,
                        in_=w1t_dram[:, (i * DC + ph * 8) * 128:
                                        (i * DC + ph * 8 + 8) * 128]), 16)

                @block.vector
                def _(vector, i=i, dslice=dslice):
                    if i == 0:
                        vector.wait_ge(ctx.sACT.h, act2_base)
                    if i >= 2:
                        vector.wait_ge(ctx.sPE.h, pe2_base + i - 1)
                    ctx.sDVE.inc(vector.tensor_copy(dslice, d16[:, i * T:(i + 1) * T]))

                @block.tensor
                def _(tensor, i=i, dslice=dslice, whalf=whalf):
                    tensor.wait_ge(ctx.qS.h, ctx.qS.n)
                    tensor.wait_ge(ctx.sDVE.h, ctx.sDVE.n)
                    for e in range(8):
                        ins = tensor.matmul(ps[:, e * 512:e * 512 + T],
                                            whalf[:, e * 128:(e + 1) * 128],
                                            dslice,
                                            start=(i == 0), stop=(i == IC - 1))
                    ctx.sPE.inc(ins)

            pe3 = ctx.sPE.n
            for e in range(8):
                eg = ph * 8 + e
                @block.scalar
                def _(scalar, e=e, eg=eg):
                    if e == 0:
                        scalar.wait_ge(ctx.sPE.h, pe3)
                    ctx.sACT.inc(scalar.activation(
                        e16[:, eg * T:(eg + 1) * T], ps[:, e * 512:e * 512 + T],
                        AF.Identity, bias=b1s[:, eg:eg + 1], scale=1.0))

        act3 = ctx.sACT.n
        for c in range(DC):
            @block.vector
            def _(vector, c=c):
                if c == 0:
                    vector.wait_ge(ctx.sACT.h, act3)
                vector.tensor_copy(e32[:], e16[:, c * T:(c + 1) * T])
                ctx.sDVE.inc(vector.tensor_tensor(
                    xr[:, c * T:(c + 1) * T], xs[:, c * T:(c + 1) * T],
                    e32[:], OP.add))

        s_tiles = [xr[:, c * T:(c + 1) * T] for c in range(DC)]
        emit_ln(ctx, T, s_tiles, s_tiles, sq_a[:], sq_b[:], ps, 0, ones2, rows)

        dve_done = ctx.sDVE.n
        @block.gpsimd
        def _(gpsimd):
            gpsimd.wait_ge(ctx.sDVE.h, dve_done)
            for c in range(DC):
                ctx.qG.inc(gpsimd.dma_start(
                    out=out_dram[c * 128:(c + 1) * 128, t0:t0 + T],
                    in_=xr[:, c * T:(c + 1) * T]), 16)


def emit_phase_barrier(ctx):
    block = ctx.block
    sems = [ctx.qS, ctx.qG, ctx.sPE, ctx.sDVE, ctx.sACT, ctx.sPOOL]
    counts = [(s, s.n) for s in sems]
    for eng in ("sync", "tensor", "vector", "scalar", "gpsimd"):
        @getattr(block, eng)
        def _(e):
            for s, n in counts:
                e.wait_ge(s.h, n)


def emit_rms(ctx, T, tiles, out_tiles, g_cols, nch, ps, pb, ones2, row,
             sq_a, sq_b, eps=RMS_EPS):
    block = ctx.block
    ps_ssq = ps[0:1, pb * 512: pb * 512 + T]
    ps_bc = ps[:, (pb + 1) * 512: (pb + 1) * 512 + T]
    ones_col = ones2[:, 0:1]
    ones_row = ones2[0:1, :]
    n = len(tiles)

    for c in range(n):
        @block.vector
        def _(vector, c=c):
            if c >= 2:
                vector.wait_ge(ctx.sPE.h, ctx.sPE.n)
            sq = sq_a if c % 2 == 0 else sq_b
            ctx.sDVE.inc(vector.tensor_tensor(sq, tiles[c], tiles[c], OP.mult))

        @block.tensor
        def _(tensor, c=c):
            tensor.wait_ge(ctx.sDVE.h, ctx.sDVE.n)
            sq = sq_a if c % 2 == 0 else sq_b
            ctx.sPE.inc(tensor.matmul(ps_ssq, ones_col, sq,
                                      start=(c == 0), stop=(c == n - 1)))

    @block.vector
    def _(vector):
        vector.wait_ge(ctx.sPE.h, ctx.sPE.n)
        ctx.sDVE.inc(vector.tensor_scalar(row, ps_ssq, 1.0 / (nch * 128.0),
                                          float(eps), OP.mult, OP.add))

    @block.scalar
    def _(scalar):
        scalar.wait_ge(ctx.sDVE.h, ctx.sDVE.n)
        ctx.sACT.inc(scalar.activation(row, row, AF.Ln, bias=0.0, scale=1.0))
        scalar.wait_ge(ctx.sACT.h, ctx.sACT.n)
        ctx.sACT.inc(scalar.activation(row, row, AF.Exp, bias=0.0, scale=-0.5))

    @block.tensor
    def _(tensor):
        tensor.wait_ge(ctx.sACT.h, ctx.sACT.n)
        ctx.sPE.inc(tensor.matmul(ps_bc, ones_row, row, start=True, stop=True))

    for c in range(n):
        @block.vector
        def _(vector, c=c):
            if c == 0:
                vector.wait_ge(ctx.sPE.h, ctx.sPE.n)
            ctx.sDVE.inc(vector.scalar_tensor_tensor(
                out_tiles[c], tiles[c], g_cols[c], ps_bc, OP.mult, OP.mult))


def emit_butterflies(ctx, tiles, tmp_d, tmp_p):
    block = ctx.block
    n = len(tiles)
    stages = int(np.log2(n))
    for s in range(stages):
        h = 1 << s
        pairs = [(i, i + h) for i in range(n) if not (i & h)]
        half = len(pairs) // 2
        dve_pairs, pool_pairs = pairs[:half], pairs[half:]

        @block.vector
        def _(vector, dve_pairs=dve_pairs, s=s):
            vector.wait_ge(ctx.sPOOL.h, ctx.sPOOL.n)
            for a, b_ in dve_pairs:
                vector.tensor_tensor(tmp_d, tiles[a], tiles[b_], OP.subtract)
                vector.tensor_tensor(tiles[a], tiles[a], tiles[b_], OP.add)
                ins = vector.tensor_copy(tiles[b_], tmp_d)
            ctx.sDVE.inc(ins)

        @block.gpsimd
        def _(gpsimd, pool_pairs=pool_pairs, s=s):
            gpsimd.wait_ge(ctx.sDVE.h, ctx.sDVE.n - 1)
            for a, b_ in pool_pairs:
                gpsimd.tensor_tensor(tmp_p, tiles[a], tiles[b_], OP.subtract)
                gpsimd.tensor_tensor(tiles[a], tiles[a], tiles[b_], OP.add)
                ins = gpsimd.tensor_copy(tiles[b_], tmp_p)
            ctx.sPOOL.inc(ins)

    @block.vector
    def _(vector):
        vector.wait_ge(ctx.sPOOL.h, ctx.sPOOL.n)
        ins = vector.tensor_copy(tmp_d[:, 0:1], tmp_d[:, 0:1])
        ctx.sDVE.inc(ins)


def emit_l0ffn(nc, block, ctx, x_dram, out_dram, wupt_dram, wgtt_dram,
               wdnt_dram, had_dram, su_ug_dram, su_dp_dram, ws_ug_dram,
               ws_dp_dram, ones_dram, stack):
    T = 256
    NCH = TOK // T
    xs = stack.enter_context(nc.sbuf_tensor("f0_x", [128, DC * T], F32))
    z = stack.enter_context(nc.sbuf_tensor("f0_z", [128, DC * T], F32))
    hbuf = stack.enter_context(nc.sbuf_tensor("f0_h", [128, IC * T], F32))
    tmp_d = stack.enter_context(nc.sbuf_tensor("f0_td", [128, T], F32))
    tmp_p = stack.enter_context(nc.sbuf_tensor("f0_tp", [128, T], F32))
    tmpg = stack.enter_context(nc.sbuf_tensor("f0_tg", [128, T], F32))
    silu_sb = stack.enter_context(nc.sbuf_tensor("f0_sl", [128, T], F32))
    hadt = stack.enter_context(nc.sbuf_tensor("f0_ht", [128, DC * 128], F32))
    hraw = stack.enter_context(nc.sbuf_tensor("f0_hr", [128, 128], F32))
    wup = stack.enter_context(nc.sbuf_tensor("f0_wu", [128, 2 * DC * 128], F32))
    wgt = stack.enter_context(nc.sbuf_tensor("f0_wg", [128, 2 * DC * 128], F32))
    wdn = stack.enter_context(nc.sbuf_tensor("f0_wd", [128, 2 * 8 * 128], F32))
    su_ug = stack.enter_context(nc.sbuf_tensor("f0_su", [128, DC], F32))
    vsu = stack.enter_context(nc.sbuf_tensor("f0_vsu", [128, IC], F32))
    vsg = stack.enter_context(nc.sbuf_tensor("f0_vsg", [128, IC], F32))
    vsd = stack.enter_context(nc.sbuf_tensor("f0_vsd", [128, DC], F32))
    vtmp = stack.enter_context(nc.sbuf_tensor("f0_vt", [128, 2 * IC], F32))
    ones2 = stack.enter_context(nc.sbuf_tensor("f0_on", [128, 128], F32))
    sq_a = stack.enter_context(nc.sbuf_tensor("f0_qa", [128, T], F32))
    sq_b = stack.enter_context(nc.sbuf_tensor("f0_qb", [128, T], F32))
    rows_t = stack.enter_context(nc.sbuf_tensor("f0_rw", [1, 4 * T], F32))
    ps = stack.enter_context(nc.psum_tensor("f0_ps", [128, 8 * 512], F32))

    rows = [rows_t[0:1, i * T:(i + 1) * T] for i in range(4)]
    htile = [hbuf[:, k * T:(k + 1) * T] for k in range(IC)]
    ztile = [z[:, k * T:(k + 1) * T] for k in range(DC)]

    @block.sync
    def _(sync):
        for dst, src in [(hraw[:], had_dram[:, :]), (su_ug[:], su_ug_dram[:, :]),
                         (vtmp[:], ws_ug_dram[:, :]),
                         (vsd[:], ws_dp_dram[:, :]), (vsu[:], su_dp_dram[:, :]),
                         (ones2[:], ones_dram[:, :])]:
            ctx.qS.inc(sync.dma_start(out=dst, in_=src), 16)

    @block.vector
    def _(vector):
        vector.wait_ge(ctx.qS.h, ctx.qS.n)
        for k in range(DC):
            vector.tensor_scalar(hadt[:, k * 128:(k + 1) * 128], hraw[:],
                                 su_ug[:, k:k + 1], None, OP.mult)
        vector.tensor_scalar(vsg[:], vtmp[:, IC:2 * IC], 1.0 / np.sqrt(2048.0),
                             None, OP.mult)
        vector.scalar_tensor_tensor(vsu[:], vtmp[:, 0:IC], 1.0 / np.sqrt(2048.0),
                                    vsu[:], OP.mult, OP.mult)
        ctx.sDVE.inc(vector.tensor_scalar(vsd[:], vsd[:], 1.0 / np.sqrt(8192.0),
                                          None, OP.mult))

    for ch in range(NCH):
        t0 = ch * T
        dve_prev = ctx.sDVE.n
        @block.sync
        def _(sync):
            sync.wait_ge(ctx.sDVE.h, dve_prev)
            for c in range(DC):
                ctx.qS.inc(sync.dma_start(
                    out=xs[:, c * T:(c + 1) * T],
                    in_=x_dram[c * 128:(c + 1) * 128, t0:t0 + T]), 16)

        pe_b = ctx.sPE.n
        dve_b = ctx.sDVE.n
        for k in range(DC):
            @block.tensor
            def _(tensor, k=k):
                if k == 0:
                    tensor.wait_ge(ctx.qS.h, ctx.qS.n)
                    tensor.wait_ge(ctx.sDVE.h, dve_b)
                if k >= 4:
                    tensor.wait_ge(ctx.sDVE.h, dve_b + k - 3)
                ctx.sPE.inc(tensor.matmul(ps[:, (k % 4) * 512:(k % 4) * 512 + T],
                                          hadt[:, k * 128:(k + 1) * 128],
                                          xs[:, k * T:(k + 1) * T],
                                          start=True, stop=True))

            @block.vector
            def _(vector, k=k):
                if k == 0:
                    vector.wait_ge(ctx.qG.h, ctx.qG.n)
                vector.wait_ge(ctx.sPE.h, pe_b + k + 1)
                ctx.sDVE.inc(vector.tensor_copy(
                    ztile[k], ps[:, (k % 4) * 512:(k % 4) * 512 + T]))

        emit_butterflies(ctx, ztile, tmp_d[:], tmp_p[:])

        pe_b = ctx.sPE.n
        dve_b = ctx.sDVE.n
        act_b = ctx.sACT.n
        for o in range(IC):
            half = o % 2
            wud = wup[:, half * DC * 128:(half + 1) * DC * 128]
            wgd = wgt[:, half * DC * 128:(half + 1) * DC * 128]
            bu = ps[:, (o % 4) * 512:(o % 4) * 512 + T]
            bg = ps[:, (4 + o % 4) * 512:(4 + o % 4) * 512 + T]

            @block.sync
            def _(sync, o=o, wud=wud, wgd=wgd):
                sync.wait_ge(ctx.sPE.h, pe_b + max(0, (o - 1) * 2))
                ctx.qS.inc(sync.dma_start(
                    out=wud, in_=wupt_dram[:, o * DC * 128:(o + 1) * DC * 128]), 16)
                ctx.qS.inc(sync.dma_start(
                    out=wgd, in_=wgtt_dram[:, o * DC * 128:(o + 1) * DC * 128]), 16)

            @block.tensor
            def _(tensor, o=o, wud=wud, wgd=wgd, bu=bu, bg=bg):
                tensor.wait_ge(ctx.qS.h, ctx.qS.n)
                if o == 0:
                    tensor.wait_ge(ctx.sDVE.h, dve_b)
                    tensor.wait_ge(ctx.sPOOL.h, ctx.sPOOL.n)
                if o >= 4:
                    tensor.wait_ge(ctx.sDVE.h, dve_b + 2 * (o - 3))
                for c in range(DC):
                    ins = tensor.matmul(bu, wud[:, c * 128:(c + 1) * 128],
                                        ztile[c], start=(c == 0), stop=(c == DC - 1))
                ctx.sPE.inc(ins)
                for c in range(DC):
                    ins = tensor.matmul(bg, wgd[:, c * 128:(c + 1) * 128],
                                        ztile[c], start=(c == 0), stop=(c == DC - 1))
                ctx.sPE.inc(ins)

            @block.vector
            def _(vector, o=o, bg=bg):
                if o == 0:
                    vector.wait_ge(ctx.sDVE.h, dve_b)
                vector.wait_ge(ctx.sPE.h, pe_b + 2 * (o + 1))
                vector.wait_ge(ctx.sACT.h, ctx.sACT.n)
                ctx.sDVE.inc(vector.tensor_scalar(tmpg[:], bg, vsg[:, o:o + 1],
                                                  None, OP.mult))

            @block.scalar
            def _(scalar, o=o):
                scalar.wait_ge(ctx.sDVE.h, ctx.sDVE.n)
                ctx.sACT.inc(scalar.activation(silu_sb[:], tmpg[:], AF.Silu,
                                               bias=0.0, scale=1.0))

            @block.vector
            def _(vector, o=o, bu=bu):
                vector.wait_ge(ctx.sACT.h, ctx.sACT.n)
                ctx.sDVE.inc(vector.scalar_tensor_tensor(
                    htile[o], bu, vsu[:, o:o + 1], silu_sb[:], OP.mult, OP.mult))

        pe_b = ctx.sPE.n
        dve_b = ctx.sDVE.n
        for k in range(IC):
            @block.tensor
            def _(tensor, k=k):
                if k == 0:
                    tensor.wait_ge(ctx.sDVE.h, dve_b)
                if k >= 4:
                    tensor.wait_ge(ctx.sDVE.h, dve_b + k - 3)
                ctx.sPE.inc(tensor.matmul(ps[:, (k % 4) * 512:(k % 4) * 512 + T],
                                          hraw[:], htile[k],
                                          start=True, stop=True))

            @block.vector
            def _(vector, k=k):
                vector.wait_ge(ctx.sPE.h, pe_b + k + 1)
                ctx.sDVE.inc(vector.tensor_copy(
                    htile[k], ps[:, (k % 4) * 512:(k % 4) * 512 + T]))

        emit_butterflies(ctx, htile, tmp_d[:], tmp_p[:])

        for phd in range(2):
            pe2 = ctx.sPE.n
            for i in range(IC):
                half = i % 2
                whalf = wdn[:, half * 8 * 128:(half + 1) * 8 * 128]

                @block.sync
                def _(sync, i=i, whalf=whalf, phd=phd):
                    if i >= 2:
                        sync.wait_ge(ctx.sPE.h, pe2 + i - 1)
                    else:
                        sync.wait_ge(ctx.sPE.h, pe2)
                    ctx.qS.inc(sync.dma_start(
                        out=whalf,
                        in_=wdnt_dram[:, (i * DC + phd * 8) * 128:
                                         (i * DC + phd * 8 + 8) * 128]), 16)

                @block.tensor
                def _(tensor, i=i, whalf=whalf, phd=phd):
                    tensor.wait_ge(ctx.qS.h, ctx.qS.n)
                    if i == 0:
                        tensor.wait_ge(ctx.sDVE.h, ctx.sDVE.n)
                        tensor.wait_ge(ctx.sPOOL.h, ctx.sPOOL.n)
                        if phd == 1:
                            tensor.wait_ge(ctx.sACT.h, ctx.sACT.n)
                    for e in range(8):
                        ins = tensor.matmul(ps[:, e * 512:e * 512 + T],
                                            whalf[:, e * 128:(e + 1) * 128],
                                            htile[i],
                                            start=(i == 0), stop=(i == IC - 1))
                    ctx.sPE.inc(ins)

            pe3 = ctx.sPE.n
            for e in range(8):
                eg = phd * 8 + e
                @block.vector
                def _(vector, e=e, eg=eg):
                    if e == 0:
                        vector.wait_ge(ctx.sPE.h, pe3)
                    ctx.sDVE.inc(vector.scalar_tensor_tensor(
                        ztile[eg], ps[:, e * 512:e * 512 + T], vsd[:, eg:eg + 1],
                        xs[:, eg * T:(eg + 1) * T], OP.mult, OP.add))

        emit_ln(ctx, T, ztile, ztile, sq_a[:], sq_b[:], ps, 0, ones2, rows)

        dve_done = ctx.sDVE.n
        @block.gpsimd
        def _(gpsimd):
            gpsimd.wait_ge(ctx.sDVE.h, dve_done)
            for c in range(DC):
                ctx.qG.inc(gpsimd.dma_start(
                    out=out_dram[c * 128:(c + 1) * 128, t0:t0 + T],
                    in_=z[:, c * T:(c + 1) * T]), 16)


KB8 = S // 512
KB128 = S // 128


def emit_l0attn(nc, block, ctx, xk_dram, xq_dram, out_dram, wcq_dram, gq_dram,
                wdq_dram, wdqr_dram, wckv_dram, gkv_dram, wdk_dram, wdv_dram,
                wkr_dram, wo_dram, rckv_dram, rskv_dram, rcq_dram, rsq_dram,
                ident_dram, ones_dram, attn_scr, qcn_scr, stack):
    T = 512
    kvc = stack.enter_context(nc.sbuf_tensor("a0_kvc", [128, 4 * S], F32))
    krs = stack.enter_context(nc.sbuf_tensor("a0_kr", [64, S], F32))
    ones2 = stack.enter_context(nc.sbuf_tensor("a0_on", [128, 128], F32))
    ident = stack.enter_context(nc.sbuf_tensor("a0_id", [128, 128], F32))
    wkr = stack.enter_context(nc.sbuf_tensor("a0_wkr", [128, DC * 64], F32))
    gq = stack.enter_context(nc.sbuf_tensor("a0_gq", [128, 6], F32))
    gkv = stack.enter_context(nc.sbuf_tensor("a0_gkv", [128, 4], F32))
    rows_t = stack.enter_context(nc.sbuf_tensor("a0_rw", [1, 4 * T], F32))
    ps = stack.enter_context(nc.psum_tensor("a0_ps", [128, 8 * 512], F32))

    rows = [rows_t[0:1, i * T:(i + 1) * T] for i in range(4)]
    kvct = [kvc[:, c * S:(c + 1) * S] for c in range(4)]

    @block.sync
    def _(sync):
        for dst, src in [(ones2[:], ones_dram[:, :]), (ident[:], ident_dram[:, :]),
                         (wkr[:], wkr_dram[:, :]), (gq[:], gq_dram[:, :]),
                         (gkv[:], gkv_dram[:, :])]:
            ctx.qS.inc(sync.dma_start(out=dst, in_=src), 16)

    with contextlib.ExitStack() as p1:
        xs = p1.enter_context(nc.sbuf_tensor("a1_x", [128, DC * T], F32))
        wbuf = p1.enter_context(nc.sbuf_tensor("a1_w", [128, 2 * DC * 128], F32))
        rck = p1.enter_context(nc.sbuf_tensor("a1_rc", [64, S], F32))
        rsk = p1.enter_context(nc.sbuf_tensor("a1_rs", [64, S], F32))
        sq_a = p1.enter_context(nc.sbuf_tensor("a1_qa", [128, T], F32))
        sq_b = p1.enter_context(nc.sbuf_tensor("a1_qb", [128, T], F32))
        rtmp = p1.enter_context(nc.sbuf_tensor("a1_rt", [64, T], F32))

        @block.sync
        def _(sync):
            ctx.qS.inc(sync.dma_start(out=rck[:], in_=rckv_dram[:, :]), 16)
            ctx.qS.inc(sync.dma_start(out=rsk[:], in_=rskv_dram[:, :]), 16)

        for ch in range(KB8):
            t0 = ch * T
            dve_prev = ctx.sDVE.n
            @block.sync
            def _(sync):
                sync.wait_ge(ctx.sDVE.h, dve_prev)
                for c in range(DC):
                    ctx.qS.inc(sync.dma_start(
                        out=xs[:, c * T:(c + 1) * T],
                        in_=xk_dram[c * 128:(c + 1) * 128, t0:t0 + T]), 16)

            pe_b = ctx.sPE.n
            dve_b = ctx.sDVE.n
            for o in range(4):
                half = o % 2
                wdst = wbuf[:, half * DC * 128:(half + 1) * DC * 128]
                @block.sync
                def _(sync, o=o, wdst=wdst):
                    sync.wait_ge(ctx.sPE.h, pe_b + max(0, o - 1))
                    ctx.qS.inc(sync.dma_start(
                        out=wdst, in_=wckv_dram[:, o * DC * 128:(o + 1) * DC * 128]), 16)

                @block.tensor
                def _(tensor, o=o, wdst=wdst):
                    tensor.wait_ge(ctx.qS.h, ctx.qS.n)
                    if o == 0:
                        tensor.wait_ge(ctx.sDVE.h, dve_b)
                    for c in range(DC):
                        ins = tensor.matmul(ps[:, o * 512:o * 512 + T],
                                            wdst[:, c * 128:(c + 1) * 128],
                                            xs[:, c * T:(c + 1) * T],
                                            start=(c == 0), stop=(c == DC - 1))
                    ctx.sPE.inc(ins)

                @block.vector
                def _(vector, o=o):
                    vector.wait_ge(ctx.sPE.h, pe_b + o + 1)
                    ctx.sDVE.inc(vector.tensor_copy(
                        kvct[o][:, t0:t0 + T], ps[:, o * 512:o * 512 + T]))

            @block.tensor
            def _(tensor):
                for c in range(DC):
                    ins = tensor.matmul(ps[0:64, 4 * 512:4 * 512 + T],
                                        wkr[:, c * 64:(c + 1) * 64],
                                        xs[:, c * T:(c + 1) * T],
                                        start=(c == 0), stop=(c == DC - 1))
                ctx.sPE.inc(ins)

            @block.vector
            def _(vector, t0=t0):
                vector.wait_ge(ctx.sPE.h, ctx.sPE.n)
                pk = ps[0:64, 4 * 512:4 * 512 + T]
                ctx.sDVE.inc(vector.tensor_tensor(rtmp[0:32, :], pk[0:32, :],
                                                  rck[0:32, t0:t0 + T], OP.mult))
                ctx.sDVE.inc(vector.tensor_tensor(rtmp[32:64, :], pk[32:64, :],
                                                  rck[32:64, t0:t0 + T], OP.mult))
                vector.wait_ge(ctx.sDVE.h, ctx.sDVE.n)
                ctx.sDVE.inc(vector.tensor_tensor(krs[32:64, t0:t0 + T], pk[0:32, :],
                                                  rsk[32:64, t0:t0 + T], OP.mult))
                ctx.sDVE.inc(vector.tensor_tensor(sq_a[0:32, :], pk[32:64, :],
                                                  rsk[0:32, t0:t0 + T], OP.mult))
                vector.wait_ge(ctx.sDVE.h, ctx.sDVE.n)
                ctx.sDVE.inc(vector.tensor_tensor(krs[0:32, t0:t0 + T], rtmp[0:32, :],
                                                  sq_a[0:32, :], OP.subtract))
                ctx.sDVE.inc(vector.tensor_tensor(krs[32:64, t0:t0 + T],
                                                  krs[32:64, t0:t0 + T],
                                                  rtmp[32:64, :], OP.add))

            emit_rms(ctx, T, [kt[:, t0:t0 + T] for kt in kvct],
                     [kt[:, t0:t0 + T] for kt in kvct],
                     [gkv[:, c:c + 1] for c in range(4)], 4, ps, 5, ones2,
                     rows[0], sq_a[:], sq_b[:])

    emit_phase_barrier(ctx)

    with contextlib.ExitStack() as p2:
        xs = p2.enter_context(nc.sbuf_tensor("a2_x", [128, DC * T], F32))
        wbuf = p2.enter_context(nc.sbuf_tensor("a2_w", [128, 2 * DC * 128], F32))
        sq_a = p2.enter_context(nc.sbuf_tensor("a2_qa", [128, T], F32))
        sq_b = p2.enter_context(nc.sbuf_tensor("a2_qb", [128, T], F32))
        qcs = p2.enter_context(nc.sbuf_tensor("a2_qc", [128, 6 * T], F32))
        qcst = [qcs[:, c * T:(c + 1) * T] for c in range(6)]

        for ch in range(TOK // T):
            t0 = ch * T
            dve_prev = ctx.sDVE.n
            qg_prev = ctx.qG.n
            @block.sync
            def _(sync):
                sync.wait_ge(ctx.sDVE.h, dve_prev)
                for c in range(DC):
                    ctx.qS.inc(sync.dma_start(
                        out=xs[:, c * T:(c + 1) * T],
                        in_=xq_dram[c * 128:(c + 1) * 128, t0:t0 + T]), 16)

            pe_b = ctx.sPE.n
            dve_b = ctx.sDVE.n
            for o in range(6):
                half = o % 2
                wdst = wbuf[:, half * DC * 128:(half + 1) * DC * 128]
                @block.sync
                def _(sync, o=o, wdst=wdst):
                    sync.wait_ge(ctx.sPE.h, pe_b + max(0, o - 1))
                    ctx.qS.inc(sync.dma_start(
                        out=wdst, in_=wcq_dram[:, o * DC * 128:(o + 1) * DC * 128]), 16)

                @block.tensor
                def _(tensor, o=o, wdst=wdst):
                    tensor.wait_ge(ctx.qS.h, ctx.qS.n)
                    if o == 0:
                        tensor.wait_ge(ctx.sDVE.h, dve_b)
                    if o >= 4:
                        tensor.wait_ge(ctx.sDVE.h, dve_b + o - 3)
                    for c in range(DC):
                        ins = tensor.matmul(ps[:, (o % 4) * 512:(o % 4) * 512 + T],
                                            wdst[:, c * 128:(c + 1) * 128],
                                            xs[:, c * T:(c + 1) * T],
                                            start=(c == 0), stop=(c == DC - 1))
                    ctx.sPE.inc(ins)

                @block.vector
                def _(vector, o=o):
                    if o == 0:
                        vector.wait_ge(ctx.qG.h, qg_prev)
                    vector.wait_ge(ctx.sPE.h, pe_b + o + 1)
                    ctx.sDVE.inc(vector.tensor_copy(
                        qcst[o], ps[:, (o % 4) * 512:(o % 4) * 512 + T]))

            emit_rms(ctx, T, qcst, qcst,
                     [gq[:, c:c + 1] for c in range(6)], 6, ps, 4, ones2,
                     rows[0], sq_a[:], sq_b[:])

            dve_done = ctx.sDVE.n
            @block.gpsimd
            def _(gpsimd, t0=t0):
                gpsimd.wait_ge(ctx.sDVE.h, dve_done)
                for c in range(6):
                    ctx.qG.inc(gpsimd.dma_start(
                        out=qcn_scr[c * 128:(c + 1) * 128, t0:t0 + T],
                        in_=qcs[:, c * T:(c + 1) * T]), 16)

    emit_phase_barrier(ctx)

    with contextlib.ExitStack() as p3:
        kh = p3.enter_context(nc.sbuf_tensor("a3_kh", [128, S], F32))
        qcs = p3.enter_context(nc.sbuf_tensor("a3_qc", [128, 6 * T], F32))
        vh = p3.enter_context(nc.sbuf_tensor("a3_vh", [128, S], F32))
        qn = p3.enter_context(nc.sbuf_tensor("a3_qn", [128, TOK], F32))
        qr = p3.enter_context(nc.sbuf_tensor("a3_qr", [64, TOK], F32))
        pbuf = p3.enter_context(nc.sbuf_tensor("a3_p", [128, S], F32))
        whb = p3.enter_context(nc.sbuf_tensor("a3_wh", [128, 2176], F32))
        rcq = p3.enter_context(nc.sbuf_tensor("a3_rc", [64, TOK], F32))
        rsq = p3.enter_context(nc.sbuf_tensor("a3_rs", [64, TOK], F32))
        ostg = p3.enter_context(nc.sbuf_tensor("a3_os", [128, 128], F32))
        mx = p3.enter_context(nc.sbuf_tensor("a3_mx", [128, 16], F32))
        rtmp = p3.enter_context(nc.sbuf_tensor("a3_rt", [64, T], F32))
        rtm2 = p3.enter_context(nc.sbuf_tensor("a3_rt2", [64, T], F32))

        @block.sync
        def _(sync):
            ctx.qS.inc(sync.dma_start(out=rcq[:], in_=rcq_dram[:, :]), 16)
            ctx.qS.inc(sync.dma_start(out=rsq[:], in_=rsq_dram[:, :]), 16)

        for h in range(N_HEADS):
            pe_done = ctx.sPE.n
            @block.sync
            def _(sync, h=h):
                sync.wait_ge(ctx.sPE.h, pe_done)
                ctx.qS.inc(sync.dma_start(
                    out=whb[:, 0:768], in_=wdq_dram[:, h * 768:(h + 1) * 768]), 16)
                ctx.qS.inc(sync.dma_start(
                    out=whb[:, 768:1152], in_=wdqr_dram[:, h * 384:(h + 1) * 384]), 16)
                ctx.qS.inc(sync.dma_start(
                    out=whb[:, 1152:1664], in_=wdk_dram[:, h * 512:(h + 1) * 512]), 16)
                for c in range(4):
                    ctx.qS.inc(sync.dma_start(
                        out=whb[:, 1664 + c * 128:1664 + (c + 1) * 128],
                        in_=wdv_dram[:, (c * 16 + h) * 128:(c * 16 + h + 1) * 128]), 16)

            pe_b = ctx.sPE.n
            dve_b = ctx.sDVE.n
            for kb in range(KB8):
                @block.tensor
                def _(tensor, kb=kb, h=h):
                    tensor.wait_ge(ctx.qS.h, ctx.qS.n)
                    if kb == 0:
                        tensor.wait_ge(ctx.sDVE.h, dve_b)
                    if kb >= 4:
                        tensor.wait_ge(ctx.sDVE.h, dve_b + kb - 3)
                    for c in range(4):
                        ins = tensor.matmul(
                            ps[:, (kb % 4) * 512:(kb % 4) * 512 + T],
                            whb[:, 1152 + c * 128:1152 + (c + 1) * 128],
                            kvct[c][:, kb * T:(kb + 1) * T],
                            start=(c == 0), stop=(c == 3))
                    ctx.sPE.inc(ins)

                @block.vector
                def _(vector, kb=kb):
                    vector.wait_ge(ctx.sPE.h, pe_b + kb + 1)
                    ctx.sDVE.inc(vector.tensor_copy(
                        kh[:, kb * T:(kb + 1) * T],
                        ps[:, (kb % 4) * 512:(kb % 4) * 512 + T]))

            pe_b = ctx.sPE.n
            dve_b = ctx.sDVE.n
            for tb in range(KB128):
                @block.tensor
                def _(tensor, tb=tb):
                    if tb == 0:
                        tensor.wait_ge(ctx.sDVE.h, dve_b)
                    if tb >= 4:
                        tensor.wait_ge(ctx.sDVE.h, dve_b + tb - 3)
                    for c in range(4):
                        ins = tensor.matmul(
                            ps[:, (tb % 4) * 512:(tb % 4) * 512 + 128],
                            kvct[c][:, tb * 128:(tb + 1) * 128],
                            whb[:, 1664 + c * 128:1664 + (c + 1) * 128],
                            start=(c == 0), stop=(c == 3))
                    ctx.sPE.inc(ins)

                @block.vector
                def _(vector, tb=tb):
                    vector.wait_ge(ctx.sPE.h, pe_b + tb + 1)
                    ctx.sDVE.inc(vector.tensor_copy(
                        vh[:, tb * 128:(tb + 1) * 128],
                        ps[:, (tb % 4) * 512:(tb % 4) * 512 + 128]))

            pe_b = ctx.sPE.n
            dve_b = ctx.sDVE.n
            for qb in range(TOK // T):
                @block.sync
                def _(sync, qb=qb):
                    if qb > 0:
                        sync.wait_ge(ctx.sPE.h, pe_b + 2 * qb)
                    for c in range(6):
                        ctx.qS.inc(sync.dma_start(
                            out=qcs[:, c * T:(c + 1) * T],
                            in_=qcn_scr[c * 128:(c + 1) * 128, qb * T:(qb + 1) * T]), 16)

                @block.tensor
                def _(tensor, qb=qb):
                    tensor.wait_ge(ctx.qS.h, ctx.qS.n)
                    if qb == 0:
                        tensor.wait_ge(ctx.sDVE.h, dve_b)
                    if qb >= 4:
                        tensor.wait_ge(ctx.sDVE.h, dve_b + 2 * (qb - 3))
                    for c in range(6):
                        ins = tensor.matmul(
                            ps[:, (qb % 4) * 512:(qb % 4) * 512 + T],
                            whb[:, c * 128:(c + 1) * 128],
                            qcs[:, c * T:(c + 1) * T],
                            start=(c == 0), stop=(c == 5))
                    ctx.sPE.inc(ins)
                    for c in range(6):
                        ins = tensor.matmul(
                            ps[0:64, (4 + qb % 4) * 512:(4 + qb % 4) * 512 + T],
                            whb[:, 768 + c * 64:768 + (c + 1) * 64],
                            qcs[:, c * T:(c + 1) * T],
                            start=(c == 0), stop=(c == 5))
                    ctx.sPE.inc(ins)

                @block.vector
                def _(vector, qb=qb):
                    vector.wait_ge(ctx.sPE.h, pe_b + 2 * (qb + 1))
                    t0 = qb * T
                    pq = ps[:, (qb % 4) * 512:(qb % 4) * 512 + T]
                    pr = ps[0:64, (4 + qb % 4) * 512:(4 + qb % 4) * 512 + T]
                    ctx.sDVE.inc(vector.tensor_scalar(
                        qn[:, t0:t0 + T], pq, float(1.0 / np.sqrt(192.0)), None, OP.mult))
                    ctx.sDVE.inc(vector.tensor_tensor(rtmp[0:32, :], pr[0:32, :],
                                                      rcq[0:32, t0:t0 + T], OP.mult))
                    ctx.sDVE.inc(vector.tensor_tensor(rtmp[32:64, :], pr[32:64, :],
                                                      rcq[32:64, t0:t0 + T], OP.mult))
                    vector.wait_ge(ctx.sDVE.h, ctx.sDVE.n)
                    ctx.sDVE.inc(vector.tensor_tensor(qr[32:64, t0:t0 + T], pr[0:32, :],
                                                      rsq[32:64, t0:t0 + T], OP.mult))
                    ctx.sDVE.inc(vector.tensor_tensor(rtm2[0:32, :], pr[32:64, :],
                                                      rsq[0:32, t0:t0 + T], OP.mult))
                    vector.wait_ge(ctx.sDVE.h, ctx.sDVE.n)
                    ctx.sDVE.inc(vector.tensor_tensor(qr[0:32, t0:t0 + T], rtmp[0:32, :],
                                                      rtm2[0:32, :], OP.subtract))
                    ctx.sDVE.inc(vector.tensor_tensor(qr[32:64, t0:t0 + T],
                                                      qr[32:64, t0:t0 + T],
                                                      rtmp[32:64, :], OP.add))

            for qt in range(NT):
                q0 = qt * 128
                dve_now = ctx.sDVE.n
                act_now = ctx.sACT.n
                pe_q0 = ctx.sPE.n
                for kb in range(KB8):
                    @block.tensor
                    def _(tensor, qt=qt, q0=q0, kb=kb):
                        if kb == 0:
                            tensor.wait_ge(ctx.sDVE.h, dve_now)
                            tensor.wait_ge(ctx.sACT.h, act_now)
                        tensor.matmul(ps[:, kb * 512:(kb + 1) * 512],
                                      qn[:, q0:q0 + 128],
                                      kh[:, kb * T:(kb + 1) * T],
                                      start=True, stop=False)
                        ctx.sPE.inc(tensor.matmul(ps[:, kb * 512:(kb + 1) * 512],
                                                  qr[:, q0:q0 + 128],
                                                  krs[:, kb * T:(kb + 1) * T],
                                                  start=False, stop=True))

                    @block.vector
                    def _(vector, kb=kb):
                        vector.wait_ge(ctx.sPE.h, pe_q0 + kb + 1)
                        ctx.sDVE.inc(vector.reduce_max(
                            mx[:, 8 + kb:9 + kb], ps[:, kb * 512:(kb + 1) * 512],
                            axis=mybir.AxisListType.X))

                @block.vector
                def _(vector, qt=qt):
                    vector.wait_ge(ctx.sDVE.h, ctx.sDVE.n)
                    ctx.sDVE.inc(vector.reduce_max(mx[:, 0:1], mx[:, 8:16],
                                                   axis=mybir.AxisListType.X))
                    vector.wait_ge(ctx.sDVE.h, ctx.sDVE.n)
                    ctx.sDVE.inc(vector.tensor_scalar(mx[:, 1:2], mx[:, 0:1],
                                                      -1.0, None, OP.mult))

                @block.scalar
                def _(scalar, qt=qt):
                    scalar.wait_ge(ctx.sDVE.h, ctx.sDVE.n)
                    ctx.sACT.inc(scalar.activation(pbuf[:], ps[:, 0:S], AF.Exp,
                                                   bias=mx[:, 1:2], scale=1.0,
                                                   accum_out=mx[:, 2:3]))

                @block.vector
                def _(vector, qt=qt):
                    vector.wait_ge(ctx.sACT.h, ctx.sACT.n)
                    ctx.sDVE.inc(vector.reciprocal(mx[:, 3:4], mx[:, 2:3]))
                    vector.wait_ge(ctx.sDVE.h, ctx.sDVE.n)
                    ctx.sDVE.inc(vector.tensor_scalar(pbuf[:], pbuf[:],
                                                      mx[:, 3:4], None, OP.mult))

                dve_pg = ctx.sDVE.n
                for r in range(4):
                    @block.tensor
                    def _(tensor, r=r):
                        if r == 0:
                            tensor.wait_ge(ctx.sDVE.h, dve_pg)
                        else:
                            tensor.wait_ge(ctx.sDVE.h, dve_pg + r)
                        for j in range(8):
                            kb = r * 8 + j
                            ins = tensor.transpose(
                                ps[:, j * 512:j * 512 + 128],
                                pbuf[:, kb * 128:(kb + 1) * 128], ident[:])
                        ctx.sPE.inc(ins)

                    @block.vector
                    def _(vector, r=r):
                        vector.wait_ge(ctx.sPE.h, ctx.sPE.n)
                        src = ps[:, 0:4096].rearrange("p (b c) -> p b c", b=8)[:, :, 0:128]
                        dst = pbuf[:, r * 1024:(r + 1) * 1024].rearrange(
                            "p (b c) -> p b c", b=8)
                        ctx.sDVE.inc(vector.tensor_copy(dst, src))

                @block.tensor
                def _(tensor, qt=qt):
                    tensor.wait_ge(ctx.sDVE.h, ctx.sDVE.n)
                    for kb in range(KB128):
                        ins = tensor.matmul(ps[:, 2 * 512:2 * 512 + 128],
                                            vh[:, kb * 128:(kb + 1) * 128],
                                            pbuf[:, kb * 128:(kb + 1) * 128],
                                            start=(kb == 0), stop=(kb == KB128 - 1))
                    ctx.sPE.inc(ins)

                @block.vector
                def _(vector, qt=qt):
                    vector.wait_ge(ctx.sPE.h, ctx.sPE.n)
                    vector.wait_ge(ctx.qG.h, ctx.qG.n)
                    ctx.sDVE.inc(vector.tensor_copy(ostg[:], ps[:, 2 * 512:2 * 512 + 128]))

                dve_st = ctx.sDVE.n
                @block.gpsimd
                def _(gpsimd, h=h, q0=q0):
                    gpsimd.wait_ge(ctx.sDVE.h, dve_st)
                    ctx.qG.inc(gpsimd.dma_start(
                        out=attn_scr[h * 128:(h + 1) * 128, q0:q0 + 128],
                        in_=ostg[:]), 16)

    emit_phase_barrier(ctx)

    with contextlib.ExitStack() as p4:
        asb = p4.enter_context(nc.sbuf_tensor("a4_a", [128, DC * T], F32))
        xr = p4.enter_context(nc.sbuf_tensor("a4_x", [128, DC * T], F32))
        wbuf = p4.enter_context(nc.sbuf_tensor("a4_w", [128, 2 * DC * 128], F32))
        sq_a = p4.enter_context(nc.sbuf_tensor("a4_qa", [128, T], F32))
        sq_b = p4.enter_context(nc.sbuf_tensor("a4_qb", [128, T], F32))

        for ch in range(TOK // T):
            t0 = ch * T
            dve_prev = ctx.sDVE.n
            qg_prev = ctx.qG.n
            @block.sync
            def _(sync):
                sync.wait_ge(ctx.sDVE.h, dve_prev)
                sync.wait_ge(ctx.qG.h, qg_prev)
                for c in range(DC):
                    ctx.qS.inc(sync.dma_start(
                        out=asb[:, c * T:(c + 1) * T],
                        in_=attn_scr[c * 128:(c + 1) * 128, t0:t0 + T]), 16)
                    ctx.qS.inc(sync.dma_start(
                        out=xr[:, c * T:(c + 1) * T],
                        in_=xq_dram[c * 128:(c + 1) * 128, t0:t0 + T]), 16)

            pe_b = ctx.sPE.n
            dve_b = ctx.sDVE.n
            for o in range(DC):
                half = o % 2
                wdst = wbuf[:, half * DC * 128:(half + 1) * DC * 128]
                @block.sync
                def _(sync, o=o, wdst=wdst):
                    sync.wait_ge(ctx.sPE.h, pe_b + max(0, o - 1))
                    ctx.qS.inc(sync.dma_start(
                        out=wdst, in_=wo_dram[:, o * DC * 128:(o + 1) * DC * 128]), 16)

                @block.tensor
                def _(tensor, o=o, wdst=wdst):
                    tensor.wait_ge(ctx.qS.h, ctx.qS.n)
                    if o == 0:
                        tensor.wait_ge(ctx.sDVE.h, dve_b)
                    if o >= 4:
                        tensor.wait_ge(ctx.sDVE.h, dve_b + o - 3)
                    for c in range(DC):
                        ins = tensor.matmul(ps[:, (o % 4) * 512:(o % 4) * 512 + T],
                                            wdst[:, c * 128:(c + 1) * 128],
                                            asb[:, c * T:(c + 1) * T],
                                            start=(c == 0), stop=(c == DC - 1))
                    ctx.sPE.inc(ins)

                @block.vector
                def _(vector, o=o):
                    vector.wait_ge(ctx.sPE.h, pe_b + o + 1)
                    ctx.sDVE.inc(vector.tensor_tensor(
                        xr[:, o * T:(o + 1) * T],
                        ps[:, (o % 4) * 512:(o % 4) * 512 + T],
                        xr[:, o * T:(o + 1) * T], OP.add))

            s_tiles = [xr[:, c * T:(c + 1) * T] for c in range(DC)]
            emit_ln(ctx, T, s_tiles, s_tiles, sq_a[:], sq_b[:], ps, 0, ones2, rows)

            dve_done = ctx.sDVE.n
            @block.gpsimd
            def _(gpsimd):
                gpsimd.wait_ge(ctx.sDVE.h, dve_done)
                for c in range(DC):
                    ctx.qG.inc(gpsimd.dma_start(
                        out=out_dram[c * 128:(c + 1) * 128, t0:t0 + T],
                        in_=xr[:, c * T:(c + 1) * T]), 16)


def xpos_ops(ctx, vector, dst, t, cs, sn, u, u2):
    ctx.sDVE.inc(vector.tensor_copy(u2[0:64, :], t[64:128, :]))
    ctx.sDVE.inc(vector.tensor_copy(u2[64:128, :], t[0:64, :]))
    ctx.sDVE.inc(vector.tensor_tensor(u, t, cs, OP.mult))
    vector.wait_ge(ctx.sDVE.h, ctx.sDVE.n)
    ctx.sDVE.inc(vector.tensor_tensor(dst, u2, sn, OP.mult))
    vector.wait_ge(ctx.sDVE.h, ctx.sDVE.n)
    ctx.sDVE.inc(vector.tensor_tensor(dst[0:64, :], u[0:64, :], dst[0:64, :], OP.subtract))
    ctx.sDVE.inc(vector.tensor_tensor(dst[64:128, :], u[64:128, :], dst[64:128, :], OP.add))


def emit_norm_rows(ctx, ps_sum_row, ps_ssq_row, denom, eps, rows):
    block = ctx.block
    r_mu, r_m2, r_A, r_B = rows

    @block.vector
    def _(vector):
        vector.wait_ge(ctx.sPE.h, ctx.sPE.n)
        ctx.sDVE.inc(vector.tensor_scalar(r_mu, ps_sum_row, 1.0 / denom, None, OP.mult))
        ctx.sDVE.inc(vector.tensor_scalar(r_m2, ps_ssq_row, 1.0 / denom, float(eps),
                                          OP.mult, OP.add))
        vector.wait_ge(ctx.sDVE.h, ctx.sDVE.n)
        ctx.sDVE.inc(vector.tensor_tensor(r_A, r_mu, r_mu, OP.mult))
        vector.wait_ge(ctx.sDVE.h, ctx.sDVE.n)
        ctx.sDVE.inc(vector.tensor_tensor(r_m2, r_m2, r_A, OP.subtract))

    @block.scalar
    def _(scalar):
        scalar.wait_ge(ctx.sDVE.h, ctx.sDVE.n)
        ctx.sACT.inc(scalar.activation(r_A, r_m2, AF.Ln, bias=0.0, scale=1.0))
        scalar.wait_ge(ctx.sACT.h, ctx.sACT.n)
        ctx.sACT.inc(scalar.activation(r_A, r_A, AF.Exp, bias=0.0, scale=-0.5))

    @block.vector
    def _(vector):
        vector.wait_ge(ctx.sACT.h, ctx.sACT.n)
        ctx.sDVE.inc(vector.scalar_tensor_tensor(r_B, r_mu, -1.0, r_A,
                                                 OP.mult, OP.mult))


def emit_l1attn(nc, block, ctx, x_dram, out_dram, wq_dram, bq_dram, wk_dram,
                bk_dram, wv_dram, bvr_dram, wg_dram, bg_dram, wo_dram, bo_dram,
                csq_dram, snq_dram, csk_dram, snk_dram, gcol_dram, arow_dram,
                mask_dram, ones_dram, kin, vin, kgo, vgo, stack):
    ones2 = stack.enter_context(nc.sbuf_tensor("r_on", [128, 128], F32))
    gcol = stack.enter_context(nc.sbuf_tensor("r_gc", [128, 16 * 272], F32))
    abcast = stack.enter_context(nc.sbuf_tensor("r_ab", [128, 16 * 128], F32))
    masks = stack.enter_context(nc.sbuf_tensor("r_mk", [128, 16 * 2 * 128], F32))
    bvec = stack.enter_context(nc.sbuf_tensor("r_bvec", [128, 64], F32))
    rows_t = stack.enter_context(nc.sbuf_tensor("r_rw", [1, 4 * 512], F32))
    ps = stack.enter_context(nc.psum_tensor("r_ps", [128, 8 * 512], F32))

    rows = [rows_t[0:1, i * 512:(i + 1) * 512] for i in range(4)]
    rows256 = [rows_t[0:1, i * 512:i * 512 + 256] for i in range(4)]
    bq_c = lambda h: bvec[:, h:h + 1]
    bk_c = lambda h: bvec[:, 16 + h:16 + h + 1]
    bg_c = lambda h: bvec[:, 32 + h:32 + h + 1]
    bo_c = lambda o: bvec[:, 48 + o:48 + o + 1]
    gcol_idx = lambda h, j, kt: h * 272 + j * j + j + kt

    @block.sync
    def _(sync):
        for dst, src in [(ones2[:], ones_dram[:, :]), (gcol[:], gcol_dram[:, :]),
                         (abcast[:], arow_dram[:, :]), (masks[:], mask_dram[:, :]),
                         (bvec[:, 0:16], bq_dram[:, :]), (bvec[:, 16:32], bk_dram[:, :]),
                         (bvec[:, 32:48], bg_dram[:, :]), (bvec[:, 48:64], bo_dram[:, :])]:
            ctx.qS.inc(sync.dma_start(out=dst, in_=src), 16)

    with contextlib.ExitStack() as pk:
        T = 512
        xs = pk.enter_context(nc.sbuf_tensor("rk_x", [128, DC * T], F32))
        wbuf = pk.enter_context(nc.sbuf_tensor("rk_w", [128, 2 * DC * 128], F32))
        kst = pk.enter_context(nc.sbuf_tensor("rk_k", [128, T], F32))
        ktmp = pk.enter_context(nc.sbuf_tensor("rk_kt", [128, T], F32))
        utmp = pk.enter_context(nc.sbuf_tensor("rk_u", [128, T], F32))
        u2tmp = pk.enter_context(nc.sbuf_tensor("rk_u2", [128, T], F32))
        vstage = pk.enter_context(nc.sbuf_tensor("rk_v", [128, 4 * D], F32))
        csk = pk.enter_context(nc.sbuf_tensor("rk_rc", [128, TOK], F32))
        snk = pk.enter_context(nc.sbuf_tensor("rk_rs", [128, TOK], F32))
        bvb = pk.enter_context(nc.sbuf_tensor("rk_bvb", [128, D], F32))
        bvrow = pk.enter_context(nc.sbuf_tensor("rk_bvr", [1, D], F32))

        @block.sync
        def _(sync):
            ctx.qS.inc(sync.dma_start(out=csk[:], in_=csk_dram[:, :]), 16)
            ctx.qS.inc(sync.dma_start(out=snk[:], in_=snk_dram[:, :]), 16)
            ctx.qS.inc(sync.dma_start(out=bvrow[:], in_=bvr_dram[:, :]), 16)

        @block.tensor
        def _(tensor):
            tensor.wait_ge(ctx.qS.h, ctx.qS.n)
            for i in range(4):
                ctx.sPE.inc(tensor.matmul(ps[:, i * 512:(i + 1) * 512],
                                          ones2[0:1, :],
                                          bvrow[0:1, i * 512:(i + 1) * 512],
                                          start=True, stop=True))

        @block.vector
        def _(vector):
            vector.wait_ge(ctx.sPE.h, ctx.sPE.n)
            for i in range(4):
                ctx.sDVE.inc(vector.tensor_copy(bvb[:, i * 512:(i + 1) * 512],
                                                ps[:, i * 512:(i + 1) * 512]))

        for ch in range(TOK // T):
            t0 = ch * T
            dve_prev = ctx.sDVE.n
            @block.sync
            def _(sync):
                sync.wait_ge(ctx.sDVE.h, dve_prev)
                for c in range(DC):
                    ctx.qS.inc(sync.dma_start(
                        out=xs[:, c * T:(c + 1) * T],
                        in_=x_dram[c * 128:(c + 1) * 128, t0:t0 + T]), 16)

            pe_sw = ctx.sPE.n
            half_user = {0: pe_sw, 1: pe_sw}
            for h in range(N_HEADS):
                half = h % 2
                wslc = wbuf[:, half * DC * 128:(half + 1) * DC * 128]

                @block.sync
                def _(sync, h=h, wslc=wslc, wait_n=half_user[half]):
                    sync.wait_ge(ctx.sPE.h, wait_n)
                    ctx.qS.inc(sync.dma_start(
                        out=wslc, in_=wk_dram[:, h * DC * 128:(h + 1) * DC * 128]), 16)

                @block.tensor
                def _(tensor, h=h, wslc=wslc):
                    tensor.wait_ge(ctx.qS.h, ctx.qS.n)
                    tensor.wait_ge(ctx.sDVE.h, ctx.sDVE.n)
                    for c in range(DC):
                        ins = tensor.matmul(ps[:, (h % 4) * 512:(h % 4) * 512 + T],
                                            wslc[:, c * 128:(c + 1) * 128],
                                            xs[:, c * T:(c + 1) * T],
                                            start=(c == 0), stop=(c == DC - 1))
                    ctx.sPE.inc(ins)
                half_user[half] = ctx.sPE.n

                @block.vector
                def _(vector, h=h, t0=t0):
                    vector.wait_ge(ctx.sPE.h, ctx.sPE.n)
                    vector.wait_ge(ctx.qG.h, ctx.qG.n)
                    pk_ = ps[:, (h % 4) * 512:(h % 4) * 512 + T]
                    ctx.sDVE.inc(vector.tensor_scalar(ktmp[:], pk_, bk_c(h), None, OP.add))
                    vector.wait_ge(ctx.sDVE.h, ctx.sDVE.n)
                    xpos_ops(ctx, vector, kst[:], ktmp[:],
                             csk[:, t0:t0 + T], snk[:, t0:t0 + T], utmp[:], u2tmp[:])

                dve_st = ctx.sDVE.n
                @block.gpsimd
                def _(gpsimd, h=h, t0=t0):
                    gpsimd.wait_ge(ctx.sDVE.h, dve_st)
                    ctx.qG.inc(gpsimd.dma_start(
                        out=kin[h * 128:(h + 1) * 128, t0:t0 + T], in_=kst[:]), 16)
                emit_phase_barrier(ctx)

            pe_sw = ctx.sPE.n
            half_user = {0: pe_sw, 1: pe_sw}
            qg_v = ctx.qG.n
            for h in range(N_HEADS):
                half = h % 2
                wslc = wbuf[:, half * DC * 128:(half + 1) * DC * 128]

                @block.sync
                def _(sync, h=h, wslc=wslc, wait_n=half_user[half]):
                    sync.wait_ge(ctx.sPE.h, wait_n)
                    ctx.qS.inc(sync.dma_start(
                        out=wslc, in_=wv_dram[:, h * DC * 128:(h + 1) * DC * 128]), 16)

                for tb in range(4):
                    @block.tensor
                    def _(tensor, h=h, tb=tb, wslc=wslc):
                        tensor.wait_ge(ctx.qS.h, ctx.qS.n)
                        tensor.wait_ge(ctx.sDVE.h, ctx.sDVE.n)
                        for c in range(DC):
                            ins = tensor.matmul(
                                ps[:, tb * 512:tb * 512 + 128],
                                xs[:, c * T + tb * 128:c * T + (tb + 1) * 128],
                                wslc[:, c * 128:(c + 1) * 128],
                                start=(c == 0), stop=(c == DC - 1))
                        ctx.sPE.inc(ins)

                    @block.vector
                    def _(vector, h=h, tb=tb):
                        vector.wait_ge(ctx.sPE.h, ctx.sPE.n)
                        if h == 0 and tb == 0:
                            vector.wait_ge(ctx.qG.h, qg_v)
                        ctx.sDVE.inc(vector.tensor_tensor(
                            vstage[:, tb * D + h * 128:tb * D + (h + 1) * 128],
                            ps[:, tb * 512:tb * 512 + 128],
                            bvb[:, h * 128:(h + 1) * 128], OP.add))
                half_user[half] = ctx.sPE.n

            dve_st = ctx.sDVE.n
            @block.gpsimd
            def _(gpsimd, t0=t0):
                gpsimd.wait_ge(ctx.sDVE.h, dve_st)
                for tb in range(4):
                    ctx.qG.inc(gpsimd.dma_start(
                        out=vin[t0 + tb * 128:t0 + (tb + 1) * 128, :],
                        in_=vstage[:, tb * D:(tb + 1) * D]), 16)

        qg_all = ctx.qG.n
        @block.gpsimd
        def _(gpsimd):
            gpsimd.wait_ge(ctx.qG.h, qg_all)
            ctx.qA.inc(gpsimd.collective_compute(
                "AllGather", OP.bypass,
                replica_groups=[[0, 1], [2, 3], [4, 5], [6, 7]],
                ins=[kin.opt()], outs=[kgo.opt()]), 1)
            ctx.qA.inc(gpsimd.collective_compute(
                "AllGather", OP.bypass,
                replica_groups=[[0, 1], [2, 3], [4, 5], [6, 7]],
                ins=[vin.opt()], outs=[vgo.opt()]), 1)

    emit_phase_barrier(ctx)

    with contextlib.ExitStack() as pq:
        T = 256
        xs = pq.enter_context(nc.sbuf_tensor("rq_x", [128, DC * T], F32))
        wbuf = pq.enter_context(nc.sbuf_tensor("rq_w", [128, 2 * DC * 128], F32))
        qall = pq.enter_context(nc.sbuf_tensor("rq_q", [128, 16 * T], F32))
        gall = pq.enter_context(nc.sbuf_tensor("rq_g", [128, 16 * T], F32))
        retall = pq.enter_context(nc.sbuf_tensor("rq_r", [128, 16 * T], F32))
        kstrip = pq.enter_context(nc.sbuf_tensor("rq_ks", [128, 32 * 128], F32))
        vstrip = pq.enter_context(nc.sbuf_tensor("rq_vs", [128, 32 * 128], F32))
        csq = pq.enter_context(nc.sbuf_tensor("rq_rc", [128, TOK], F32))
        snq = pq.enter_context(nc.sbuf_tensor("rq_rs", [128, TOK], F32))
        ttmp = pq.enter_context(nc.sbuf_tensor("rq_t", [128, T], F32))
        utmp = pq.enter_context(nc.sbuf_tensor("rq_u", [128, T], F32))
        u2tmp = pq.enter_context(nc.sbuf_tensor("rq_u2", [128, T], F32))
        pts = pq.enter_context(nc.sbuf_tensor("rq_pt", [128, 4 * 128], F32))
        sq_a = pq.enter_context(nc.sbuf_tensor("rq_qa", [128, T], F32))
        sq_b = pq.enter_context(nc.sbuf_tensor("rq_qb", [128, T], F32))

        @block.sync
        def _(sync):
            ctx.qS.inc(sync.dma_start(out=csq[:], in_=csq_dram[:, :]), 16)
            ctx.qS.inc(sync.dma_start(out=snq[:], in_=snq_dram[:, :]), 16)
            sync.wait_ge(ctx.qA.h, 2)

        for ch in range(TOK // T):
            t0 = ch * T
            nloc = 2 * ch + 2
            dve_prev = ctx.sDVE.n
            qg_prev = ctx.qG.n
            @block.sync
            def _(sync):
                sync.wait_ge(ctx.sDVE.h, dve_prev)
                sync.wait_ge(ctx.qG.h, qg_prev)
                for c in range(DC):
                    ctx.qS.inc(sync.dma_start(
                        out=xs[:, c * T:(c + 1) * T],
                        in_=x_dram[c * 128:(c + 1) * 128, t0:t0 + T]), 16)

            @block.vector
            def _(vector, qg_prev=qg_prev):
                vector.wait_ge(ctx.qG.h, qg_prev)

            pe_sw = ctx.sPE.n
            half_user = {0: pe_sw, 1: pe_sw}
            for h in range(N_HEADS):
                half = h % 2
                wq_s = wbuf[:, half * DC * 128:(half + 1) * DC * 128]
                wg_s = wbuf[:, (1 - half) * DC * 128:(2 - half) * DC * 128]

                @block.sync
                def _(sync, h=h, wq_s=wq_s, wait_n=half_user[half]):
                    sync.wait_ge(ctx.sPE.h, wait_n)
                    ctx.qS.inc(sync.dma_start(
                        out=wq_s, in_=wq_dram[:, h * DC * 128:(h + 1) * DC * 128]), 16)

                @block.tensor
                def _(tensor, h=h, wq_s=wq_s):
                    tensor.wait_ge(ctx.qS.h, ctx.qS.n)
                    tensor.wait_ge(ctx.sDVE.h, ctx.sDVE.n)
                    for c in range(DC):
                        ins = tensor.matmul(ps[:, 0 * 512:0 * 512 + T],
                                            wq_s[:, c * 128:(c + 1) * 128],
                                            xs[:, c * T:(c + 1) * T],
                                            start=(c == 0), stop=(c == DC - 1))
                    ctx.sPE.inc(ins)
                half_user[half] = ctx.sPE.n

                @block.vector
                def _(vector, h=h, t0=t0):
                    vector.wait_ge(ctx.sPE.h, ctx.sPE.n)
                    vector.wait_ge(ctx.sACT.h, ctx.sACT.n)
                    ctx.sDVE.inc(vector.tensor_scalar(ttmp[:], ps[:, 0:T],
                                                      bq_c(h), None, OP.add))
                    vector.wait_ge(ctx.sDVE.h, ctx.sDVE.n)
                    xpos_ops(ctx, vector, qall[:, h * T:(h + 1) * T], ttmp[:],
                             csq[:, t0:t0 + T], snq[:, t0:t0 + T], utmp[:], u2tmp[:])

                @block.sync
                def _(sync, h=h, wg_s=wg_s, wait_n=half_user[1 - half]):
                    sync.wait_ge(ctx.sPE.h, wait_n)
                    ctx.qS.inc(sync.dma_start(
                        out=wg_s, in_=wg_dram[:, h * DC * 128:(h + 1) * DC * 128]), 16)

                @block.tensor
                def _(tensor, h=h, wg_s=wg_s):
                    tensor.wait_ge(ctx.qS.h, ctx.qS.n)
                    tensor.wait_ge(ctx.sDVE.h, ctx.sDVE.n)
                    for c in range(DC):
                        ins = tensor.matmul(ps[:, 1 * 512:1 * 512 + T],
                                            wg_s[:, c * 128:(c + 1) * 128],
                                            xs[:, c * T:(c + 1) * T],
                                            start=(c == 0), stop=(c == DC - 1))
                    ctx.sPE.inc(ins)
                half_user[1 - half] = ctx.sPE.n

                @block.vector
                def _(vector, h=h):
                    vector.wait_ge(ctx.sPE.h, ctx.sPE.n)
                    vector.wait_ge(ctx.sACT.h, ctx.sACT.n)
                    ctx.sDVE.inc(vector.tensor_scalar(
                        ttmp[:], ps[:, 1 * 512:1 * 512 + T], bg_c(h), None, OP.add))

                @block.scalar
                def _(scalar, h=h):
                    scalar.wait_ge(ctx.sDVE.h, ctx.sDVE.n)
                    ctx.sACT.inc(scalar.activation(gall[:, h * T:(h + 1) * T],
                                                   ttmp[:], AF.Silu,
                                                   bias=0.0, scale=1.0))
                emit_phase_barrier(ctx)

            koff = lambda kt: (0 if kt % 2 == 0 else 2048) + (kt // 2) * 128
            for h in range(N_HEADS):
                pe_h = ctx.sPE.n
                @block.sync
                def _(sync, h=h, nloc=nloc, wait_n=pe_h):
                    sync.wait_ge(ctx.sPE.h, wait_n)
                    ctx.qS.inc(sync.dma_start(
                        out=kstrip[:, 0:nloc * 128],
                        in_=kgo[h * 128:(h + 1) * 128, 0:nloc * 128]), 16)
                    ctx.qS.inc(sync.dma_start(
                        out=kstrip[:, 2048:2048 + nloc * 128],
                        in_=kgo[D + h * 128:D + (h + 1) * 128, 0:nloc * 128]), 16)
                    ctx.qS.inc(sync.dma_start(
                        out=vstrip[:, 0:nloc * 128].rearrange("p (l m) -> p l m", l=nloc),
                        in_=vgo[0:nloc * 128, h * 128:(h + 1) * 128].rearrange(
                            "(l p) m -> p l m", p=128)), 16)
                    ctx.qS.inc(sync.dma_start(
                        out=vstrip[:, 2048:2048 + nloc * 128].rearrange("p (l m) -> p l m", l=nloc),
                        in_=vgo[TOK:TOK + nloc * 128, h * 128:(h + 1) * 128].rearrange(
                            "(l p) m -> p l m", p=128)), 16)

                for j in range(2):
                    jl = 2 * ch + j
                    ktmax = 2 * jl + 1
                    dve_bj = ctx.sDVE.n
                    pe_bj = ctx.sPE.n
                    qrhs = qall[:, h * T + j * 128:h * T + (j + 1) * 128]
                    score_done = {}
                    evac_done = {}
                    pv_done = {}

                    def emit_score(kt):
                        sbank = 2 + kt % 4
                        @block.tensor
                        def _(tensor, kt=kt, sbank=sbank, qrhs=qrhs):
                            if kt == 0:
                                tensor.wait_ge(ctx.qS.h, ctx.qS.n)
                                tensor.wait_ge(ctx.sDVE.h, dve_bj)
                            if kt >= 4:
                                tensor.wait_ge(ctx.sDVE.h, evac_done[kt - 4])
                            ctx.sPE.inc(tensor.matmul(
                                ps[:, sbank * 512:sbank * 512 + 128],
                                kstrip[:, koff(kt):koff(kt) + 128],
                                qrhs, start=True, stop=True))
                        score_done[kt] = ctx.sPE.n

                    def emit_evac(kt):
                        sbank = 2 + kt % 4
                        @block.vector
                        def _(vector, h=h, jl=jl, kt=kt, sbank=sbank):
                            vector.wait_ge(ctx.sPE.h, score_done[kt])
                            if kt >= 4:
                                vector.wait_ge(ctx.sPE.h, pv_done[kt - 4])
                            else:
                                vector.wait_ge(ctx.sPE.h, pe_bj)
                            slot = pts[:, (kt % 4) * 128:(kt % 4 + 1) * 128]
                            sc = ps[:, sbank * 512:sbank * 512 + 128]
                            if kt < 2 * jl:
                                ctx.sDVE.inc(vector.tensor_scalar(
                                    slot, sc, gcol[:, gcol_idx(h, jl, kt):
                                                   gcol_idx(h, jl, kt) + 1],
                                    None, OP.mult))
                            else:
                                w = kt - 2 * jl
                                ctx.sDVE.inc(vector.tensor_tensor(
                                    slot, sc,
                                    masks[:, (h * 2 + w) * 128:(h * 2 + w + 1) * 128],
                                    OP.mult))
                        evac_done[kt] = ctx.sDVE.n

                    def emit_pv(kt):
                        @block.tensor
                        def _(tensor, j=j, kt=kt, ktmax=ktmax):
                            tensor.wait_ge(ctx.sDVE.h, evac_done[kt])
                            ctx.sPE.inc(tensor.matmul(
                                ps[:, (6 + j) * 512:(6 + j) * 512 + 128],
                                vstrip[:, koff(kt):koff(kt) + 128],
                                pts[:, (kt % 4) * 128:(kt % 4 + 1) * 128],
                                start=(kt == 0), stop=(kt == ktmax)))
                        pv_done[kt] = ctx.sPE.n

                    for kt in range(ktmax + 1):
                        emit_score(kt)
                        if kt >= 2:
                            emit_pv(kt - 2)
                        emit_evac(kt)
                    for kt in (ktmax - 1, ktmax):
                        if kt >= 0:
                            emit_pv(kt)

                    @block.vector
                    def _(vector, h=h, j=j, ktmax=ktmax, done=ctx.sPE.n):
                        vector.wait_ge(ctx.sPE.h, done)
                        ctx.sDVE.inc(vector.tensor_tensor(
                            retall[:, h * T + j * 128:h * T + (j + 1) * 128],
                            ps[:, (6 + j) * 512:(6 + j) * 512 + 128],
                            abcast[:, h * 128:(h + 1) * 128], OP.mult))

            for h in range(N_HEADS):
                rslc = retall[:, h * T:(h + 1) * T]
                @block.tensor
                def _(tensor, h=h, rslc=rslc):
                    tensor.wait_ge(ctx.sDVE.h, ctx.sDVE.n)
                    ctx.sPE.inc(tensor.matmul(ps[0:1, 0:T], ones2[:, 0:1], rslc,
                                              start=True, stop=True))

                @block.vector
                def _(vector, h=h, rslc=rslc):
                    vector.wait_ge(ctx.sPE.h, ctx.sPE.n)
                    ctx.sDVE.inc(vector.tensor_tensor(sq_a[:], rslc, rslc, OP.mult))

                @block.tensor
                def _(tensor, h=h):
                    tensor.wait_ge(ctx.sDVE.h, ctx.sDVE.n)
                    ctx.sPE.inc(tensor.matmul(ps[0:1, 512:512 + T], ones2[:, 0:1],
                                              sq_a[:], start=True, stop=True))

                emit_norm_rows(ctx, ps[0:1, 0:T], ps[0:1, 512:512 + T], 128.0,
                               GN_EPS, rows256)

                @block.tensor
                def _(tensor, h=h):
                    tensor.wait_ge(ctx.sDVE.h, ctx.sDVE.n)
                    tensor.matmul(ps[:, 2 * 512:2 * 512 + T], ones2[0:1, :],
                                  rows256[2], start=True, stop=True)
                    ctx.sPE.inc(tensor.matmul(ps[:, 3 * 512:3 * 512 + T], ones2[0:1, :],
                                              rows256[3], start=True, stop=True))

                @block.vector
                def _(vector, h=h, rslc=rslc):
                    vector.wait_ge(ctx.sPE.h, ctx.sPE.n)
                    ctx.sDVE.inc(vector.tensor_tensor(
                        ttmp[:], rslc, ps[:, 2 * 512:2 * 512 + T], OP.mult))
                    vector.wait_ge(ctx.sDVE.h, ctx.sDVE.n)
                    ctx.sDVE.inc(vector.tensor_tensor(
                        ttmp[:], ttmp[:], ps[:, 3 * 512:3 * 512 + T], OP.add))
                    vector.wait_ge(ctx.sDVE.h, ctx.sDVE.n)
                    ctx.sDVE.inc(vector.tensor_tensor(
                        gall[:, h * T:(h + 1) * T], ttmp[:],
                        gall[:, h * T:(h + 1) * T], OP.mult))


            pe_sw = ctx.sPE.n
            half_user = {0: ctx.sPE.n, 1: ctx.sPE.n}
            for o in range(DC):
                half = o % 2
                wo_s = wbuf[:, half * DC * 128:(half + 1) * DC * 128]
                @block.sync
                def _(sync, o=o, wo_s=wo_s, wait_n=half_user[half]):
                    sync.wait_ge(ctx.sPE.h, wait_n)
                    ctx.qS.inc(sync.dma_start(
                        out=wo_s, in_=wo_dram[:, o * DC * 128:(o + 1) * DC * 128]), 16)

                @block.tensor
                def _(tensor, o=o, wo_s=wo_s):
                    tensor.wait_ge(ctx.qS.h, ctx.qS.n)
                    if o == 0:
                        tensor.wait_ge(ctx.sDVE.h, ctx.sDVE.n)
                    if o >= 4:
                        tensor.wait_ge(ctx.sDVE.h, dve_sw + o - 3)
                    for c in range(DC):
                        ins = tensor.matmul(ps[:, (o % 4) * 512:(o % 4) * 512 + T],
                                            wo_s[:, c * 128:(c + 1) * 128],
                                            gall[:, c * T:(c + 1) * T],
                                            start=(c == 0), stop=(c == DC - 1))
                    ctx.sPE.inc(ins)
                half_user[half] = ctx.sPE.n
                if o == 0:
                    dve_sw = ctx.sDVE.n

                @block.vector
                def _(vector, o=o):
                    vector.wait_ge(ctx.sPE.h, ctx.sPE.n)
                    ctx.sDVE.inc(vector.scalar_tensor_tensor(
                        qall[:, o * T:(o + 1) * T],
                        ps[:, (o % 4) * 512:(o % 4) * 512 + T],
                        bo_c(o), xs[:, o * T:(o + 1) * T], OP.add, OP.add))


            s_tiles = [qall[:, c * T:(c + 1) * T] for c in range(DC)]
            emit_ln(ctx, T, s_tiles, s_tiles, sq_a[:], sq_b[:], ps, 0, ones2, rows256)

            dve_done = ctx.sDVE.n
            @block.gpsimd
            def _(gpsimd, t0=t0):
                gpsimd.wait_ge(ctx.sDVE.h, dve_done)
                for c in range(DC):
                    ctx.qG.inc(gpsimd.dma_start(
                        out=out_dram[c * 128:(c + 1) * 128, t0:t0 + T],
                        in_=qall[:, c * T:(c + 1) * T]), 16)


def own_token_tiles(core):
    par = core % 2
    return [2 * j + par for j in range(NT)]


def own_token_index(core):
    return np.concatenate([np.arange(g * 128, (g + 1) * 128)
                           for g in own_token_tiles(core)])


def vec_tiles(v):
    n = v.shape[0] // 128
    return np.ascontiguousarray(v.reshape(n, 128).T)


def lhsT_dev(W):
    din, dout = W.shape
    nc_, no_ = din // 128, dout // 128
    return np.ascontiguousarray(
        W.reshape(nc_, 128, no_, 128).transpose(1, 2, 0, 3).reshape(128, -1))


def lhsT_dev_k(W):
    din, dout = W.shape
    nc_, no_ = din // 128, dout // 128
    return np.ascontiguousarray(
        W.reshape(nc_, 128, no_, 128).transpose(1, 0, 2, 3).reshape(128, -1))


def build_l1ffn_program():
    nc = bass.Bass("TRN2", target_bir_lowering=False)
    x = nc.dram_tensor("x", [D, TOK], F32, kind="ExternalInput").ap()
    w0t = nc.dram_tensor("w0t", [128, IC * DC * 128], F32, kind="ExternalInput").ap()
    w1t = nc.dram_tensor("w1t", [128, IC * DC * 128], F32, kind="ExternalInput").ap()
    b0 = nc.dram_tensor("b0", [128, IC], F32, kind="ExternalInput").ap()
    b1 = nc.dram_tensor("b1", [128, DC], F32, kind="ExternalInput").ap()
    ones = nc.dram_tensor("ones", [128, 128], F32, kind="ExternalInput").ap()
    out = nc.dram_tensor("out", [D, TOK], F32, kind="ExternalOutput").ap()

    with contextlib.ExitStack() as stack:
        block = stack.enter_context(nc.Block())
        ctx = make_ctx(nc, block, stack)
        emit_l1ffn(nc, block, ctx, x, out, w0t, w1t, b0, b1, ones, stack)
        @block.gpsimd
        def _(gpsimd):
            gpsimd.wait_ge(ctx.qG.h, ctx.qG.n)
    return nc


def rope_tables(dim, positions, scale=1.0):
    inv_freq = (1.0 / (np.float32(10000.0) **
                       (np.arange(0, dim, 2, dtype=np.float32) / np.float32(dim))))
    inv_freq = inv_freq.astype(np.float32)
    pos = positions.astype(np.float32)
    ang = (pos[:, None] * inv_freq[None, :]).astype(np.float32)
    cos = np.cos(ang.astype(np.float64)).astype(np.float32) * np.float32(scale)
    sin = np.sin(ang.astype(np.float64)).astype(np.float32) * np.float32(scale)
    ct = np.concatenate([cos, cos], axis=1).T
    st = np.concatenate([sin, sin], axis=1).T
    return np.ascontiguousarray(ct), np.ascontiguousarray(st)


def build_l0attn_program():
    nc = bass.Bass("TRN2", target_bir_lowering=False)
    gi = lambda nm, shp: nc.dram_tensor(nm, shp, F32, kind="ExternalInput").ap()
    xk = gi("xk", [D, S])
    xq = gi("xq", [D, TOK])
    wcq = gi("wcq", [128, 6 * DC * 128])
    gq = gi("gq", [128, 6])
    wdq = gi("wdq", [128, 16 * 6 * 128])
    wdqr = gi("wdqr", [128, 16 * 6 * 64])
    wckv = gi("wckv", [128, 4 * DC * 128])
    gkv = gi("gkv", [128, 4])
    wdk = gi("wdk", [128, 16 * 4 * 128])
    wdv = gi("wdv", [128, 4 * 16 * 128])
    wkr = gi("wkr", [128, DC * 64])
    wo = gi("wo", [128, DC * DC * 128])
    rckv = gi("rckv", [64, S])
    rskv = gi("rskv", [64, S])
    rcq = gi("rcq", [64, TOK])
    rsq = gi("rsq", [64, TOK])
    ident = gi("ident", [128, 128])
    ones = gi("ones", [128, 128])
    attn_scr = nc.dram_tensor("attn_scr", [D, TOK], F32, kind="Internal").ap()
    qcn_scr = nc.dram_tensor("qcn_scr", [6 * 128, TOK], F32, kind="Internal").ap()
    out = nc.dram_tensor("out", [D, TOK], F32, kind="ExternalOutput").ap()

    with contextlib.ExitStack() as stack:
        block = stack.enter_context(nc.Block())
        ctx = make_ctx(nc, block, stack)
        emit_l0attn(nc, block, ctx, xk, xq, out, wcq, gq, wdq, wdqr, wckv,
                    gkv, wdk, wdv, wkr, wo, rckv, rskv, rcq, rsq, ident,
                    ones, attn_scr, qcn_scr, stack)
        @block.gpsimd
        def _(gpsimd):
            gpsimd.wait_ge(ctx.qG.h, ctx.qG.n)
    return nc


def stage_l0attn_shared(Wcq, g_q, Wdq, Wdqr, Wckv, g_kv, Wdk, Wdv, Wkr, Wo):
    rckv, rskv = rope_tables(64, np.arange(S))
    return {
        "wcq": lhsT_dev(Wcq),
        "gq": vec_tiles(g_q),
        "wdq": lhsT_dev(Wdq),
        "wdqr": np.ascontiguousarray(
            Wdqr.reshape(6, 128, 16, 64).transpose(1, 2, 0, 3).reshape(128, -1)),
        "wckv": lhsT_dev(Wckv),
        "gkv": vec_tiles(g_kv),
        "wdk": lhsT_dev(Wdk),
        "wdv": np.ascontiguousarray(
            Wdv.reshape(4, 128, 16, 128).transpose(1, 0, 2, 3).reshape(128, -1)),
        "wkr": np.ascontiguousarray(
            Wkr.reshape(16, 128, 64).transpose(1, 0, 2).reshape(128, -1)),
        "wo": lhsT_dev(Wo),
        "rckv": rckv, "rskv": rskv,
        "ident": np.ascontiguousarray(np.eye(128, dtype=np.float32)),
        "ones": np.ones((128, 128), np.float32),
    }


def run_l0attn(x_full, Wcq, g_q, Wdq, Wdqr, Wckv, g_kv, Wdk, Wdv, Wkr, Wo):
    nc = build_l0attn_program()
    shared = stage_l0attn_shared(Wcq, g_q, Wdq, Wdqr, Wckv, g_kv, Wdk, Wdv, Wkr, Wo)
    in_maps = []
    for c in range(NCORES):
        b = c // 2
        idx = own_token_index(c)
        rcq, rsq = rope_tables(64, idx, scale=1.0 / np.sqrt(192.0))
        in_maps.append({
            "xk": np.ascontiguousarray(x_full[b].T),
            "xq": np.ascontiguousarray(x_full[b][idx].T),
            "rcq": rcq, "rsq": rsq, **shared})
    res = run_bass_kernel_spmd(nc, in_maps, core_ids=list(range(NCORES)))
    out = np.zeros((B, S, D), np.float32)
    for c in range(NCORES):
        b = c // 2
        idx = own_token_index(c)
        out[b, idx] = res.results[c]["out"].T
    return out


def build_l1attn_program():
    nc = bass.Bass("TRN2", target_bir_lowering=False)
    gi = lambda nm, shp: nc.dram_tensor(nm, shp, F32, kind="ExternalInput").ap()
    x = gi("x", [D, TOK])
    wq = gi("wq", [128, 16 * DC * 128]); bq = gi("bq", [128, 16])
    wk = gi("wk", [128, 16 * DC * 128]); bk = gi("bk", [128, 16])
    wv = gi("wv", [128, 16 * DC * 128]); bvr = gi("bvr", [1, D])
    wg = gi("wg", [128, 16 * DC * 128]); bg = gi("bg", [128, 16])
    wo = gi("wo", [128, DC * DC * 128]); bo = gi("bo", [128, 16])
    csq = gi("csq", [128, TOK]); snq = gi("snq", [128, TOK])
    csk = gi("csk", [128, TOK]); snk = gi("snk", [128, TOK])
    gcol = gi("gcol", [128, 16 * 272])
    arow = gi("arow", [128, 16 * 128])
    mask = gi("mask", [128, 16 * 2 * 128])
    ones = gi("ones", [128, 128])
    kin = nc.dram_tensor("kin", [D, TOK], F32, kind="Internal").ap()
    vin = nc.dram_tensor("vin", [TOK, D], F32, kind="Internal").ap()
    kgo = nc.dram_tensor("kgo", [2 * D, TOK], F32, kind="Internal").ap()
    vgo = nc.dram_tensor("vgo", [2 * TOK, D], F32, kind="Internal").ap()
    out = nc.dram_tensor("out", [D, TOK], F32, kind="ExternalOutput").ap()

    with contextlib.ExitStack() as stack:
        block = stack.enter_context(nc.Block())
        ctx = make_ctx(nc, block, stack)
        emit_l1attn(nc, block, ctx, x, out, wq, bq, wk, bk, wv, bvr, wg, bg,
                    wo, bo, csq, snq, csk, snk, gcol, arow, mask, ones,
                    kin, vin, kgo, vgo, stack)
        @block.gpsimd
        def _(gpsimd):
            gpsimd.wait_ge(ctx.qG.h, ctx.qG.n)
    return nc


def xpos_tables(positions):
    dim, TH, SB = 128, 10000.0, 512.0
    inv_freq = (1.0 / (np.float32(TH) **
                       (np.arange(0, dim, 2, dtype=np.float32) / np.float32(dim))))
    inv_freq = inv_freq.astype(np.float32)
    pos = positions.astype(np.float32)
    fr = (pos[:, None] * inv_freq[None, :]).astype(np.float32)
    cos = np.cos(fr.astype(np.float64))
    sin = np.sin(fr.astype(np.float64))
    sbase = ((np.arange(0, dim, 2, dtype=np.float32) + np.float32(0.4 * dim))
             / np.float32(1.4 * dim)).astype(np.float32)
    power = ((pos - np.float32(S // 2)) / np.float32(SB)).astype(np.float32)
    scale = sbase.astype(np.float64)[None, :] ** power.astype(np.float64)[:, None]
    def lay(m):
        t = np.concatenate([m.T, m.T], axis=0)
        return np.ascontiguousarray(t.astype(np.float32))
    return (lay(cos * scale), lay(sin * scale),
            lay(cos / scale), lay(sin / scale))


def stage_l1attn_shared(Wq, bq, Wk, bk, Wv, bv, Wg, bg, Wo, bo):
    perm = np.concatenate([np.arange(0, 128, 2), np.arange(1, 128, 2)])
    colperm = np.concatenate([h * 128 + perm for h in range(16)])
    Wqp = np.ascontiguousarray(Wq[:, colperm])
    Wkp = np.ascontiguousarray(Wk[:, colperm])
    return {
        "wq": lhsT_dev(Wqp), "bq": vec_tiles(bq[colperm]),
        "wk": lhsT_dev(Wkp), "bk": vec_tiles(bk[colperm]),
        "wv": np.ascontiguousarray(
            Wv.reshape(16, 128, 16, 128).transpose(1, 2, 0, 3).reshape(128, -1)),
        "bvr": np.ascontiguousarray(bv[None, :]),
        "wg": lhsT_dev(Wg), "bg": vec_tiles(bg),
        "wo": lhsT_dev(Wo), "bo": vec_tiles(bo),
        "ones": np.ones((128, 128), np.float32),
    }


def stage_l1attn_core(core):
    par = core % 2
    idx = own_token_index(core)
    csq, snq, csk, snk = xpos_tables(idx)
    gam = (1.0 - 2.0 ** (-5.0 - np.arange(16, dtype=np.float64)))
    p = np.arange(128, dtype=np.float64)
    gcol = np.zeros((128, 16 * 272), np.float64)
    for h in range(16):
        for j in range(16):
            for kt in range(2 * j + 2):
                d = 128.0 * ((2 * j + par) - kt)
                gcol[:, h * 272 + j * j + j + kt] = gam[h] ** (d - p)
    qq = np.arange(128, dtype=np.float64)
    arowt = np.repeat(
        np.concatenate([gam[h] ** qq for h in range(16)])[None, :], 128, axis=0)
    q = np.arange(128, dtype=np.float64)
    mask = np.zeros((128, 16 * 2 * 128), np.float64)
    caus = (q[None, :] >= p[:, None])
    for h in range(16):
        diag = (gam[h] ** (-p))[:, None] * caus
        if par == 0:
            m0, m1 = diag, np.zeros((128, 128))
        else:
            m0 = np.repeat((gam[h] ** (128.0 - p))[:, None], 128, axis=1)
            m1 = diag
        mask[:, (h * 2) * 128:(h * 2 + 1) * 128] = m0
        mask[:, (h * 2 + 1) * 128:(h * 2 + 2) * 128] = m1
    return {
        "csq": csq, "snq": snq, "csk": csk, "snk": snk,
        "gcol": np.ascontiguousarray(gcol.astype(np.float32)),
        "arow": np.ascontiguousarray(arowt.astype(np.float32)),
        "mask": np.ascontiguousarray(mask.astype(np.float32)),
    }


def run_l1attn(x_full, Wq, bq, Wk, bk, Wv, bv, Wg, bg, Wo, bo):
    nc = build_l1attn_program()
    shared = stage_l1attn_shared(Wq, bq, Wk, bk, Wv, bv, Wg, bg, Wo, bo)
    in_maps = []
    for c in range(NCORES):
        b = c // 2
        idx = own_token_index(c)
        in_maps.append({"x": np.ascontiguousarray(x_full[b][idx].T),
                        **stage_l1attn_core(c), **shared})
    res = run_bass_kernel_spmd(nc, in_maps, core_ids=list(range(NCORES)))
    out = np.zeros((B, S, D), np.float32)
    for c in range(NCORES):
        b = c // 2
        idx = own_token_index(c)
        out[b, idx] = res.results[c]["out"].T
    return out


def hadamard128():
    H = np.array([[1.0]], np.float32)
    for _ in range(7):
        H = np.block([[H, H], [H, -H]])
    return np.ascontiguousarray(H.astype(np.float32))


def build_l0ffn_program():
    nc = bass.Bass("TRN2", target_bir_lowering=False)
    x = nc.dram_tensor("x", [D, TOK], F32, kind="ExternalInput").ap()
    wupt = nc.dram_tensor("wupt", [128, IC * DC * 128], F32, kind="ExternalInput").ap()
    wgtt = nc.dram_tensor("wgtt", [128, IC * DC * 128], F32, kind="ExternalInput").ap()
    wdnt = nc.dram_tensor("wdnt", [128, IC * DC * 128], F32, kind="ExternalInput").ap()
    had = nc.dram_tensor("had", [128, 128], F32, kind="ExternalInput").ap()
    su_ug = nc.dram_tensor("su_ug", [128, DC], F32, kind="ExternalInput").ap()
    su_dp = nc.dram_tensor("su_dp", [128, IC], F32, kind="ExternalInput").ap()
    ws_ug = nc.dram_tensor("ws_ug", [128, 2 * IC], F32, kind="ExternalInput").ap()
    ws_dp = nc.dram_tensor("ws_dp", [128, DC], F32, kind="ExternalInput").ap()
    ones = nc.dram_tensor("ones", [128, 128], F32, kind="ExternalInput").ap()
    out = nc.dram_tensor("out", [D, TOK], F32, kind="ExternalOutput").ap()

    with contextlib.ExitStack() as stack:
        block = stack.enter_context(nc.Block())
        ctx = make_ctx(nc, block, stack)
        emit_l0ffn(nc, block, ctx, x, out, wupt, wgtt, wdnt, had, su_ug,
                   su_dp, ws_ug, ws_dp, ones, stack)
        @block.gpsimd
        def _(gpsimd):
            gpsimd.wait_ge(ctx.qG.h, ctx.qG.n)
    return nc


def stage_l0ffn(SU_ug, SU_dp, Wscale_ug, Wscale_dp, W_up, W_gate, W_down):
    return {
        "wupt": lhsT_dev(np.ascontiguousarray(W_up.T)),
        "wgtt": lhsT_dev(np.ascontiguousarray(W_gate.T)),
        "wdnt": lhsT_dev_k(np.ascontiguousarray(W_down.T)),
        "had": hadamard128(),
        "su_ug": vec_tiles(SU_ug),
        "su_dp": vec_tiles(SU_dp),
        "ws_ug": vec_tiles(Wscale_ug),
        "ws_dp": vec_tiles(Wscale_dp),
        "ones": np.ones((128, 128), np.float32),
    }


def run_l0ffn(x_full, SU_ug, SU_dp, Wscale_ug, Wscale_dp, W_up, W_gate, W_down):
    xf = x_full.reshape(-1, D)
    nc = build_l0ffn_program()
    shared = stage_l0ffn(SU_ug, SU_dp, Wscale_ug, Wscale_dp, W_up, W_gate, W_down)
    in_maps = []
    for c in range(NCORES):
        b = c // 2
        idx = own_token_index(c) + b * S
        xc = np.ascontiguousarray(xf[idx].T)
        in_maps.append({"x": xc, **shared})
    res = run_bass_kernel_spmd(nc, in_maps, core_ids=list(range(NCORES)))
    out = np.zeros((B * S, D), np.float32)
    for c in range(NCORES):
        b = c // 2
        idx = own_token_index(c) + b * S
        out[idx] = res.results[c]["out"].T
    return out.reshape(x_full.shape)


def stage_l1ffn(W0, bias0, W1, bias1):
    return {
        "w0t": lhsT_dev(np.ascontiguousarray(W0.T)),
        "w1t": lhsT_dev_k(np.ascontiguousarray(W1.T)),
        "b0": vec_tiles(bias0),
        "b1": vec_tiles(bias1),
        "ones": np.ones((128, 128), np.float32),
    }


def run_l1ffn(x_full, W0, bias0, W1, bias1):
    xf = x_full.reshape(-1, D)
    nc = build_l1ffn_program()
    shared = stage_l1ffn(W0, bias0, W1, bias1)
    in_maps = []
    for c in range(NCORES):
        b = c // 2
        idx = own_token_index(c) + b * S
        xc = np.ascontiguousarray(xf[idx].T)
        in_maps.append({"x": xc, **shared})
    res = run_bass_kernel_spmd(nc, in_maps, core_ids=list(range(NCORES)))
    out = np.zeros((B * S, D), np.float32)
    for c in range(NCORES):
        b = c // 2
        idx = own_token_index(c) + b * S
        out[idx] = res.results[c]["out"].T
    return out.reshape(x_full.shape)


def build_fused_program():
    nc = bass.Bass("TRN2", target_bir_lowering=False)
    gi = lambda nm, shp: nc.dram_tensor(nm, shp, F32, kind="ExternalInput").ap()
    xk = gi("xk", [D, S])
    xq = gi("xq", [D, TOK])
    wcq = gi("wcq", [128, 6 * DC * 128]); gq = gi("gq", [128, 6])
    wdq = gi("wdq", [128, 16 * 6 * 128]); wdqr = gi("wdqr", [128, 16 * 6 * 64])
    wckv = gi("wckv", [128, 4 * DC * 128]); gkv = gi("gkv", [128, 4])
    wdk = gi("wdk", [128, 16 * 4 * 128]); wdv = gi("wdv", [128, 4 * 16 * 128])
    wkr = gi("wkr", [128, DC * 64]); wo = gi("wo", [128, DC * DC * 128])
    rckv = gi("rckv", [64, S]); rskv = gi("rskv", [64, S])
    rcq = gi("rcq", [64, TOK]); rsq = gi("rsq", [64, TOK])
    ident = gi("ident", [128, 128]); ones = gi("ones", [128, 128])
    wupt = gi("wupt", [128, IC * DC * 128]); wgtt = gi("wgtt", [128, IC * DC * 128])
    wdnt = gi("wdnt", [128, IC * DC * 128]); had = gi("had", [128, 128])
    su_ug = gi("su_ug", [128, DC]); su_dp = gi("su_dp", [128, IC])
    ws_ug = gi("ws_ug", [128, 2 * IC]); ws_dp = gi("ws_dp", [128, DC])
    wq = gi("wq", [128, 16 * DC * 128]); bq = gi("bq", [128, 16])
    wk = gi("wk", [128, 16 * DC * 128]); bk = gi("bk", [128, 16])
    wv = gi("wv", [128, 16 * DC * 128]); bvr = gi("bvr", [1, D])
    wg = gi("wg", [128, 16 * DC * 128]); bg = gi("bg", [128, 16])
    wo1 = gi("wo1", [128, DC * DC * 128]); bo = gi("bo", [128, 16])
    csq = gi("csq", [128, TOK]); snq = gi("snq", [128, TOK])
    csk = gi("csk", [128, TOK]); snk = gi("snk", [128, TOK])
    gcol = gi("gcol", [128, 16 * 272]); arow = gi("arow", [128, 16 * 128])
    mask = gi("mask", [128, 16 * 2 * 128])
    w0t = gi("w0t", [128, IC * DC * 128]); w1t = gi("w1t", [128, IC * DC * 128])
    b0 = gi("b0", [128, IC]); b1 = gi("b1", [128, DC])
    sc = lambda nm, shp: nc.dram_tensor(nm, shp, F32, kind="Internal").ap()
    attn_scr = sc("attn_scr", [D, TOK])
    qcn_scr = sc("qcn_scr", [6 * 128, TOK])
    x1s = sc("x1s", [D, TOK])
    x2s = sc("x2s", [D, TOK])
    x3s = sc("x3s", [D, TOK])
    kin = sc("kin", [D, TOK]); vin = sc("vin", [TOK, D])
    kgo = sc("kgo", [2 * D, TOK]); vgo = sc("vgo", [2 * TOK, D])
    out = nc.dram_tensor("out", [D, TOK], F32, kind="ExternalOutput").ap()

    with contextlib.ExitStack() as stack0:
        block = stack0.enter_context(nc.Block())
        ctxs = [make_ctx(nc, block, stack0) for _ in range(4)]

        with contextlib.ExitStack() as s1:
            emit_l0attn(nc, block, ctxs[0], xk, xq, x1s, wcq, gq, wdq, wdqr,
                        wckv, gkv, wdk, wdv, wkr, wo, rckv, rskv, rcq, rsq,
                        ident, ones, attn_scr, qcn_scr, s1)
        emit_phase_barrier(ctxs[0])

        with contextlib.ExitStack() as s2:
            emit_l0ffn(nc, block, ctxs[1], x1s, x2s, wupt, wgtt, wdnt, had,
                       su_ug, su_dp, ws_ug, ws_dp, ones, s2)
        emit_phase_barrier(ctxs[1])

        with contextlib.ExitStack() as s3:
            emit_l1attn(nc, block, ctxs[2], x2s, x3s, wq, bq, wk, bk, wv, bvr,
                        wg, bg, wo1, bo, csq, snq, csk, snk, gcol, arow, mask,
                        ones, kin, vin, kgo, vgo, s3)
        emit_phase_barrier(ctxs[2])

        with contextlib.ExitStack() as s4:
            emit_l1ffn(nc, block, ctxs[3], x3s, out, w0t, w1t, b0, b1, ones, s4)

        @block.gpsimd
        def _(gpsimd):
            gpsimd.wait_ge(ctxs[3].qG.h, ctxs[3].qG.n)
    return nc


def kernel(**inputs):
    x = np.asarray(inputs["L0_attn_x"], np.float32)
    nc = build_fused_program()
    shared = {}
    shared.update(stage_l0attn_shared(
        inputs["L0_attn_Wcq"], inputs["L0_attn_g_q"], inputs["L0_attn_Wdq"],
        inputs["L0_attn_Wdqr"], inputs["L0_attn_Wckv"], inputs["L0_attn_g_kv"],
        inputs["L0_attn_Wdk"], inputs["L0_attn_Wdv"], inputs["L0_attn_Wkr"],
        inputs["L0_attn_Wo"]))
    shared.update(stage_l0ffn(
        inputs["L0_ffn_SU_ug"], inputs["L0_ffn_SU_dp"], inputs["L0_ffn_Wscale_ug"],
        inputs["L0_ffn_Wscale_dp"], inputs["L0_ffn_W_up"], inputs["L0_ffn_W_gate"],
        inputs["L0_ffn_W_down"]))
    l1a = stage_l1attn_shared(
        inputs["L1_attn_Wq"], inputs["L1_attn_bq"], inputs["L1_attn_Wk"],
        inputs["L1_attn_bk"], inputs["L1_attn_Wv"], inputs["L1_attn_bv"],
        inputs["L1_attn_Wg"], inputs["L1_attn_bg"], inputs["L1_attn_Wo"],
        inputs["L1_attn_bo"])
    l1a["wo1"] = l1a.pop("wo")
    shared.update(l1a)
    shared.update(stage_l1ffn(inputs["L1_ffn_W0"], inputs["L1_ffn_bias0"],
                              inputs["L1_ffn_W1"], inputs["L1_ffn_bias1"]))
    in_maps = []
    for c in range(NCORES):
        b = c // 2
        idx = own_token_index(c)
        rcq, rsq = rope_tables(64, idx, scale=1.0 / np.sqrt(192.0))
        in_maps.append({
            "xk": np.ascontiguousarray(x[b].T),
            "xq": np.ascontiguousarray(x[b][idx].T),
            "rcq": rcq, "rsq": rsq,
            **stage_l1attn_core(c), **shared})
    res = run_bass_kernel_spmd(nc, in_maps, core_ids=list(range(NCORES)))
    outa = np.zeros((B, S, D), np.float32)
    for c in range(NCORES):
        b = c // 2
        idx = own_token_index(c)
        outa[b, idx] = res.results[c]["out"].T
    return outa


def kernel_unfused(**inputs):
    x = np.asarray(inputs["L0_attn_x"], np.float32)
    x1 = run_l0attn(x, inputs["L0_attn_Wcq"], inputs["L0_attn_g_q"],
                    inputs["L0_attn_Wdq"], inputs["L0_attn_Wdqr"],
                    inputs["L0_attn_Wckv"], inputs["L0_attn_g_kv"],
                    inputs["L0_attn_Wdk"], inputs["L0_attn_Wdv"],
                    inputs["L0_attn_Wkr"], inputs["L0_attn_Wo"])
    x2 = run_l0ffn(x1, inputs["L0_ffn_SU_ug"], inputs["L0_ffn_SU_dp"],
                   inputs["L0_ffn_Wscale_ug"], inputs["L0_ffn_Wscale_dp"],
                   inputs["L0_ffn_W_up"], inputs["L0_ffn_W_gate"],
                   inputs["L0_ffn_W_down"])
    x3 = run_l1attn(x2, inputs["L1_attn_Wq"], inputs["L1_attn_bq"],
                    inputs["L1_attn_Wk"], inputs["L1_attn_bk"],
                    inputs["L1_attn_Wv"], inputs["L1_attn_bv"],
                    inputs["L1_attn_Wg"], inputs["L1_attn_bg"],
                    inputs["L1_attn_Wo"], inputs["L1_attn_bo"])
    out = run_l1ffn(x3, inputs["L1_ffn_W0"], inputs["L1_ffn_bias0"],
                    inputs["L1_ffn_W1"], inputs["L1_ffn_bias1"])
    return out


# version 05 (final)
# speedup vs baseline: 1.0879x; 1.0130x over previous
import contextlib
import numpy as np
import concourse.bass as bass
import concourse.mybir as mybir
from concourse.bass_utils import run_bass_kernel_spmd

F32 = mybir.dt.float32
F16 = mybir.dt.float16
BF16 = mybir.dt.bfloat16
AF = mybir.ActivationFunctionType
OP = mybir.AluOpType

NCORES = 8
B, S, D = 4, 4096, 2048
TOK = 2048
NT = TOK // 128
DC = D // 128
INTER = 8192
IC = INTER // 128
N_HEADS = 16
LN_EPS = 1e-5
RMS_EPS = 1e-6
GN_EPS = 1e-5


class Sem:
    __slots__ = ("h", "n")
    def __init__(self, h):
        self.h = h
        self.n = 0

    def inc(self, instr, k=1):
        instr.then_inc(self.h, k)
        self.n += k
        return instr


class Ctx:
    def __init__(self, nc, block, sems):
        self.nc = nc
        self.block = block
        for k, v in sems.items():
            setattr(self, k, v)


_CTX_N = [0]
def make_ctx(nc, block, stack):
    names = ["qS", "qG", "qA", "sPE", "sDVE", "sACT", "sPOOL"]
    pfx = f"c{_CTX_N[0]}_"
    _CTX_N[0] += 1
    sems = {nm: Sem(stack.enter_context(nc.semaphore(pfx + nm))) for nm in names}
    return Ctx(nc, block, sems)


def emit_ln(ctx, T, s_tiles, y_tiles, sq_a, sq_b, ps, pb, ones2, rows,
            eps=LN_EPS):
    block = ctx.block
    r_mu, r_m2, r_A, r_B = rows
    ps_sum = ps[0:1, pb * 512: pb * 512 + T]
    ps_ssq = ps[0:1, (pb + 1) * 512: (pb + 1) * 512 + T]
    ps_A = ps[:, (pb + 2) * 512: (pb + 2) * 512 + T]
    ps_B = ps[:, (pb + 3) * 512: (pb + 3) * 512 + T]
    ones_col = ones2[:, 0:1]
    ones_row = ones2[0:1, :]

    @block.tensor
    def _(tensor):
        tensor.wait_ge(ctx.sDVE.h, ctx.sDVE.n)
        tensor.wait_ge(ctx.sACT.h, ctx.sACT.n)
        for c in range(DC):
            ins = tensor.matmul(ps_sum, ones_col, s_tiles[c],
                                start=(c == 0), stop=(c == DC - 1))
        ctx.sPE.inc(ins)

    for c in range(DC):
        sq = sq_a if c % 2 == 0 else sq_b
        @block.vector
        def _(vector, c=c, sq=sq):
            if c >= 2:
                vector.wait_ge(ctx.sPE.h, ctx.sPE.n)
            ctx.sDVE.inc(vector.tensor_tensor(sq, s_tiles[c], s_tiles[c], OP.mult))

        @block.tensor
        def _(tensor, c=c, sq=sq):
            tensor.wait_ge(ctx.sDVE.h, ctx.sDVE.n)
            ctx.sPE.inc(tensor.matmul(ps_ssq, ones_col, sq,
                                      start=(c == 0), stop=(c == DC - 1)))

    @block.vector
    def _(vector):
        vector.wait_ge(ctx.sPE.h, ctx.sPE.n)
        ctx.sDVE.inc(vector.tensor_scalar(r_mu, ps_sum, 1.0 / D, None, OP.mult))
        ctx.sDVE.inc(vector.tensor_scalar(r_m2, ps_ssq, 1.0 / D, float(eps),
                                          OP.mult, OP.add))
        vector.wait_ge(ctx.sDVE.h, ctx.sDVE.n)
        ctx.sDVE.inc(vector.tensor_tensor(r_A, r_mu, r_mu, OP.mult))
        vector.wait_ge(ctx.sDVE.h, ctx.sDVE.n)
        ctx.sDVE.inc(vector.tensor_tensor(r_m2, r_m2, r_A, OP.subtract))

    @block.scalar
    def _(scalar):
        scalar.wait_ge(ctx.sDVE.h, ctx.sDVE.n)
        ctx.sACT.inc(scalar.activation(r_A, r_m2, AF.Ln, bias=0.0, scale=1.0))
        scalar.wait_ge(ctx.sACT.h, ctx.sACT.n)
        ctx.sACT.inc(scalar.activation(r_A, r_A, AF.Exp, bias=0.0, scale=-0.5))

    @block.vector
    def _(vector):
        vector.wait_ge(ctx.sACT.h, ctx.sACT.n)
        ctx.sDVE.inc(vector.scalar_tensor_tensor(r_B, r_mu, -1.0, r_A,
                                                 OP.mult, OP.mult))

    @block.tensor
    def _(tensor):
        tensor.wait_ge(ctx.sDVE.h, ctx.sDVE.n)
        tensor.matmul(ps_A, ones_row, r_A, start=True, stop=True)
        ctx.sPE.inc(tensor.matmul(ps_B, ones_row, r_B, start=True, stop=True))

    for c in range(DC):
        @block.vector
        def _(vector, c=c):
            if c == 0:
                vector.wait_ge(ctx.sPE.h, ctx.sPE.n)
            vector.tensor_tensor(y_tiles[c], s_tiles[c], ps_A, OP.mult)
            ctx.sDVE.inc(vector.tensor_tensor(y_tiles[c], y_tiles[c], ps_B, OP.add))


def emit_l1ffn(nc, block, ctx, x_dram, out_dram, w0t_dram, w1t_dram,
               b0_dram, b1_dram, ones_dram, stack):
    T = 512
    NCH = TOK // T
    xs = stack.enter_context(nc.sbuf_tensor("ff_x", [128, DC * T], F32))
    xr = stack.enter_context(nc.sbuf_tensor("ff_xr", [128, DC * T], F32))
    x16 = stack.enter_context(nc.sbuf_tensor("ff_x16", [128, T], F16))
    d16 = stack.enter_context(nc.sbuf_tensor("ff_d16", [128, IC * T], F16))
    d32 = stack.enter_context(nc.sbuf_tensor("ff_d32", [128, 2 * T], F32))
    e32 = stack.enter_context(nc.sbuf_tensor("ff_e32", [128, T], F32))
    wup = stack.enter_context(nc.sbuf_tensor("ff_w0", [128, 2 * DC * 128], F32))
    wdn = stack.enter_context(nc.sbuf_tensor("ff_w1", [128, 2 * DC * 128], F32))
    e16 = stack.enter_context(nc.sbuf_tensor("ff_e16", [128, DC * T], F16))
    b0s = stack.enter_context(nc.sbuf_tensor("ff_b0", [128, IC], F32))
    b1s = stack.enter_context(nc.sbuf_tensor("ff_b1", [128, DC], F32))
    ones2 = stack.enter_context(nc.sbuf_tensor("ff_ones", [128, 128], F32))
    sq_a = stack.enter_context(nc.sbuf_tensor("ff_sqa", [128, T], F32))
    sq_b = stack.enter_context(nc.sbuf_tensor("ff_sqb", [128, T], F32))
    rows_t = stack.enter_context(nc.sbuf_tensor("ff_rows", [1, 4 * T], F32))
    ps = stack.enter_context(nc.psum_tensor("ff_ps", [128, 8 * 512], F32))

    rows = [rows_t[0:1, i * T:(i + 1) * T] for i in range(4)]

    @block.sync
    def _(sync):
        ctx.qS.inc(sync.dma_start(out=b0s[:], in_=b0_dram[:, :]), 16)
        ctx.qS.inc(sync.dma_start(out=b1s[:], in_=b1_dram[:, :]), 16)
        ctx.qS.inc(sync.dma_start(out=ones2[:], in_=ones_dram[:, :]), 16)

    for ch in range(NCH):
        t0 = ch * T
        dve_resid_done = ctx.sDVE.n
        @block.sync
        def _(sync):
            sync.wait_ge(ctx.sDVE.h, dve_resid_done)
            for c in range(DC):
                ctx.qS.inc(sync.dma_start(
                    out=xs[:, c * T:(c + 1) * T],
                    in_=x_dram[c * 128:(c + 1) * 128, t0:t0 + T]), 16)

        qg_store_done = ctx.qG.n
        for c in range(DC):
            @block.vector
            def _(vector, c=c):
                if c == 0:
                    vector.wait_ge(ctx.qS.h, ctx.qS.n)
                    vector.wait_ge(ctx.qG.h, qg_store_done)
                vector.tensor_copy(x16[:], xs[:, c * T:(c + 1) * T])
                ctx.sDVE.inc(vector.tensor_copy(xr[:, c * T:(c + 1) * T], x16[:]))

        pe_base = ctx.sPE.n
        act_base = ctx.sACT.n
        dve_ready = ctx.sDVE.n
        for o in range(IC):
            half = o % 2
            wdst = wup[:, half * DC * 128:(half + 1) * DC * 128]
            pslice = ps[:, (o % 4) * 512:(o % 4) * 512 + T]

            @block.sync
            def _(sync, o=o, wdst=wdst):
                if o >= 2:
                    sync.wait_ge(ctx.sPE.h, pe_base + o - 1)
                else:
                    sync.wait_ge(ctx.sPE.h, pe_base)
                ctx.qS.inc(sync.dma_start(
                    out=wdst,
                    in_=w0t_dram[:, o * DC * 128:(o + 1) * DC * 128]), 16)

            @block.tensor
            def _(tensor, o=o, wdst=wdst, pslice=pslice):
                tensor.wait_ge(ctx.qS.h, ctx.qS.n)
                if o == 0:
                    tensor.wait_ge(ctx.sDVE.h, dve_ready)
                if o >= 4:
                    tensor.wait_ge(ctx.sACT.h, act_base + o - 3)
                for c in range(DC):
                    ins = tensor.matmul(pslice, wdst[:, c * 128:(c + 1) * 128],
                                        xr[:, c * T:(c + 1) * T],
                                        start=(c == 0), stop=(c == DC - 1))
                ctx.sPE.inc(ins)

            @block.scalar
            def _(scalar, o=o, pslice=pslice):
                scalar.wait_ge(ctx.sPE.h, pe_base + o + 1)
                ctx.sACT.inc(scalar.activation(
                    d16[:, o * T:(o + 1) * T], pslice, AF.Gelu,
                    bias=b0s[:, o:o + 1], scale=1.0))

        for ph in range(2):
            pe2_base = ctx.sPE.n
            act2_base = ctx.sACT.n
            for i in range(IC):
                half = i % 2
                dslice = d32[:, half * T:(half + 1) * T]
                whalf = wdn[:, half * 8 * 128:(half + 1) * 8 * 128]

                @block.sync
                def _(sync, i=i, whalf=whalf, ph=ph):
                    if i >= 2:
                        sync.wait_ge(ctx.sPE.h, pe2_base + i - 1)
                    else:
                        sync.wait_ge(ctx.sPE.h, pe2_base)
                    ctx.qS.inc(sync.dma_start(
                        out=whalf,
                        in_=w1t_dram[:, (i * DC + ph * 8) * 128:
                                        (i * DC + ph * 8 + 8) * 128]), 16)

                @block.vector
                def _(vector, i=i, dslice=dslice):
                    if i == 0:
                        vector.wait_ge(ctx.sACT.h, act2_base)
                    if i >= 2:
                        vector.wait_ge(ctx.sPE.h, pe2_base + i - 1)
                    ctx.sDVE.inc(vector.tensor_copy(dslice, d16[:, i * T:(i + 1) * T]))

                @block.tensor
                def _(tensor, i=i, dslice=dslice, whalf=whalf):
                    tensor.wait_ge(ctx.qS.h, ctx.qS.n)
                    tensor.wait_ge(ctx.sDVE.h, ctx.sDVE.n)
                    for e in range(8):
                        ins = tensor.matmul(ps[:, e * 512:e * 512 + T],
                                            whalf[:, e * 128:(e + 1) * 128],
                                            dslice,
                                            start=(i == 0), stop=(i == IC - 1))
                    ctx.sPE.inc(ins)

            pe3 = ctx.sPE.n
            for e in range(8):
                eg = ph * 8 + e
                @block.scalar
                def _(scalar, e=e, eg=eg):
                    if e == 0:
                        scalar.wait_ge(ctx.sPE.h, pe3)
                    ctx.sACT.inc(scalar.activation(
                        e16[:, eg * T:(eg + 1) * T], ps[:, e * 512:e * 512 + T],
                        AF.Identity, bias=b1s[:, eg:eg + 1], scale=1.0))

        act3 = ctx.sACT.n
        for c in range(DC):
            @block.vector
            def _(vector, c=c):
                if c == 0:
                    vector.wait_ge(ctx.sACT.h, act3)
                vector.tensor_copy(e32[:], e16[:, c * T:(c + 1) * T])
                ctx.sDVE.inc(vector.tensor_tensor(
                    xr[:, c * T:(c + 1) * T], xs[:, c * T:(c + 1) * T],
                    e32[:], OP.add))

        s_tiles = [xr[:, c * T:(c + 1) * T] for c in range(DC)]
        emit_ln(ctx, T, s_tiles, s_tiles, sq_a[:], sq_b[:], ps, 0, ones2, rows)

        dve_done = ctx.sDVE.n
        @block.gpsimd
        def _(gpsimd):
            gpsimd.wait_ge(ctx.sDVE.h, dve_done)
            for c in range(DC):
                ctx.qG.inc(gpsimd.dma_start(
                    out=out_dram[c * 128:(c + 1) * 128, t0:t0 + T],
                    in_=xr[:, c * T:(c + 1) * T]), 16)


def emit_phase_barrier(ctx):
    block = ctx.block
    sems = [ctx.qS, ctx.qG, ctx.sPE, ctx.sDVE, ctx.sACT, ctx.sPOOL]
    counts = [(s, s.n) for s in sems]
    for eng in ("sync", "tensor", "vector", "scalar", "gpsimd"):
        @getattr(block, eng)
        def _(e):
            for s, n in counts:
                e.wait_ge(s.h, n)


def emit_rms(ctx, T, tiles, out_tiles, g_cols, nch, ps, pb, ones2, row,
             sq_a, sq_b, eps=RMS_EPS):
    block = ctx.block
    ps_ssq = ps[0:1, pb * 512: pb * 512 + T]
    ps_bc = ps[:, (pb + 1) * 512: (pb + 1) * 512 + T]
    ones_col = ones2[:, 0:1]
    ones_row = ones2[0:1, :]
    n = len(tiles)

    for c in range(n):
        @block.vector
        def _(vector, c=c):
            if c >= 2:
                vector.wait_ge(ctx.sPE.h, ctx.sPE.n)
            sq = sq_a if c % 2 == 0 else sq_b
            ctx.sDVE.inc(vector.tensor_tensor(sq, tiles[c], tiles[c], OP.mult))

        @block.tensor
        def _(tensor, c=c):
            tensor.wait_ge(ctx.sDVE.h, ctx.sDVE.n)
            sq = sq_a if c % 2 == 0 else sq_b
            ctx.sPE.inc(tensor.matmul(ps_ssq, ones_col, sq,
                                      start=(c == 0), stop=(c == n - 1)))

    @block.vector
    def _(vector):
        vector.wait_ge(ctx.sPE.h, ctx.sPE.n)
        ctx.sDVE.inc(vector.tensor_scalar(row, ps_ssq, 1.0 / (nch * 128.0),
                                          float(eps), OP.mult, OP.add))

    @block.scalar
    def _(scalar):
        scalar.wait_ge(ctx.sDVE.h, ctx.sDVE.n)
        ctx.sACT.inc(scalar.activation(row, row, AF.Ln, bias=0.0, scale=1.0))
        scalar.wait_ge(ctx.sACT.h, ctx.sACT.n)
        ctx.sACT.inc(scalar.activation(row, row, AF.Exp, bias=0.0, scale=-0.5))

    @block.tensor
    def _(tensor):
        tensor.wait_ge(ctx.sACT.h, ctx.sACT.n)
        ctx.sPE.inc(tensor.matmul(ps_bc, ones_row, row, start=True, stop=True))

    for c in range(n):
        @block.vector
        def _(vector, c=c):
            if c == 0:
                vector.wait_ge(ctx.sPE.h, ctx.sPE.n)
            ctx.sDVE.inc(vector.scalar_tensor_tensor(
                out_tiles[c], tiles[c], g_cols[c], ps_bc, OP.mult, OP.mult))


def emit_butterflies(ctx, tiles, tmp_d, tmp_p):
    block = ctx.block
    n = len(tiles)
    stages = int(np.log2(n))
    for s in range(stages):
        h = 1 << s
        pairs = [(i, i + h) for i in range(n) if not (i & h)]
        half = len(pairs) // 2
        dve_pairs, pool_pairs = pairs[:half], pairs[half:]

        @block.vector
        def _(vector, dve_pairs=dve_pairs, s=s):
            vector.wait_ge(ctx.sPOOL.h, ctx.sPOOL.n)
            for a, b_ in dve_pairs:
                vector.tensor_tensor(tmp_d, tiles[a], tiles[b_], OP.subtract)
                vector.tensor_tensor(tiles[a], tiles[a], tiles[b_], OP.add)
                ins = vector.tensor_copy(tiles[b_], tmp_d)
            ctx.sDVE.inc(ins)

        @block.gpsimd
        def _(gpsimd, pool_pairs=pool_pairs, s=s):
            gpsimd.wait_ge(ctx.sDVE.h, ctx.sDVE.n - 1)
            for a, b_ in pool_pairs:
                gpsimd.tensor_tensor(tmp_p, tiles[a], tiles[b_], OP.subtract)
                gpsimd.tensor_tensor(tiles[a], tiles[a], tiles[b_], OP.add)
                ins = gpsimd.tensor_copy(tiles[b_], tmp_p)
            ctx.sPOOL.inc(ins)

    @block.vector
    def _(vector):
        vector.wait_ge(ctx.sPOOL.h, ctx.sPOOL.n)
        ins = vector.tensor_copy(tmp_d[:, 0:1], tmp_d[:, 0:1])
        ctx.sDVE.inc(ins)


def emit_l0ffn(nc, block, ctx, x_dram, out_dram, wupt_dram, wgtt_dram,
               wdnt_dram, had_dram, su_ug_dram, su_dp_dram, ws_ug_dram,
               ws_dp_dram, ones_dram, stack):
    T = 256
    NCH = TOK // T
    xs = stack.enter_context(nc.sbuf_tensor("f0_x", [128, DC * T], F32))
    z = stack.enter_context(nc.sbuf_tensor("f0_z", [128, DC * T], F32))
    hbuf = stack.enter_context(nc.sbuf_tensor("f0_h", [128, IC * T], F32))
    tmp_d = stack.enter_context(nc.sbuf_tensor("f0_td", [128, T], F32))
    tmp_p = stack.enter_context(nc.sbuf_tensor("f0_tp", [128, T], F32))
    tmpg = stack.enter_context(nc.sbuf_tensor("f0_tg", [128, T], F32))
    silu_sb = stack.enter_context(nc.sbuf_tensor("f0_sl", [128, T], F32))
    hadt = stack.enter_context(nc.sbuf_tensor("f0_ht", [128, DC * 128], F32))
    hraw = stack.enter_context(nc.sbuf_tensor("f0_hr", [128, 128], F32))
    wup = stack.enter_context(nc.sbuf_tensor("f0_wu", [128, 2 * DC * 128], F32))
    wgt = stack.enter_context(nc.sbuf_tensor("f0_wg", [128, 2 * DC * 128], F32))
    wdn = stack.enter_context(nc.sbuf_tensor("f0_wd", [128, 2 * 8 * 128], F32))
    su_ug = stack.enter_context(nc.sbuf_tensor("f0_su", [128, DC], F32))
    vsu = stack.enter_context(nc.sbuf_tensor("f0_vsu", [128, IC], F32))
    vsg = stack.enter_context(nc.sbuf_tensor("f0_vsg", [128, IC], F32))
    vsd = stack.enter_context(nc.sbuf_tensor("f0_vsd", [128, DC], F32))
    vtmp = stack.enter_context(nc.sbuf_tensor("f0_vt", [128, 2 * IC], F32))
    ones2 = stack.enter_context(nc.sbuf_tensor("f0_on", [128, 128], F32))
    sq_a = stack.enter_context(nc.sbuf_tensor("f0_qa", [128, T], F32))
    sq_b = stack.enter_context(nc.sbuf_tensor("f0_qb", [128, T], F32))
    rows_t = stack.enter_context(nc.sbuf_tensor("f0_rw", [1, 4 * T], F32))
    ps = stack.enter_context(nc.psum_tensor("f0_ps", [128, 8 * 512], F32))

    rows = [rows_t[0:1, i * T:(i + 1) * T] for i in range(4)]
    htile = [hbuf[:, k * T:(k + 1) * T] for k in range(IC)]
    ztile = [z[:, k * T:(k + 1) * T] for k in range(DC)]

    @block.sync
    def _(sync):
        for dst, src in [(hraw[:], had_dram[:, :]), (su_ug[:], su_ug_dram[:, :]),
                         (vtmp[:], ws_ug_dram[:, :]),
                         (vsd[:], ws_dp_dram[:, :]), (vsu[:], su_dp_dram[:, :]),
                         (ones2[:], ones_dram[:, :])]:
            ctx.qS.inc(sync.dma_start(out=dst, in_=src), 16)

    @block.vector
    def _(vector):
        vector.wait_ge(ctx.qS.h, ctx.qS.n)
        for k in range(DC):
            vector.tensor_scalar(hadt[:, k * 128:(k + 1) * 128], hraw[:],
                                 su_ug[:, k:k + 1], None, OP.mult)
        vector.tensor_scalar(vsg[:], vtmp[:, IC:2 * IC], 1.0 / np.sqrt(2048.0),
                             None, OP.mult)
        vector.scalar_tensor_tensor(vsu[:], vtmp[:, 0:IC], 1.0 / np.sqrt(2048.0),
                                    vsu[:], OP.mult, OP.mult)
        ctx.sDVE.inc(vector.tensor_scalar(vsd[:], vsd[:], 1.0 / np.sqrt(8192.0),
                                          None, OP.mult))

    for ch in range(NCH):
        t0 = ch * T
        dve_prev = ctx.sDVE.n
        @block.sync
        def _(sync):
            sync.wait_ge(ctx.sDVE.h, dve_prev)
            for c in range(DC):
                ctx.qS.inc(sync.dma_start(
                    out=xs[:, c * T:(c + 1) * T],
                    in_=x_dram[c * 128:(c + 1) * 128, t0:t0 + T]), 16)

        pe_b = ctx.sPE.n
        dve_b = ctx.sDVE.n
        for k in range(DC):
            @block.tensor
            def _(tensor, k=k):
                if k == 0:
                    tensor.wait_ge(ctx.qS.h, ctx.qS.n)
                    tensor.wait_ge(ctx.sDVE.h, dve_b)
                if k >= 4:
                    tensor.wait_ge(ctx.sDVE.h, dve_b + k - 3)
                ctx.sPE.inc(tensor.matmul(ps[:, (k % 4) * 512:(k % 4) * 512 + T],
                                          hadt[:, k * 128:(k + 1) * 128],
                                          xs[:, k * T:(k + 1) * T],
                                          start=True, stop=True))

            @block.vector
            def _(vector, k=k):
                if k == 0:
                    vector.wait_ge(ctx.qG.h, ctx.qG.n)
                vector.wait_ge(ctx.sPE.h, pe_b + k + 1)
                ctx.sDVE.inc(vector.tensor_copy(
                    ztile[k], ps[:, (k % 4) * 512:(k % 4) * 512 + T]))

        emit_butterflies(ctx, ztile, tmp_d[:], tmp_p[:])

        pe_b = ctx.sPE.n
        dve_b = ctx.sDVE.n
        act_b = ctx.sACT.n
        for o in range(IC):
            half = o % 2
            wud = wup[:, half * DC * 128:(half + 1) * DC * 128]
            wgd = wgt[:, half * DC * 128:(half + 1) * DC * 128]
            bu = ps[:, (o % 4) * 512:(o % 4) * 512 + T]
            bg = ps[:, (4 + o % 4) * 512:(4 + o % 4) * 512 + T]

            @block.sync
            def _(sync, o=o, wud=wud, wgd=wgd):
                sync.wait_ge(ctx.sPE.h, pe_b + max(0, (o - 1) * 2))
                ctx.qS.inc(sync.dma_start(
                    out=wud, in_=wupt_dram[:, o * DC * 128:(o + 1) * DC * 128]), 16)
                ctx.qS.inc(sync.dma_start(
                    out=wgd, in_=wgtt_dram[:, o * DC * 128:(o + 1) * DC * 128]), 16)

            @block.tensor
            def _(tensor, o=o, wud=wud, wgd=wgd, bu=bu, bg=bg):
                tensor.wait_ge(ctx.qS.h, ctx.qS.n)
                if o == 0:
                    tensor.wait_ge(ctx.sDVE.h, dve_b)
                    tensor.wait_ge(ctx.sPOOL.h, ctx.sPOOL.n)
                if o >= 4:
                    tensor.wait_ge(ctx.sDVE.h, dve_b + 2 * (o - 3))
                for c in range(DC):
                    ins = tensor.matmul(bu, wud[:, c * 128:(c + 1) * 128],
                                        ztile[c], start=(c == 0), stop=(c == DC - 1))
                ctx.sPE.inc(ins)
                for c in range(DC):
                    ins = tensor.matmul(bg, wgd[:, c * 128:(c + 1) * 128],
                                        ztile[c], start=(c == 0), stop=(c == DC - 1))
                ctx.sPE.inc(ins)

            @block.vector
            def _(vector, o=o, bg=bg):
                if o == 0:
                    vector.wait_ge(ctx.sDVE.h, dve_b)
                vector.wait_ge(ctx.sPE.h, pe_b + 2 * (o + 1))
                vector.wait_ge(ctx.sACT.h, ctx.sACT.n)
                ctx.sDVE.inc(vector.tensor_scalar(tmpg[:], bg, vsg[:, o:o + 1],
                                                  None, OP.mult))

            @block.scalar
            def _(scalar, o=o):
                scalar.wait_ge(ctx.sDVE.h, ctx.sDVE.n)
                ctx.sACT.inc(scalar.activation(silu_sb[:], tmpg[:], AF.Silu,
                                               bias=0.0, scale=1.0))

            @block.vector
            def _(vector, o=o, bu=bu):
                vector.wait_ge(ctx.sACT.h, ctx.sACT.n)
                ctx.sDVE.inc(vector.scalar_tensor_tensor(
                    htile[o], bu, vsu[:, o:o + 1], silu_sb[:], OP.mult, OP.mult))

        pe_b = ctx.sPE.n
        dve_b = ctx.sDVE.n
        for k in range(IC):
            @block.tensor
            def _(tensor, k=k):
                if k == 0:
                    tensor.wait_ge(ctx.sDVE.h, dve_b)
                if k >= 4:
                    tensor.wait_ge(ctx.sDVE.h, dve_b + k - 3)
                ctx.sPE.inc(tensor.matmul(ps[:, (k % 4) * 512:(k % 4) * 512 + T],
                                          hraw[:], htile[k],
                                          start=True, stop=True))

            @block.vector
            def _(vector, k=k):
                vector.wait_ge(ctx.sPE.h, pe_b + k + 1)
                ctx.sDVE.inc(vector.tensor_copy(
                    htile[k], ps[:, (k % 4) * 512:(k % 4) * 512 + T]))

        emit_butterflies(ctx, htile, tmp_d[:], tmp_p[:])

        for phd in range(2):
            pe2 = ctx.sPE.n
            for i in range(IC):
                half = i % 2
                whalf = wdn[:, half * 8 * 128:(half + 1) * 8 * 128]

                @block.sync
                def _(sync, i=i, whalf=whalf, phd=phd):
                    if i >= 2:
                        sync.wait_ge(ctx.sPE.h, pe2 + i - 1)
                    else:
                        sync.wait_ge(ctx.sPE.h, pe2)
                    ctx.qS.inc(sync.dma_start(
                        out=whalf,
                        in_=wdnt_dram[:, (i * DC + phd * 8) * 128:
                                         (i * DC + phd * 8 + 8) * 128]), 16)

                @block.tensor
                def _(tensor, i=i, whalf=whalf, phd=phd):
                    tensor.wait_ge(ctx.qS.h, ctx.qS.n)
                    if i == 0:
                        tensor.wait_ge(ctx.sDVE.h, ctx.sDVE.n)
                        tensor.wait_ge(ctx.sPOOL.h, ctx.sPOOL.n)
                        if phd == 1:
                            tensor.wait_ge(ctx.sACT.h, ctx.sACT.n)
                    for e in range(8):
                        ins = tensor.matmul(ps[:, e * 512:e * 512 + T],
                                            whalf[:, e * 128:(e + 1) * 128],
                                            htile[i],
                                            start=(i == 0), stop=(i == IC - 1))
                    ctx.sPE.inc(ins)

            pe3 = ctx.sPE.n
            for e in range(8):
                eg = phd * 8 + e
                @block.vector
                def _(vector, e=e, eg=eg):
                    if e == 0:
                        vector.wait_ge(ctx.sPE.h, pe3)
                    ctx.sDVE.inc(vector.scalar_tensor_tensor(
                        ztile[eg], ps[:, e * 512:e * 512 + T], vsd[:, eg:eg + 1],
                        xs[:, eg * T:(eg + 1) * T], OP.mult, OP.add))

        emit_ln(ctx, T, ztile, ztile, sq_a[:], sq_b[:], ps, 0, ones2, rows)

        dve_done = ctx.sDVE.n
        @block.gpsimd
        def _(gpsimd):
            gpsimd.wait_ge(ctx.sDVE.h, dve_done)
            for c in range(DC):
                ctx.qG.inc(gpsimd.dma_start(
                    out=out_dram[c * 128:(c + 1) * 128, t0:t0 + T],
                    in_=z[:, c * T:(c + 1) * T]), 16)


KB8 = S // 512
KB128 = S // 128


def emit_l0attn(nc, block, ctx, xk_dram, xq_dram, out_dram, wcq_dram, gq_dram,
                wdq_dram, wdqr_dram, wckv_dram, gkv_dram, wdk_dram, wdv_dram,
                wkr_dram, wo_dram, rckv_dram, rskv_dram, rcq_dram, rsq_dram,
                ident_dram, ones_dram, attn_scr, qcn_scr, stack):
    T = 512
    kvc = stack.enter_context(nc.sbuf_tensor("a0_kvc", [128, 4 * S], F32))
    krs = stack.enter_context(nc.sbuf_tensor("a0_kr", [64, S], F32))
    ones2 = stack.enter_context(nc.sbuf_tensor("a0_on", [128, 128], F32))
    ident = stack.enter_context(nc.sbuf_tensor("a0_id", [128, 128], F32))
    wkr = stack.enter_context(nc.sbuf_tensor("a0_wkr", [128, DC * 64], F32))
    gq = stack.enter_context(nc.sbuf_tensor("a0_gq", [128, 6], F32))
    gkv = stack.enter_context(nc.sbuf_tensor("a0_gkv", [128, 4], F32))
    rows_t = stack.enter_context(nc.sbuf_tensor("a0_rw", [1, 4 * T], F32))
    ps = stack.enter_context(nc.psum_tensor("a0_ps", [128, 8 * 512], F32))

    rows = [rows_t[0:1, i * T:(i + 1) * T] for i in range(4)]
    kvct = [kvc[:, c * S:(c + 1) * S] for c in range(4)]

    @block.sync
    def _(sync):
        for dst, src in [(ones2[:], ones_dram[:, :]), (ident[:], ident_dram[:, :]),
                         (wkr[:], wkr_dram[:, :]), (gq[:], gq_dram[:, :]),
                         (gkv[:], gkv_dram[:, :])]:
            ctx.qS.inc(sync.dma_start(out=dst, in_=src), 16)

    with contextlib.ExitStack() as p1:
        xs = p1.enter_context(nc.sbuf_tensor("a1_x", [128, DC * T], F32))
        wbuf = p1.enter_context(nc.sbuf_tensor("a1_w", [128, 2 * DC * 128], F32))
        rck = p1.enter_context(nc.sbuf_tensor("a1_rc", [64, S], F32))
        rsk = p1.enter_context(nc.sbuf_tensor("a1_rs", [64, S], F32))
        sq_a = p1.enter_context(nc.sbuf_tensor("a1_qa", [128, T], F32))
        sq_b = p1.enter_context(nc.sbuf_tensor("a1_qb", [128, T], F32))
        rtmp = p1.enter_context(nc.sbuf_tensor("a1_rt", [64, T], F32))

        @block.sync
        def _(sync):
            ctx.qS.inc(sync.dma_start(out=rck[:], in_=rckv_dram[:, :]), 16)
            ctx.qS.inc(sync.dma_start(out=rsk[:], in_=rskv_dram[:, :]), 16)

        for ch in range(KB8):
            t0 = ch * T
            dve_prev = ctx.sDVE.n
            @block.sync
            def _(sync):
                sync.wait_ge(ctx.sDVE.h, dve_prev)
                for c in range(DC):
                    ctx.qS.inc(sync.dma_start(
                        out=xs[:, c * T:(c + 1) * T],
                        in_=xk_dram[c * 128:(c + 1) * 128, t0:t0 + T]), 16)

            pe_b = ctx.sPE.n
            dve_b = ctx.sDVE.n
            for o in range(4):
                half = o % 2
                wdst = wbuf[:, half * DC * 128:(half + 1) * DC * 128]
                @block.sync
                def _(sync, o=o, wdst=wdst):
                    sync.wait_ge(ctx.sPE.h, pe_b + max(0, o - 1))
                    ctx.qS.inc(sync.dma_start(
                        out=wdst, in_=wckv_dram[:, o * DC * 128:(o + 1) * DC * 128]), 16)

                @block.tensor
                def _(tensor, o=o, wdst=wdst):
                    tensor.wait_ge(ctx.qS.h, ctx.qS.n)
                    if o == 0:
                        tensor.wait_ge(ctx.sDVE.h, dve_b)
                    for c in range(DC):
                        ins = tensor.matmul(ps[:, o * 512:o * 512 + T],
                                            wdst[:, c * 128:(c + 1) * 128],
                                            xs[:, c * T:(c + 1) * T],
                                            start=(c == 0), stop=(c == DC - 1))
                    ctx.sPE.inc(ins)

                @block.vector
                def _(vector, o=o):
                    vector.wait_ge(ctx.sPE.h, pe_b + o + 1)
                    ctx.sDVE.inc(vector.tensor_copy(
                        kvct[o][:, t0:t0 + T], ps[:, o * 512:o * 512 + T]))

            @block.tensor
            def _(tensor):
                for c in range(DC):
                    ins = tensor.matmul(ps[0:64, 4 * 512:4 * 512 + T],
                                        wkr[:, c * 64:(c + 1) * 64],
                                        xs[:, c * T:(c + 1) * T],
                                        start=(c == 0), stop=(c == DC - 1))
                ctx.sPE.inc(ins)

            @block.vector
            def _(vector, t0=t0):
                vector.wait_ge(ctx.sPE.h, ctx.sPE.n)
                pk = ps[0:64, 4 * 512:4 * 512 + T]
                ctx.sDVE.inc(vector.tensor_tensor(rtmp[0:32, :], pk[0:32, :],
                                                  rck[0:32, t0:t0 + T], OP.mult))
                ctx.sDVE.inc(vector.tensor_tensor(rtmp[32:64, :], pk[32:64, :],
                                                  rck[32:64, t0:t0 + T], OP.mult))
                vector.wait_ge(ctx.sDVE.h, ctx.sDVE.n)
                ctx.sDVE.inc(vector.tensor_tensor(krs[32:64, t0:t0 + T], pk[0:32, :],
                                                  rsk[32:64, t0:t0 + T], OP.mult))
                ctx.sDVE.inc(vector.tensor_tensor(sq_a[0:32, :], pk[32:64, :],
                                                  rsk[0:32, t0:t0 + T], OP.mult))
                vector.wait_ge(ctx.sDVE.h, ctx.sDVE.n)
                ctx.sDVE.inc(vector.tensor_tensor(krs[0:32, t0:t0 + T], rtmp[0:32, :],
                                                  sq_a[0:32, :], OP.subtract))
                ctx.sDVE.inc(vector.tensor_tensor(krs[32:64, t0:t0 + T],
                                                  krs[32:64, t0:t0 + T],
                                                  rtmp[32:64, :], OP.add))

            emit_rms(ctx, T, [kt[:, t0:t0 + T] for kt in kvct],
                     [kt[:, t0:t0 + T] for kt in kvct],
                     [gkv[:, c:c + 1] for c in range(4)], 4, ps, 5, ones2,
                     rows[0], sq_a[:], sq_b[:])

    emit_phase_barrier(ctx)

    with contextlib.ExitStack() as p2:
        xs = p2.enter_context(nc.sbuf_tensor("a2_x", [128, DC * T], F32))
        wbuf = p2.enter_context(nc.sbuf_tensor("a2_w", [128, 2 * DC * 128], F32))
        sq_a = p2.enter_context(nc.sbuf_tensor("a2_qa", [128, T], F32))
        sq_b = p2.enter_context(nc.sbuf_tensor("a2_qb", [128, T], F32))
        qcs = p2.enter_context(nc.sbuf_tensor("a2_qc", [128, 6 * T], F32))
        qcst = [qcs[:, c * T:(c + 1) * T] for c in range(6)]

        for ch in range(TOK // T):
            t0 = ch * T
            dve_prev = ctx.sDVE.n
            qg_prev = ctx.qG.n
            @block.sync
            def _(sync):
                sync.wait_ge(ctx.sDVE.h, dve_prev)
                for c in range(DC):
                    ctx.qS.inc(sync.dma_start(
                        out=xs[:, c * T:(c + 1) * T],
                        in_=xq_dram[c * 128:(c + 1) * 128, t0:t0 + T]), 16)

            pe_b = ctx.sPE.n
            dve_b = ctx.sDVE.n
            for o in range(6):
                half = o % 2
                wdst = wbuf[:, half * DC * 128:(half + 1) * DC * 128]
                @block.sync
                def _(sync, o=o, wdst=wdst):
                    sync.wait_ge(ctx.sPE.h, pe_b + max(0, o - 1))
                    ctx.qS.inc(sync.dma_start(
                        out=wdst, in_=wcq_dram[:, o * DC * 128:(o + 1) * DC * 128]), 16)

                @block.tensor
                def _(tensor, o=o, wdst=wdst):
                    tensor.wait_ge(ctx.qS.h, ctx.qS.n)
                    if o == 0:
                        tensor.wait_ge(ctx.sDVE.h, dve_b)
                    if o >= 4:
                        tensor.wait_ge(ctx.sDVE.h, dve_b + o - 3)
                    for c in range(DC):
                        ins = tensor.matmul(ps[:, (o % 4) * 512:(o % 4) * 512 + T],
                                            wdst[:, c * 128:(c + 1) * 128],
                                            xs[:, c * T:(c + 1) * T],
                                            start=(c == 0), stop=(c == DC - 1))
                    ctx.sPE.inc(ins)

                @block.vector
                def _(vector, o=o):
                    if o == 0:
                        vector.wait_ge(ctx.qG.h, qg_prev)
                    vector.wait_ge(ctx.sPE.h, pe_b + o + 1)
                    ctx.sDVE.inc(vector.tensor_copy(
                        qcst[o], ps[:, (o % 4) * 512:(o % 4) * 512 + T]))

            emit_rms(ctx, T, qcst, qcst,
                     [gq[:, c:c + 1] for c in range(6)], 6, ps, 4, ones2,
                     rows[0], sq_a[:], sq_b[:])

            dve_done = ctx.sDVE.n
            @block.gpsimd
            def _(gpsimd, t0=t0):
                gpsimd.wait_ge(ctx.sDVE.h, dve_done)
                for c in range(6):
                    ctx.qG.inc(gpsimd.dma_start(
                        out=qcn_scr[c * 128:(c + 1) * 128, t0:t0 + T],
                        in_=qcs[:, c * T:(c + 1) * T]), 16)

    emit_phase_barrier(ctx)

    with contextlib.ExitStack() as p3:
        kh = p3.enter_context(nc.sbuf_tensor("a3_kh", [128, S], F32))
        qcs = p3.enter_context(nc.sbuf_tensor("a3_qc", [128, 6 * T], F32))
        vh = p3.enter_context(nc.sbuf_tensor("a3_vh", [128, S], F32))
        qn = p3.enter_context(nc.sbuf_tensor("a3_qn", [128, TOK], F32))
        qr = p3.enter_context(nc.sbuf_tensor("a3_qr", [64, TOK], F32))
        pbuf = p3.enter_context(nc.sbuf_tensor("a3_p", [128, S], F32))
        whb = p3.enter_context(nc.sbuf_tensor("a3_wh", [128, 2176], F32))
        rcq = p3.enter_context(nc.sbuf_tensor("a3_rc", [64, TOK], F32))
        rsq = p3.enter_context(nc.sbuf_tensor("a3_rs", [64, TOK], F32))
        ostg = p3.enter_context(nc.sbuf_tensor("a3_os", [128, 128], F32))
        mx = p3.enter_context(nc.sbuf_tensor("a3_mx", [128, 16], F32))
        rtmp = p3.enter_context(nc.sbuf_tensor("a3_rt", [64, T], F32))
        rtm2 = p3.enter_context(nc.sbuf_tensor("a3_rt2", [64, T], F32))

        @block.sync
        def _(sync):
            ctx.qS.inc(sync.dma_start(out=rcq[:], in_=rcq_dram[:, :]), 16)
            ctx.qS.inc(sync.dma_start(out=rsq[:], in_=rsq_dram[:, :]), 16)

        for h in range(N_HEADS):
            pe_done = ctx.sPE.n
            @block.sync
            def _(sync, h=h):
                sync.wait_ge(ctx.sPE.h, pe_done)
                ctx.qS.inc(sync.dma_start(
                    out=whb[:, 0:768], in_=wdq_dram[:, h * 768:(h + 1) * 768]), 16)
                ctx.qS.inc(sync.dma_start(
                    out=whb[:, 768:1152], in_=wdqr_dram[:, h * 384:(h + 1) * 384]), 16)
                ctx.qS.inc(sync.dma_start(
                    out=whb[:, 1152:1664], in_=wdk_dram[:, h * 512:(h + 1) * 512]), 16)
                for c in range(4):
                    ctx.qS.inc(sync.dma_start(
                        out=whb[:, 1664 + c * 128:1664 + (c + 1) * 128],
                        in_=wdv_dram[:, (c * 16 + h) * 128:(c * 16 + h + 1) * 128]), 16)

            pe_b = ctx.sPE.n
            dve_b = ctx.sDVE.n
            for kb in range(KB8):
                @block.tensor
                def _(tensor, kb=kb, h=h):
                    tensor.wait_ge(ctx.qS.h, ctx.qS.n)
                    if kb == 0:
                        tensor.wait_ge(ctx.sDVE.h, dve_b)
                    if kb >= 4:
                        tensor.wait_ge(ctx.sDVE.h, dve_b + kb - 3)
                    for c in range(4):
                        ins = tensor.matmul(
                            ps[:, (kb % 4) * 512:(kb % 4) * 512 + T],
                            whb[:, 1152 + c * 128:1152 + (c + 1) * 128],
                            kvct[c][:, kb * T:(kb + 1) * T],
                            start=(c == 0), stop=(c == 3))
                    ctx.sPE.inc(ins)

                @block.vector
                def _(vector, kb=kb):
                    vector.wait_ge(ctx.sPE.h, pe_b + kb + 1)
                    ctx.sDVE.inc(vector.tensor_copy(
                        kh[:, kb * T:(kb + 1) * T],
                        ps[:, (kb % 4) * 512:(kb % 4) * 512 + T]))

            pe_b = ctx.sPE.n
            dve_b = ctx.sDVE.n
            for tb in range(KB128):
                @block.tensor
                def _(tensor, tb=tb):
                    if tb == 0:
                        tensor.wait_ge(ctx.sDVE.h, dve_b)
                    if tb >= 4:
                        tensor.wait_ge(ctx.sDVE.h, dve_b + tb - 3)
                    for c in range(4):
                        ins = tensor.matmul(
                            ps[:, (tb % 4) * 512:(tb % 4) * 512 + 128],
                            kvct[c][:, tb * 128:(tb + 1) * 128],
                            whb[:, 1664 + c * 128:1664 + (c + 1) * 128],
                            start=(c == 0), stop=(c == 3))
                    ctx.sPE.inc(ins)

                @block.vector
                def _(vector, tb=tb):
                    vector.wait_ge(ctx.sPE.h, pe_b + tb + 1)
                    ctx.sDVE.inc(vector.tensor_copy(
                        vh[:, tb * 128:(tb + 1) * 128],
                        ps[:, (tb % 4) * 512:(tb % 4) * 512 + 128]))

            pe_b = ctx.sPE.n
            dve_b = ctx.sDVE.n
            for qb in range(TOK // T):
                @block.sync
                def _(sync, qb=qb):
                    if qb > 0:
                        sync.wait_ge(ctx.sPE.h, pe_b + 2 * qb)
                    for c in range(6):
                        ctx.qS.inc(sync.dma_start(
                            out=qcs[:, c * T:(c + 1) * T],
                            in_=qcn_scr[c * 128:(c + 1) * 128, qb * T:(qb + 1) * T]), 16)

                @block.tensor
                def _(tensor, qb=qb):
                    tensor.wait_ge(ctx.qS.h, ctx.qS.n)
                    if qb == 0:
                        tensor.wait_ge(ctx.sDVE.h, dve_b)
                    if qb >= 4:
                        tensor.wait_ge(ctx.sDVE.h, dve_b + 2 * (qb - 3))
                    for c in range(6):
                        ins = tensor.matmul(
                            ps[:, (qb % 4) * 512:(qb % 4) * 512 + T],
                            whb[:, c * 128:(c + 1) * 128],
                            qcs[:, c * T:(c + 1) * T],
                            start=(c == 0), stop=(c == 5))
                    ctx.sPE.inc(ins)
                    for c in range(6):
                        ins = tensor.matmul(
                            ps[0:64, (4 + qb % 4) * 512:(4 + qb % 4) * 512 + T],
                            whb[:, 768 + c * 64:768 + (c + 1) * 64],
                            qcs[:, c * T:(c + 1) * T],
                            start=(c == 0), stop=(c == 5))
                    ctx.sPE.inc(ins)

                @block.vector
                def _(vector, qb=qb):
                    vector.wait_ge(ctx.sPE.h, pe_b + 2 * (qb + 1))
                    t0 = qb * T
                    pq = ps[:, (qb % 4) * 512:(qb % 4) * 512 + T]
                    pr = ps[0:64, (4 + qb % 4) * 512:(4 + qb % 4) * 512 + T]
                    ctx.sDVE.inc(vector.tensor_scalar(
                        qn[:, t0:t0 + T], pq, float(1.0 / np.sqrt(192.0)), None, OP.mult))
                    ctx.sDVE.inc(vector.tensor_tensor(rtmp[0:32, :], pr[0:32, :],
                                                      rcq[0:32, t0:t0 + T], OP.mult))
                    ctx.sDVE.inc(vector.tensor_tensor(rtmp[32:64, :], pr[32:64, :],
                                                      rcq[32:64, t0:t0 + T], OP.mult))
                    vector.wait_ge(ctx.sDVE.h, ctx.sDVE.n)
                    ctx.sDVE.inc(vector.tensor_tensor(qr[32:64, t0:t0 + T], pr[0:32, :],
                                                      rsq[32:64, t0:t0 + T], OP.mult))
                    ctx.sDVE.inc(vector.tensor_tensor(rtm2[0:32, :], pr[32:64, :],
                                                      rsq[0:32, t0:t0 + T], OP.mult))
                    vector.wait_ge(ctx.sDVE.h, ctx.sDVE.n)
                    ctx.sDVE.inc(vector.tensor_tensor(qr[0:32, t0:t0 + T], rtmp[0:32, :],
                                                      rtm2[0:32, :], OP.subtract))
                    ctx.sDVE.inc(vector.tensor_tensor(qr[32:64, t0:t0 + T],
                                                      qr[32:64, t0:t0 + T],
                                                      rtmp[32:64, :], OP.add))

            for qt in range(NT):
                q0 = qt * 128
                dve_now = ctx.sDVE.n
                act_now = ctx.sACT.n
                pe_q0 = ctx.sPE.n
                for kb in range(KB8):
                    @block.tensor
                    def _(tensor, qt=qt, q0=q0, kb=kb):
                        if kb == 0:
                            tensor.wait_ge(ctx.sDVE.h, dve_now)
                            tensor.wait_ge(ctx.sACT.h, act_now)
                        tensor.matmul(ps[:, kb * 512:(kb + 1) * 512],
                                      qn[:, q0:q0 + 128],
                                      kh[:, kb * T:(kb + 1) * T],
                                      start=True, stop=False)
                        ctx.sPE.inc(tensor.matmul(ps[:, kb * 512:(kb + 1) * 512],
                                                  qr[:, q0:q0 + 128],
                                                  krs[:, kb * T:(kb + 1) * T],
                                                  start=False, stop=True))

                    @block.vector
                    def _(vector, kb=kb):
                        vector.wait_ge(ctx.sPE.h, pe_q0 + kb + 1)
                        ctx.sDVE.inc(vector.reduce_max(
                            mx[:, 8 + kb:9 + kb], ps[:, kb * 512:(kb + 1) * 512],
                            axis=mybir.AxisListType.X))

                @block.vector
                def _(vector, qt=qt):
                    vector.wait_ge(ctx.sDVE.h, ctx.sDVE.n)
                    ctx.sDVE.inc(vector.reduce_max(mx[:, 0:1], mx[:, 8:16],
                                                   axis=mybir.AxisListType.X))
                    vector.wait_ge(ctx.sDVE.h, ctx.sDVE.n)
                    ctx.sDVE.inc(vector.tensor_scalar(mx[:, 1:2], mx[:, 0:1],
                                                      -1.0, None, OP.mult))

                @block.scalar
                def _(scalar, qt=qt):
                    scalar.wait_ge(ctx.sDVE.h, ctx.sDVE.n)
                    ctx.sACT.inc(scalar.activation(pbuf[:], ps[:, 0:S], AF.Exp,
                                                   bias=mx[:, 1:2], scale=1.0,
                                                   accum_out=mx[:, 2:3]))

                @block.vector
                def _(vector, qt=qt):
                    vector.wait_ge(ctx.sACT.h, ctx.sACT.n)
                    ctx.sDVE.inc(vector.reciprocal(mx[:, 3:4], mx[:, 2:3]))
                    vector.wait_ge(ctx.sDVE.h, ctx.sDVE.n)
                    ctx.sDVE.inc(vector.tensor_scalar(pbuf[:], pbuf[:],
                                                      mx[:, 3:4], None, OP.mult))

                dve_pg = ctx.sDVE.n
                groups = [(0, 7), (7, 14), (14, 21), (21, 28), (28, 32)]

                def emit_pv_group(gi):
                    gb0, gb1 = groups[gi]
                    @block.tensor
                    def _(tensor, gi=gi, gb0=gb0, gb1=gb1):
                        tensor.wait_ge(ctx.sDVE.h, dve_pg + gi + 1)
                        for kb in range(gb0, gb1):
                            ins = tensor.matmul(ps[:, 7 * 512:7 * 512 + 128],
                                                vh[:, kb * 128:(kb + 1) * 128],
                                                pbuf[:, kb * 128:(kb + 1) * 128],
                                                start=(kb == 0), stop=(kb == KB128 - 1))
                        ctx.sPE.inc(ins)

                for r, (b0, b1) in enumerate(groups):
                    nblk = b1 - b0
                    @block.tensor
                    def _(tensor, r=r, b0=b0, nblk=nblk):
                        if r == 0:
                            tensor.wait_ge(ctx.sDVE.h, dve_pg)
                        else:
                            tensor.wait_ge(ctx.sDVE.h, dve_pg + r)
                        for j in range(nblk):
                            ins = tensor.transpose(
                                ps[:, j * 512:j * 512 + 128],
                                pbuf[:, (b0 + j) * 128:(b0 + j + 1) * 128], ident[:])
                        ctx.sPE.inc(ins)

                    @block.vector
                    def _(vector, r=r, b0=b0, nblk=nblk):
                        vector.wait_ge(ctx.sPE.h, ctx.sPE.n)
                        src = ps[:, 0:nblk * 512].rearrange(
                            "p (b c) -> p b c", b=nblk)[:, :, 0:128]
                        dst = pbuf[:, b0 * 128:(b0 + nblk) * 128].rearrange(
                            "p (b c) -> p b c", b=nblk)
                        ctx.sDVE.inc(vector.tensor_copy(dst, src))

                    if r >= 1:
                        emit_pv_group(r - 1)
                emit_pv_group(len(groups) - 1)

                @block.vector
                def _(vector, qt=qt):
                    vector.wait_ge(ctx.sPE.h, ctx.sPE.n)
                    vector.wait_ge(ctx.qG.h, ctx.qG.n)
                    ctx.sDVE.inc(vector.tensor_copy(ostg[:], ps[:, 7 * 512:7 * 512 + 128]))

                dve_st = ctx.sDVE.n
                @block.gpsimd
                def _(gpsimd, h=h, q0=q0):
                    gpsimd.wait_ge(ctx.sDVE.h, dve_st)
                    ctx.qG.inc(gpsimd.dma_start(
                        out=attn_scr[h * 128:(h + 1) * 128, q0:q0 + 128],
                        in_=ostg[:]), 16)

    emit_phase_barrier(ctx)

    with contextlib.ExitStack() as p4:
        asb = p4.enter_context(nc.sbuf_tensor("a4_a", [128, DC * T], F32))
        xr = p4.enter_context(nc.sbuf_tensor("a4_x", [128, DC * T], F32))
        wbuf = p4.enter_context(nc.sbuf_tensor("a4_w", [128, 2 * DC * 128], F32))
        sq_a = p4.enter_context(nc.sbuf_tensor("a4_qa", [128, T], F32))
        sq_b = p4.enter_context(nc.sbuf_tensor("a4_qb", [128, T], F32))

        for ch in range(TOK // T):
            t0 = ch * T
            dve_prev = ctx.sDVE.n
            qg_prev = ctx.qG.n
            @block.sync
            def _(sync):
                sync.wait_ge(ctx.sDVE.h, dve_prev)
                sync.wait_ge(ctx.qG.h, qg_prev)
                for c in range(DC):
                    ctx.qS.inc(sync.dma_start(
                        out=asb[:, c * T:(c + 1) * T],
                        in_=attn_scr[c * 128:(c + 1) * 128, t0:t0 + T]), 16)
                    ctx.qS.inc(sync.dma_start(
                        out=xr[:, c * T:(c + 1) * T],
                        in_=xq_dram[c * 128:(c + 1) * 128, t0:t0 + T]), 16)

            pe_b = ctx.sPE.n
            dve_b = ctx.sDVE.n
            for o in range(DC):
                half = o % 2
                wdst = wbuf[:, half * DC * 128:(half + 1) * DC * 128]
                @block.sync
                def _(sync, o=o, wdst=wdst):
                    sync.wait_ge(ctx.sPE.h, pe_b + max(0, o - 1))
                    ctx.qS.inc(sync.dma_start(
                        out=wdst, in_=wo_dram[:, o * DC * 128:(o + 1) * DC * 128]), 16)

                @block.tensor
                def _(tensor, o=o, wdst=wdst):
                    tensor.wait_ge(ctx.qS.h, ctx.qS.n)
                    if o == 0:
                        tensor.wait_ge(ctx.sDVE.h, dve_b)
                    if o >= 4:
                        tensor.wait_ge(ctx.sDVE.h, dve_b + o - 3)
                    for c in range(DC):
                        ins = tensor.matmul(ps[:, (o % 4) * 512:(o % 4) * 512 + T],
                                            wdst[:, c * 128:(c + 1) * 128],
                                            asb[:, c * T:(c + 1) * T],
                                            start=(c == 0), stop=(c == DC - 1))
                    ctx.sPE.inc(ins)

                @block.vector
                def _(vector, o=o):
                    vector.wait_ge(ctx.sPE.h, pe_b + o + 1)
                    ctx.sDVE.inc(vector.tensor_tensor(
                        xr[:, o * T:(o + 1) * T],
                        ps[:, (o % 4) * 512:(o % 4) * 512 + T],
                        xr[:, o * T:(o + 1) * T], OP.add))

            s_tiles = [xr[:, c * T:(c + 1) * T] for c in range(DC)]
            emit_ln(ctx, T, s_tiles, s_tiles, sq_a[:], sq_b[:], ps, 0, ones2, rows)

            dve_done = ctx.sDVE.n
            @block.gpsimd
            def _(gpsimd):
                gpsimd.wait_ge(ctx.sDVE.h, dve_done)
                for c in range(DC):
                    ctx.qG.inc(gpsimd.dma_start(
                        out=out_dram[c * 128:(c + 1) * 128, t0:t0 + T],
                        in_=xr[:, c * T:(c + 1) * T]), 16)


def xpos_ops(ctx, vector, dst, t, cs, sn, u, u2):
    ctx.sDVE.inc(vector.tensor_copy(u2[0:64, :], t[64:128, :]))
    ctx.sDVE.inc(vector.tensor_copy(u2[64:128, :], t[0:64, :]))
    ctx.sDVE.inc(vector.tensor_tensor(u, t, cs, OP.mult))
    vector.wait_ge(ctx.sDVE.h, ctx.sDVE.n)
    ctx.sDVE.inc(vector.tensor_tensor(dst, u2, sn, OP.mult))
    vector.wait_ge(ctx.sDVE.h, ctx.sDVE.n)
    ctx.sDVE.inc(vector.tensor_tensor(dst[0:64, :], u[0:64, :], dst[0:64, :], OP.subtract))
    ctx.sDVE.inc(vector.tensor_tensor(dst[64:128, :], u[64:128, :], dst[64:128, :], OP.add))


def emit_norm_rows(ctx, ps_sum_row, ps_ssq_row, denom, eps, rows):
    block = ctx.block
    r_mu, r_m2, r_A, r_B = rows

    @block.vector
    def _(vector):
        vector.wait_ge(ctx.sPE.h, ctx.sPE.n)
        ctx.sDVE.inc(vector.tensor_scalar(r_mu, ps_sum_row, 1.0 / denom, None, OP.mult))
        ctx.sDVE.inc(vector.tensor_scalar(r_m2, ps_ssq_row, 1.0 / denom, float(eps),
                                          OP.mult, OP.add))
        vector.wait_ge(ctx.sDVE.h, ctx.sDVE.n)
        ctx.sDVE.inc(vector.tensor_tensor(r_A, r_mu, r_mu, OP.mult))
        vector.wait_ge(ctx.sDVE.h, ctx.sDVE.n)
        ctx.sDVE.inc(vector.tensor_tensor(r_m2, r_m2, r_A, OP.subtract))

    @block.scalar
    def _(scalar):
        scalar.wait_ge(ctx.sDVE.h, ctx.sDVE.n)
        ctx.sACT.inc(scalar.activation(r_A, r_m2, AF.Ln, bias=0.0, scale=1.0))
        scalar.wait_ge(ctx.sACT.h, ctx.sACT.n)
        ctx.sACT.inc(scalar.activation(r_A, r_A, AF.Exp, bias=0.0, scale=-0.5))

    @block.vector
    def _(vector):
        vector.wait_ge(ctx.sACT.h, ctx.sACT.n)
        ctx.sDVE.inc(vector.scalar_tensor_tensor(r_B, r_mu, -1.0, r_A,
                                                 OP.mult, OP.mult))


def emit_l1attn(nc, block, ctx, x_dram, out_dram, wq_dram, bq_dram, wk_dram,
                bk_dram, wv_dram, bvr_dram, wg_dram, bg_dram, wo_dram, bo_dram,
                csq_dram, snq_dram, csk_dram, snk_dram, gcol_dram, arow_dram,
                mask_dram, ones_dram, kin, vin, kgo, vgo, stack):
    ones2 = stack.enter_context(nc.sbuf_tensor("r_on", [128, 128], F32))
    gcol = stack.enter_context(nc.sbuf_tensor("r_gc", [128, 16 * 272], F32))
    abcast = stack.enter_context(nc.sbuf_tensor("r_ab", [128, 16 * 128], F32))
    masks = stack.enter_context(nc.sbuf_tensor("r_mk", [128, 16 * 2 * 128], F32))
    bvec = stack.enter_context(nc.sbuf_tensor("r_bvec", [128, 64], F32))
    rows_t = stack.enter_context(nc.sbuf_tensor("r_rw", [1, 4 * 512], F32))
    ps = stack.enter_context(nc.psum_tensor("r_ps", [128, 8 * 512], F32))

    rows = [rows_t[0:1, i * 512:(i + 1) * 512] for i in range(4)]
    rows256 = [rows_t[0:1, i * 512:i * 512 + 256] for i in range(4)]
    bq_c = lambda h: bvec[:, h:h + 1]
    bk_c = lambda h: bvec[:, 16 + h:16 + h + 1]
    bg_c = lambda h: bvec[:, 32 + h:32 + h + 1]
    bo_c = lambda o: bvec[:, 48 + o:48 + o + 1]
    gcol_idx = lambda h, j, kt: h * 272 + j * j + j + kt

    @block.sync
    def _(sync):
        for dst, src in [(ones2[:], ones_dram[:, :]), (gcol[:], gcol_dram[:, :]),
                         (abcast[:], arow_dram[:, :]), (masks[:], mask_dram[:, :]),
                         (bvec[:, 0:16], bq_dram[:, :]), (bvec[:, 16:32], bk_dram[:, :]),
                         (bvec[:, 32:48], bg_dram[:, :]), (bvec[:, 48:64], bo_dram[:, :])]:
            ctx.qS.inc(sync.dma_start(out=dst, in_=src), 16)

    with contextlib.ExitStack() as pk:
        T = 512
        xs = pk.enter_context(nc.sbuf_tensor("rk_x", [128, DC * T], F32))
        wbuf = pk.enter_context(nc.sbuf_tensor("rk_w", [128, 2 * DC * 128], F32))
        kst = pk.enter_context(nc.sbuf_tensor("rk_k", [128, T], F32))
        ktmp = pk.enter_context(nc.sbuf_tensor("rk_kt", [128, T], F32))
        utmp = pk.enter_context(nc.sbuf_tensor("rk_u", [128, T], F32))
        u2tmp = pk.enter_context(nc.sbuf_tensor("rk_u2", [128, T], F32))
        vstage = pk.enter_context(nc.sbuf_tensor("rk_v", [128, 4 * D], F32))
        csk = pk.enter_context(nc.sbuf_tensor("rk_rc", [128, TOK], F32))
        snk = pk.enter_context(nc.sbuf_tensor("rk_rs", [128, TOK], F32))
        bvb = pk.enter_context(nc.sbuf_tensor("rk_bvb", [128, D], F32))
        bvrow = pk.enter_context(nc.sbuf_tensor("rk_bvr", [1, D], F32))

        @block.sync
        def _(sync):
            ctx.qS.inc(sync.dma_start(out=csk[:], in_=csk_dram[:, :]), 16)
            ctx.qS.inc(sync.dma_start(out=snk[:], in_=snk_dram[:, :]), 16)
            ctx.qS.inc(sync.dma_start(out=bvrow[:], in_=bvr_dram[:, :]), 16)

        @block.tensor
        def _(tensor):
            tensor.wait_ge(ctx.qS.h, ctx.qS.n)
            for i in range(4):
                ctx.sPE.inc(tensor.matmul(ps[:, i * 512:(i + 1) * 512],
                                          ones2[0:1, :],
                                          bvrow[0:1, i * 512:(i + 1) * 512],
                                          start=True, stop=True))

        @block.vector
        def _(vector):
            vector.wait_ge(ctx.sPE.h, ctx.sPE.n)
            for i in range(4):
                ctx.sDVE.inc(vector.tensor_copy(bvb[:, i * 512:(i + 1) * 512],
                                                ps[:, i * 512:(i + 1) * 512]))

        for ch in range(TOK // T):
            t0 = ch * T
            dve_prev = ctx.sDVE.n
            @block.sync
            def _(sync):
                sync.wait_ge(ctx.sDVE.h, dve_prev)
                for c in range(DC):
                    ctx.qS.inc(sync.dma_start(
                        out=xs[:, c * T:(c + 1) * T],
                        in_=x_dram[c * 128:(c + 1) * 128, t0:t0 + T]), 16)

            pe_sw = ctx.sPE.n
            half_user = {0: pe_sw, 1: pe_sw}
            for h in range(N_HEADS):
                half = h % 2
                wslc = wbuf[:, half * DC * 128:(half + 1) * DC * 128]

                @block.sync
                def _(sync, h=h, wslc=wslc, wait_n=half_user[half]):
                    sync.wait_ge(ctx.sPE.h, wait_n)
                    ctx.qS.inc(sync.dma_start(
                        out=wslc, in_=wk_dram[:, h * DC * 128:(h + 1) * DC * 128]), 16)

                @block.tensor
                def _(tensor, h=h, wslc=wslc):
                    tensor.wait_ge(ctx.qS.h, ctx.qS.n)
                    tensor.wait_ge(ctx.sDVE.h, ctx.sDVE.n)
                    for c in range(DC):
                        ins = tensor.matmul(ps[:, (h % 4) * 512:(h % 4) * 512 + T],
                                            wslc[:, c * 128:(c + 1) * 128],
                                            xs[:, c * T:(c + 1) * T],
                                            start=(c == 0), stop=(c == DC - 1))
                    ctx.sPE.inc(ins)
                half_user[half] = ctx.sPE.n

                @block.vector
                def _(vector, h=h, t0=t0):
                    vector.wait_ge(ctx.sPE.h, ctx.sPE.n)
                    vector.wait_ge(ctx.qG.h, ctx.qG.n)
                    pk_ = ps[:, (h % 4) * 512:(h % 4) * 512 + T]
                    ctx.sDVE.inc(vector.tensor_scalar(ktmp[:], pk_, bk_c(h), None, OP.add))
                    vector.wait_ge(ctx.sDVE.h, ctx.sDVE.n)
                    xpos_ops(ctx, vector, kst[:], ktmp[:],
                             csk[:, t0:t0 + T], snk[:, t0:t0 + T], utmp[:], u2tmp[:])

                dve_st = ctx.sDVE.n
                @block.gpsimd
                def _(gpsimd, h=h, t0=t0):
                    gpsimd.wait_ge(ctx.sDVE.h, dve_st)
                    ctx.qG.inc(gpsimd.dma_start(
                        out=kin[h * 128:(h + 1) * 128, t0:t0 + T], in_=kst[:]), 16)
                emit_phase_barrier(ctx)

            pe_sw = ctx.sPE.n
            half_user = {0: pe_sw, 1: pe_sw}
            qg_v = ctx.qG.n
            for h in range(N_HEADS):
                half = h % 2
                wslc = wbuf[:, half * DC * 128:(half + 1) * DC * 128]

                @block.sync
                def _(sync, h=h, wslc=wslc, wait_n=half_user[half]):
                    sync.wait_ge(ctx.sPE.h, wait_n)
                    ctx.qS.inc(sync.dma_start(
                        out=wslc, in_=wv_dram[:, h * DC * 128:(h + 1) * DC * 128]), 16)

                for tb in range(4):
                    @block.tensor
                    def _(tensor, h=h, tb=tb, wslc=wslc):
                        tensor.wait_ge(ctx.qS.h, ctx.qS.n)
                        tensor.wait_ge(ctx.sDVE.h, ctx.sDVE.n)
                        for c in range(DC):
                            ins = tensor.matmul(
                                ps[:, tb * 512:tb * 512 + 128],
                                xs[:, c * T + tb * 128:c * T + (tb + 1) * 128],
                                wslc[:, c * 128:(c + 1) * 128],
                                start=(c == 0), stop=(c == DC - 1))
                        ctx.sPE.inc(ins)

                    @block.vector
                    def _(vector, h=h, tb=tb):
                        vector.wait_ge(ctx.sPE.h, ctx.sPE.n)
                        if h == 0 and tb == 0:
                            vector.wait_ge(ctx.qG.h, qg_v)
                        ctx.sDVE.inc(vector.tensor_tensor(
                            vstage[:, tb * D + h * 128:tb * D + (h + 1) * 128],
                            ps[:, tb * 512:tb * 512 + 128],
                            bvb[:, h * 128:(h + 1) * 128], OP.add))
                half_user[half] = ctx.sPE.n

            dve_st = ctx.sDVE.n
            @block.gpsimd
            def _(gpsimd, t0=t0):
                gpsimd.wait_ge(ctx.sDVE.h, dve_st)
                for tb in range(4):
                    ctx.qG.inc(gpsimd.dma_start(
                        out=vin[t0 + tb * 128:t0 + (tb + 1) * 128, :],
                        in_=vstage[:, tb * D:(tb + 1) * D]), 16)

        qg_all = ctx.qG.n
        @block.gpsimd
        def _(gpsimd):
            gpsimd.wait_ge(ctx.qG.h, qg_all)
            ctx.qA.inc(gpsimd.collective_compute(
                "AllGather", OP.bypass,
                replica_groups=[[0, 1], [2, 3], [4, 5], [6, 7]],
                ins=[kin.opt()], outs=[kgo.opt()]), 1)
            ctx.qA.inc(gpsimd.collective_compute(
                "AllGather", OP.bypass,
                replica_groups=[[0, 1], [2, 3], [4, 5], [6, 7]],
                ins=[vin.opt()], outs=[vgo.opt()]), 1)

    emit_phase_barrier(ctx)

    with contextlib.ExitStack() as pq:
        T = 256
        xs = pq.enter_context(nc.sbuf_tensor("rq_x", [128, DC * T], F32))
        wbuf = pq.enter_context(nc.sbuf_tensor("rq_w", [128, 2 * DC * 128], F32))
        qall = pq.enter_context(nc.sbuf_tensor("rq_q", [128, 16 * T], F32))
        gall = pq.enter_context(nc.sbuf_tensor("rq_g", [128, 16 * T], F32))
        retall = pq.enter_context(nc.sbuf_tensor("rq_r", [128, 16 * T], F32))
        kstrip = pq.enter_context(nc.sbuf_tensor("rq_ks", [128, 32 * 128], F32))
        vstrip = pq.enter_context(nc.sbuf_tensor("rq_vs", [128, 32 * 128], F32))
        csq = pq.enter_context(nc.sbuf_tensor("rq_rc", [128, TOK], F32))
        snq = pq.enter_context(nc.sbuf_tensor("rq_rs", [128, TOK], F32))
        ttmp = pq.enter_context(nc.sbuf_tensor("rq_t", [128, T], F32))
        utmp = pq.enter_context(nc.sbuf_tensor("rq_u", [128, T], F32))
        u2tmp = pq.enter_context(nc.sbuf_tensor("rq_u2", [128, T], F32))
        pts = pq.enter_context(nc.sbuf_tensor("rq_pt", [128, 4 * 128], F32))
        sq_a = pq.enter_context(nc.sbuf_tensor("rq_qa", [128, T], F32))
        sq_b = pq.enter_context(nc.sbuf_tensor("rq_qb", [128, T], F32))

        @block.sync
        def _(sync):
            ctx.qS.inc(sync.dma_start(out=csq[:], in_=csq_dram[:, :]), 16)
            ctx.qS.inc(sync.dma_start(out=snq[:], in_=snq_dram[:, :]), 16)
            sync.wait_ge(ctx.qA.h, 2)

        for ch in range(TOK // T):
            t0 = ch * T
            nloc = 2 * ch + 2
            dve_prev = ctx.sDVE.n
            qg_prev = ctx.qG.n
            @block.sync
            def _(sync):
                sync.wait_ge(ctx.sDVE.h, dve_prev)
                sync.wait_ge(ctx.qG.h, qg_prev)
                for c in range(DC):
                    ctx.qS.inc(sync.dma_start(
                        out=xs[:, c * T:(c + 1) * T],
                        in_=x_dram[c * 128:(c + 1) * 128, t0:t0 + T]), 16)

            @block.vector
            def _(vector, qg_prev=qg_prev):
                vector.wait_ge(ctx.qG.h, qg_prev)

            pe_sw = ctx.sPE.n
            half_user = {0: pe_sw, 1: pe_sw}
            for h in range(N_HEADS):
                half = h % 2
                wq_s = wbuf[:, half * DC * 128:(half + 1) * DC * 128]
                wg_s = wbuf[:, (1 - half) * DC * 128:(2 - half) * DC * 128]

                @block.sync
                def _(sync, h=h, wq_s=wq_s, wait_n=half_user[half]):
                    sync.wait_ge(ctx.sPE.h, wait_n)
                    ctx.qS.inc(sync.dma_start(
                        out=wq_s, in_=wq_dram[:, h * DC * 128:(h + 1) * DC * 128]), 16)

                @block.tensor
                def _(tensor, h=h, wq_s=wq_s):
                    tensor.wait_ge(ctx.qS.h, ctx.qS.n)
                    tensor.wait_ge(ctx.sDVE.h, ctx.sDVE.n)
                    for c in range(DC):
                        ins = tensor.matmul(ps[:, 0 * 512:0 * 512 + T],
                                            wq_s[:, c * 128:(c + 1) * 128],
                                            xs[:, c * T:(c + 1) * T],
                                            start=(c == 0), stop=(c == DC - 1))
                    ctx.sPE.inc(ins)
                half_user[half] = ctx.sPE.n

                @block.vector
                def _(vector, h=h, t0=t0):
                    vector.wait_ge(ctx.sPE.h, ctx.sPE.n)
                    vector.wait_ge(ctx.sACT.h, ctx.sACT.n)
                    ctx.sDVE.inc(vector.tensor_scalar(ttmp[:], ps[:, 0:T],
                                                      bq_c(h), None, OP.add))
                    vector.wait_ge(ctx.sDVE.h, ctx.sDVE.n)
                    xpos_ops(ctx, vector, qall[:, h * T:(h + 1) * T], ttmp[:],
                             csq[:, t0:t0 + T], snq[:, t0:t0 + T], utmp[:], u2tmp[:])

                @block.sync
                def _(sync, h=h, wg_s=wg_s, wait_n=half_user[1 - half]):
                    sync.wait_ge(ctx.sPE.h, wait_n)
                    ctx.qS.inc(sync.dma_start(
                        out=wg_s, in_=wg_dram[:, h * DC * 128:(h + 1) * DC * 128]), 16)

                @block.tensor
                def _(tensor, h=h, wg_s=wg_s):
                    tensor.wait_ge(ctx.qS.h, ctx.qS.n)
                    tensor.wait_ge(ctx.sDVE.h, ctx.sDVE.n)
                    for c in range(DC):
                        ins = tensor.matmul(ps[:, 1 * 512:1 * 512 + T],
                                            wg_s[:, c * 128:(c + 1) * 128],
                                            xs[:, c * T:(c + 1) * T],
                                            start=(c == 0), stop=(c == DC - 1))
                    ctx.sPE.inc(ins)
                half_user[1 - half] = ctx.sPE.n

                @block.vector
                def _(vector, h=h):
                    vector.wait_ge(ctx.sPE.h, ctx.sPE.n)
                    vector.wait_ge(ctx.sACT.h, ctx.sACT.n)
                    ctx.sDVE.inc(vector.tensor_scalar(
                        ttmp[:], ps[:, 1 * 512:1 * 512 + T], bg_c(h), None, OP.add))

                @block.scalar
                def _(scalar, h=h):
                    scalar.wait_ge(ctx.sDVE.h, ctx.sDVE.n)
                    ctx.sACT.inc(scalar.activation(gall[:, h * T:(h + 1) * T],
                                                   ttmp[:], AF.Silu,
                                                   bias=0.0, scale=1.0))
                emit_phase_barrier(ctx)

            koff = lambda kt: (0 if kt % 2 == 0 else 2048) + (kt // 2) * 128
            for h in range(N_HEADS):
                pe_h = ctx.sPE.n
                @block.sync
                def _(sync, h=h, nloc=nloc, wait_n=pe_h):
                    sync.wait_ge(ctx.sPE.h, wait_n)
                    ctx.qS.inc(sync.dma_start(
                        out=kstrip[:, 0:nloc * 128],
                        in_=kgo[h * 128:(h + 1) * 128, 0:nloc * 128]), 16)
                    ctx.qS.inc(sync.dma_start(
                        out=kstrip[:, 2048:2048 + nloc * 128],
                        in_=kgo[D + h * 128:D + (h + 1) * 128, 0:nloc * 128]), 16)
                    ctx.qS.inc(sync.dma_start(
                        out=vstrip[:, 0:nloc * 128].rearrange("p (l m) -> p l m", l=nloc),
                        in_=vgo[0:nloc * 128, h * 128:(h + 1) * 128].rearrange(
                            "(l p) m -> p l m", p=128)), 16)
                    ctx.qS.inc(sync.dma_start(
                        out=vstrip[:, 2048:2048 + nloc * 128].rearrange("p (l m) -> p l m", l=nloc),
                        in_=vgo[TOK:TOK + nloc * 128, h * 128:(h + 1) * 128].rearrange(
                            "(l p) m -> p l m", p=128)), 16)

                for j in range(2):
                    jl = 2 * ch + j
                    ktmax = 2 * jl + 1
                    dve_bj = ctx.sDVE.n
                    pe_bj = ctx.sPE.n
                    qrhs = qall[:, h * T + j * 128:h * T + (j + 1) * 128]
                    score_done = {}
                    evac_done = {}
                    pv_done = {}

                    def emit_score(kt):
                        sbank = 2 + kt % 4
                        @block.tensor
                        def _(tensor, kt=kt, sbank=sbank, qrhs=qrhs):
                            if kt == 0:
                                tensor.wait_ge(ctx.qS.h, ctx.qS.n)
                                tensor.wait_ge(ctx.sDVE.h, dve_bj)
                            if kt >= 4:
                                tensor.wait_ge(ctx.sDVE.h, evac_done[kt - 4])
                            ctx.sPE.inc(tensor.matmul(
                                ps[:, sbank * 512:sbank * 512 + 128],
                                kstrip[:, koff(kt):koff(kt) + 128],
                                qrhs, start=True, stop=True))
                        score_done[kt] = ctx.sPE.n

                    def emit_evac(kt):
                        sbank = 2 + kt % 4
                        @block.vector
                        def _(vector, h=h, jl=jl, kt=kt, sbank=sbank):
                            vector.wait_ge(ctx.sPE.h, score_done[kt])
                            if kt >= 4:
                                vector.wait_ge(ctx.sPE.h, pv_done[kt - 4])
                            else:
                                vector.wait_ge(ctx.sPE.h, pe_bj)
                            slot = pts[:, (kt % 4) * 128:(kt % 4 + 1) * 128]
                            sc = ps[:, sbank * 512:sbank * 512 + 128]
                            if kt < 2 * jl:
                                ctx.sDVE.inc(vector.tensor_scalar(
                                    slot, sc, gcol[:, gcol_idx(h, jl, kt):
                                                   gcol_idx(h, jl, kt) + 1],
                                    None, OP.mult))
                            else:
                                w = kt - 2 * jl
                                ctx.sDVE.inc(vector.tensor_tensor(
                                    slot, sc,
                                    masks[:, (h * 2 + w) * 128:(h * 2 + w + 1) * 128],
                                    OP.mult))
                        evac_done[kt] = ctx.sDVE.n

                    def emit_pv(kt):
                        @block.tensor
                        def _(tensor, j=j, kt=kt, ktmax=ktmax):
                            tensor.wait_ge(ctx.sDVE.h, evac_done[kt])
                            ctx.sPE.inc(tensor.matmul(
                                ps[:, (6 + j) * 512:(6 + j) * 512 + 128],
                                vstrip[:, koff(kt):koff(kt) + 128],
                                pts[:, (kt % 4) * 128:(kt % 4 + 1) * 128],
                                start=(kt == 0), stop=(kt == ktmax)))
                        pv_done[kt] = ctx.sPE.n

                    for kt in range(ktmax + 1):
                        emit_score(kt)
                        if kt >= 2:
                            emit_pv(kt - 2)
                        emit_evac(kt)
                    for kt in (ktmax - 1, ktmax):
                        if kt >= 0:
                            emit_pv(kt)

                    @block.vector
                    def _(vector, h=h, j=j, ktmax=ktmax, done=ctx.sPE.n):
                        vector.wait_ge(ctx.sPE.h, done)
                        ctx.sDVE.inc(vector.tensor_tensor(
                            retall[:, h * T + j * 128:h * T + (j + 1) * 128],
                            ps[:, (6 + j) * 512:(6 + j) * 512 + 128],
                            abcast[:, h * 128:(h + 1) * 128], OP.mult))

            for h in range(N_HEADS):
                rslc = retall[:, h * T:(h + 1) * T]
                @block.tensor
                def _(tensor, h=h, rslc=rslc):
                    tensor.wait_ge(ctx.sDVE.h, ctx.sDVE.n)
                    ctx.sPE.inc(tensor.matmul(ps[0:1, 0:T], ones2[:, 0:1], rslc,
                                              start=True, stop=True))

                @block.vector
                def _(vector, h=h, rslc=rslc):
                    vector.wait_ge(ctx.sPE.h, ctx.sPE.n)
                    ctx.sDVE.inc(vector.tensor_tensor(sq_a[:], rslc, rslc, OP.mult))

                @block.tensor
                def _(tensor, h=h):
                    tensor.wait_ge(ctx.sDVE.h, ctx.sDVE.n)
                    ctx.sPE.inc(tensor.matmul(ps[0:1, 512:512 + T], ones2[:, 0:1],
                                              sq_a[:], start=True, stop=True))

                emit_norm_rows(ctx, ps[0:1, 0:T], ps[0:1, 512:512 + T], 128.0,
                               GN_EPS, rows256)

                @block.tensor
                def _(tensor, h=h):
                    tensor.wait_ge(ctx.sDVE.h, ctx.sDVE.n)
                    tensor.matmul(ps[:, 2 * 512:2 * 512 + T], ones2[0:1, :],
                                  rows256[2], start=True, stop=True)
                    ctx.sPE.inc(tensor.matmul(ps[:, 3 * 512:3 * 512 + T], ones2[0:1, :],
                                              rows256[3], start=True, stop=True))

                @block.vector
                def _(vector, h=h, rslc=rslc):
                    vector.wait_ge(ctx.sPE.h, ctx.sPE.n)
                    ctx.sDVE.inc(vector.tensor_tensor(
                        ttmp[:], rslc, ps[:, 2 * 512:2 * 512 + T], OP.mult))
                    vector.wait_ge(ctx.sDVE.h, ctx.sDVE.n)
                    ctx.sDVE.inc(vector.tensor_tensor(
                        ttmp[:], ttmp[:], ps[:, 3 * 512:3 * 512 + T], OP.add))
                    vector.wait_ge(ctx.sDVE.h, ctx.sDVE.n)
                    ctx.sDVE.inc(vector.tensor_tensor(
                        gall[:, h * T:(h + 1) * T], ttmp[:],
                        gall[:, h * T:(h + 1) * T], OP.mult))


            pe_sw = ctx.sPE.n
            half_user = {0: ctx.sPE.n, 1: ctx.sPE.n}
            for o in range(DC):
                half = o % 2
                wo_s = wbuf[:, half * DC * 128:(half + 1) * DC * 128]
                @block.sync
                def _(sync, o=o, wo_s=wo_s, wait_n=half_user[half]):
                    sync.wait_ge(ctx.sPE.h, wait_n)
                    ctx.qS.inc(sync.dma_start(
                        out=wo_s, in_=wo_dram[:, o * DC * 128:(o + 1) * DC * 128]), 16)

                @block.tensor
                def _(tensor, o=o, wo_s=wo_s):
                    tensor.wait_ge(ctx.qS.h, ctx.qS.n)
                    if o == 0:
                        tensor.wait_ge(ctx.sDVE.h, ctx.sDVE.n)
                    if o >= 4:
                        tensor.wait_ge(ctx.sDVE.h, dve_sw + o - 3)
                    for c in range(DC):
                        ins = tensor.matmul(ps[:, (o % 4) * 512:(o % 4) * 512 + T],
                                            wo_s[:, c * 128:(c + 1) * 128],
                                            gall[:, c * T:(c + 1) * T],
                                            start=(c == 0), stop=(c == DC - 1))
                    ctx.sPE.inc(ins)
                half_user[half] = ctx.sPE.n
                if o == 0:
                    dve_sw = ctx.sDVE.n

                @block.vector
                def _(vector, o=o):
                    vector.wait_ge(ctx.sPE.h, ctx.sPE.n)
                    ctx.sDVE.inc(vector.scalar_tensor_tensor(
                        qall[:, o * T:(o + 1) * T],
                        ps[:, (o % 4) * 512:(o % 4) * 512 + T],
                        bo_c(o), xs[:, o * T:(o + 1) * T], OP.add, OP.add))


            s_tiles = [qall[:, c * T:(c + 1) * T] for c in range(DC)]
            emit_ln(ctx, T, s_tiles, s_tiles, sq_a[:], sq_b[:], ps, 0, ones2, rows256)

            dve_done = ctx.sDVE.n
            @block.gpsimd
            def _(gpsimd, t0=t0):
                gpsimd.wait_ge(ctx.sDVE.h, dve_done)
                for c in range(DC):
                    ctx.qG.inc(gpsimd.dma_start(
                        out=out_dram[c * 128:(c + 1) * 128, t0:t0 + T],
                        in_=qall[:, c * T:(c + 1) * T]), 16)


def own_token_tiles(core):
    par = core % 2
    return [2 * j + par for j in range(NT)]


def own_token_index(core):
    return np.concatenate([np.arange(g * 128, (g + 1) * 128)
                           for g in own_token_tiles(core)])


def vec_tiles(v):
    n = v.shape[0] // 128
    return np.ascontiguousarray(v.reshape(n, 128).T)


def lhsT_dev(W):
    din, dout = W.shape
    nc_, no_ = din // 128, dout // 128
    return np.ascontiguousarray(
        W.reshape(nc_, 128, no_, 128).transpose(1, 2, 0, 3).reshape(128, -1))


def lhsT_dev_k(W):
    din, dout = W.shape
    nc_, no_ = din // 128, dout // 128
    return np.ascontiguousarray(
        W.reshape(nc_, 128, no_, 128).transpose(1, 0, 2, 3).reshape(128, -1))


def build_l1ffn_program():
    nc = bass.Bass("TRN2", target_bir_lowering=False)
    x = nc.dram_tensor("x", [D, TOK], F32, kind="ExternalInput").ap()
    w0t = nc.dram_tensor("w0t", [128, IC * DC * 128], F32, kind="ExternalInput").ap()
    w1t = nc.dram_tensor("w1t", [128, IC * DC * 128], F32, kind="ExternalInput").ap()
    b0 = nc.dram_tensor("b0", [128, IC], F32, kind="ExternalInput").ap()
    b1 = nc.dram_tensor("b1", [128, DC], F32, kind="ExternalInput").ap()
    ones = nc.dram_tensor("ones", [128, 128], F32, kind="ExternalInput").ap()
    out = nc.dram_tensor("out", [D, TOK], F32, kind="ExternalOutput").ap()

    with contextlib.ExitStack() as stack:
        block = stack.enter_context(nc.Block())
        ctx = make_ctx(nc, block, stack)
        emit_l1ffn(nc, block, ctx, x, out, w0t, w1t, b0, b1, ones, stack)
        @block.gpsimd
        def _(gpsimd):
            gpsimd.wait_ge(ctx.qG.h, ctx.qG.n)
    return nc


def rope_tables(dim, positions, scale=1.0):
    inv_freq = (1.0 / (np.float32(10000.0) **
                       (np.arange(0, dim, 2, dtype=np.float32) / np.float32(dim))))
    inv_freq = inv_freq.astype(np.float32)
    pos = positions.astype(np.float32)
    ang = (pos[:, None] * inv_freq[None, :]).astype(np.float32)
    cos = np.cos(ang.astype(np.float64)).astype(np.float32) * np.float32(scale)
    sin = np.sin(ang.astype(np.float64)).astype(np.float32) * np.float32(scale)
    ct = np.concatenate([cos, cos], axis=1).T
    st = np.concatenate([sin, sin], axis=1).T
    return np.ascontiguousarray(ct), np.ascontiguousarray(st)


def build_l0attn_program():
    nc = bass.Bass("TRN2", target_bir_lowering=False)
    gi = lambda nm, shp: nc.dram_tensor(nm, shp, F32, kind="ExternalInput").ap()
    xk = gi("xk", [D, S])
    xq = gi("xq", [D, TOK])
    wcq = gi("wcq", [128, 6 * DC * 128])
    gq = gi("gq", [128, 6])
    wdq = gi("wdq", [128, 16 * 6 * 128])
    wdqr = gi("wdqr", [128, 16 * 6 * 64])
    wckv = gi("wckv", [128, 4 * DC * 128])
    gkv = gi("gkv", [128, 4])
    wdk = gi("wdk", [128, 16 * 4 * 128])
    wdv = gi("wdv", [128, 4 * 16 * 128])
    wkr = gi("wkr", [128, DC * 64])
    wo = gi("wo", [128, DC * DC * 128])
    rckv = gi("rckv", [64, S])
    rskv = gi("rskv", [64, S])
    rcq = gi("rcq", [64, TOK])
    rsq = gi("rsq", [64, TOK])
    ident = gi("ident", [128, 128])
    ones = gi("ones", [128, 128])
    attn_scr = nc.dram_tensor("attn_scr", [D, TOK], F32, kind="Internal").ap()
    qcn_scr = nc.dram_tensor("qcn_scr", [6 * 128, TOK], F32, kind="Internal").ap()
    out = nc.dram_tensor("out", [D, TOK], F32, kind="ExternalOutput").ap()

    with contextlib.ExitStack() as stack:
        block = stack.enter_context(nc.Block())
        ctx = make_ctx(nc, block, stack)
        emit_l0attn(nc, block, ctx, xk, xq, out, wcq, gq, wdq, wdqr, wckv,
                    gkv, wdk, wdv, wkr, wo, rckv, rskv, rcq, rsq, ident,
                    ones, attn_scr, qcn_scr, stack)
        @block.gpsimd
        def _(gpsimd):
            gpsimd.wait_ge(ctx.qG.h, ctx.qG.n)
    return nc


def stage_l0attn_shared(Wcq, g_q, Wdq, Wdqr, Wckv, g_kv, Wdk, Wdv, Wkr, Wo):
    rckv, rskv = rope_tables(64, np.arange(S))
    return {
        "wcq": lhsT_dev(Wcq),
        "gq": vec_tiles(g_q),
        "wdq": lhsT_dev(Wdq),
        "wdqr": np.ascontiguousarray(
            Wdqr.reshape(6, 128, 16, 64).transpose(1, 2, 0, 3).reshape(128, -1)),
        "wckv": lhsT_dev(Wckv),
        "gkv": vec_tiles(g_kv),
        "wdk": lhsT_dev(Wdk),
        "wdv": np.ascontiguousarray(
            Wdv.reshape(4, 128, 16, 128).transpose(1, 0, 2, 3).reshape(128, -1)),
        "wkr": np.ascontiguousarray(
            Wkr.reshape(16, 128, 64).transpose(1, 0, 2).reshape(128, -1)),
        "wo": lhsT_dev(Wo),
        "rckv": rckv, "rskv": rskv,
        "ident": np.ascontiguousarray(np.eye(128, dtype=np.float32)),
        "ones": np.ones((128, 128), np.float32),
    }


def run_l0attn(x_full, Wcq, g_q, Wdq, Wdqr, Wckv, g_kv, Wdk, Wdv, Wkr, Wo):
    nc = build_l0attn_program()
    shared = stage_l0attn_shared(Wcq, g_q, Wdq, Wdqr, Wckv, g_kv, Wdk, Wdv, Wkr, Wo)
    in_maps = []
    for c in range(NCORES):
        b = c // 2
        idx = own_token_index(c)
        rcq, rsq = rope_tables(64, idx, scale=1.0 / np.sqrt(192.0))
        in_maps.append({
            "xk": np.ascontiguousarray(x_full[b].T),
            "xq": np.ascontiguousarray(x_full[b][idx].T),
            "rcq": rcq, "rsq": rsq, **shared})
    res = run_bass_kernel_spmd(nc, in_maps, core_ids=list(range(NCORES)))
    out = np.zeros((B, S, D), np.float32)
    for c in range(NCORES):
        b = c // 2
        idx = own_token_index(c)
        out[b, idx] = res.results[c]["out"].T
    return out


def build_l1attn_program():
    nc = bass.Bass("TRN2", target_bir_lowering=False)
    gi = lambda nm, shp: nc.dram_tensor(nm, shp, F32, kind="ExternalInput").ap()
    x = gi("x", [D, TOK])
    wq = gi("wq", [128, 16 * DC * 128]); bq = gi("bq", [128, 16])
    wk = gi("wk", [128, 16 * DC * 128]); bk = gi("bk", [128, 16])
    wv = gi("wv", [128, 16 * DC * 128]); bvr = gi("bvr", [1, D])
    wg = gi("wg", [128, 16 * DC * 128]); bg = gi("bg", [128, 16])
    wo = gi("wo", [128, DC * DC * 128]); bo = gi("bo", [128, 16])
    csq = gi("csq", [128, TOK]); snq = gi("snq", [128, TOK])
    csk = gi("csk", [128, TOK]); snk = gi("snk", [128, TOK])
    gcol = gi("gcol", [128, 16 * 272])
    arow = gi("arow", [128, 16 * 128])
    mask = gi("mask", [128, 16 * 2 * 128])
    ones = gi("ones", [128, 128])
    kin = nc.dram_tensor("kin", [D, TOK], F32, kind="Internal").ap()
    vin = nc.dram_tensor("vin", [TOK, D], F32, kind="Internal").ap()
    kgo = nc.dram_tensor("kgo", [2 * D, TOK], F32, kind="Internal").ap()
    vgo = nc.dram_tensor("vgo", [2 * TOK, D], F32, kind="Internal").ap()
    out = nc.dram_tensor("out", [D, TOK], F32, kind="ExternalOutput").ap()

    with contextlib.ExitStack() as stack:
        block = stack.enter_context(nc.Block())
        ctx = make_ctx(nc, block, stack)
        emit_l1attn(nc, block, ctx, x, out, wq, bq, wk, bk, wv, bvr, wg, bg,
                    wo, bo, csq, snq, csk, snk, gcol, arow, mask, ones,
                    kin, vin, kgo, vgo, stack)
        @block.gpsimd
        def _(gpsimd):
            gpsimd.wait_ge(ctx.qG.h, ctx.qG.n)
    return nc


def xpos_tables(positions):
    dim, TH, SB = 128, 10000.0, 512.0
    inv_freq = (1.0 / (np.float32(TH) **
                       (np.arange(0, dim, 2, dtype=np.float32) / np.float32(dim))))
    inv_freq = inv_freq.astype(np.float32)
    pos = positions.astype(np.float32)
    fr = (pos[:, None] * inv_freq[None, :]).astype(np.float32)
    cos = np.cos(fr.astype(np.float64))
    sin = np.sin(fr.astype(np.float64))
    sbase = ((np.arange(0, dim, 2, dtype=np.float32) + np.float32(0.4 * dim))
             / np.float32(1.4 * dim)).astype(np.float32)
    power = ((pos - np.float32(S // 2)) / np.float32(SB)).astype(np.float32)
    scale = sbase.astype(np.float64)[None, :] ** power.astype(np.float64)[:, None]
    def lay(m):
        t = np.concatenate([m.T, m.T], axis=0)
        return np.ascontiguousarray(t.astype(np.float32))
    return (lay(cos * scale), lay(sin * scale),
            lay(cos / scale), lay(sin / scale))


def stage_l1attn_shared(Wq, bq, Wk, bk, Wv, bv, Wg, bg, Wo, bo):
    perm = np.concatenate([np.arange(0, 128, 2), np.arange(1, 128, 2)])
    colperm = np.concatenate([h * 128 + perm for h in range(16)])
    Wqp = np.ascontiguousarray(Wq[:, colperm])
    Wkp = np.ascontiguousarray(Wk[:, colperm])
    return {
        "wq": lhsT_dev(Wqp), "bq": vec_tiles(bq[colperm]),
        "wk": lhsT_dev(Wkp), "bk": vec_tiles(bk[colperm]),
        "wv": np.ascontiguousarray(
            Wv.reshape(16, 128, 16, 128).transpose(1, 2, 0, 3).reshape(128, -1)),
        "bvr": np.ascontiguousarray(bv[None, :]),
        "wg": lhsT_dev(Wg), "bg": vec_tiles(bg),
        "wo": lhsT_dev(Wo), "bo": vec_tiles(bo),
        "ones": np.ones((128, 128), np.float32),
    }


def stage_l1attn_core(core):
    par = core % 2
    idx = own_token_index(core)
    csq, snq, csk, snk = xpos_tables(idx)
    gam = (1.0 - 2.0 ** (-5.0 - np.arange(16, dtype=np.float64)))
    p = np.arange(128, dtype=np.float64)
    gcol = np.zeros((128, 16 * 272), np.float64)
    for h in range(16):
        for j in range(16):
            for kt in range(2 * j + 2):
                d = 128.0 * ((2 * j + par) - kt)
                gcol[:, h * 272 + j * j + j + kt] = gam[h] ** (d - p)
    qq = np.arange(128, dtype=np.float64)
    arowt = np.repeat(
        np.concatenate([gam[h] ** qq for h in range(16)])[None, :], 128, axis=0)
    q = np.arange(128, dtype=np.float64)
    mask = np.zeros((128, 16 * 2 * 128), np.float64)
    caus = (q[None, :] >= p[:, None])
    for h in range(16):
        diag = (gam[h] ** (-p))[:, None] * caus
        if par == 0:
            m0, m1 = diag, np.zeros((128, 128))
        else:
            m0 = np.repeat((gam[h] ** (128.0 - p))[:, None], 128, axis=1)
            m1 = diag
        mask[:, (h * 2) * 128:(h * 2 + 1) * 128] = m0
        mask[:, (h * 2 + 1) * 128:(h * 2 + 2) * 128] = m1
    return {
        "csq": csq, "snq": snq, "csk": csk, "snk": snk,
        "gcol": np.ascontiguousarray(gcol.astype(np.float32)),
        "arow": np.ascontiguousarray(arowt.astype(np.float32)),
        "mask": np.ascontiguousarray(mask.astype(np.float32)),
    }


def run_l1attn(x_full, Wq, bq, Wk, bk, Wv, bv, Wg, bg, Wo, bo):
    nc = build_l1attn_program()
    shared = stage_l1attn_shared(Wq, bq, Wk, bk, Wv, bv, Wg, bg, Wo, bo)
    in_maps = []
    for c in range(NCORES):
        b = c // 2
        idx = own_token_index(c)
        in_maps.append({"x": np.ascontiguousarray(x_full[b][idx].T),
                        **stage_l1attn_core(c), **shared})
    res = run_bass_kernel_spmd(nc, in_maps, core_ids=list(range(NCORES)))
    out = np.zeros((B, S, D), np.float32)
    for c in range(NCORES):
        b = c // 2
        idx = own_token_index(c)
        out[b, idx] = res.results[c]["out"].T
    return out


def hadamard128():
    H = np.array([[1.0]], np.float32)
    for _ in range(7):
        H = np.block([[H, H], [H, -H]])
    return np.ascontiguousarray(H.astype(np.float32))


def build_l0ffn_program():
    nc = bass.Bass("TRN2", target_bir_lowering=False)
    x = nc.dram_tensor("x", [D, TOK], F32, kind="ExternalInput").ap()
    wupt = nc.dram_tensor("wupt", [128, IC * DC * 128], F32, kind="ExternalInput").ap()
    wgtt = nc.dram_tensor("wgtt", [128, IC * DC * 128], F32, kind="ExternalInput").ap()
    wdnt = nc.dram_tensor("wdnt", [128, IC * DC * 128], F32, kind="ExternalInput").ap()
    had = nc.dram_tensor("had", [128, 128], F32, kind="ExternalInput").ap()
    su_ug = nc.dram_tensor("su_ug", [128, DC], F32, kind="ExternalInput").ap()
    su_dp = nc.dram_tensor("su_dp", [128, IC], F32, kind="ExternalInput").ap()
    ws_ug = nc.dram_tensor("ws_ug", [128, 2 * IC], F32, kind="ExternalInput").ap()
    ws_dp = nc.dram_tensor("ws_dp", [128, DC], F32, kind="ExternalInput").ap()
    ones = nc.dram_tensor("ones", [128, 128], F32, kind="ExternalInput").ap()
    out = nc.dram_tensor("out", [D, TOK], F32, kind="ExternalOutput").ap()

    with contextlib.ExitStack() as stack:
        block = stack.enter_context(nc.Block())
        ctx = make_ctx(nc, block, stack)
        emit_l0ffn(nc, block, ctx, x, out, wupt, wgtt, wdnt, had, su_ug,
                   su_dp, ws_ug, ws_dp, ones, stack)
        @block.gpsimd
        def _(gpsimd):
            gpsimd.wait_ge(ctx.qG.h, ctx.qG.n)
    return nc


def stage_l0ffn(SU_ug, SU_dp, Wscale_ug, Wscale_dp, W_up, W_gate, W_down):
    return {
        "wupt": lhsT_dev(np.ascontiguousarray(W_up.T)),
        "wgtt": lhsT_dev(np.ascontiguousarray(W_gate.T)),
        "wdnt": lhsT_dev_k(np.ascontiguousarray(W_down.T)),
        "had": hadamard128(),
        "su_ug": vec_tiles(SU_ug),
        "su_dp": vec_tiles(SU_dp),
        "ws_ug": vec_tiles(Wscale_ug),
        "ws_dp": vec_tiles(Wscale_dp),
        "ones": np.ones((128, 128), np.float32),
    }


def run_l0ffn(x_full, SU_ug, SU_dp, Wscale_ug, Wscale_dp, W_up, W_gate, W_down):
    xf = x_full.reshape(-1, D)
    nc = build_l0ffn_program()
    shared = stage_l0ffn(SU_ug, SU_dp, Wscale_ug, Wscale_dp, W_up, W_gate, W_down)
    in_maps = []
    for c in range(NCORES):
        b = c // 2
        idx = own_token_index(c) + b * S
        xc = np.ascontiguousarray(xf[idx].T)
        in_maps.append({"x": xc, **shared})
    res = run_bass_kernel_spmd(nc, in_maps, core_ids=list(range(NCORES)))
    out = np.zeros((B * S, D), np.float32)
    for c in range(NCORES):
        b = c // 2
        idx = own_token_index(c) + b * S
        out[idx] = res.results[c]["out"].T
    return out.reshape(x_full.shape)


def stage_l1ffn(W0, bias0, W1, bias1):
    return {
        "w0t": lhsT_dev(np.ascontiguousarray(W0.T)),
        "w1t": lhsT_dev_k(np.ascontiguousarray(W1.T)),
        "b0": vec_tiles(bias0),
        "b1": vec_tiles(bias1),
        "ones": np.ones((128, 128), np.float32),
    }


def run_l1ffn(x_full, W0, bias0, W1, bias1):
    xf = x_full.reshape(-1, D)
    nc = build_l1ffn_program()
    shared = stage_l1ffn(W0, bias0, W1, bias1)
    in_maps = []
    for c in range(NCORES):
        b = c // 2
        idx = own_token_index(c) + b * S
        xc = np.ascontiguousarray(xf[idx].T)
        in_maps.append({"x": xc, **shared})
    res = run_bass_kernel_spmd(nc, in_maps, core_ids=list(range(NCORES)))
    out = np.zeros((B * S, D), np.float32)
    for c in range(NCORES):
        b = c // 2
        idx = own_token_index(c) + b * S
        out[idx] = res.results[c]["out"].T
    return out.reshape(x_full.shape)


def build_fused_program():
    nc = bass.Bass("TRN2", target_bir_lowering=False)
    gi = lambda nm, shp: nc.dram_tensor(nm, shp, F32, kind="ExternalInput").ap()
    xk = gi("xk", [D, S])
    xq = gi("xq", [D, TOK])
    wcq = gi("wcq", [128, 6 * DC * 128]); gq = gi("gq", [128, 6])
    wdq = gi("wdq", [128, 16 * 6 * 128]); wdqr = gi("wdqr", [128, 16 * 6 * 64])
    wckv = gi("wckv", [128, 4 * DC * 128]); gkv = gi("gkv", [128, 4])
    wdk = gi("wdk", [128, 16 * 4 * 128]); wdv = gi("wdv", [128, 4 * 16 * 128])
    wkr = gi("wkr", [128, DC * 64]); wo = gi("wo", [128, DC * DC * 128])
    rckv = gi("rckv", [64, S]); rskv = gi("rskv", [64, S])
    rcq = gi("rcq", [64, TOK]); rsq = gi("rsq", [64, TOK])
    ident = gi("ident", [128, 128]); ones = gi("ones", [128, 128])
    wupt = gi("wupt", [128, IC * DC * 128]); wgtt = gi("wgtt", [128, IC * DC * 128])
    wdnt = gi("wdnt", [128, IC * DC * 128]); had = gi("had", [128, 128])
    su_ug = gi("su_ug", [128, DC]); su_dp = gi("su_dp", [128, IC])
    ws_ug = gi("ws_ug", [128, 2 * IC]); ws_dp = gi("ws_dp", [128, DC])
    wq = gi("wq", [128, 16 * DC * 128]); bq = gi("bq", [128, 16])
    wk = gi("wk", [128, 16 * DC * 128]); bk = gi("bk", [128, 16])
    wv = gi("wv", [128, 16 * DC * 128]); bvr = gi("bvr", [1, D])
    wg = gi("wg", [128, 16 * DC * 128]); bg = gi("bg", [128, 16])
    wo1 = gi("wo1", [128, DC * DC * 128]); bo = gi("bo", [128, 16])
    csq = gi("csq", [128, TOK]); snq = gi("snq", [128, TOK])
    csk = gi("csk", [128, TOK]); snk = gi("snk", [128, TOK])
    gcol = gi("gcol", [128, 16 * 272]); arow = gi("arow", [128, 16 * 128])
    mask = gi("mask", [128, 16 * 2 * 128])
    w0t = gi("w0t", [128, IC * DC * 128]); w1t = gi("w1t", [128, IC * DC * 128])
    b0 = gi("b0", [128, IC]); b1 = gi("b1", [128, DC])
    sc = lambda nm, shp: nc.dram_tensor(nm, shp, F32, kind="Internal").ap()
    attn_scr = sc("attn_scr", [D, TOK])
    qcn_scr = sc("qcn_scr", [6 * 128, TOK])
    x1s = sc("x1s", [D, TOK])
    x2s = sc("x2s", [D, TOK])
    x3s = sc("x3s", [D, TOK])
    kin = sc("kin", [D, TOK]); vin = sc("vin", [TOK, D])
    kgo = sc("kgo", [2 * D, TOK]); vgo = sc("vgo", [2 * TOK, D])
    out = nc.dram_tensor("out", [D, TOK], F32, kind="ExternalOutput").ap()

    with contextlib.ExitStack() as stack0:
        block = stack0.enter_context(nc.Block())
        ctxs = [make_ctx(nc, block, stack0) for _ in range(4)]

        with contextlib.ExitStack() as s1:
            emit_l0attn(nc, block, ctxs[0], xk, xq, x1s, wcq, gq, wdq, wdqr,
                        wckv, gkv, wdk, wdv, wkr, wo, rckv, rskv, rcq, rsq,
                        ident, ones, attn_scr, qcn_scr, s1)
        emit_phase_barrier(ctxs[0])

        with contextlib.ExitStack() as s2:
            emit_l0ffn(nc, block, ctxs[1], x1s, x2s, wupt, wgtt, wdnt, had,
                       su_ug, su_dp, ws_ug, ws_dp, ones, s2)
        emit_phase_barrier(ctxs[1])

        with contextlib.ExitStack() as s3:
            emit_l1attn(nc, block, ctxs[2], x2s, x3s, wq, bq, wk, bk, wv, bvr,
                        wg, bg, wo1, bo, csq, snq, csk, snk, gcol, arow, mask,
                        ones, kin, vin, kgo, vgo, s3)
        emit_phase_barrier(ctxs[2])

        with contextlib.ExitStack() as s4:
            emit_l1ffn(nc, block, ctxs[3], x3s, out, w0t, w1t, b0, b1, ones, s4)

        @block.gpsimd
        def _(gpsimd):
            gpsimd.wait_ge(ctxs[3].qG.h, ctxs[3].qG.n)
    return nc


def kernel(**inputs):
    x = np.asarray(inputs["L0_attn_x"], np.float32)
    nc = build_fused_program()
    shared = {}
    shared.update(stage_l0attn_shared(
        inputs["L0_attn_Wcq"], inputs["L0_attn_g_q"], inputs["L0_attn_Wdq"],
        inputs["L0_attn_Wdqr"], inputs["L0_attn_Wckv"], inputs["L0_attn_g_kv"],
        inputs["L0_attn_Wdk"], inputs["L0_attn_Wdv"], inputs["L0_attn_Wkr"],
        inputs["L0_attn_Wo"]))
    shared.update(stage_l0ffn(
        inputs["L0_ffn_SU_ug"], inputs["L0_ffn_SU_dp"], inputs["L0_ffn_Wscale_ug"],
        inputs["L0_ffn_Wscale_dp"], inputs["L0_ffn_W_up"], inputs["L0_ffn_W_gate"],
        inputs["L0_ffn_W_down"]))
    l1a = stage_l1attn_shared(
        inputs["L1_attn_Wq"], inputs["L1_attn_bq"], inputs["L1_attn_Wk"],
        inputs["L1_attn_bk"], inputs["L1_attn_Wv"], inputs["L1_attn_bv"],
        inputs["L1_attn_Wg"], inputs["L1_attn_bg"], inputs["L1_attn_Wo"],
        inputs["L1_attn_bo"])
    l1a["wo1"] = l1a.pop("wo")
    shared.update(l1a)
    shared.update(stage_l1ffn(inputs["L1_ffn_W0"], inputs["L1_ffn_bias0"],
                              inputs["L1_ffn_W1"], inputs["L1_ffn_bias1"]))
    in_maps = []
    for c in range(NCORES):
        b = c // 2
        idx = own_token_index(c)
        rcq, rsq = rope_tables(64, idx, scale=1.0 / np.sqrt(192.0))
        in_maps.append({
            "xk": np.ascontiguousarray(x[b].T),
            "xq": np.ascontiguousarray(x[b][idx].T),
            "rcq": rcq, "rsq": rsq,
            **stage_l1attn_core(c), **shared})
    res = run_bass_kernel_spmd(nc, in_maps, core_ids=list(range(NCORES)))
    outa = np.zeros((B, S, D), np.float32)
    for c in range(NCORES):
        b = c // 2
        idx = own_token_index(c)
        outa[b, idx] = res.results[c]["out"].T
    return outa


def kernel_unfused(**inputs):
    x = np.asarray(inputs["L0_attn_x"], np.float32)
    x1 = run_l0attn(x, inputs["L0_attn_Wcq"], inputs["L0_attn_g_q"],
                    inputs["L0_attn_Wdq"], inputs["L0_attn_Wdqr"],
                    inputs["L0_attn_Wckv"], inputs["L0_attn_g_kv"],
                    inputs["L0_attn_Wdk"], inputs["L0_attn_Wdv"],
                    inputs["L0_attn_Wkr"], inputs["L0_attn_Wo"])
    x2 = run_l0ffn(x1, inputs["L0_ffn_SU_ug"], inputs["L0_ffn_SU_dp"],
                   inputs["L0_ffn_Wscale_ug"], inputs["L0_ffn_Wscale_dp"],
                   inputs["L0_ffn_W_up"], inputs["L0_ffn_W_gate"],
                   inputs["L0_ffn_W_down"])
    x3 = run_l1attn(x2, inputs["L1_attn_Wq"], inputs["L1_attn_bq"],
                    inputs["L1_attn_Wk"], inputs["L1_attn_bk"],
                    inputs["L1_attn_Wv"], inputs["L1_attn_bv"],
                    inputs["L1_attn_Wg"], inputs["L1_attn_bg"],
                    inputs["L1_attn_Wo"], inputs["L1_attn_bo"])
    out = run_l1ffn(x3, inputs["L1_ffn_W0"], inputs["L1_ffn_bias0"],
                    inputs["L1_ffn_W1"], inputs["L1_ffn_bias1"])
    return out
